# version 07
# baseline (speedup 1.0000x reference)
import os
import numpy as np

import concourse.bass as bass
import concourse.mybir as mybir
from concourse.bass_utils import run_bass_kernel_spmd

F32 = mybir.dt.float32
BF16 = mybir.dt.bfloat16
I32 = mybir.dt.int32
AF = mybir.ActivationFunctionType
OP = mybir.AluOpType
AX = mybir.AxisListType.X

SIM_MODE = (os.environ.get("BASS_SIM_FP64") == "1"
            or os.environ.get("KERNEL_SIMROUND") == "1")

EPS = 1e-5
THR_M = 1e-5 * (EPS ** 0.5)
P = 128


class Sem:
    def __init__(self, handle):
        self.h = handle
        self.val = 0

    def add(self, n):
        self.val += n
        return self.val


def _ap_range(ap):
    t = ap.tensor
    isz = 2 if t.dtype == BF16 else 4
    span = 1
    dims = list(ap.ap)
    for step, count in dims[1:]:
        span += int(step) * (int(count) - 1)
    base = 0
    space = type(t).__name__
    r = getattr(t, "manual_sbuf_range", None)
    if r is not None:
        base = int(r[0])
    lo = base + int(ap.offset) * isz
    return (space, lo, lo + span * isz)


def _overlap(a, b):
    return a[0] == b[0] and a[1] < b[2] and b[1] < a[2]


class VecGuard:

    _OPS = {
        "tensor_copy": (1, [0]), "tensor_tensor": (2, [0]),
        "tensor_add": (2, [0]), "tensor_sub": (2, [0]),
        "tensor_mul": (2, [0]), "tensor_max": (2, [0]),
        "tensor_scalar": (3, [0]), "tensor_scalar_mul": (2, [0]),
        "tensor_scalar_add": (2, [0]), "tensor_scalar_max": (2, [0]),
        "scalar_tensor_tensor": (4, [0]), "reduce_sum": (2, [0]),
        "reduce_max": (2, [0]), "reciprocal": (2, [0]),
    }

    def __init__(self, eng, sem):
        self._e = eng
        self._sem = sem
        self._writes = []

    def _flush(self):
        if self._writes:
            self._writes = []
            self._e.engine_nop().then_inc(self._sem.h, 1)
            self._e.wait_ge(self._sem.h, self._sem.add(1))

    def wait_ge(self, *a, **k):
        return self._e.wait_ge(*a, **k)

    def memset(self, ap, v):
        i = self._e.memset(ap, v)
        self._writes.append(_ap_range(ap))
        return i

    def engine_nop(self):
        return self._e.engine_nop()

    def __getattr__(self, name):
        fn = getattr(self._e, name)
        spec = self._OPS.get(name)
        if spec is None:
            return fn
        npos, outpos = spec

        def wrapped(*args, **kwargs):
            ins = []
            outs = []
            for i, a in enumerate(args[:npos + 2]):
                if hasattr(a, "tensor"):
                    (outs if i in outpos else ins).append(_ap_range(a))
            for k_, a in kwargs.items():
                if hasattr(a, "tensor"):
                    (outs if k_ == "accum_out" else ins).append(_ap_range(a))
            if any(_overlap(r, w) for r in ins for w in self._writes):
                self._flush()
            res = fn(*args, **kwargs)
            self._writes.extend(outs)
            return res
        return wrapped


class ScalGuard(VecGuard):

    _OPS = {
        "activation": (2, [0]), "copy": (2, [0]),
        "add": (2, [0]), "mul": (2, [0]),
    }

    def __init__(self, eng, sem, scratch):
        super().__init__(eng, sem)
        self._scr = scratch

    def _flush(self):
        if self._writes:
            self._writes = []
            self._e.copy(self._scr[:], self._scr[:]).then_inc(self._sem.h, 1)
            self._e.wait_ge(self._sem.h, self._sem.add(1))


def build_nc(T, D, H, NS, dbg=False):
    TT = T // P
    DC = D // P
    HC = H // P
    NCH = NS // P
    SLAB = min(512, T)
    NSLAB = T // SLAB
    TPS = SLAB // P
    DS = min(512, D)
    NDS = D // DS
    NSEG = 4
    HCS = HC // NSEG
    HSEG = H // NSEG
    assert TT <= 64 and DC <= 64

    nc = bass.Bass("TRN2", target_bir_lowering=False,
                   detect_race_conditions=False)

    x_hbm = nc.dram_tensor("x", [T, D], F32, kind="ExternalInput")
    b_hbm = nc.dram_tensor("bw", [NS, D], F32, kind="ExternalInput")
    c_hbm = nc.dram_tensor("cw", [D, NS], F32, kind="ExternalInput")
    d_hbm = nc.dram_tensor("dv", [1, D], F32, kind="ExternalInput")
    gw_hbm = nc.dram_tensor("gw", [H, D], F32, kind="ExternalInput")
    vw_hbm = nc.dram_tensor("vw", [H, D], F32, kind="ExternalInput")
    ow_hbm = nc.dram_tensor("ow", [D, H], F32, kind="ExternalInput")
    gb_hbm = nc.dram_tensor("gb", [P, HC], F32, kind="ExternalInput")
    vb_hbm = nc.dram_tensor("vb", [P, HC], F32, kind="ExternalInput")
    ob_hbm = nc.dram_tensor("ob", [1, D], F32, kind="ExternalInput")
    id_hbm = nc.dram_tensor("idm", [P, P], F32, kind="ExternalInput")
    out_hbm = nc.dram_tensor("out", [T, D], F32, kind="ExternalOutput")

    _spk = "ExternalOutput" if dbg else "Internal"
    yc_sp = nc.dram_tensor("yc_sp", [T, D], BF16, kind=_spk)
    h_sp = nc.dram_tensor("h_sp", [TT, HC, P, P], F32, kind=_spk)
    part_sp = nc.dram_tensor("part_sp", [T, D], F32, kind=_spk)
    if dbg:
        state_d = nc.dram_tensor("state_d", [P, (NS // P) * T], BF16,
                                 kind="ExternalOutput")
        qyT_d = nc.dram_tensor("qyT_d", [P, (D // P) * T], BF16,
                               kind="ExternalOutput")
        rho_d = nc.dram_tensor("rho_d", [P, T], F32, kind="ExternalOutput")
        srep_d = nc.dram_tensor("srep_d", [P, T], F32, kind="ExternalOutput")
        gcol_d = nc.dram_tensor("gcol_d", [P, 16], F32, kind="ExternalOutput")
        sc3_d = nc.dram_tensor("sc3_d", [P, T // P], F32, kind="ExternalOutput")

    base0 = (int(nc.sbuf_base) + 63) & ~63
    top = int(nc.sbuf_top)
    _cur = [base0]

    def place(name, shape, dtype, at=None):
        nbytes = int(np.prod(shape[1:])) * (2 if dtype == BF16 else 4)
        if at is None:
            off = _cur[0]
            _cur[0] = (off + nbytes + 63) & ~63
        else:
            off = at
        assert off + nbytes <= top, f"SBUF overflow placing {name} at {off}"
        return nc.alloc_sbuf_tensor_at(name, list(shape), dtype, offset=off), off

    def bump():
        return _cur[0]

    ident, _ = place("ident", [P, P], F32)
    gb_sb, _ = place("gb_sb", [P, HC], F32)
    vb_sb, _ = place("vb_sb", [P, HC], F32)
    rep8k, _ = place("rep8k", [P, D], F32)
    sq_scr, _ = place("sq_scr", [P, D], BF16)
    gacc, GACC_OFF = place("gacc", [P, 64], F32)
    gred, _ = place("gred", [P, 1], F32)
    gfull, _ = place("gfull", [P, 1], F32)
    wcols = {}
    for wname in ("g", "v", "o"):
        wcols[wname] = {}
        for k in ("thp", "thn", "nsc"):
            wcols[wname][k], _ = place(f"{k}_{wname}", [P, 1], F32)
    m_all, _ = place("m_all", [P, TT], F32)
    M_all, _ = place("M_all", [P, TT], F32)
    ssq_all, _ = place("ssq_all", [P, TT], F32)
    rstd_all, _ = place("rstd_all", [P, TT], F32)
    rho_all, _ = place("rho_all", [P, TT], F32)
    sc3_all, _ = place("sc3_all", [P, TT], F32)
    scol, _ = place("scol", [P, 1], F32)
    mcol, _ = place("mcol", [P, 1], F32)
    tcol, _ = place("tcol", [P, 1], F32)
    tcol2, _ = place("tcol2", [P, 1], F32)
    eps_col, _ = place("eps_col", [P, 1], F32)
    ones_row, _ = place("ones_row", [1, P], F32)
    ones_col, _ = place("ones_col", [P, 1], F32)
    rowb, _ = place("rowb", [1, P], F32)
    sclr, _ = place("sclr", [1, 1], F32)
    sflush, _ = place("sflush", [1, 1], F32)

    PERS_END = bump()

    GW = min(max(D, H), 4096)
    A_SZ = max(DC * T * 2, HCS * D * 2, 2 * GW * 4 + GW * 2)
    SLOT_A = bump()
    _cur[0] = SLOT_A + ((A_SZ + 63) & ~63)
    B_SZ = max(NCH * T * 2, 4 * D * 4, 2 * HCS * P * 4 + 2 * HCS * P * 2 + D * 4)
    SLOT_B = bump()
    _cur[0] = SLOT_B + ((B_SZ + 63) & ~63)
    C_SZ = max(DC * NS * 2, NCH * D * 2,
               3 * D * 2 + 4 * D * 2,
               2 * HSEG * 4 + 2 * HSEG * 2)
    SLOT_C = bump()
    _cur[0] = SLOT_C + ((C_SZ + 63) & ~63)
    D_SZ = max(DC * SLAB * 2,
               3 * D * 4,
               6 * SLAB * 4 + 2 * T * 4,
               D * 2 + 2 * D * 4)
    SLOT_D = bump()
    _cur[0] = SLOT_D + ((D_SZ + 63) & ~63)
    E_SZ = max(2 * D * 4, D * 4)
    SLOT_E = bump()
    _cur[0] = SLOT_E + ((E_SZ + 63) & ~63)
    F_SZ = max(2 * D * 2, 2 * NS * 4, T * 4)
    SLOT_F = bump()
    _cur[0] = SLOT_F + ((F_SZ + 63) & ~63)
    G_SZ = 2 * NS * 2
    SLOT_G = bump()
    _cur[0] = SLOT_G + ((G_SZ + 63) & ~63)
    H_SZ = D * 2
    SLOT_H = bump()
    _cur[0] = SLOT_H + ((H_SZ + 63) & ~63)
    I_SZ = D * 2
    SLOT_I = bump()
    _cur[0] = SLOT_I + ((I_SZ + 63) & ~63)
    assert bump() <= top, f"arena overflow: {bump()} > {top}"

    def at(name, shape, dtype, off):
        t_, _ = place(name, shape, dtype, at=off)
        return t_

    s_h_rep = at("s_h_rep", [P, T], F32, SLOT_F)

    gld = [at(f"gld{i}", [P, GW], F32, SLOT_A + i * GW * 4) for i in range(2)]
    gscr = at("gscr", [P, GW], BF16, SLOT_A + 2 * GW * 4)
    qyT_sb = at("qyT_sb", [P, DC * T], BF16, SLOT_A)
    state_sb = at("state_sb", [P, NCH * T], BF16, SLOT_B)
    bt_sb = at("bt_sb", [P, DC * NS], BF16, SLOT_C)
    ct_sb = at("ct_sb", [P, NCH * D], BF16, SLOT_C)
    xT_sb = at("xT_sb", [P, DC * SLAB], BF16, SLOT_D)
    wbuf = [at(f"wbuf{i}", [P, D], F32, SLOT_D + i * D * 4) for i in range(2)]
    t256 = at("t256", [P, D], F32, SLOT_D + 2 * D * 4)
    xld = [at(f"xld{i}", [P, D], F32, SLOT_E + i * D * 4) for i in range(2)]
    xbf = [at(f"xbf{i}", [P, D], BF16, SLOT_F + i * D * 2) for i in range(2)]
    cld = [at(f"cld{i}", [P, NS], F32, SLOT_F + i * NS * 4) for i in range(2)]
    cbf = [at(f"cbf{i}", [P, NS], BF16, SLOT_G + i * NS * 2) for i in range(2)]
    ycbf = at("ycbf", [P, D], BF16, SLOT_H)
    qybf = at("qybf", [P, D], BF16, SLOT_I)
    ib_q = at("ib_q", [P, D], I32, SLOT_D + 2 * D * 4)
    graw = [at(f"graw{i}", [P, D], F32, SLOT_B + i * D * 4) for i in range(2)]
    vraw = [at(f"vraw{i}", [P, D], F32, SLOT_B + (2 + i) * D * 4)
            for i in range(2)]
    gtb = at("gtb", [P, D], BF16, SLOT_C)
    vtb = at("vtb", [P, D], BF16, SLOT_C + D * 2)
    dsel = at("dsel", [P, D], BF16, SLOT_C + 2 * D * 2)
    gwT = [at(f"gwT{i}", [P, D], BF16, SLOT_C + (3 + i) * D * 2)
           for i in range(2)]
    vwT = [at(f"vwT{i}", [P, D], BF16, SLOT_C + (5 + i) * D * 2)
           for i in range(2)]
    gsl = [at(f"gsl{i}", [P, SLAB], F32, SLOT_D + i * SLAB * 4)
           for i in range(2)]
    vsl = [at(f"vsl{i}", [P, SLAB], F32, SLOT_D + (2 + i) * SLAB * 4)
           for i in range(2)]
    hsl = [at(f"hsl{i}", [P, SLAB], F32, SLOT_D + (4 + i) * SLAB * 4)
           for i in range(2)]
    mrun = at("mrun", [P, T], F32, SLOT_D + 6 * SLAB * 4)
    rho_rep = at("rho_rep", [P, T], F32, SLOT_D + 6 * SLAB * 4 + T * 4)
    owh = at("owh", [P, HCS * D], BF16, SLOT_A)
    oraw = [at(f"oraw{i}", [P, HSEG], F32, SLOT_C + i * HSEG * 4)
            for i in range(2)]
    otb = at("otb", [P, HSEG], BF16, SLOT_C + 2 * HSEG * 4)
    osel = at("osel", [P, HSEG], BF16, SLOT_C + 2 * HSEG * 4 + HSEG * 2)
    hstg = [at(f"hstg{i}", [P, HCS * P], F32, SLOT_B + i * HCS * P * 4)
            for i in range(2)]
    ib_h = [at(f"ib_h{i}", [P, HCS * P], I32, SLOT_B + i * HCS * P * 4)
            for i in range(2)]
    qhT = [at(f"qhT{i}", [P, HCS * P], BF16,
              SLOT_B + 2 * HCS * P * 4 + i * HCS * P * 2) for i in range(2)]
    pbuf = at("pbuf", [P, D], F32, SLOT_E)
    ycld = at("ycld", [P, D], BF16, SLOT_D)
    obuf = [at(f"obuf{i}", [P, D], F32, SLOT_D + D * 2 + i * D * 4)
            for i in range(2)]

    PS = nc.alloc_psum_tensor("PS", [P, 4096], F32)
    BANK = 512

    def pbank(b, width=BANK):
        return PS[:, b * BANK:b * BANK + width]

    import contextlib
    st = contextlib.ExitStack()
    with st:
        S = st.enter_context
        sems = {}
        for name in ("ldc", "ldx", "ldw", "ldh", "trq", "mm", "ev", "vv",
                     "hv", "ss", "gg", "sty", "sth", "stp", "sto",
                     "vb", "sb"):
            sems[name] = Sem(S(nc.semaphore(name)))
        ldc, ldx, ldw, ldh = (sems[k] for k in ("ldc", "ldx", "ldw", "ldh"))
        trq, mm, ev, vv = (sems[k] for k in ("trq", "mm", "ev", "vv"))
        ss, gg, hv = sems["ss"], sems["gg"], sems["hv"]
        sty, sth, stp, sto = (sems[k] for k in ("sty", "sth", "stp", "sto"))

        block = S(nc.Block())
        prog = {k: [] for k in ("sync", "scalar", "vector", "tensor", "gpsimd")}

        def em(engine, fn):
            prog[engine].append(fn)

        def s0_sync(e):
            e.dma_start(ident[:], id_hbm.ap()).then_inc(ldc.h, 16)
            e.dma_start(gb_sb[:], gb_hbm.ap()).then_inc(ldc.h, 16)
            e.dma_start(vb_sb[:], vb_hbm.ap()).then_inc(ldc.h, 16)
            e.dma_start(rep8k[0:1, :], d_hbm.ap()).then_inc(ldc.h, 16)
        em("sync", s0_sync)
        ldc.add(64)

        def s0_vec(e):
            e.memset(sflush[:], 0.0)
            e.memset(eps_col[:], EPS)
            e.memset(ones_row[:], 1.0)
            e.memset(ones_col[:], 1.0).then_inc(vv.h, 1)
        em("vector", s0_vec)
        VV_ONES = vv.add(1)

        def s0_mm(e):
            e.wait_ge(vv.h, VV_ONES)
            e.wait_ge(ldc.h, 64)
            for ds in range(NDS):
                e.matmul(pbank(ds, DS), ones_row[:],
                         rep8k[0:1, ds * DS:(ds + 1) * DS],
                         start=True, stop=True).then_inc(mm.h, 1)
        em("tensor", s0_mm)
        MM_DREP = mm.add(NDS)

        def s0_vec2(e):
            e.wait_ge(mm.h, MM_DREP)
            for ds in range(NDS):
                e.tensor_scalar(rep8k[:, ds * DS:(ds + 1) * DS],
                                pbank(ds, DS), 1.0, None, OP.add)
            e.engine_nop().then_inc(vv.h, 1)
        em("vector", s0_vec2)
        VV_DREP = vv.add(1)

        GBIAS = 1.0 if SIM_MODE else (1.0 + 2.3e-7)
        for wname, whbm, rows, cols in (("g", gw_hbm, H, D),
                                        ("v", vw_hbm, H, D),
                                        ("o", ow_hbm, D, H)):
            cpr = max(1, cols // GW)
            w_ = min(cols, GW)
            ntile = (rows // P) * cpr
            inv_n = GBIAS / float(rows * cols)
            assert ntile <= 64

            ld0 = ldw.val
            ab0 = ss.val
            prev_gred = vv.val

            def g_sync(e, whbm=whbm, rows=rows, cpr=cpr, w_=w_, ab0=ab0):
                for i in range(rows // P):
                    for j in range(cpr):
                        k = i * cpr + j
                        if k < 2:
                            e.wait_ge(ss.h, ab0)
                        else:
                            e.wait_ge(ss.h, ab0 + k - 1)
                        e.dma_start(
                            gld[k % 2][:, 0:w_],
                            whbm.ap()[i * P:(i + 1) * P, j * w_:(j + 1) * w_]
                        ).then_inc(ldw.h, 16)
            em("sync", g_sync)
            ldw.add(16 * ntile)

            def g_act(e, ntile=ntile, w_=w_, ld0=ld0, first=(wname == "g"),
                      prev_gred=prev_gred):
                if first:
                    e.wait_ge(ldc.h, 64)
                    e.wait_ge(vv.h, VV_ONES)
                e.wait_ge(vv.h, prev_gred)
                for k in range(ntile):
                    e.wait_ge(ldw.h, ld0 + 16 * (k + 1))
                    e.activation(gscr[:, 0:w_], gld[k % 2][:, 0:w_], AF.Abs,
                                 accum_out=gacc[:, k:k + 1]).then_inc(ss.h, 1)
            em("scalar", g_act)
            SS_ABS = ss.add(ntile)

            def g_vec(e, ntile=ntile, tgt=SS_ABS):
                e.wait_ge(ss.h, tgt)
                e.reduce_sum(gred[:], gacc[:, 0:ntile], axis=AX).then_inc(vv.h, 1)
            em("vector", g_vec)
            VV_GRED = vv.add(1)

            def g_mm1(e, tgt=VV_GRED):
                e.wait_ge(vv.h, tgt)
                e.matmul(PS[0:1, 3584:3585], gred[:], ones_col[:],
                         start=True, stop=True).then_inc(mm.h, 1)
            em("tensor", g_mm1)
            MM_GSUM = mm.add(1)

            def g_vsc(e, tgt=MM_GSUM):
                e.wait_ge(mm.h, tgt)
                e.tensor_copy(sclr[:], PS[0:1, 3584:3585]).then_inc(vv.h, 1)
            em("vector", g_vsc)
            VV_GSC = vv.add(1)

            def g_mm2(e, tgt=VV_GSC):
                e.wait_ge(vv.h, tgt)
                e.matmul(PS[:, 3585:3586], ones_row[:], sclr[:],
                         start=True, stop=True).then_inc(mm.h, 1)
            em("tensor", g_mm2)
            MM_GBC = mm.add(1)

            def g_vec2(e, wname=wname, inv_n=inv_n, tgt=MM_GBC):
                c = wcols[wname]
                e.wait_ge(mm.h, tgt)
                e.tensor_scalar(gfull[:], PS[:, 3585:3586], inv_n, EPS,
                                OP.mult, OP.max)
                e.tensor_scalar_mul(c["thp"][:], gfull[:], 0.5)
                e.tensor_scalar_mul(c["thn"][:], gfull[:], -0.5)
                e.tensor_scalar_mul(c["nsc"][:], gfull[:], -1.0).then_inc(vv.h, 1)
            em("vector", g_vec2)
            vv.add(1)
        VV_GAMMA = vv.val

        bld0 = ldx.val
        bcast0 = vv.val
        btr0 = trq.val

        def b_sync(e):
            for i in range(min(2, NCH)):
                e.dma_start(xld[i][:], b_hbm.ap()[i * P:(i + 1) * P, :]
                            ).then_inc(ldx.h, 16)
            for i in range(NCH):
                e.wait_ge(vv.h, bcast0 + i + 1)
                for dc in range(DC):
                    e.dma_start_transpose(
                        bt_sb[:, dc * NS + i * P:dc * NS + (i + 1) * P],
                        xbf[i % 2][:, dc * P:(dc + 1) * P]).then_inc(trq.h, 16)
                if i + 2 < NCH:
                    e.dma_start(xld[i % 2][:],
                                b_hbm.ap()[(i + 2) * P:(i + 3) * P, :]
                                ).then_inc(ldx.h, 16)
        em("sync", b_sync)
        ldx.add(16 * NCH)
        TRQ_BT = trq.add(16 * DC * NCH)

        def b_vec(e):
            e.wait_ge(vv.h, VV_GAMMA)
            for i in range(NCH):
                e.wait_ge(ldx.h, bld0 + 16 * (i + 1))
                if i >= 2:
                    e.wait_ge(trq.h, btr0 + 16 * DC * (i - 1))
                e.tensor_copy(xbf[i % 2][:], xld[i % 2][:]).then_inc(vv.h, 1)
        em("vector", b_vec)
        vv.add(NCH)
        VV_BCAST = vv.val

        xld0 = ldx.val
        xcast0 = vv.val
        xtr0 = trq.val
        z1mm0 = mm.val
        SS_TANH0 = None

        def x_sync(e):
            def load(k):
                if k < 2:
                    e.wait_ge(vv.h, VV_BCAST)
                e.dma_start(xld[k % 2][:],
                            x_hbm.ap()[k * P:(k + 1) * P, :]
                            ).then_inc(ldx.h, 16)
            for k in range(min(2, TT)):
                load(k)
            k = 0
            for s_ in range(NSLAB):
                if s_ >= 1:
                    e.wait_ge(mm.h, z1mm0 + s_ * NCH)
                for j in range(TPS):
                    e.wait_ge(vv.h, xcast0 + k + 1)
                    for dc in range(DC):
                        e.dma_start_transpose(
                            xT_sb[:, dc * SLAB + j * P:dc * SLAB + (j + 1) * P],
                            xbf[k % 2][:, dc * P:(dc + 1) * P]).then_inc(trq.h, 16)
                    if k + 2 < TT:
                        load(k + 2)
                    k += 1
        em("sync", x_sync)
        ldx.add(16 * TT)
        TRQ_XT = trq.add(16 * DC * TT)

        def x_vec(e):
            e.wait_ge(trq.h, TRQ_BT)
            k = 0
            for s_ in range(NSLAB):
                for j in range(TPS):
                    e.wait_ge(ldx.h, xld0 + 16 * (k + 1))
                    if k >= 2:
                        e.wait_ge(trq.h, xtr0 + 16 * DC * (k - 1))
                    e.tensor_copy(xbf[k % 2][:], xld[k % 2][:]).then_inc(vv.h, 1)
                    k += 1
        em("vector", x_vec)
        vv.add(TT)

        def z1_mm(e):
            e.wait_ge(trq.h, TRQ_BT)
            k = 0
            for s_ in range(NSLAB):
                e.wait_ge(trq.h, xtr0 + 16 * DC * TPS * (s_ + 1))
                for nch in range(NCH):
                    ps = pbank(4 + (k % 2), SLAB)
                    if k >= 2:
                        e.wait_ge(ss.h, SS_TANH0 + k - 1)
                    for dc in range(DC):
                        i_ = e.matmul(
                            ps,
                            bt_sb[:, dc * NS + nch * P:dc * NS + (nch + 1) * P],
                            xT_sb[:, dc * SLAB:(dc + 1) * SLAB],
                            start=(dc == 0), stop=(dc == DC - 1))
                    i_.then_inc(mm.h, 1)
                    k += 1
        em("tensor", z1_mm)
        mm.add(NSLAB * NCH)

        SS_TANH0 = ss.val

        def z1_tanh(e):
            k = 0
            for s_ in range(NSLAB):
                for nch in range(NCH):
                    e.wait_ge(mm.h, z1mm0 + k + 1)
                    e.activation(
                        state_sb[:, nch * T + s_ * SLAB:nch * T + (s_ + 1) * SLAB],
                        pbank(4 + (k % 2), SLAB), AF.Tanh).then_inc(ss.h, 1)
                    k += 1
        em("scalar", z1_tanh)
        ss.add(NSLAB * NCH)
        SS_P1A = ss.val

        cld0 = ldx.val
        ccast0 = vv.val
        ctr0 = trq.val

        def c_sync(e):
            e.wait_ge(trq.h, TRQ_XT)
            for i in range(min(2, DC)):
                e.dma_start(cld[i][:], c_hbm.ap()[i * P:(i + 1) * P, :]
                            ).then_inc(ldx.h, 16)
            e.wait_ge(mm.h, z1mm0 + NSLAB * NCH)
            for i in range(DC):
                e.wait_ge(vv.h, ccast0 + i + 1)
                for nch in range(NCH):
                    e.dma_start_transpose(
                        ct_sb[:, nch * D + i * P:nch * D + (i + 1) * P],
                        cbf[i % 2][:, nch * P:(nch + 1) * P]).then_inc(trq.h, 16)
                if i + 2 < DC:
                    e.dma_start(cld[i % 2][:],
                                c_hbm.ap()[(i + 2) * P:(i + 3) * P, :]
                                ).then_inc(ldx.h, 16)
        em("sync", c_sync)
        ldx.add(16 * DC)
        TRQ_CT = trq.add(16 * NCH * DC)

        def c_vec(e):
            for i in range(DC):
                e.wait_ge(ldx.h, cld0 + 16 * (i + 1))
                if i >= 2:
                    e.wait_ge(trq.h, ctr0 + 16 * NCH * (i - 1))
                e.tensor_copy(cbf[i % 2][:], cld[i % 2][:]).then_inc(vv.h, 1)
        em("vector", c_vec)
        vv.add(DC)

        x2ld0 = ldx.val
        P1B_V0 = None
        P1B_EV0 = None

        def x2_sync(e):
            e.wait_ge(vv.h, xcast0 + TT)
            for tt in range(min(2, TT)):
                e.dma_start(xld[tt][:], x_hbm.ap()[tt * P:(tt + 1) * P, :]
                            ).then_inc(ldx.h, 16)
        em("sync", x2_sync)
        ldx.add(16 * min(2, TT))

        mm2_0 = mm.val

        def mm2(e):
            e.wait_ge(trq.h, TRQ_CT)
            e.wait_ge(ss.h, SS_P1A)
            for tt in range(TT):
                if tt >= 1:
                    e.wait_ge(ev.h, P1B_EV0 + tt)
                for ds in range(NDS):
                    for nch in range(NCH):
                        i_ = e.matmul(
                            pbank(ds, DS),
                            state_sb[:, nch * T + tt * P:nch * T + (tt + 1) * P],
                            ct_sb[:, nch * D + ds * DS:nch * D + (ds + 1) * DS],
                            start=(nch == 0), stop=(nch == NCH - 1))
                i_.then_inc(mm.h, 1)
        em("tensor", mm2)
        mm.add(TT)

        P1B_V0 = vv.val
        P1B_EV0 = ev.val
        SS_SQ0 = None
        QYTR0 = None

        def p1b_vec(e):
            for tt in range(TT):
                e.wait_ge(ldx.h, x2ld0 + 16 * (tt + 1))
                e.wait_ge(mm.h, mm2_0 + tt + 1)
                w = wbuf[tt % 2]
                if tt >= 2:
                    e.wait_ge(ss.h, SS_SQ0 + tt - 1)
                if tt >= 1:
                    e.wait_ge(sty.h, 16 * tt)
                e.tensor_tensor(w[:], xld[tt % 2][:], rep8k[:], OP.mult)
                e.tensor_tensor(w[:], w[:], PS[:, 0:D], OP.add).then_inc(ev.h, 1)
                e.reduce_sum(mcol[:], w[:], axis=AX)
                e.tensor_scalar_mul(mcol[:], mcol[:], 1.0 / D)
                e.tensor_copy(m_all[:, tt:tt + 1], mcol[:])
                e.tensor_scalar(w[:], w[:], mcol[:], None, OP.subtract)
                e.reduce_max(M_all[:, tt:tt + 1], w[:], axis=AX,
                             apply_absolute_value=True)
                e.tensor_copy(ycbf[:], w[:]).then_inc(vv.h, 1)
                e.tensor_scalar(tcol[:], M_all[:, tt:tt + 1], THR_M, None, OP.max)
                e.reciprocal(tcol[:], tcol[:])
                e.tensor_scalar_mul(scol[:], tcol[:], 127.0)
                if tt >= 1:
                    e.wait_ge(trq.h, QYTR0 + 16 * DC * tt)
                if SIM_MODE:
                    e.tensor_scalar(t256[:], w[:], scol[:], 256.5,
                                    OP.mult, OP.add)
                    e.tensor_scalar(w[:], t256[:], 1.0, None, OP.mod)
                    e.scalar_tensor_tensor(qybf[:], t256[:], -256.0, w[:],
                                           OP.add, OP.subtract).then_inc(vv.h, 1)
                else:
                    e.tensor_scalar(t256[:], w[:], scol[:], None, OP.mult)
                    e.tensor_copy(ib_q[:], t256[:])
                    e.tensor_copy(qybf[:], ib_q[:]).then_inc(vv.h, 1)
        em("vector", p1b_vec)
        vv.add(2 * TT)
        ev.add(TT)

        SS_SQ0 = ss.val

        def p1b_scal(e):
            for tt in range(TT):
                e.wait_ge(vv.h, P1B_V0 + 2 * tt + 1)
                e.activation(sq_scr[:], wbuf[tt % 2][:], AF.Square,
                             accum_out=ssq_all[:, tt:tt + 1]).then_inc(ss.h, 1)
        em("scalar", p1b_scal)
        ss.add(TT)

        def p1b_gp(e):
            for tt in range(TT):
                e.wait_ge(vv.h, P1B_V0 + 2 * tt + 1)
                e.dma_start(yc_sp.ap()[tt * P:(tt + 1) * P, :], ycbf[:]
                            ).then_inc(sty.h, 16)
        em("gpsimd", p1b_gp)
        STY_DONE = sty.add(16 * TT)

        if dbg:
            def dbg1_gp(e):
                e.wait_ge(ss.h, SS_P1A)
                e.dma_start(state_d.ap(), state_sb[:]).then_inc(gg.h, 16)
            em("gpsimd", dbg1_gp)
            gg.add(16)

        QYTR0 = trq.val

        def qy_tr(e):
            for tt in range(TT):
                e.wait_ge(vv.h, P1B_V0 + 2 * tt + 2)
                for dc in range(DC):
                    e.dma_start_transpose(
                        qyT_sb[:, dc * T + tt * P:dc * T + (tt + 1) * P],
                        qybf[:, dc * P:(dc + 1) * P]).then_inc(trq.h, 16)
                if tt + 2 < TT:
                    e.dma_start(xld[tt % 2][:],
                                x_hbm.ap()[(tt + 2) * P:(tt + 3) * P, :]
                                ).then_inc(ldx.h, 16)
        em("sync", qy_tr)
        TRQ_QY = trq.add(16 * DC * TT)
        ldx.add(16 * max(0, TT - 2))

        TRPC = 6 * BANK

        def p1e_scal(e):
            e.wait_ge(vv.h, P1B_V0 + 2 * TT)
            e.activation(rstd_all[:], ssq_all[:], AF.Sqrt,
                         scale=1.0 / D, bias=eps_col[:]).then_inc(ss.h, 1)
        em("scalar", p1e_scal)
        SS_RSTD = ss.add(1)

        def p1e_vec(e):
            e.wait_ge(ss.h, SS_RSTD)
            e.reciprocal(rstd_all[:], rstd_all[:])
            e.tensor_scalar(rho_all[:], M_all[:], THR_M, None, OP.max)
            e.tensor_tensor(rho_all[:], rho_all[:], rstd_all[:], OP.mult)
            e.tensor_scalar_mul(rho_all[:], rho_all[:], 1.0 / 127.0
                                ).then_inc(vv.h, 1)
        em("vector", p1e_vec)
        VV_RHO = vv.add(1)

        BCC = 7 * BANK
        RB_M0 = mm.val
        RB_V0 = vv.val

        def p1e_mmT(e):
            e.wait_ge(vv.h, VV_RHO)
            for tt in range(TT):
                if tt > 0:
                    e.wait_ge(vv.h, RB_V0 + 2 * tt - 1)
                e.transpose(PS[0:1, TRPC:TRPC + P], rho_all[:, tt:tt + 1],
                            ident[:]).then_inc(mm.h, 1)
                e.wait_ge(vv.h, RB_V0 + 2 * tt + 1)
                e.matmul(PS[:, BCC:BCC + P], ones_row[:], rowb[:],
                         start=True, stop=True).then_inc(mm.h, 1)
        em("tensor", p1e_mmT)
        mm.add(2 * TT)

        def p1e_vec2(e):
            for tt in range(TT):
                e.wait_ge(mm.h, RB_M0 + 2 * tt + 1)
                e.tensor_copy(rowb[:], PS[0:1, TRPC:TRPC + P]).then_inc(vv.h, 1)
                e.wait_ge(mm.h, RB_M0 + 2 * tt + 2)
                e.tensor_copy(rho_rep[:, tt * P:(tt + 1) * P],
                              PS[:, BCC:BCC + P]).then_inc(vv.h, 1)
        em("vector", p1e_vec2)
        RHOREP_V = vv.add(2 * TT)

        def mrun_vec(e):
            e.memset(mrun[:], 0.0).then_inc(vv.h, 1)
        em("vector", mrun_vec)
        VV_MRUN = vv.add(1)

        P2LD0 = ldw.val
        P2TR0 = trq.val
        P2TERN0 = vv.val + 0
        P2MM0 = mm.val
        P2EV0 = ev.val
        P2SS0 = ss.val
        P2HV0 = None

        def p2_sync(e):
            e.wait_ge(vv.h, RHOREP_V)
            if dbg:
                e.wait_ge(gg.h, 16)
            for hc in range(HC):
                if hc >= 2:
                    e.wait_ge(vv.h, P2TERN0 + 2 * (hc - 1))
                e.dma_start(graw[hc % 2][:], gw_hbm.ap()[hc * P:(hc + 1) * P, :]
                            ).then_inc(ldw.h, 16)
                e.dma_start(vraw[hc % 2][:], vw_hbm.ap()[hc * P:(hc + 1) * P, :]
                            ).then_inc(ldw.h, 16)
                e.wait_ge(vv.h, P2TERN0 + 2 * hc + 1)
                if hc >= 2:
                    e.wait_ge(mm.h, P2MM0 + NSLAB * (hc - 1))
                for dc in range(DC):
                    e.dma_start_transpose(gwT[hc % 2][:, dc * P:(dc + 1) * P],
                                          gtb[:, dc * P:(dc + 1) * P]
                                          ).then_inc(trq.h, 16)
                e.wait_ge(vv.h, P2TERN0 + 2 * hc + 2)
                for dc in range(DC):
                    e.dma_start_transpose(vwT[hc % 2][:, dc * P:(dc + 1) * P],
                                          vtb[:, dc * P:(dc + 1) * P]
                                          ).then_inc(trq.h, 16)
        em("sync", p2_sync)
        ldw.add(32 * HC)
        trq.add(32 * DC * HC)

        P2TERN0 = vv.val

        def p2_mm(e):
            e.wait_ge(trq.h, TRQ_QY)
            for hc in range(HC):
                e.wait_ge(trq.h, P2TR0 + 32 * DC * (hc + 1))
                for s_ in range(NSLAB):
                    k = hc * NSLAB + s_
                    lag = 1 if NSLAB == 1 else 2
                    if k >= lag:
                        e.wait_ge(ev.h, P2EV0 + k - lag + 1)
                    for dc in range(DC):
                        e.matmul(pbank(4 + (s_ % 2), SLAB),
                                 gwT[hc % 2][:, dc * P:(dc + 1) * P],
                                 qyT_sb[:, dc * T + s_ * SLAB:dc * T + (s_ + 1) * SLAB],
                                 start=(dc == 0), stop=(dc == DC - 1))
                    for dc in range(DC):
                        i_ = e.matmul(pbank(6 + (s_ % 2), SLAB),
                                      vwT[hc % 2][:, dc * P:(dc + 1) * P],
                                      qyT_sb[:, dc * T + s_ * SLAB:dc * T + (s_ + 1) * SLAB],
                                      start=(dc == 0), stop=(dc == DC - 1))
                    i_.then_inc(mm.h, 1)
        em("tensor", p2_mm)
        mm.add(HC * NSLAB)

        P2HV0 = hv.val

        def p2_vec(e):
            e.wait_ge(vv.h, VV_MRUN)
            g = wcols["g"]
            v = wcols["v"]

            def tern(hc):
                e.wait_ge(ldw.h, P2LD0 + 32 * (hc + 1))
                if hc >= 1:
                    e.wait_ge(trq.h, P2TR0 + 32 * DC * hc)
                e.tensor_scalar(dsel[:], graw[hc % 2][:], g["thp"][:], None,
                                OP.is_gt)
                e.scalar_tensor_tensor(gtb[:], graw[hc % 2][:], g["thn"][:],
                                       dsel[:], OP.is_lt, OP.subtract
                                       ).then_inc(vv.h, 1)
                e.tensor_scalar(dsel[:], vraw[hc % 2][:], v["thp"][:], None,
                                OP.is_gt)
                e.scalar_tensor_tensor(vtb[:], vraw[hc % 2][:], v["thn"][:],
                                       dsel[:], OP.is_lt, OP.subtract
                                       ).then_inc(vv.h, 1)

            def epi(hc):
                for s_ in range(NSLAB):
                    k = hc * NSLAB + s_
                    e.wait_ge(mm.h, P2MM0 + k + 1)
                    if k >= 2:
                        e.wait_ge(sth.h, 16 * (k - 1))
                    e.tensor_tensor(gsl[s_ % 2][:], pbank(4 + (s_ % 2), SLAB),
                                    rho_rep[:, s_ * SLAB:(s_ + 1) * SLAB],
                                    OP.mult)
                    e.tensor_tensor(vsl[s_ % 2][:], pbank(6 + (s_ % 2), SLAB),
                                    rho_rep[:, s_ * SLAB:(s_ + 1) * SLAB],
                                    OP.mult).then_inc(ev.h, 1)
                    e.wait_ge(ss.h, P2SS0 + 2 * k + 2)
                    e.tensor_tensor(hsl[s_ % 2][:], gsl[s_ % 2][:],
                                    vsl[s_ % 2][:], OP.mult)
                    e.tensor_tensor(mrun[:, s_ * SLAB:(s_ + 1) * SLAB],
                                    mrun[:, s_ * SLAB:(s_ + 1) * SLAB],
                                    hsl[s_ % 2][:], OP.max)
                    e.scalar_tensor_tensor(mrun[:, s_ * SLAB:(s_ + 1) * SLAB],
                                           hsl[s_ % 2][:], -1.0,
                                           mrun[:, s_ * SLAB:(s_ + 1) * SLAB],
                                           OP.mult, OP.max).then_inc(hv.h, 1)

            for hc in range(HC):
                tern(hc)
                if hc >= 1:
                    epi(hc - 1)
            epi(HC - 1)
        em("vector", p2_vec)
        vv.add(2 * HC)
        hv.add(HC * NSLAB)
        ev.add(HC * NSLAB)

        def p2_scal(e):
            for hc in range(HC):
                for s_ in range(NSLAB):
                    k = hc * NSLAB + s_
                    e.wait_ge(ev.h, P2EV0 + k + 1)
                    e.activation(gsl[s_ % 2][:], gsl[s_ % 2][:], AF.Silu,
                                 bias=gb_sb[:, hc:hc + 1],
                                 scale=wcols["g"]["nsc"][:]).then_inc(ss.h, 1)
                    e.activation(vsl[s_ % 2][:], vsl[s_ % 2][:], AF.Identity,
                                 bias=vb_sb[:, hc:hc + 1],
                                 scale=wcols["v"]["nsc"][:]).then_inc(ss.h, 1)
        em("scalar", p2_scal)
        ss.add(2 * HC * NSLAB)

        def p2_gp(e):
            for hc in range(HC):
                for s_ in range(NSLAB):
                    k = hc * NSLAB + s_
                    e.wait_ge(hv.h, P2HV0 + k + 1)
                    e.dma_start(h_sp.ap()[s_ * TPS:(s_ + 1) * TPS, hc, :, :]
                                .rearrange("a p b -> p a b"),
                                hsl[s_ % 2][:].rearrange("p (a b) -> p a b",
                                                         a=TPS)
                                ).then_inc(sth.h, 16)
        em("gpsimd", p2_gp)
        STH_DONE = sth.add(16 * HC * NSLAB)

        VV_SREP_HOLDER = [None]
        if dbg:
            GG_D2 = gg.val + 13 * 16

            def dbg2_gp(e):
                e.wait_ge(trq.h, TRQ_QY)
                e.wait_ge(vv.h, VV_SREP_HOLDER[0])
                e.dma_start(qyT_d.ap(), qyT_sb[:]).then_inc(gg.h, 16)
                e.dma_start(rho_d.ap(), rho_rep[:]).then_inc(gg.h, 16)
                e.dma_start(srep_d.ap(), s_h_rep[:]).then_inc(gg.h, 16)
                e.dma_start(sc3_d.ap(), sc3_all[:]).then_inc(gg.h, 16)
                for j, (wn, kk) in enumerate(
                        (w_, k_) for w_ in ("g", "v", "o")
                        for k_ in ("thp", "thn", "nsc")):
                    e.dma_start(gcol_d.ap()[:, j:j + 1], wcols[wn][kk][:],
                                allow_slow_non_contiguous=True
                                ).then_inc(gg.h, 16)
                e.wait_ge(gg.h, GG_D2)
            em("gpsimd", dbg2_gp)
            gg.add(13 * 16)

        BCC = 7 * BANK
        SR_M0 = mm.val
        SR_V0 = vv.val + 1

        def p2e_mm(e):
            e.wait_ge(hv.h, P2HV0 + HC * NSLAB)
            for tt in range(TT):
                if tt > 0:
                    e.wait_ge(vv.h, SR_V0 + 3 * (tt - 1) + 1)
                e.transpose(PS[:, TRPC:TRPC + P], mrun[:, tt * P:(tt + 1) * P],
                            ident[:]).then_inc(mm.h, 1)
                e.wait_ge(vv.h, SR_V0 + 3 * tt + 1)
                e.transpose(PS[0:1, TRPC + P:TRPC + 2 * P], scol[:],
                            ident[:]).then_inc(mm.h, 1)
                e.wait_ge(vv.h, SR_V0 + 3 * tt + 2)
                e.matmul(PS[:, BCC:BCC + P], ones_row[:], rowb[:],
                         start=True, stop=True).then_inc(mm.h, 1)
        em("tensor", p2e_mm)
        mm.add(3 * TT)

        def p2e_vec(e):
            e.tensor_scalar_mul(tcol2[:], wcols["o"]["nsc"][:], 1.0 / 127.0
                                ).then_inc(vv.h, 1)
            for tt in range(TT):
                e.wait_ge(mm.h, SR_M0 + 3 * tt + 1)
                e.reduce_max(tcol[:], PS[:, TRPC:TRPC + P], axis=AX)
                e.tensor_scalar(tcol[:], tcol[:], EPS, None, OP.max)
                e.tensor_scalar(sc3_all[:, tt:tt + 1], tcol[:], tcol2[:],
                                None, OP.mult)
                e.reciprocal(tcol[:], tcol[:])
                e.tensor_scalar_mul(scol[:], tcol[:], 127.0).then_inc(vv.h, 1)
                e.wait_ge(mm.h, SR_M0 + 3 * tt + 2)
                e.tensor_copy(rowb[:], PS[0:1, TRPC + P:TRPC + 2 * P]
                              ).then_inc(vv.h, 1)
                e.wait_ge(mm.h, SR_M0 + 3 * tt + 3)
                e.tensor_copy(s_h_rep[:, tt * P:(tt + 1) * P],
                              PS[:, BCC:BCC + P]).then_inc(vv.h, 1)
        em("vector", p2e_vec)
        VV_SREP = vv.add(1 + 3 * TT)
        VV_SREP_HOLDER[0] = VV_SREP

        def ob_sync(e):
            e.wait_ge(vv.h, VV_SREP)
            e.dma_start(rep8k[0:1, :], ob_hbm.ap()).then_inc(ldx.h, 16)
        em("sync", ob_sync)
        LDX_OB = ldx.add(16)

        def ob_mm(e):
            e.wait_ge(ldx.h, LDX_OB)
            for ds in range(NDS):
                e.matmul(pbank(ds, DS), ones_row[:],
                         rep8k[0:1, ds * DS:(ds + 1) * DS],
                         start=True, stop=True).then_inc(mm.h, 1)
        em("tensor", ob_mm)
        MM_OBREP = mm.add(NDS)

        def ob_vec(e):
            e.wait_ge(mm.h, MM_OBREP)
            for ds in range(NDS):
                e.tensor_copy(rep8k[:, ds * DS:(ds + 1) * DS], pbank(ds, DS))
            e.engine_nop().then_inc(vv.h, 1)
        em("vector", ob_vec)
        VV_OBREP = vv.add(1)

        p3 = [dict() for _ in range(NSEG)]
        for seg in range(NSEG):
            c = p3[seg]
            c["wld0"] = ldw.val
            c["tr0"] = trq.val

            def p3w_sync(e, seg=seg, c=c):
                if seg == 0:
                    e.wait_ge(vv.h, VV_OBREP)
                    if dbg:
                        e.wait_ge(gg.h, 14 * 16)
                else:
                    e.wait_ge(vv.h, p3[seg - 1]["tern_end"])

                def load(i):
                    e.dma_start(
                        oraw[i % 2][:],
                        ow_hbm.ap()[i * P:(i + 1) * P,
                                    seg * HSEG:(seg + 1) * HSEG]
                    ).then_inc(ldw.h, 16)
                for i in range(min(2, DC)):
                    load(i)
                if seg > 0:
                    e.wait_ge(mm.h, p3[seg - 1]["mm_end"])
                for i in range(DC):
                    e.wait_ge(vv.h, c["tern0"] + i + 1)
                    for hcl in range(HCS):
                        e.dma_start_transpose(
                            owh[:, hcl * D + i * P:hcl * D + (i + 1) * P],
                            otb[:, hcl * P:(hcl + 1) * P]).then_inc(trq.h, 16)
                    if i + 2 < DC:
                        load(i + 2)
            em("sync", p3w_sync)
            ldw.add(16 * DC)
            c["trdone"] = trq.add(16 * HCS * DC)

            c["tern0"] = vv.val

            def p3w_vec(e, seg=seg, c=c):
                o = wcols["o"]
                for i in range(DC):
                    e.wait_ge(ldw.h, c["wld0"] + 16 * (i + 1))
                    if i >= 1:
                        e.wait_ge(trq.h, c["tr0"] + 16 * HCS * i)
                    e.tensor_scalar(osel[:], oraw[i % 2][:], o["thp"][:], None,
                                    OP.is_gt)
                    e.scalar_tensor_tensor(otb[:], oraw[i % 2][:], o["thn"][:],
                                           osel[:], OP.is_lt, OP.subtract
                                           ).then_inc(vv.h, 1)
            em("vector", p3w_vec)
            c["tern_end"] = vv.add(DC)

            c["hld0"] = ldh.val
            nper = 1 + (1 if seg > 0 else 0) + (1 if seg == NSEG - 1 else 0)
            c["per_tt"] = 16 * nper
            c["evs"] = 4 if seg == NSEG - 1 else 1

            def p3h_sync(e, seg=seg, c=c):
                e.wait_ge(sth.h, STH_DONE)
                if seg > 0:
                    e.wait_ge(stp.h, p3[seg - 1]["stp_end"])
                if seg == NSEG - 1:
                    e.wait_ge(sty.h, STY_DONE)
                for tt in range(TT):
                    if tt >= 2:
                        e.wait_ge(vv.h, c["qv0"] + tt - 1)
                    e.dma_start(
                        hstg[tt % 2][:].rearrange("p (a b) -> p a b", a=HCS),
                        h_sp.ap()[tt, seg * HCS:(seg + 1) * HCS, :, :]
                        .rearrange("a p b -> p a b")
                    ).then_inc(ldh.h, 16)
                    if seg > 0:
                        if tt >= 1:
                            e.wait_ge(hv.h, c["ev0"] + c["evs"] * (tt - 1) + 1)
                        e.dma_start(pbuf[:],
                                    part_sp.ap()[tt * P:(tt + 1) * P, :]
                                    ).then_inc(ldh.h, 16)
                    if seg == NSEG - 1:
                        if tt >= 1:
                            e.wait_ge(hv.h, c["ev0"] + c["evs"] * (tt - 1) + 2)
                        e.dma_start(ycld[:],
                                    yc_sp.ap()[tt * P:(tt + 1) * P, :]
                                    ).then_inc(ldh.h, 16)
            em("sync", p3h_sync)
            ldh.add(c["per_tt"] * TT)

            c["qv0"] = vv.val
            c["ev0"] = hv.val
            c["mm0"] = mm.val
            c["stp0"] = stp.val
            c["ss0"] = ss.val
            last = seg == NSEG - 1

            def p3_vec(e, seg=seg, c=c, last=last):
                def q(tt):
                    e.wait_ge(ldh.h, c["hld0"] + c["per_tt"] * tt + 16)
                    if tt >= 2:
                        e.wait_ge(mm.h, c["mm0"] + tt - 1)
                    hb = hstg[tt % 2]
                    qb = qhT[tt % 2]
                    for hcl in range(HCS):
                        e.tensor_tensor(hb[:, hcl * P:(hcl + 1) * P],
                                        hb[:, hcl * P:(hcl + 1) * P],
                                        s_h_rep[:, tt * P:(tt + 1) * P],
                                        OP.mult)
                    if SIM_MODE:
                        e.tensor_scalar(hb[:], hb[:], 256.5, None, OP.add)
                        e.tensor_scalar(qb[:], hb[:], 1.0, None, OP.mod)
                        e.scalar_tensor_tensor(qb[:], hb[:], -256.0, qb[:],
                                               OP.add, OP.subtract
                                               ).then_inc(vv.h, 1)
                    else:
                        e.tensor_copy(ib_h[tt % 2][:], hb[:])
                        e.tensor_copy(qb[:], ib_h[tt % 2][:]).then_inc(vv.h, 1)

                def evac(tt):
                    e.wait_ge(mm.h, c["mm0"] + tt + 1)
                    if seg > 0:
                        e.wait_ge(ldh.h, c["hld0"] + c["per_tt"] * tt + 32)
                    if last:
                        e.wait_ge(ldh.h, c["hld0"] + c["per_tt"] * (tt + 1))
                    o = obuf[tt % 2]
                    if not last:
                        if tt >= 2:
                            e.wait_ge(stp.h, c["stp0"] + 16 * (tt - 1))
                        for ds in range(NDS):
                            if seg == 0:
                                e.tensor_copy(o[:, ds * DS:(ds + 1) * DS],
                                              pbank(ds, DS))
                            else:
                                e.tensor_tensor(o[:, ds * DS:(ds + 1) * DS],
                                                pbank(ds, DS),
                                                pbuf[:, ds * DS:(ds + 1) * DS],
                                                OP.add)
                        e.engine_nop().then_inc(hv.h, 1)
                        return
                    if tt >= 2:
                        e.wait_ge(sto.h, 16 * (tt - 1))
                    for ds in range(NDS):
                        e.tensor_tensor(o[:, ds * DS:(ds + 1) * DS],
                                        pbank(ds, DS),
                                        pbuf[:, ds * DS:(ds + 1) * DS],
                                        OP.add)
                    e.engine_nop().then_inc(hv.h, 1)
                    e.tensor_scalar(o[:], o[:], sc3_all[:, tt:tt + 1],
                                    None, OP.mult)
                    e.tensor_tensor(o[:], o[:], rep8k[:], OP.add)
                    e.scalar_tensor_tensor(o[:], ycld[:],
                                           rstd_all[:, tt:tt + 1], o[:],
                                           OP.mult, OP.add)
                    e.reduce_sum(mcol[:], o[:], axis=AX)
                    e.tensor_scalar_mul(mcol[:], mcol[:], 1.0 / D
                                        ).then_inc(hv.h, 1)
                    e.wait_ge(ss.h, c["ss0"] + 3 * tt + 1)
                    e.tensor_scalar_mul(tcol[:], tcol[:], 1.0 / D)
                    e.tensor_tensor(tcol2[:], mcol[:], mcol[:], OP.mult)
                    e.tensor_tensor(tcol[:], tcol[:], tcol2[:],
                                    OP.subtract).then_inc(hv.h, 1)
                    e.wait_ge(ss.h, c["ss0"] + 3 * tt + 2)
                    e.reciprocal(tcol[:], tcol[:])
                    e.tensor_tensor(tcol2[:], mcol[:], tcol[:], OP.mult)
                    e.tensor_scalar_mul(tcol2[:], tcol2[:], -1.0
                                        ).then_inc(hv.h, 1)

                for tt in range(TT):
                    q(tt)
                    if tt >= 1:
                        evac(tt - 1)
                evac(TT - 1)
            em("vector", p3_vec)
            vv.add(TT)
            hv.add(c["evs"] * TT)

            def p3_mm(e, seg=seg, c=c):
                e.wait_ge(trq.h, c["trdone"])
                for tt in range(TT):
                    e.wait_ge(vv.h, c["qv0"] + tt + 1)
                    if tt >= 1:
                        e.wait_ge(hv.h, c["ev0"] + c["evs"] * (tt - 1) + 1)
                    for hcl in range(HCS):
                        for ds in range(NDS):
                            i_ = e.matmul(
                                pbank(ds, DS),
                                qhT[tt % 2][:, hcl * P:(hcl + 1) * P],
                                owh[:, hcl * D + ds * DS:hcl * D + (ds + 1) * DS],
                                start=(hcl == 0), stop=(hcl == HCS - 1))
                    i_.then_inc(mm.h, 1)
            em("tensor", p3_mm)
            c["mm_end"] = mm.add(TT)

            if not last:
                def p3e_gp(e, seg=seg, c=c):
                    for tt in range(TT):
                        e.wait_ge(hv.h, c["ev0"] + tt + 1)
                        e.dma_start(part_sp.ap()[tt * P:(tt + 1) * P, :],
                                    obuf[tt % 2][:]).then_inc(stp.h, 16)
                em("gpsimd", p3e_gp)
                c["stp_end"] = stp.add(16 * TT)
            else:
                def p3f_scal(e, c=c):
                    for tt in range(TT):
                        e.wait_ge(hv.h, c["ev0"] + 4 * tt + 2)
                        e.activation(sq_scr[:], obuf[tt % 2][:], AF.Square,
                                     accum_out=tcol[:]).then_inc(ss.h, 1)
                        e.wait_ge(hv.h, c["ev0"] + 4 * tt + 3)
                        e.activation(tcol[:], tcol[:], AF.Sqrt,
                                     bias=eps_col[:]).then_inc(ss.h, 1)
                        e.wait_ge(hv.h, c["ev0"] + 4 * tt + 4)
                        e.activation(obuf[tt % 2][:], obuf[tt % 2][:],
                                     AF.Identity, bias=tcol2[:],
                                     scale=tcol[:]).then_inc(ss.h, 1)
                em("scalar", p3f_scal)
                ss.add(3 * TT)

                def p3f_gp(e, c=c):
                    for tt in range(TT):
                        e.wait_ge(ss.h, c["ss0"] + 3 * tt + 3)
                        e.dma_start(out_hbm.ap()[tt * P:(tt + 1) * P, :],
                                    obuf[tt % 2][:]).then_inc(sto.h, 16)
                    e.wait_ge(sto.h, 16 * TT)
                em("gpsimd", p3f_gp)
                sto.add(16 * TT)

        @block.sync
        def _(e):
            for fn in prog["sync"]:
                fn(e)

        @block.scalar
        def _(e):
            g = ScalGuard(e, sems["sb"], sflush)
            for fn in prog["scalar"]:
                fn(g)

        @block.vector
        def _(e):
            g = VecGuard(e, sems["vb"])
            for fn in prog["vector"]:
                fn(g)

        @block.tensor
        def _(e):
            for fn in prog["tensor"]:
                fn(e)

        @block.gpsimd
        def _(e):
            for fn in prog["gpsimd"]:
                fn(e)

    return nc


def _prep_inputs(L0_attn_x, L0_attn_B, L0_attn_C, L0_attn_D,
                 L0_ffn_gate_w, L0_ffn_gate_b, L0_ffn_val_w, L0_ffn_val_b,
                 L0_ffn_out_w, L0_ffn_out_b, n_cores=8):
    x = np.ascontiguousarray(np.asarray(L0_attn_x, dtype=np.float32)
                             ).reshape(-1, np.asarray(L0_attn_x).shape[-1])
    T_total, D = x.shape
    T = T_total // n_cores
    H = np.asarray(L0_ffn_gate_w).shape[0]
    NS = np.asarray(L0_attn_B).shape[0]
    HC = H // P

    common = {
        "bw": np.ascontiguousarray(L0_attn_B, dtype=np.float32),
        "cw": np.ascontiguousarray(L0_attn_C, dtype=np.float32),
        "dv": np.ascontiguousarray(L0_attn_D, dtype=np.float32).reshape(1, D),
        "gw": np.ascontiguousarray(L0_ffn_gate_w, dtype=np.float32),
        "vw": np.ascontiguousarray(L0_ffn_val_w, dtype=np.float32),
        "ow": np.ascontiguousarray(L0_ffn_out_w, dtype=np.float32),
        "gb": np.ascontiguousarray(
            np.asarray(L0_ffn_gate_b, dtype=np.float32).reshape(HC, P).T),
        "vb": np.ascontiguousarray(
            np.asarray(L0_ffn_val_b, dtype=np.float32).reshape(HC, P).T),
        "ob": np.ascontiguousarray(L0_ffn_out_b, dtype=np.float32).reshape(1, D),
        "idm": np.eye(P, dtype=np.float32),
    }
    in_maps = []
    for c in range(n_cores):
        m = dict(common)
        m["x"] = np.ascontiguousarray(x[c * T:(c + 1) * T])
        in_maps.append(m)
    return in_maps, (T, D, H, NS)


def kernel(L0_attn_x, L0_attn_B, L0_attn_C, L0_attn_D,
           L0_ffn_gate_w, L0_ffn_gate_b, L0_ffn_val_w, L0_ffn_val_b,
           L0_ffn_out_w, L0_ffn_out_b):
    n_cores = 8
    in_maps, (T, D, H, NS) = _prep_inputs(
        L0_attn_x, L0_attn_B, L0_attn_C, L0_attn_D,
        L0_ffn_gate_w, L0_ffn_gate_b, L0_ffn_val_w, L0_ffn_val_b,
        L0_ffn_out_w, L0_ffn_out_b, n_cores)
    nc = build_nc(T, D, H, NS)
    res = run_bass_kernel_spmd(nc, in_maps, core_ids=list(range(n_cores)))
    out = np.concatenate([r["out"] for r in res.results], axis=0)
    return out.reshape(np.asarray(L0_attn_x).shape).astype(np.float32)


# version 08
# speedup vs baseline: 1.0193x; 1.0193x over previous
import os
import numpy as np

import concourse.bass as bass
import concourse.mybir as mybir
from concourse.bass_utils import run_bass_kernel_spmd

F32 = mybir.dt.float32
BF16 = mybir.dt.bfloat16
I32 = mybir.dt.int32
AF = mybir.ActivationFunctionType
OP = mybir.AluOpType
AX = mybir.AxisListType.X

SIM_MODE = (os.environ.get("BASS_SIM_FP64") == "1"
            or os.environ.get("KERNEL_SIMROUND") == "1")

EPS = 1e-5
THR_M = 1e-5 * (EPS ** 0.5)
P = 128


class Sem:
    def __init__(self, handle):
        self.h = handle
        self.val = 0

    def add(self, n):
        self.val += n
        return self.val


def _ap_range(ap):
    t = ap.tensor
    isz = 2 if t.dtype == BF16 else 4
    span = 1
    dims = list(ap.ap)
    for step, count in dims[1:]:
        span += int(step) * (int(count) - 1)
    base = 0
    space = type(t).__name__
    r = getattr(t, "manual_sbuf_range", None)
    if r is not None:
        base = int(r[0])
    lo = base + int(ap.offset) * isz
    return (space, lo, lo + span * isz)


def _overlap(a, b):
    return a[0] == b[0] and a[1] < b[2] and b[1] < a[2]


class VecGuard:

    _OPS = {
        "tensor_copy": (1, [0]), "tensor_tensor": (2, [0]),
        "tensor_add": (2, [0]), "tensor_sub": (2, [0]),
        "tensor_mul": (2, [0]), "tensor_max": (2, [0]),
        "tensor_scalar": (3, [0]), "tensor_scalar_mul": (2, [0]),
        "tensor_scalar_add": (2, [0]), "tensor_scalar_max": (2, [0]),
        "scalar_tensor_tensor": (4, [0]), "reduce_sum": (2, [0]),
        "reduce_max": (2, [0]), "reciprocal": (2, [0]),
    }

    def __init__(self, eng, sem):
        self._e = eng
        self._sem = sem
        self._writes = []

    def _flush(self):
        if self._writes:
            self._writes = []
            self._e.engine_nop().then_inc(self._sem.h, 1)
            self._e.wait_ge(self._sem.h, self._sem.add(1))

    def wait_ge(self, *a, **k):
        return self._e.wait_ge(*a, **k)

    def memset(self, ap, v):
        i = self._e.memset(ap, v)
        self._writes.append(_ap_range(ap))
        return i

    def engine_nop(self):
        return self._e.engine_nop()

    def __getattr__(self, name):
        fn = getattr(self._e, name)
        spec = self._OPS.get(name)
        if spec is None:
            return fn
        npos, outpos = spec

        def wrapped(*args, **kwargs):
            ins = []
            outs = []
            for i, a in enumerate(args[:npos + 2]):
                if hasattr(a, "tensor"):
                    (outs if i in outpos else ins).append(_ap_range(a))
            for k_, a in kwargs.items():
                if hasattr(a, "tensor"):
                    (outs if k_ == "accum_out" else ins).append(_ap_range(a))
            if any(_overlap(r, w) for r in ins for w in self._writes):
                self._flush()
            res = fn(*args, **kwargs)
            self._writes.extend(outs)
            return res
        return wrapped


class ScalGuard(VecGuard):

    _OPS = {
        "activation": (2, [0]), "copy": (2, [0]),
        "add": (2, [0]), "mul": (2, [0]),
    }

    def __init__(self, eng, sem, scratch):
        super().__init__(eng, sem)
        self._scr = scratch

    def _flush(self):
        if self._writes:
            self._writes = []
            self._e.copy(self._scr[:], self._scr[:]).then_inc(self._sem.h, 1)
            self._e.wait_ge(self._sem.h, self._sem.add(1))


def build_nc(T, D, H, NS, dbg=False):
    TT = T // P
    DC = D // P
    HC = H // P
    NCH = NS // P
    SLAB = min(512, T)
    NSLAB = T // SLAB
    TPS = SLAB // P
    DS = min(512, D)
    NDS = D // DS
    NSEG = 4
    HCS = HC // NSEG
    HSEG = H // NSEG
    assert TT <= 64 and DC <= 64

    nc = bass.Bass("TRN2", target_bir_lowering=False,
                   detect_race_conditions=False)

    x_hbm = nc.dram_tensor("x", [T, D], F32, kind="ExternalInput")
    b_hbm = nc.dram_tensor("bw", [NS, D], F32, kind="ExternalInput")
    c_hbm = nc.dram_tensor("cw", [D, NS], F32, kind="ExternalInput")
    d_hbm = nc.dram_tensor("dv", [1, D], F32, kind="ExternalInput")
    gw_hbm = nc.dram_tensor("gw", [H, D], F32, kind="ExternalInput")
    vw_hbm = nc.dram_tensor("vw", [H, D], F32, kind="ExternalInput")
    ow_hbm = nc.dram_tensor("ow", [D, H], F32, kind="ExternalInput")
    gb_hbm = nc.dram_tensor("gb", [P, HC], F32, kind="ExternalInput")
    vb_hbm = nc.dram_tensor("vb", [P, HC], F32, kind="ExternalInput")
    ob_hbm = nc.dram_tensor("ob", [1, D], F32, kind="ExternalInput")
    id_hbm = nc.dram_tensor("idm", [P, P], F32, kind="ExternalInput")
    out_hbm = nc.dram_tensor("out", [T, D], F32, kind="ExternalOutput")

    _spk = "ExternalOutput" if dbg else "Internal"
    yc_sp = nc.dram_tensor("yc_sp", [T, D], BF16, kind=_spk)
    h_sp = nc.dram_tensor("h_sp", [TT, HC, P, P], F32, kind=_spk)
    part_sp = nc.dram_tensor("part_sp", [T, D], F32, kind=_spk)
    if dbg:
        state_d = nc.dram_tensor("state_d", [P, (NS // P) * T], BF16,
                                 kind="ExternalOutput")
        qyT_d = nc.dram_tensor("qyT_d", [P, (D // P) * T], BF16,
                               kind="ExternalOutput")
        rho_d = nc.dram_tensor("rho_d", [P, T], F32, kind="ExternalOutput")
        srep_d = nc.dram_tensor("srep_d", [P, T], F32, kind="ExternalOutput")
        gcol_d = nc.dram_tensor("gcol_d", [P, 16], F32, kind="ExternalOutput")
        sc3_d = nc.dram_tensor("sc3_d", [P, T // P], F32, kind="ExternalOutput")

    base0 = (int(nc.sbuf_base) + 63) & ~63
    top = int(nc.sbuf_top)
    _cur = [base0]

    def place(name, shape, dtype, at=None):
        nbytes = int(np.prod(shape[1:])) * (2 if dtype == BF16 else 4)
        if at is None:
            off = _cur[0]
            _cur[0] = (off + nbytes + 63) & ~63
        else:
            off = at
        assert off + nbytes <= top, f"SBUF overflow placing {name} at {off}"
        return nc.alloc_sbuf_tensor_at(name, list(shape), dtype, offset=off), off

    def bump():
        return _cur[0]

    ident, _ = place("ident", [P, P], F32)
    gb_sb, _ = place("gb_sb", [P, HC], F32)
    vb_sb, _ = place("vb_sb", [P, HC], F32)
    rep8k, _ = place("rep8k", [P, D], F32)
    sq_scr, _ = place("sq_scr", [P, D], BF16)
    gacc, GACC_OFF = place("gacc", [P, 64], F32)
    gred, _ = place("gred", [P, 1], F32)
    gfull, _ = place("gfull", [P, 1], F32)
    wcols = {}
    for wname in ("g", "v", "o"):
        wcols[wname] = {}
        for k in ("thp", "thn", "nsc"):
            wcols[wname][k], _ = place(f"{k}_{wname}", [P, 1], F32)
    m_all, _ = place("m_all", [P, TT], F32)
    M_all, _ = place("M_all", [P, TT], F32)
    ssq_all, _ = place("ssq_all", [P, TT], F32)
    rstd_all, _ = place("rstd_all", [P, TT], F32)
    rho_all, _ = place("rho_all", [P, TT], F32)
    sc3_all, _ = place("sc3_all", [P, TT], F32)
    scol, _ = place("scol", [P, 1], F32)
    mcol, _ = place("mcol", [P, 1], F32)
    tcol, _ = place("tcol", [P, 1], F32)
    tcol2, _ = place("tcol2", [P, 1], F32)
    eps_col, _ = place("eps_col", [P, 1], F32)
    ones_row, _ = place("ones_row", [1, P], F32)
    ones_col, _ = place("ones_col", [P, 1], F32)
    rowb, _ = place("rowb", [1, P], F32)
    sclr, _ = place("sclr", [1, 1], F32)
    sflush, _ = place("sflush", [1, 1], F32)

    PERS_END = bump()

    GW = min(max(D, H), 4096)
    A_SZ = max(DC * T * 2, HCS * D * 2, 2 * GW * 4 + GW * 2)
    SLOT_A = bump()
    _cur[0] = SLOT_A + ((A_SZ + 63) & ~63)
    B_SZ = max(NCH * T * 2, 4 * D * 4, 2 * HCS * P * 4 + 2 * HCS * P * 2 + D * 4)
    SLOT_B = bump()
    _cur[0] = SLOT_B + ((B_SZ + 63) & ~63)
    C_SZ = max(DC * NS * 2, NCH * D * 2,
               8 * D * 2,
               2 * HSEG * 4 + 2 * HSEG * 2)
    SLOT_C = bump()
    _cur[0] = SLOT_C + ((C_SZ + 63) & ~63)
    D_SZ = max(DC * SLAB * 2,
               3 * D * 4,
               6 * SLAB * 4 + 2 * T * 4,
               D * 2 + 2 * D * 4)
    SLOT_D = bump()
    _cur[0] = SLOT_D + ((D_SZ + 63) & ~63)
    E_SZ = max(2 * D * 4, D * 4)
    SLOT_E = bump()
    _cur[0] = SLOT_E + ((E_SZ + 63) & ~63)
    F_SZ = max(2 * D * 2, 2 * NS * 4, T * 4)
    SLOT_F = bump()
    _cur[0] = SLOT_F + ((F_SZ + 63) & ~63)
    G_SZ = 2 * NS * 2
    SLOT_G = bump()
    _cur[0] = SLOT_G + ((G_SZ + 63) & ~63)
    H_SZ = D * 2
    SLOT_H = bump()
    _cur[0] = SLOT_H + ((H_SZ + 63) & ~63)
    I_SZ = D * 2
    SLOT_I = bump()
    _cur[0] = SLOT_I + ((I_SZ + 63) & ~63)
    assert bump() <= top, f"arena overflow: {bump()} > {top}"

    def at(name, shape, dtype, off):
        t_, _ = place(name, shape, dtype, at=off)
        return t_

    s_h_rep = at("s_h_rep", [P, T], F32, SLOT_F)

    gld = [at(f"gld{i}", [P, GW], F32, SLOT_A + i * GW * 4) for i in range(2)]
    gscr = at("gscr", [P, GW], BF16, SLOT_A + 2 * GW * 4)
    qyT_sb = at("qyT_sb", [P, DC * T], BF16, SLOT_A)
    state_sb = at("state_sb", [P, NCH * T], BF16, SLOT_B)
    bt_sb = at("bt_sb", [P, DC * NS], BF16, SLOT_C)
    ct_sb = at("ct_sb", [P, NCH * D], BF16, SLOT_C)
    xT_sb = at("xT_sb", [P, DC * SLAB], BF16, SLOT_D)
    wbuf = [at(f"wbuf{i}", [P, D], F32, SLOT_D + i * D * 4) for i in range(2)]
    t256 = at("t256", [P, D], F32, SLOT_D + 2 * D * 4)
    xld = [at(f"xld{i}", [P, D], F32, SLOT_E + i * D * 4) for i in range(2)]
    xbf = [at(f"xbf{i}", [P, D], BF16, SLOT_F + i * D * 2) for i in range(2)]
    cld = [at(f"cld{i}", [P, NS], F32, SLOT_F + i * NS * 4) for i in range(2)]
    cbf = [at(f"cbf{i}", [P, NS], BF16, SLOT_G + i * NS * 2) for i in range(2)]
    ycbf = at("ycbf", [P, D], BF16, SLOT_H)
    qybf = at("qybf", [P, D], BF16, SLOT_I)
    ib_q = at("ib_q", [P, D], I32, SLOT_D + 2 * D * 4)
    graw = [at(f"graw{i}", [P, D], F32, SLOT_B + i * D * 4) for i in range(2)]
    vraw = [at(f"vraw{i}", [P, D], F32, SLOT_B + (2 + i) * D * 4)
            for i in range(2)]
    gtb = at("gtb", [P, D], BF16, SLOT_C)
    vtb = at("vtb", [P, D], BF16, SLOT_C + D * 2)
    dsel = at("dsel", [P, D], BF16, SLOT_C + 2 * D * 2)
    dse2 = at("dse2", [P, D], BF16, SLOT_C + 7 * D * 2)
    gwT = [at(f"gwT{i}", [P, D], BF16, SLOT_C + (3 + i) * D * 2)
           for i in range(2)]
    vwT = [at(f"vwT{i}", [P, D], BF16, SLOT_C + (5 + i) * D * 2)
           for i in range(2)]
    gsl = [at(f"gsl{i}", [P, SLAB], F32, SLOT_D + i * SLAB * 4)
           for i in range(2)]
    vsl = [at(f"vsl{i}", [P, SLAB], F32, SLOT_D + (2 + i) * SLAB * 4)
           for i in range(2)]
    hsl = [at(f"hsl{i}", [P, SLAB], F32, SLOT_D + (4 + i) * SLAB * 4)
           for i in range(2)]
    mrun = at("mrun", [P, T], F32, SLOT_D + 6 * SLAB * 4)
    rho_rep = at("rho_rep", [P, T], F32, SLOT_D + 6 * SLAB * 4 + T * 4)
    owh = at("owh", [P, HCS * D], BF16, SLOT_A)
    oraw = [at(f"oraw{i}", [P, HSEG], F32, SLOT_C + i * HSEG * 4)
            for i in range(2)]
    otb = at("otb", [P, HSEG], BF16, SLOT_C + 2 * HSEG * 4)
    osel = at("osel", [P, HSEG], BF16, SLOT_C + 2 * HSEG * 4 + HSEG * 2)
    hstg = [at(f"hstg{i}", [P, HCS * P], F32, SLOT_B + i * HCS * P * 4)
            for i in range(2)]
    ib_h = [at(f"ib_h{i}", [P, HCS * P], I32, SLOT_B + i * HCS * P * 4)
            for i in range(2)]
    qhT = [at(f"qhT{i}", [P, HCS * P], BF16,
              SLOT_B + 2 * HCS * P * 4 + i * HCS * P * 2) for i in range(2)]
    pbuf = at("pbuf", [P, D], F32, SLOT_E)
    ycld = at("ycld", [P, D], BF16, SLOT_D)
    obuf = [at(f"obuf{i}", [P, D], F32, SLOT_D + D * 2 + i * D * 4)
            for i in range(2)]

    PS = nc.alloc_psum_tensor("PS", [P, 4096], F32)
    BANK = 512

    def pbank(b, width=BANK):
        return PS[:, b * BANK:b * BANK + width]

    import contextlib
    st = contextlib.ExitStack()
    with st:
        S = st.enter_context
        sems = {}
        for name in ("ldc", "ldx", "ldw", "ldh", "trq", "mm", "ev", "vv",
                     "hv", "ss", "gg", "sty", "sth", "stp", "sto",
                     "vb", "sb", "ts2", "th", "sq", "ga"):
            sems[name] = Sem(S(nc.semaphore(name)))
        ldc, ldx, ldw, ldh = (sems[k] for k in ("ldc", "ldx", "ldw", "ldh"))
        trq, mm, ev, vv = (sems[k] for k in ("trq", "mm", "ev", "vv"))
        ss, gg, hv = sems["ss"], sems["gg"], sems["hv"]
        sty, sth, stp, sto = (sems[k] for k in ("sty", "sth", "stp", "sto"))

        block = S(nc.Block())
        prog = {k: [] for k in ("sync", "scalar", "vector", "tensor", "gpsimd")}
        _scope = ["s0"]

        def em(engine, fn, scope=None):
            sc = scope or _scope[0]
            def tagged(e, fn=fn, sc=sc):
                with nc.named_scope(sc):
                    fn(e)
            prog[engine].append(tagged)

        def set_scope(name):
            _scope[0] = name

        def s0_sync(e):
            e.dma_start(ident[:], id_hbm.ap()).then_inc(ldc.h, 16)
            e.dma_start(gb_sb[:], gb_hbm.ap()).then_inc(ldc.h, 16)
            e.dma_start(vb_sb[:], vb_hbm.ap()).then_inc(ldc.h, 16)
            e.dma_start(rep8k[0:1, :], d_hbm.ap()).then_inc(ldc.h, 16)
        em("sync", s0_sync)
        ldc.add(64)

        def s0_vec(e):
            e.memset(sflush[:], 0.0)
            e.memset(eps_col[:], EPS)
            e.memset(ones_row[:], 1.0)
            e.memset(ones_col[:], 1.0).then_inc(vv.h, 1)
        em("vector", s0_vec)
        VV_ONES = vv.add(1)

        def s0_mm(e):
            e.wait_ge(vv.h, VV_ONES)
            e.wait_ge(ldc.h, 64)
            for ds in range(NDS):
                e.matmul(pbank(ds, DS), ones_row[:],
                         rep8k[0:1, ds * DS:(ds + 1) * DS],
                         start=True, stop=True).then_inc(mm.h, 1)
        em("tensor", s0_mm)
        MM_DREP = mm.add(NDS)

        def s0_vec2(e):
            e.wait_ge(mm.h, MM_DREP)
            for ds in range(NDS):
                e.tensor_scalar(rep8k[:, ds * DS:(ds + 1) * DS],
                                pbank(ds, DS), 1.0, None, OP.add)
            e.engine_nop().then_inc(vv.h, 1)
        em("vector", s0_vec2)
        VV_DREP = vv.add(1)

        set_scope("gamma")
        GBIAS = 1.0 if SIM_MODE else (1.0 + 2.3e-7)
        for wname, whbm, rows, cols in (("g", gw_hbm, H, D),
                                        ("v", vw_hbm, H, D),
                                        ("o", ow_hbm, D, H)):
            cpr = max(1, cols // GW)
            w_ = min(cols, GW)
            ntile = (rows // P) * cpr
            inv_n = GBIAS / float(rows * cols)
            assert ntile <= 64

            ld0 = ldw.val
            ab0 = ss.val
            prev_gred = vv.val

            def g_sync(e, whbm=whbm, rows=rows, cpr=cpr, w_=w_, ab0=ab0):
                for i in range(rows // P):
                    for j in range(cpr):
                        k = i * cpr + j
                        if k < 2:
                            e.wait_ge(ss.h, ab0)
                        else:
                            e.wait_ge(ss.h, ab0 + k - 1)
                        e.dma_start(
                            gld[k % 2][:, 0:w_],
                            whbm.ap()[i * P:(i + 1) * P, j * w_:(j + 1) * w_]
                        ).then_inc(ldw.h, 16)
            em("sync", g_sync)
            ldw.add(16 * ntile)

            def g_act(e, ntile=ntile, w_=w_, ld0=ld0, first=(wname == "g"),
                      prev_gred=prev_gred):
                if first:
                    e.wait_ge(ldc.h, 64)
                    e.wait_ge(vv.h, VV_ONES)
                e.wait_ge(vv.h, prev_gred)
                for k in range(ntile):
                    e.wait_ge(ldw.h, ld0 + 16 * (k + 1))
                    e.activation(gscr[:, 0:w_], gld[k % 2][:, 0:w_], AF.Abs,
                                 accum_out=gacc[:, k:k + 1]).then_inc(ss.h, 1)
            em("scalar", g_act)
            SS_ABS = ss.add(ntile)

            def g_vec(e, ntile=ntile, tgt=SS_ABS):
                e.wait_ge(ss.h, tgt)
                e.reduce_sum(gred[:], gacc[:, 0:ntile], axis=AX).then_inc(vv.h, 1)
            em("vector", g_vec)
            VV_GRED = vv.add(1)

            def g_mm1(e, tgt=VV_GRED):
                e.wait_ge(vv.h, tgt)
                e.matmul(PS[0:1, 3584:3585], gred[:], ones_col[:],
                         start=True, stop=True).then_inc(mm.h, 1)
            em("tensor", g_mm1)
            MM_GSUM = mm.add(1)

            def g_vsc(e, tgt=MM_GSUM):
                e.wait_ge(mm.h, tgt)
                e.tensor_copy(sclr[:], PS[0:1, 3584:3585]).then_inc(vv.h, 1)
            em("vector", g_vsc)
            VV_GSC = vv.add(1)

            def g_mm2(e, tgt=VV_GSC):
                e.wait_ge(vv.h, tgt)
                e.matmul(PS[:, 3585:3586], ones_row[:], sclr[:],
                         start=True, stop=True).then_inc(mm.h, 1)
            em("tensor", g_mm2)
            MM_GBC = mm.add(1)

            def g_vec2(e, wname=wname, inv_n=inv_n, tgt=MM_GBC):
                c = wcols[wname]
                e.wait_ge(mm.h, tgt)
                e.tensor_scalar(gfull[:], PS[:, 3585:3586], inv_n, EPS,
                                OP.mult, OP.max)
                e.tensor_scalar_mul(c["thp"][:], gfull[:], 0.5)
                e.tensor_scalar_mul(c["thn"][:], gfull[:], -0.5)
                e.tensor_scalar_mul(c["nsc"][:], gfull[:],
                                    -0.5 if wname in ("g", "v") else -1.0
                                    ).then_inc(vv.h, 1)
            em("vector", g_vec2)
            vv.add(1)
        VV_GAMMA = vv.val

        set_scope("p1a")
        bld0 = ldx.val
        bcast0 = vv.val
        btr0 = trq.val

        def b_sync(e):
            for i in range(min(2, NCH)):
                e.dma_start(xld[i][:], b_hbm.ap()[i * P:(i + 1) * P, :]
                            ).then_inc(ldx.h, 16)
            for i in range(NCH):
                e.wait_ge(vv.h, bcast0 + i + 1)
                for dc in range(DC):
                    e.dma_start_transpose(
                        bt_sb[:, dc * NS + i * P:dc * NS + (i + 1) * P],
                        xbf[i % 2][:, dc * P:(dc + 1) * P]).then_inc(trq.h, 16)
                if i + 2 < NCH:
                    e.dma_start(xld[i % 2][:],
                                b_hbm.ap()[(i + 2) * P:(i + 3) * P, :]
                                ).then_inc(ldx.h, 16)
        em("sync", b_sync)
        ldx.add(16 * NCH)
        TRQ_BT = trq.add(16 * DC * NCH)

        def b_vec(e):
            e.wait_ge(vv.h, VV_GAMMA)
            for i in range(NCH):
                e.wait_ge(ldx.h, bld0 + 16 * (i + 1))
                if i >= 2:
                    e.wait_ge(trq.h, btr0 + 16 * DC * (i - 1))
                e.tensor_copy(xbf[i % 2][:], xld[i % 2][:]).then_inc(vv.h, 1)
        em("vector", b_vec)
        vv.add(NCH)
        VV_BCAST = vv.val

        xld0 = ldx.val
        xcast0 = vv.val
        xtr0 = trq.val
        z1mm0 = mm.val
        SS_TANH0 = None

        def x_sync(e):
            def load(k):
                if k < 2:
                    e.wait_ge(vv.h, VV_BCAST)
                e.dma_start(xld[k % 2][:],
                            x_hbm.ap()[k * P:(k + 1) * P, :]
                            ).then_inc(ldx.h, 16)
            for k in range(min(2, TT)):
                load(k)
            k = 0
            for s_ in range(NSLAB):
                if s_ >= 1:
                    e.wait_ge(mm.h, z1mm0 + s_ * NCH)
                for j in range(TPS):
                    e.wait_ge(vv.h, xcast0 + k + 1)
                    for dc in range(DC):
                        e.dma_start_transpose(
                            xT_sb[:, dc * SLAB + j * P:dc * SLAB + (j + 1) * P],
                            xbf[k % 2][:, dc * P:(dc + 1) * P]).then_inc(trq.h, 16)
                    if k + 2 < TT:
                        load(k + 2)
                    k += 1
        em("sync", x_sync)
        ldx.add(16 * TT)
        TRQ_XT = trq.add(16 * DC * TT)

        def x_vec(e):
            e.wait_ge(trq.h, TRQ_BT)
            k = 0
            for s_ in range(NSLAB):
                for j in range(TPS):
                    e.wait_ge(ldx.h, xld0 + 16 * (k + 1))
                    if k >= 2:
                        e.wait_ge(trq.h, xtr0 + 16 * DC * (k - 1))
                    e.tensor_copy(xbf[k % 2][:], xld[k % 2][:]).then_inc(vv.h, 1)
                    k += 1
        em("vector", x_vec)
        vv.add(TT)

        def z1_mm(e):
            e.wait_ge(trq.h, TRQ_BT)
            k = 0
            for s_ in range(NSLAB):
                e.wait_ge(trq.h, xtr0 + 16 * DC * TPS * (s_ + 1))
                for nch in range(NCH):
                    ps = pbank(4 + (k % 2), SLAB)
                    if k >= 2:
                        e.wait_ge(ss.h, SS_TANH0 + k - 1)
                    for dc in range(DC):
                        i_ = e.matmul(
                            ps,
                            bt_sb[:, dc * NS + nch * P:dc * NS + (nch + 1) * P],
                            xT_sb[:, dc * SLAB:(dc + 1) * SLAB],
                            start=(dc == 0), stop=(dc == DC - 1))
                    i_.then_inc(mm.h, 1)
                    k += 1
        em("tensor", z1_mm)
        mm.add(NSLAB * NCH)

        SS_TANH0 = ss.val

        def z1_tanh(e):
            k = 0
            for s_ in range(NSLAB):
                for nch in range(NCH):
                    e.wait_ge(mm.h, z1mm0 + k + 1)
                    e.activation(
                        state_sb[:, nch * T + s_ * SLAB:nch * T + (s_ + 1) * SLAB],
                        pbank(4 + (k % 2), SLAB), AF.Tanh).then_inc(ss.h, 1)
                    k += 1
        em("scalar", z1_tanh)
        ss.add(NSLAB * NCH)
        SS_P1A = ss.val

        set_scope("p1b")
        cld0 = ldx.val
        ccast0 = vv.val
        ctr0 = trq.val

        def c_sync(e):
            e.wait_ge(trq.h, TRQ_XT)
            for i in range(min(2, DC)):
                e.dma_start(cld[i][:], c_hbm.ap()[i * P:(i + 1) * P, :]
                            ).then_inc(ldx.h, 16)
            e.wait_ge(mm.h, z1mm0 + NSLAB * NCH)
            for i in range(DC):
                e.wait_ge(vv.h, ccast0 + i + 1)
                for nch in range(NCH):
                    e.dma_start_transpose(
                        ct_sb[:, nch * D + i * P:nch * D + (i + 1) * P],
                        cbf[i % 2][:, nch * P:(nch + 1) * P]).then_inc(trq.h, 16)
                if i + 2 < DC:
                    e.dma_start(cld[i % 2][:],
                                c_hbm.ap()[(i + 2) * P:(i + 3) * P, :]
                                ).then_inc(ldx.h, 16)
        em("sync", c_sync)
        ldx.add(16 * DC)
        TRQ_CT = trq.add(16 * NCH * DC)

        def c_vec(e):
            for i in range(DC):
                e.wait_ge(ldx.h, cld0 + 16 * (i + 1))
                if i >= 2:
                    e.wait_ge(trq.h, ctr0 + 16 * NCH * (i - 1))
                e.tensor_copy(cbf[i % 2][:], cld[i % 2][:]).then_inc(vv.h, 1)
        em("vector", c_vec)
        vv.add(DC)

        x2ld0 = ldx.val
        P1B_V0 = None
        P1B_EV0 = None

        def x2_sync(e):
            e.wait_ge(vv.h, xcast0 + TT)
            for tt in range(min(2, TT)):
                e.dma_start(xld[tt][:], x_hbm.ap()[tt * P:(tt + 1) * P, :]
                            ).then_inc(ldx.h, 16)
        em("sync", x2_sync)
        ldx.add(16 * min(2, TT))

        mm2_0 = mm.val

        def mm2(e):
            e.wait_ge(trq.h, TRQ_CT)
            e.wait_ge(ss.h, SS_P1A)
            for tt in range(TT):
                if tt >= 1:
                    e.wait_ge(ev.h, P1B_EV0 + tt)
                for ds in range(NDS):
                    for nch in range(NCH):
                        i_ = e.matmul(
                            pbank(ds, DS),
                            state_sb[:, nch * T + tt * P:nch * T + (tt + 1) * P],
                            ct_sb[:, nch * D + ds * DS:nch * D + (ds + 1) * DS],
                            start=(nch == 0), stop=(nch == NCH - 1))
                i_.then_inc(mm.h, 1)
        em("tensor", mm2)
        mm.add(TT)

        P1B_V0 = vv.val
        P1B_EV0 = ev.val
        SS_SQ0 = None
        QYTR0 = None

        def p1b_vec(e):
            for tt in range(TT):
                e.wait_ge(ldx.h, x2ld0 + 16 * (tt + 1))
                e.wait_ge(mm.h, mm2_0 + tt + 1)
                w = wbuf[tt % 2]
                if tt >= 2:
                    e.wait_ge(ss.h, SS_SQ0 + tt - 1)
                if tt >= 1:
                    e.wait_ge(sty.h, 16 * tt)
                e.tensor_tensor(w[:], xld[tt % 2][:], rep8k[:], OP.mult)
                e.tensor_tensor(w[:], w[:], PS[:, 0:D], OP.add).then_inc(ev.h, 1)
                e.reduce_sum(mcol[:], w[:], axis=AX)
                e.tensor_scalar_mul(mcol[:], mcol[:], 1.0 / D)
                e.tensor_copy(m_all[:, tt:tt + 1], mcol[:])
                e.tensor_scalar(w[:], w[:], mcol[:], None, OP.subtract)
                e.reduce_max(M_all[:, tt:tt + 1], w[:], axis=AX,
                             apply_absolute_value=True)
                e.tensor_copy(ycbf[:], w[:]).then_inc(vv.h, 1)
                e.tensor_scalar(tcol[:], M_all[:, tt:tt + 1], THR_M, None, OP.max)
                e.reciprocal(tcol[:], tcol[:])
                e.tensor_scalar_mul(scol[:], tcol[:], 127.0)
                if tt >= 1:
                    e.wait_ge(trq.h, QYTR0 + 16 * DC * tt)
                if SIM_MODE:
                    e.tensor_scalar(t256[:], w[:], scol[:], 256.5,
                                    OP.mult, OP.add)
                    e.tensor_scalar(w[:], t256[:], 1.0, None, OP.mod)
                    e.scalar_tensor_tensor(qybf[:], t256[:], -256.0, w[:],
                                           OP.add, OP.subtract).then_inc(vv.h, 1)
                else:
                    e.tensor_scalar(t256[:], w[:], scol[:], None, OP.mult)
                    e.tensor_copy(ib_q[:], t256[:])
                    e.tensor_copy(qybf[:], ib_q[:]).then_inc(vv.h, 1)
        em("vector", p1b_vec)
        vv.add(2 * TT)
        ev.add(TT)

        SS_SQ0 = ss.val

        def p1b_scal(e):
            for tt in range(TT):
                e.wait_ge(vv.h, P1B_V0 + 2 * tt + 1)
                e.activation(sq_scr[:], wbuf[tt % 2][:], AF.Square,
                             accum_out=ssq_all[:, tt:tt + 1]).then_inc(ss.h, 1)
        em("scalar", p1b_scal)
        ss.add(TT)

        def p1b_gp(e):
            for tt in range(TT):
                e.wait_ge(vv.h, P1B_V0 + 2 * tt + 1)
                e.dma_start(yc_sp.ap()[tt * P:(tt + 1) * P, :], ycbf[:]
                            ).then_inc(sty.h, 16)
        em("gpsimd", p1b_gp)
        STY_DONE = sty.add(16 * TT)

        if dbg:
            def dbg1_gp(e):
                e.wait_ge(ss.h, SS_P1A)
                e.dma_start(state_d.ap(), state_sb[:]).then_inc(gg.h, 16)
            em("gpsimd", dbg1_gp)
            gg.add(16)

        QYTR0 = trq.val

        def qy_tr(e):
            for tt in range(TT):
                e.wait_ge(vv.h, P1B_V0 + 2 * tt + 2)
                for dc in range(DC):
                    e.dma_start_transpose(
                        qyT_sb[:, dc * T + tt * P:dc * T + (tt + 1) * P],
                        qybf[:, dc * P:(dc + 1) * P]).then_inc(trq.h, 16)
                if tt + 2 < TT:
                    e.dma_start(xld[tt % 2][:],
                                x_hbm.ap()[(tt + 2) * P:(tt + 3) * P, :]
                                ).then_inc(ldx.h, 16)
        em("sync", qy_tr)
        TRQ_QY = trq.add(16 * DC * TT)
        ldx.add(16 * max(0, TT - 2))

        set_scope("p1e")
        TRPC = 6 * BANK

        def p1e_scal(e):
            e.wait_ge(vv.h, P1B_V0 + 2 * TT)
            e.activation(rstd_all[:], ssq_all[:], AF.Sqrt,
                         scale=1.0 / D, bias=eps_col[:]).then_inc(ss.h, 1)
        em("scalar", p1e_scal)
        SS_RSTD = ss.add(1)

        def p1e_vec(e):
            e.wait_ge(ss.h, SS_RSTD)
            e.reciprocal(rstd_all[:], rstd_all[:])
            e.tensor_scalar(rho_all[:], M_all[:], THR_M, None, OP.max)
            e.tensor_tensor(rho_all[:], rho_all[:], rstd_all[:], OP.mult)
            e.tensor_scalar_mul(rho_all[:], rho_all[:], 1.0 / 127.0
                                ).then_inc(vv.h, 1)
        em("vector", p1e_vec)
        VV_RHO = vv.add(1)

        BCC = 7 * BANK
        RB_M0 = mm.val
        RB_V0 = vv.val

        def p1e_mmT(e):
            e.wait_ge(vv.h, VV_RHO)
            for tt in range(TT):
                if tt > 0:
                    e.wait_ge(vv.h, RB_V0 + 2 * tt - 1)
                e.transpose(PS[0:1, TRPC:TRPC + P], rho_all[:, tt:tt + 1],
                            ident[:]).then_inc(mm.h, 1)
                e.wait_ge(vv.h, RB_V0 + 2 * tt + 1)
                e.matmul(PS[:, BCC:BCC + P], ones_row[:], rowb[:],
                         start=True, stop=True).then_inc(mm.h, 1)
        em("tensor", p1e_mmT)
        mm.add(2 * TT)

        def p1e_vec2(e):
            for tt in range(TT):
                e.wait_ge(mm.h, RB_M0 + 2 * tt + 1)
                e.tensor_copy(rowb[:], PS[0:1, TRPC:TRPC + P]).then_inc(vv.h, 1)
                e.wait_ge(mm.h, RB_M0 + 2 * tt + 2)
                e.tensor_copy(rho_rep[:, tt * P:(tt + 1) * P],
                              PS[:, BCC:BCC + P]).then_inc(vv.h, 1)
        em("vector", p1e_vec2)
        RHOREP_V = vv.add(2 * TT)

        def mrun_vec(e):
            e.memset(mrun[:], 0.0).then_inc(vv.h, 1)
        em("vector", mrun_vec)
        VV_MRUN = vv.add(1)

        set_scope("p2")
        P2LD0 = ldw.val
        P2TR0 = trq.val
        P2TERN0 = vv.val + 0
        P2MM0 = mm.val
        P2EV0 = ev.val
        P2SS0 = ss.val
        P2HV0 = None

        ts2 = sems["ts2"]
        P2TS0 = ts2.val

        def p2_sync(e):
            e.wait_ge(vv.h, RHOREP_V)
            if dbg:
                e.wait_ge(gg.h, 16)
            for hc in range(HC):
                if hc >= 2:
                    e.wait_ge(ts2.h, P2TS0 + 2 * (hc - 1))
                e.dma_start(graw[hc % 2][:], gw_hbm.ap()[hc * P:(hc + 1) * P, :]
                            ).then_inc(ldw.h, 16)
                e.dma_start(vraw[hc % 2][:], vw_hbm.ap()[hc * P:(hc + 1) * P, :]
                            ).then_inc(ldw.h, 16)
                e.wait_ge(vv.h, P2TERN0 + 2 * hc + 1)
                if hc >= 2:
                    e.wait_ge(mm.h, P2MM0 + NSLAB * (hc - 1))
                for dc in range(DC):
                    e.dma_start_transpose(gwT[hc % 2][:, dc * P:(dc + 1) * P],
                                          gtb[:, dc * P:(dc + 1) * P]
                                          ).then_inc(trq.h, 16)
                e.wait_ge(vv.h, P2TERN0 + 2 * hc + 2)
                for dc in range(DC):
                    e.dma_start_transpose(vwT[hc % 2][:, dc * P:(dc + 1) * P],
                                          vtb[:, dc * P:(dc + 1) * P]
                                          ).then_inc(trq.h, 16)
        em("sync", p2_sync)
        ldw.add(32 * HC)
        trq.add(32 * DC * HC)

        P2TERN0 = vv.val

        def p2_mm(e):
            e.wait_ge(trq.h, TRQ_QY)
            for hc in range(HC):
                e.wait_ge(trq.h, P2TR0 + 32 * DC * (hc + 1))
                for s_ in range(NSLAB):
                    k = hc * NSLAB + s_
                    if k >= NSLAB:
                        e.wait_ge(ev.h, P2EV0 + k - NSLAB + 1)
                    for dc in range(DC):
                        e.matmul(pbank(s_, SLAB),
                                 gwT[hc % 2][:, dc * P:(dc + 1) * P],
                                 qyT_sb[:, dc * T + s_ * SLAB:dc * T + (s_ + 1) * SLAB],
                                 start=(dc == 0), stop=(dc == DC - 1))
                    for dc in range(DC):
                        i_ = e.matmul(pbank(4 + s_, SLAB),
                                      vwT[hc % 2][:, dc * P:(dc + 1) * P],
                                      qyT_sb[:, dc * T + s_ * SLAB:dc * T + (s_ + 1) * SLAB],
                                      start=(dc == 0), stop=(dc == DC - 1))
                    i_.then_inc(mm.h, 1)
        em("tensor", p2_mm)
        mm.add(HC * NSLAB)

        P2HV0 = hv.val

        def p2_vec(e):
            e.wait_ge(vv.h, VV_MRUN)

            def tern(hc):
                e.wait_ge(ts2.h, P2TS0 + 2 * hc + 1)
                if hc >= 1:
                    e.wait_ge(trq.h, P2TR0 + 32 * DC * hc)
                e.tensor_tensor(gtb[:], dsel[:], dse2[:], OP.add
                                ).then_inc(vv.h, 1)
                e.wait_ge(ts2.h, P2TS0 + 2 * hc + 2)
                e.tensor_tensor(vtb[:], dsel[:], dse2[:], OP.add
                                ).then_inc(vv.h, 1)

            def epi(hc):
                for s_ in range(NSLAB):
                    k = hc * NSLAB + s_
                    e.wait_ge(mm.h, P2MM0 + k + 1)
                    if k >= 2:
                        e.wait_ge(sth.h, 16 * (k - 1))
                    e.tensor_tensor(gsl[s_ % 2][:], pbank(s_, SLAB),
                                    rho_rep[:, s_ * SLAB:(s_ + 1) * SLAB],
                                    OP.mult)
                    e.tensor_tensor(vsl[s_ % 2][:], pbank(4 + s_, SLAB),
                                    rho_rep[:, s_ * SLAB:(s_ + 1) * SLAB],
                                    OP.mult).then_inc(ev.h, 1)
                    e.wait_ge(ss.h, P2SS0 + 2 * k + 2)
                    e.tensor_tensor(hsl[s_ % 2][:], gsl[s_ % 2][:],
                                    vsl[s_ % 2][:], OP.mult)
                    e.tensor_tensor(mrun[:, s_ * SLAB:(s_ + 1) * SLAB],
                                    mrun[:, s_ * SLAB:(s_ + 1) * SLAB],
                                    hsl[s_ % 2][:], OP.max)
                    e.scalar_tensor_tensor(mrun[:, s_ * SLAB:(s_ + 1) * SLAB],
                                           hsl[s_ % 2][:], -1.0,
                                           mrun[:, s_ * SLAB:(s_ + 1) * SLAB],
                                           OP.mult, OP.max).then_inc(hv.h, 1)

            for hc in range(HC):
                tern(hc)
                if hc >= 1:
                    epi(hc - 1)
            epi(HC - 1)
        em("vector", p2_vec)
        vv.add(2 * HC)
        hv.add(HC * NSLAB)
        ev.add(HC * NSLAB)

        def p2_scal(e):
            def signs(hc):
                e.wait_ge(ldw.h, P2LD0 + 32 * hc + 16)
                if hc >= 1:
                    e.wait_ge(vv.h, P2TERN0 + 2 * hc)
                e.activation(dsel[:], graw[hc % 2][:], AF.Sign,
                             scale=-1.0, bias=wcols["g"]["thp"][:])
                e.activation(dse2[:], graw[hc % 2][:], AF.Sign,
                             scale=-1.0, bias=wcols["g"]["thn"][:]
                             ).then_inc(ts2.h, 1)
                e.wait_ge(ldw.h, P2LD0 + 32 * (hc + 1))
                e.wait_ge(vv.h, P2TERN0 + 2 * hc + 1)
                e.activation(dsel[:], vraw[hc % 2][:], AF.Sign,
                             scale=-1.0, bias=wcols["v"]["thp"][:])
                e.activation(dse2[:], vraw[hc % 2][:], AF.Sign,
                             scale=-1.0, bias=wcols["v"]["thn"][:]
                             ).then_inc(ts2.h, 1)

            def silus(hc):
                for s_ in range(NSLAB):
                    k = hc * NSLAB + s_
                    e.wait_ge(ev.h, P2EV0 + k + 1)
                    e.activation(gsl[s_ % 2][:], gsl[s_ % 2][:], AF.Silu,
                                 bias=gb_sb[:, hc:hc + 1],
                                 scale=wcols["g"]["nsc"][:]).then_inc(ss.h, 1)
                    e.activation(vsl[s_ % 2][:], vsl[s_ % 2][:], AF.Identity,
                                 bias=vb_sb[:, hc:hc + 1],
                                 scale=wcols["v"]["nsc"][:]).then_inc(ss.h, 1)

            for hc in range(HC):
                signs(hc)
                if hc >= 1:
                    silus(hc - 1)
            silus(HC - 1)
        em("scalar", p2_scal)
        ss.add(2 * HC * NSLAB)
        ts2.add(2 * HC)

        def p2_gp(e):
            for hc in range(HC):
                for s_ in range(NSLAB):
                    k = hc * NSLAB + s_
                    e.wait_ge(hv.h, P2HV0 + k + 1)
                    e.dma_start(h_sp.ap()[s_ * TPS:(s_ + 1) * TPS, hc, :, :]
                                .rearrange("a p b -> p a b"),
                                hsl[s_ % 2][:].rearrange("p (a b) -> p a b",
                                                         a=TPS)
                                ).then_inc(sth.h, 16)
        em("gpsimd", p2_gp)
        STH_DONE = sth.add(16 * HC * NSLAB)

        VV_SREP_HOLDER = [None]
        if dbg:
            GG_D2 = gg.val + 13 * 16

            def dbg2_gp(e):
                e.wait_ge(trq.h, TRQ_QY)
                e.wait_ge(vv.h, VV_SREP_HOLDER[0])
                e.dma_start(qyT_d.ap(), qyT_sb[:]).then_inc(gg.h, 16)
                e.dma_start(rho_d.ap(), rho_rep[:]).then_inc(gg.h, 16)
                e.dma_start(srep_d.ap(), s_h_rep[:]).then_inc(gg.h, 16)
                e.dma_start(sc3_d.ap(), sc3_all[:]).then_inc(gg.h, 16)
                for j, (wn, kk) in enumerate(
                        (w_, k_) for w_ in ("g", "v", "o")
                        for k_ in ("thp", "thn", "nsc")):
                    e.dma_start(gcol_d.ap()[:, j:j + 1], wcols[wn][kk][:],
                                allow_slow_non_contiguous=True
                                ).then_inc(gg.h, 16)
                e.wait_ge(gg.h, GG_D2)
            em("gpsimd", dbg2_gp)
            gg.add(13 * 16)

        set_scope("p2e")
        BCC = 7 * BANK
        SR_M0 = mm.val
        SR_V0 = vv.val + 1

        def p2e_mm(e):
            e.wait_ge(hv.h, P2HV0 + HC * NSLAB)
            for tt in range(TT):
                if tt > 0:
                    e.wait_ge(vv.h, SR_V0 + 3 * (tt - 1) + 1)
                e.transpose(PS[:, TRPC:TRPC + P], mrun[:, tt * P:(tt + 1) * P],
                            ident[:]).then_inc(mm.h, 1)
                e.wait_ge(vv.h, SR_V0 + 3 * tt + 1)
                e.transpose(PS[0:1, TRPC + P:TRPC + 2 * P], scol[:],
                            ident[:]).then_inc(mm.h, 1)
                e.wait_ge(vv.h, SR_V0 + 3 * tt + 2)
                e.matmul(PS[:, BCC:BCC + P], ones_row[:], rowb[:],
                         start=True, stop=True).then_inc(mm.h, 1)
        em("tensor", p2e_mm)
        mm.add(3 * TT)

        def p2e_vec(e):
            e.tensor_scalar_mul(tcol2[:], wcols["o"]["nsc"][:], 1.0 / 127.0
                                ).then_inc(vv.h, 1)
            for tt in range(TT):
                e.wait_ge(mm.h, SR_M0 + 3 * tt + 1)
                e.reduce_max(tcol[:], PS[:, TRPC:TRPC + P], axis=AX)
                e.tensor_scalar(tcol[:], tcol[:], EPS, None, OP.max)
                e.tensor_scalar(sc3_all[:, tt:tt + 1], tcol[:], tcol2[:],
                                None, OP.mult)
                e.reciprocal(tcol[:], tcol[:])
                e.tensor_scalar_mul(scol[:], tcol[:], 127.0).then_inc(vv.h, 1)
                e.wait_ge(mm.h, SR_M0 + 3 * tt + 2)
                e.tensor_copy(rowb[:], PS[0:1, TRPC + P:TRPC + 2 * P]
                              ).then_inc(vv.h, 1)
                e.wait_ge(mm.h, SR_M0 + 3 * tt + 3)
                e.tensor_copy(s_h_rep[:, tt * P:(tt + 1) * P],
                              PS[:, BCC:BCC + P]).then_inc(vv.h, 1)
        em("vector", p2e_vec)
        VV_SREP = vv.add(1 + 3 * TT)
        VV_SREP_HOLDER[0] = VV_SREP

        def ob_sync(e):
            e.wait_ge(vv.h, VV_SREP)
            e.dma_start(rep8k[0:1, :], ob_hbm.ap()).then_inc(ldx.h, 16)
        em("sync", ob_sync)
        LDX_OB = ldx.add(16)

        def ob_mm(e):
            e.wait_ge(ldx.h, LDX_OB)
            for ds in range(NDS):
                e.matmul(pbank(ds, DS), ones_row[:],
                         rep8k[0:1, ds * DS:(ds + 1) * DS],
                         start=True, stop=True).then_inc(mm.h, 1)
        em("tensor", ob_mm)
        MM_OBREP = mm.add(NDS)

        def ob_vec(e):
            e.wait_ge(mm.h, MM_OBREP)
            for ds in range(NDS):
                e.tensor_copy(rep8k[:, ds * DS:(ds + 1) * DS], pbank(ds, DS))
            e.engine_nop().then_inc(vv.h, 1)
        em("vector", ob_vec)
        VV_OBREP = vv.add(1)

        set_scope("p3")
        p3 = [dict() for _ in range(NSEG)]
        for seg in range(NSEG):
            c = p3[seg]
            c["wld0"] = ldw.val
            c["tr0"] = trq.val

            def p3w_sync(e, seg=seg, c=c):
                if seg == 0:
                    e.wait_ge(vv.h, VV_OBREP)
                    if dbg:
                        e.wait_ge(gg.h, 14 * 16)
                else:
                    e.wait_ge(vv.h, p3[seg - 1]["tern_end"])

                def load(i):
                    e.dma_start(
                        oraw[i % 2][:],
                        ow_hbm.ap()[i * P:(i + 1) * P,
                                    seg * HSEG:(seg + 1) * HSEG]
                    ).then_inc(ldw.h, 16)
                for i in range(min(2, DC)):
                    load(i)
                if seg > 0:
                    e.wait_ge(mm.h, p3[seg - 1]["mm_end"])
                for i in range(DC):
                    e.wait_ge(vv.h, c["tern0"] + i + 1)
                    for hcl in range(HCS):
                        e.dma_start_transpose(
                            owh[:, hcl * D + i * P:hcl * D + (i + 1) * P],
                            otb[:, hcl * P:(hcl + 1) * P]).then_inc(trq.h, 16)
                    if i + 2 < DC:
                        load(i + 2)
            em("sync", p3w_sync)
            ldw.add(16 * DC)
            c["trdone"] = trq.add(16 * HCS * DC)

            c["tern0"] = vv.val

            def p3w_vec(e, seg=seg, c=c):
                o = wcols["o"]
                for i in range(DC):
                    e.wait_ge(ldw.h, c["wld0"] + 16 * (i + 1))
                    if i >= 1:
                        e.wait_ge(trq.h, c["tr0"] + 16 * HCS * i)
                    e.tensor_scalar(osel[:], oraw[i % 2][:], o["thp"][:], None,
                                    OP.is_gt)
                    e.scalar_tensor_tensor(otb[:], oraw[i % 2][:], o["thn"][:],
                                           osel[:], OP.is_lt, OP.subtract
                                           ).then_inc(vv.h, 1)
            em("vector", p3w_vec)
            c["tern_end"] = vv.add(DC)

            c["hld0"] = ldh.val
            nper = 1 + (1 if seg > 0 else 0) + (1 if seg == NSEG - 1 else 0)
            c["per_tt"] = 16 * nper
            c["evs"] = 4 if seg == NSEG - 1 else 1

            def p3h_sync(e, seg=seg, c=c):
                e.wait_ge(sth.h, STH_DONE)
                if seg > 0:
                    e.wait_ge(stp.h, p3[seg - 1]["stp_end"])
                if seg == NSEG - 1:
                    e.wait_ge(sty.h, STY_DONE)
                for tt in range(TT):
                    if tt >= 2:
                        e.wait_ge(vv.h, c["qv0"] + tt - 1)
                    e.dma_start(
                        hstg[tt % 2][:].rearrange("p (a b) -> p a b", a=HCS),
                        h_sp.ap()[tt, seg * HCS:(seg + 1) * HCS, :, :]
                        .rearrange("a p b -> p a b")
                    ).then_inc(ldh.h, 16)
                    if seg > 0:
                        if tt >= 1:
                            e.wait_ge(hv.h, c["ev0"] + c["evs"] * (tt - 1) + 1)
                        e.dma_start(pbuf[:],
                                    part_sp.ap()[tt * P:(tt + 1) * P, :]
                                    ).then_inc(ldh.h, 16)
                    if seg == NSEG - 1:
                        if tt >= 1:
                            e.wait_ge(hv.h, c["ev0"] + c["evs"] * (tt - 1) + 2)
                        e.dma_start(ycld[:],
                                    yc_sp.ap()[tt * P:(tt + 1) * P, :]
                                    ).then_inc(ldh.h, 16)
            em("sync", p3h_sync)
            ldh.add(c["per_tt"] * TT)

            c["qv0"] = vv.val
            c["ev0"] = hv.val
            c["mm0"] = mm.val
            c["stp0"] = stp.val
            c["ss0"] = ss.val
            last = seg == NSEG - 1

            def p3_vec(e, seg=seg, c=c, last=last):
                def q(tt):
                    e.wait_ge(ldh.h, c["hld0"] + c["per_tt"] * tt + 16)
                    if tt >= 2:
                        e.wait_ge(mm.h, c["mm0"] + tt - 1)
                    hb = hstg[tt % 2]
                    qb = qhT[tt % 2]
                    for hcl in range(HCS):
                        e.tensor_tensor(hb[:, hcl * P:(hcl + 1) * P],
                                        hb[:, hcl * P:(hcl + 1) * P],
                                        s_h_rep[:, tt * P:(tt + 1) * P],
                                        OP.mult)
                    if SIM_MODE:
                        e.tensor_scalar(hb[:], hb[:], 256.5, None, OP.add)
                        e.tensor_scalar(qb[:], hb[:], 1.0, None, OP.mod)
                        e.scalar_tensor_tensor(qb[:], hb[:], -256.0, qb[:],
                                               OP.add, OP.subtract
                                               ).then_inc(vv.h, 1)
                    else:
                        e.tensor_copy(ib_h[tt % 2][:], hb[:])
                        e.tensor_copy(qb[:], ib_h[tt % 2][:]).then_inc(vv.h, 1)

                def evac(tt):
                    e.wait_ge(mm.h, c["mm0"] + tt + 1)
                    if seg > 0:
                        e.wait_ge(ldh.h, c["hld0"] + c["per_tt"] * tt + 32)
                    if last:
                        e.wait_ge(ldh.h, c["hld0"] + c["per_tt"] * (tt + 1))
                    o = obuf[tt % 2]
                    if not last:
                        if tt >= 2:
                            e.wait_ge(stp.h, c["stp0"] + 16 * (tt - 1))
                        for ds in range(NDS):
                            if seg == 0:
                                e.tensor_copy(o[:, ds * DS:(ds + 1) * DS],
                                              pbank(ds, DS))
                            else:
                                e.tensor_tensor(o[:, ds * DS:(ds + 1) * DS],
                                                pbank(ds, DS),
                                                pbuf[:, ds * DS:(ds + 1) * DS],
                                                OP.add)
                        e.engine_nop().then_inc(hv.h, 1)
                        return
                    if tt >= 2:
                        e.wait_ge(sto.h, 16 * (tt - 1))
                    for ds in range(NDS):
                        e.tensor_tensor(o[:, ds * DS:(ds + 1) * DS],
                                        pbank(ds, DS),
                                        pbuf[:, ds * DS:(ds + 1) * DS],
                                        OP.add)
                    e.engine_nop().then_inc(hv.h, 1)
                    e.tensor_scalar(o[:], o[:], sc3_all[:, tt:tt + 1],
                                    None, OP.mult)
                    e.tensor_tensor(o[:], o[:], rep8k[:], OP.add)
                    e.scalar_tensor_tensor(o[:], ycld[:],
                                           rstd_all[:, tt:tt + 1], o[:],
                                           OP.mult, OP.add)
                    e.reduce_sum(mcol[:], o[:], axis=AX)
                    e.tensor_scalar_mul(mcol[:], mcol[:], 1.0 / D
                                        ).then_inc(hv.h, 1)
                    e.wait_ge(ss.h, c["ss0"] + 3 * tt + 1)
                    e.tensor_scalar_mul(tcol[:], tcol[:], 1.0 / D)
                    e.tensor_tensor(tcol2[:], mcol[:], mcol[:], OP.mult)
                    e.tensor_tensor(tcol[:], tcol[:], tcol2[:],
                                    OP.subtract).then_inc(hv.h, 1)
                    e.wait_ge(ss.h, c["ss0"] + 3 * tt + 2)
                    e.reciprocal(tcol[:], tcol[:])
                    e.tensor_tensor(tcol2[:], mcol[:], tcol[:], OP.mult)
                    e.tensor_scalar_mul(tcol2[:], tcol2[:], -1.0
                                        ).then_inc(hv.h, 1)

                for tt in range(TT):
                    q(tt)
                    if tt >= 1:
                        evac(tt - 1)
                evac(TT - 1)
            em("vector", p3_vec)
            vv.add(TT)
            hv.add(c["evs"] * TT)

            def p3_mm(e, seg=seg, c=c):
                e.wait_ge(trq.h, c["trdone"])
                for tt in range(TT):
                    e.wait_ge(vv.h, c["qv0"] + tt + 1)
                    if tt >= 1:
                        e.wait_ge(hv.h, c["ev0"] + c["evs"] * (tt - 1) + 1)
                    for hcl in range(HCS):
                        for ds in range(NDS):
                            i_ = e.matmul(
                                pbank(ds, DS),
                                qhT[tt % 2][:, hcl * P:(hcl + 1) * P],
                                owh[:, hcl * D + ds * DS:hcl * D + (ds + 1) * DS],
                                start=(hcl == 0), stop=(hcl == HCS - 1))
                    i_.then_inc(mm.h, 1)
            em("tensor", p3_mm)
            c["mm_end"] = mm.add(TT)

            if not last:
                def p3e_gp(e, seg=seg, c=c):
                    for tt in range(TT):
                        e.wait_ge(hv.h, c["ev0"] + tt + 1)
                        e.dma_start(part_sp.ap()[tt * P:(tt + 1) * P, :],
                                    obuf[tt % 2][:]).then_inc(stp.h, 16)
                em("gpsimd", p3e_gp)
                c["stp_end"] = stp.add(16 * TT)
            else:
                def p3f_scal(e, c=c):
                    for tt in range(TT):
                        e.wait_ge(hv.h, c["ev0"] + 4 * tt + 2)
                        e.activation(sq_scr[:], obuf[tt % 2][:], AF.Square,
                                     accum_out=tcol[:]).then_inc(ss.h, 1)
                        e.wait_ge(hv.h, c["ev0"] + 4 * tt + 3)
                        e.activation(tcol[:], tcol[:], AF.Sqrt,
                                     bias=eps_col[:]).then_inc(ss.h, 1)
                        e.wait_ge(hv.h, c["ev0"] + 4 * tt + 4)
                        e.activation(obuf[tt % 2][:], obuf[tt % 2][:],
                                     AF.Identity, bias=tcol2[:],
                                     scale=tcol[:]).then_inc(ss.h, 1)
                em("scalar", p3f_scal)
                ss.add(3 * TT)

                def p3f_gp(e, c=c):
                    for tt in range(TT):
                        e.wait_ge(ss.h, c["ss0"] + 3 * tt + 3)
                        e.dma_start(out_hbm.ap()[tt * P:(tt + 1) * P, :],
                                    obuf[tt % 2][:]).then_inc(sto.h, 16)
                    e.wait_ge(sto.h, 16 * TT)
                em("gpsimd", p3f_gp)
                sto.add(16 * TT)

        @block.sync
        def _(e):
            for fn in prog["sync"]:
                fn(e)

        @block.scalar
        def _(e):
            g = ScalGuard(e, sems["sb"], sflush)
            for fn in prog["scalar"]:
                fn(g)

        @block.vector
        def _(e):
            g = VecGuard(e, sems["vb"])
            for fn in prog["vector"]:
                fn(g)

        @block.tensor
        def _(e):
            for fn in prog["tensor"]:
                fn(e)

        @block.gpsimd
        def _(e):
            for fn in prog["gpsimd"]:
                fn(e)

    return nc


def _prep_inputs(L0_attn_x, L0_attn_B, L0_attn_C, L0_attn_D,
                 L0_ffn_gate_w, L0_ffn_gate_b, L0_ffn_val_w, L0_ffn_val_b,
                 L0_ffn_out_w, L0_ffn_out_b, n_cores=8):
    x = np.ascontiguousarray(np.asarray(L0_attn_x, dtype=np.float32)
                             ).reshape(-1, np.asarray(L0_attn_x).shape[-1])
    T_total, D = x.shape
    T = T_total // n_cores
    H = np.asarray(L0_ffn_gate_w).shape[0]
    NS = np.asarray(L0_attn_B).shape[0]
    HC = H // P

    common = {
        "bw": np.ascontiguousarray(L0_attn_B, dtype=np.float32),
        "cw": np.ascontiguousarray(L0_attn_C, dtype=np.float32),
        "dv": np.ascontiguousarray(L0_attn_D, dtype=np.float32).reshape(1, D),
        "gw": np.ascontiguousarray(L0_ffn_gate_w, dtype=np.float32),
        "vw": np.ascontiguousarray(L0_ffn_val_w, dtype=np.float32),
        "ow": np.ascontiguousarray(L0_ffn_out_w, dtype=np.float32),
        "gb": np.ascontiguousarray(
            np.asarray(L0_ffn_gate_b, dtype=np.float32).reshape(HC, P).T),
        "vb": np.ascontiguousarray(
            np.asarray(L0_ffn_val_b, dtype=np.float32).reshape(HC, P).T),
        "ob": np.ascontiguousarray(L0_ffn_out_b, dtype=np.float32).reshape(1, D),
        "idm": np.eye(P, dtype=np.float32),
    }
    in_maps = []
    for c in range(n_cores):
        m = dict(common)
        m["x"] = np.ascontiguousarray(x[c * T:(c + 1) * T])
        in_maps.append(m)
    return in_maps, (T, D, H, NS)


def kernel(L0_attn_x, L0_attn_B, L0_attn_C, L0_attn_D,
           L0_ffn_gate_w, L0_ffn_gate_b, L0_ffn_val_w, L0_ffn_val_b,
           L0_ffn_out_w, L0_ffn_out_b):
    n_cores = 8
    in_maps, (T, D, H, NS) = _prep_inputs(
        L0_attn_x, L0_attn_B, L0_attn_C, L0_attn_D,
        L0_ffn_gate_w, L0_ffn_gate_b, L0_ffn_val_w, L0_ffn_val_b,
        L0_ffn_out_w, L0_ffn_out_b, n_cores)
    nc = build_nc(T, D, H, NS)
    res = run_bass_kernel_spmd(nc, in_maps, core_ids=list(range(n_cores)))
    out = np.concatenate([r["out"] for r in res.results], axis=0)
    return out.reshape(np.asarray(L0_attn_x).shape).astype(np.float32)


# version 12
# speedup vs baseline: 1.1138x; 1.0927x over previous
import os
import numpy as np

import concourse.bass as bass
import concourse.mybir as mybir
from concourse.bass_utils import run_bass_kernel_spmd

F32 = mybir.dt.float32
BF16 = mybir.dt.bfloat16
I32 = mybir.dt.int32
AF = mybir.ActivationFunctionType
OP = mybir.AluOpType
AX = mybir.AxisListType.X

SIM_MODE = (os.environ.get("BASS_SIM_FP64") == "1"
            or os.environ.get("KERNEL_SIMROUND") == "1")

EPS = 1e-5
THR_M = 1e-5 * (EPS ** 0.5)
P = 128


class Sem:
    def __init__(self, handle):
        self.h = handle
        self.val = 0

    def add(self, n):
        self.val += n
        return self.val


def _ap_range(ap):
    t = ap.tensor
    isz = 2 if t.dtype == BF16 else 4
    span = 1
    dims = list(ap.ap)
    for step, count in dims[1:]:
        span += int(step) * (int(count) - 1)
    base = 0
    space = type(t).__name__
    r = getattr(t, "manual_sbuf_range", None)
    if r is not None:
        base = int(r[0])
    lo = base + int(ap.offset) * isz
    return (space, lo, lo + span * isz)


def _overlap(a, b):
    return a[0] == b[0] and a[1] < b[2] and b[1] < a[2]


class VecGuard:

    _OPS = {
        "tensor_copy": (1, [0]), "tensor_tensor": (2, [0]),
        "tensor_add": (2, [0]), "tensor_sub": (2, [0]),
        "tensor_mul": (2, [0]), "tensor_max": (2, [0]),
        "tensor_scalar": (3, [0]), "tensor_scalar_mul": (2, [0]),
        "tensor_scalar_add": (2, [0]), "tensor_scalar_max": (2, [0]),
        "scalar_tensor_tensor": (4, [0]), "reduce_sum": (2, [0]),
        "reduce_max": (2, [0]), "reciprocal": (2, [0]),
    }

    def __init__(self, eng, sem):
        self._e = eng
        self._sem = sem
        self._writes = []

    def _flush(self):
        if self._writes:
            self._writes = []
            self._e.engine_nop().then_inc(self._sem.h, 1)
            self._e.wait_ge(self._sem.h, self._sem.add(1))

    def wait_ge(self, *a, **k):
        return self._e.wait_ge(*a, **k)

    def memset(self, ap, v):
        i = self._e.memset(ap, v)
        self._writes.append(_ap_range(ap))
        return i

    def engine_nop(self):
        return self._e.engine_nop()

    def __getattr__(self, name):
        fn = getattr(self._e, name)
        spec = self._OPS.get(name)
        if spec is None:
            return fn
        npos, outpos = spec

        def wrapped(*args, **kwargs):
            ins = []
            outs = []
            for i, a in enumerate(args[:npos + 2]):
                if hasattr(a, "tensor"):
                    (outs if i in outpos else ins).append(_ap_range(a))
            for k_, a in kwargs.items():
                if hasattr(a, "tensor"):
                    (outs if k_ == "accum_out" else ins).append(_ap_range(a))
            if any(_overlap(r, w) for r in ins for w in self._writes):
                self._flush()
            res = fn(*args, **kwargs)
            self._writes.extend(outs)
            return res
        return wrapped


class ScalGuard(VecGuard):

    _OPS = {
        "activation": (2, [0]), "copy": (2, [0]),
        "add": (2, [0]), "mul": (2, [0]),
    }

    def __init__(self, eng, sem, scratch):
        super().__init__(eng, sem)
        self._scr = scratch

    def _flush(self):
        if self._writes:
            self._writes = []
            self._e.copy(self._scr[:], self._scr[:]).then_inc(self._sem.h, 1)
            self._e.wait_ge(self._sem.h, self._sem.add(1))


def build_nc(T, D, H, NS, dbg=False):
    TT = T // P
    DC = D // P
    HC = H // P
    NCH = NS // P
    SLAB = min(512, T)
    NSLAB = T // SLAB
    TPS = SLAB // P
    DS = min(512, D)
    NDS = D // DS
    NSEG = 4
    HCS = HC // NSEG
    HSEG = H // NSEG
    GH = 4
    NG = HC // GH
    assert TT <= 64 and DC <= 64

    nc = bass.Bass("TRN2", target_bir_lowering=False,
                   detect_race_conditions=False)

    x_hbm = nc.dram_tensor("x", [T, D], F32, kind="ExternalInput")
    b_hbm = nc.dram_tensor("bw", [NS, D], F32, kind="ExternalInput")
    c_hbm = nc.dram_tensor("cw", [D, NS], F32, kind="ExternalInput")
    d_hbm = nc.dram_tensor("dv", [1, D], F32, kind="ExternalInput")
    gw_hbm = nc.dram_tensor("gw", [H, D], F32, kind="ExternalInput")
    vw_hbm = nc.dram_tensor("vw", [H, D], F32, kind="ExternalInput")
    ow_hbm = nc.dram_tensor("ow", [D, H], F32, kind="ExternalInput")
    gb_hbm = nc.dram_tensor("gb", [P, HC], F32, kind="ExternalInput")
    vb_hbm = nc.dram_tensor("vb", [P, HC], F32, kind="ExternalInput")
    ob_hbm = nc.dram_tensor("ob", [1, D], F32, kind="ExternalInput")
    id_hbm = nc.dram_tensor("idm", [P, P], F32, kind="ExternalInput")
    out_hbm = nc.dram_tensor("out", [T, D], F32, kind="ExternalOutput")

    _spk = "ExternalOutput" if dbg else "Internal"
    yc_sp = nc.dram_tensor("yc_sp", [T, D], BF16, kind=_spk)
    h_sp = nc.dram_tensor("h_sp", [TT, HC, P, P], F32, kind=_spk)
    part_sp = nc.dram_tensor("part_sp", [T, D], F32, kind=_spk)
    if dbg:
        state_d = nc.dram_tensor("state_d", [P, (NS // P) * T], BF16,
                                 kind="ExternalOutput")
        qyT_d = nc.dram_tensor("qyT_d", [P, (D // P) * T], BF16,
                               kind="ExternalOutput")
        rho_d = nc.dram_tensor("rho_d", [P, T], F32, kind="ExternalOutput")
        srep_d = nc.dram_tensor("srep_d", [P, T], F32, kind="ExternalOutput")
        gcol_d = nc.dram_tensor("gcol_d", [P, 16], F32, kind="ExternalOutput")
        sc3_d = nc.dram_tensor("sc3_d", [P, T // P], F32, kind="ExternalOutput")

    base0 = (int(nc.sbuf_base) + 63) & ~63
    top = int(nc.sbuf_top)
    _cur = [base0]

    def place(name, shape, dtype, at=None):
        nbytes = int(np.prod(shape[1:])) * (2 if dtype == BF16 else 4)
        if at is None:
            off = _cur[0]
            _cur[0] = (off + nbytes + 63) & ~63
        else:
            off = at
        assert off + nbytes <= top, f"SBUF overflow placing {name} at {off}"
        return nc.alloc_sbuf_tensor_at(name, list(shape), dtype, offset=off), off

    def bump():
        return _cur[0]

    ident, _ = place("ident", [P, P], F32)
    ident_bf, _ = place("ident_bf", [P, P], BF16)
    gb_sb, _ = place("gb_sb", [P, HC], F32)
    vb_sb, _ = place("vb_sb", [P, HC], F32)
    rep8k, REP8K_OFF = place("rep8k", [P, D], F32)
    gacc, GACC_OFF = place("gacc", [P, 64], F32)
    gred, _ = place("gred", [P, 1], F32)
    gfull, _ = place("gfull", [P, 1], F32)
    wcols = {}
    for wname in ("g", "v", "o"):
        wcols[wname] = {}
        for k in ("thp", "thn", "nsc"):
            wcols[wname][k], _ = place(f"{k}_{wname}", [P, 1], F32)
    m_all, _ = place("m_all", [P, TT], F32)
    M_all, _ = place("M_all", [P, TT], F32)
    ssq_all, _ = place("ssq_all", [P, TT], F32)
    rstd_all, _ = place("rstd_all", [P, TT], F32)
    rho_all, _ = place("rho_all", [P, TT], F32)
    sc3_all, _ = place("sc3_all", [P, TT], F32)
    scol, _ = place("scol", [P, 1], F32)
    mcol, _ = place("mcol", [P, 1], F32)
    tcol, _ = place("tcol", [P, 1], F32)
    tcol2, _ = place("tcol2", [P, 1], F32)
    eps_col, _ = place("eps_col", [P, 1], F32)
    ones_row, _ = place("ones_row", [1, P], F32)
    ones_col, _ = place("ones_col", [P, 1], F32)
    rowb, _ = place("rowb", [1, P], F32)
    sclr, _ = place("sclr", [1, 1], F32)
    sflush, _ = place("sflush", [1, 1], F32)

    PERS_END = bump()

    GW = min(max(D, H), 4096)
    A_SZ = max(DC * T * 2, HCS * D * 2, 2 * GW * 4 + GW * 2)
    SLOT_A = bump()
    _cur[0] = SLOT_A + ((A_SZ + 63) & ~63)
    B_SZ = max(NCH * T * 2, 4 * D * 4, 2 * HCS * P * 4 + 2 * HCS * P * 2 + D * 4)
    SLOT_B = bump()
    _cur[0] = SLOT_B + ((B_SZ + 63) & ~63)
    C_SZ = max(DC * NS * 2, NCH * D * 2,
               8 * D * 2,
               2 * HSEG * 4 + 2 * HSEG * 2)
    SLOT_C = bump()
    _cur[0] = SLOT_C + ((C_SZ + 63) & ~63)
    D_SZ = max(DC * SLAB * 2,
               3 * D * 4,
               6 * SLAB * 4 + 2 * T * 4,
               D * 2 + 2 * D * 4)
    SLOT_D = bump()
    _cur[0] = SLOT_D + ((D_SZ + 63) & ~63)
    E_SZ = max(2 * D * 4, D * 4)
    SLOT_E = bump()
    _cur[0] = SLOT_E + ((E_SZ + 63) & ~63)
    F_SZ = max(2 * D * 2, 2 * NS * 4, T * 4)
    SLOT_F = bump()
    _cur[0] = SLOT_F + ((F_SZ + 63) & ~63)
    G_SZ = 2 * NS * 2
    SLOT_G = bump()
    _cur[0] = SLOT_G + ((G_SZ + 63) & ~63)
    H_SZ = D * 2
    SLOT_H = bump()
    _cur[0] = SLOT_H + ((H_SZ + 63) & ~63)
    I_SZ = D * 2
    SLOT_I = bump()
    _cur[0] = SLOT_I + ((I_SZ + 63) & ~63)
    assert bump() <= top, f"arena overflow: {bump()} > {top}"

    def at(name, shape, dtype, off):
        t_, _ = place(name, shape, dtype, at=off)
        return t_

    s_h_rep = at("s_h_rep", [P, T], F32, SLOT_F)
    sq_scr = at("sq_scr", [P, D], BF16, SLOT_D + 3 * D * 4)

    gld = [at(f"gld{i}", [P, GW], F32, SLOT_A + i * GW * 4) for i in range(2)]
    gscr = at("gscr", [P, GW], BF16, SLOT_A + 2 * GW * 4)
    qyT_sb = at("qyT_sb", [P, DC * T], BF16, SLOT_A)
    state_sb = at("state_sb", [P, NCH * T], BF16, SLOT_B)
    bt_sb = at("bt_sb", [P, DC * NS], BF16, SLOT_C)
    ct_sb = at("ct_sb", [P, NCH * D], BF16, SLOT_C)
    xT_sb = at("xT_sb", [P, DC * SLAB], BF16, SLOT_D)
    wbuf = [at(f"wbuf{i}", [P, D], F32, SLOT_D + i * D * 4) for i in range(2)]
    t256 = at("t256", [P, D], F32, SLOT_D + 2 * D * 4)
    xld = [at(f"xld{i}", [P, D], F32, SLOT_E + i * D * 4) for i in range(2)]
    xbf = [at(f"xbf{i}", [P, D], BF16, SLOT_F + i * D * 2) for i in range(2)]
    cld = [at(f"cld{i}", [P, NS], F32, SLOT_F + i * NS * 4) for i in range(2)]
    cbf = [at(f"cbf{i}", [P, NS], BF16, SLOT_G + i * NS * 2) for i in range(2)]
    ycbf = at("ycbf", [P, D], BF16, SLOT_H)
    qybf = at("qybf", [P, D], BF16, SLOT_I)
    ib_q = at("ib_q", [P, D], I32, SLOT_D + 2 * D * 4)
    graw = [at(f"graw{i}", [P, D], F32, SLOT_B + i * D * 4) for i in range(2)]
    vraw = [at(f"vraw{i}", [P, D], F32, SLOT_B + (2 + i) * D * 4)
            for i in range(2)]
    gtb = at("gtb", [P, D], BF16, SLOT_C)
    vtb = at("vtb", [P, D], BF16, SLOT_C + D * 2)
    dsel = at("dsel", [P, D], BF16, SLOT_C + 2 * D * 2)
    gwT = [at(f"gwT{i}", [P, D], BF16, SLOT_C + (3 + i) * D * 2)
           for i in range(2)]
    vwT = [at(f"vwT{i}", [P, D], BF16, SLOT_C + (5 + i) * D * 2)
           for i in range(2)]
    gsl = [at(f"gsl{i}", [P, SLAB], F32, SLOT_D + i * SLAB * 4)
           for i in range(2)]
    vsl = [at(f"vsl{i}", [P, SLAB], F32, SLOT_D + (2 + i) * SLAB * 4)
           for i in range(2)]
    hsl = [at(f"hsl{i}", [P, SLAB], F32, SLOT_D + (4 + i) * SLAB * 4)
           for i in range(2)]
    mrun = at("mrun", [P, T], F32, SLOT_D + 6 * SLAB * 4)
    rho_rep = at("rho_rep", [P, T], F32, SLOT_D + 6 * SLAB * 4 + T * 4)
    owh = at("owh", [P, HCS * D], BF16, SLOT_A)
    oraw = [at(f"oraw{i}", [P, HSEG], F32, SLOT_C + i * HSEG * 4)
            for i in range(2)]
    otb = at("otb", [P, HSEG], BF16, SLOT_C + 2 * HSEG * 4)
    osel = at("osel", [P, HSEG], BF16, SLOT_C + 2 * HSEG * 4 + HSEG * 2)
    hstg = [at(f"hstg{i}", [P, HCS * P], F32, SLOT_B + i * HCS * P * 4)
            for i in range(2)]
    ib_h = [at(f"ib_h{i}", [P, HCS * P], I32, SLOT_B + i * HCS * P * 4)
            for i in range(2)]
    qhT = [at(f"qhT{i}", [P, HCS * P], BF16,
              SLOT_B + 2 * HCS * P * 4 + i * HCS * P * 2) for i in range(2)]
    pbuf = at("pbuf", [P, D], F32, SLOT_E)
    ycld = at("ycld", [P, D], BF16, SLOT_D)
    obuf = [at(f"obuf{i}", [P, D], F32, SLOT_D + D * 2 + i * D * 4)
            for i in range(2)]

    PS = nc.alloc_psum_tensor("PS", [P, 3072], F32)
    PSW = nc.alloc_psum_tensor("PSW", [P, 2048], BF16)
    BANK = 512

    def pbank(b, width=BANK):
        return PS[:, b * BANK:b * BANK + width]

    import contextlib
    st = contextlib.ExitStack()
    with st:
        S = st.enter_context
        sems = {}
        for name in ("ldc", "ldx", "ldw", "ldh", "trq", "mm", "ev", "vv",
                     "hv", "ss", "gg", "sty", "sth", "stp", "sto",
                     "vb", "sb", "ts2", "th", "sq", "ga"):
            sems[name] = Sem(S(nc.semaphore(name)))
        ldc, ldx, ldw, ldh = (sems[k] for k in ("ldc", "ldx", "ldw", "ldh"))
        trq, mm, ev, vv = (sems[k] for k in ("trq", "mm", "ev", "vv"))
        ss, gg, hv = sems["ss"], sems["gg"], sems["hv"]
        sty, sth, stp, sto = (sems[k] for k in ("sty", "sth", "stp", "sto"))

        block = S(nc.Block())
        prog = {k: [] for k in ("sync", "scalar", "vector", "tensor", "gpsimd")}
        _scope = ["s0"]

        def em(engine, fn, scope=None):
            sc = scope or _scope[0]
            def tagged(e, fn=fn, sc=sc):
                with nc.named_scope(sc):
                    fn(e)
            prog[engine].append(tagged)

        def set_scope(name):
            _scope[0] = name

        def s0_sync(e):
            e.dma_start(ident[:], id_hbm.ap()).then_inc(ldc.h, 16)
            e.dma_start(gb_sb[:], gb_hbm.ap()).then_inc(ldc.h, 16)
            e.dma_start(vb_sb[:], vb_hbm.ap()).then_inc(ldc.h, 16)
            e.dma_start(rep8k[0:1, :], d_hbm.ap()).then_inc(ldc.h, 16)
        em("sync", s0_sync)
        ldc.add(64)

        def s0_vec(e):
            e.memset(sflush[:], 0.0)
            e.memset(eps_col[:], EPS)
            e.memset(ones_row[:], 1.0)
            e.memset(ones_col[:], 1.0).then_inc(vv.h, 1)
        em("vector", s0_vec)
        VV_ONES = vv.add(1)

        def s0_mm(e):
            e.wait_ge(vv.h, VV_ONES)
            e.wait_ge(ldc.h, 64)
            for ds in range(NDS):
                e.matmul(pbank(ds, DS), ones_row[:],
                         rep8k[0:1, ds * DS:(ds + 1) * DS],
                         start=True, stop=True).then_inc(mm.h, 1)
        em("tensor", s0_mm)
        MM_DREP = mm.add(NDS)

        def s0_vec2(e):
            e.wait_ge(mm.h, MM_DREP)
            for ds in range(NDS):
                e.tensor_scalar(rep8k[:, ds * DS:(ds + 1) * DS],
                                pbank(ds, DS), 1.0, None, OP.add)
            e.tensor_copy(ident_bf[:], ident[:])
            e.engine_nop().then_inc(vv.h, 1)
        em("vector", s0_vec2)
        VV_DREP = vv.add(1)

        set_scope("gamma")
        GBIAS = 1.0 if SIM_MODE else (1.0 + 2.3e-7)
        for wname, whbm, rows, cols in (("g", gw_hbm, H, D),
                                        ("v", vw_hbm, H, D),
                                        ("o", ow_hbm, D, H)):
            cpr = max(1, cols // GW)
            w_ = min(cols, GW)
            ntile = (rows // P) * cpr
            inv_n = GBIAS / float(rows * cols)
            assert ntile <= 64

            ld0 = ldw.val
            ab0 = ss.val
            prev_gred = vv.val

            def g_sync(e, whbm=whbm, rows=rows, cpr=cpr, w_=w_, ab0=ab0):
                for i in range(rows // P):
                    for j in range(cpr):
                        k = i * cpr + j
                        if k < 2:
                            e.wait_ge(ss.h, ab0)
                        else:
                            e.wait_ge(ss.h, ab0 + k - 1)
                        e.dma_start(
                            gld[k % 2][:, 0:w_],
                            whbm.ap()[i * P:(i + 1) * P, j * w_:(j + 1) * w_]
                        ).then_inc(ldw.h, 16)
            em("sync", g_sync)
            ldw.add(16 * ntile)

            def g_act(e, ntile=ntile, w_=w_, ld0=ld0, first=(wname == "g"),
                      prev_gred=prev_gred):
                if first:
                    e.wait_ge(ldc.h, 64)
                    e.wait_ge(vv.h, VV_ONES)
                e.wait_ge(vv.h, prev_gred)
                for k in range(ntile):
                    e.wait_ge(ldw.h, ld0 + 16 * (k + 1))
                    e.activation(gscr[:, 0:w_], gld[k % 2][:, 0:w_], AF.Abs,
                                 accum_out=gacc[:, k:k + 1]).then_inc(ss.h, 1)
            em("scalar", g_act)
            SS_ABS = ss.add(ntile)

            def g_vec(e, ntile=ntile, tgt=SS_ABS):
                e.wait_ge(ss.h, tgt)
                e.reduce_sum(gred[:], gacc[:, 0:ntile], axis=AX).then_inc(vv.h, 1)
            em("vector", g_vec)
            VV_GRED = vv.add(1)

            def g_mm1(e, tgt=VV_GRED):
                e.wait_ge(vv.h, tgt)
                e.matmul(PS[0:1, 2560:2561], gred[:], ones_col[:],
                         start=True, stop=True).then_inc(mm.h, 1)
            em("tensor", g_mm1)
            MM_GSUM = mm.add(1)

            def g_vsc(e, tgt=MM_GSUM):
                e.wait_ge(mm.h, tgt)
                e.tensor_copy(sclr[:], PS[0:1, 2560:2561]).then_inc(vv.h, 1)
            em("vector", g_vsc)
            VV_GSC = vv.add(1)

            def g_mm2(e, tgt=VV_GSC):
                e.wait_ge(vv.h, tgt)
                e.matmul(PS[:, 2561:2562], ones_row[:], sclr[:],
                         start=True, stop=True).then_inc(mm.h, 1)
            em("tensor", g_mm2)
            MM_GBC = mm.add(1)

            def g_vec2(e, wname=wname, inv_n=inv_n, tgt=MM_GBC):
                c = wcols[wname]
                e.wait_ge(mm.h, tgt)
                e.tensor_scalar(gfull[:], PS[:, 2561:2562], inv_n, EPS,
                                OP.mult, OP.max)
                e.tensor_scalar_mul(c["thp"][:], gfull[:], 0.5)
                e.tensor_scalar_mul(c["thn"][:], gfull[:], -0.5)
                e.tensor_scalar_mul(c["nsc"][:], gfull[:], -1.0
                                    ).then_inc(vv.h, 1)
            em("vector", g_vec2)
            vv.add(1)
        VV_GAMMA = vv.val

        set_scope("p1a")
        bld0 = ldx.val
        bcast0 = vv.val
        btr0 = trq.val

        def b_sync(e):
            for i in range(min(2, NCH)):
                e.dma_start(xld[i][:], b_hbm.ap()[i * P:(i + 1) * P, :]
                            ).then_inc(ldx.h, 16)
            for i in range(NCH):
                e.wait_ge(vv.h, bcast0 + i + 1)
                for dc in range(DC):
                    e.dma_start_transpose(
                        bt_sb[:, dc * NS + i * P:dc * NS + (i + 1) * P],
                        xbf[i % 2][:, dc * P:(dc + 1) * P]).then_inc(trq.h, 16)
                if i + 2 < NCH:
                    e.dma_start(xld[i % 2][:],
                                b_hbm.ap()[(i + 2) * P:(i + 3) * P, :]
                                ).then_inc(ldx.h, 16)
        em("sync", b_sync)
        ldx.add(16 * NCH)
        TRQ_BT = trq.add(16 * DC * NCH)

        def b_vec(e):
            e.wait_ge(vv.h, VV_GAMMA)
            for i in range(NCH):
                e.wait_ge(ldx.h, bld0 + 16 * (i + 1))
                if i >= 2:
                    e.wait_ge(trq.h, btr0 + 16 * DC * (i - 1))
                e.tensor_copy(xbf[i % 2][:], xld[i % 2][:]).then_inc(vv.h, 1)
        em("vector", b_vec)
        vv.add(NCH)
        VV_BCAST = vv.val

        xld0 = ldx.val
        xcast0 = vv.val
        xtr0 = trq.val
        z1mm0 = mm.val
        SS_TANH0 = None

        def x_sync(e):
            def load(k):
                if k < 2:
                    e.wait_ge(vv.h, VV_BCAST)
                e.dma_start(xld[k % 2][:],
                            x_hbm.ap()[k * P:(k + 1) * P, :]
                            ).then_inc(ldx.h, 16)
            for k in range(min(2, TT)):
                load(k)
            k = 0
            for s_ in range(NSLAB):
                if s_ >= 1:
                    e.wait_ge(mm.h, z1mm0 + s_ * NCH)
                for j in range(TPS):
                    e.wait_ge(vv.h, xcast0 + k + 1)
                    for dc in range(DC):
                        e.dma_start_transpose(
                            xT_sb[:, dc * SLAB + j * P:dc * SLAB + (j + 1) * P],
                            xbf[k % 2][:, dc * P:(dc + 1) * P]).then_inc(trq.h, 16)
                    if k + 2 < TT:
                        load(k + 2)
                    k += 1
        em("sync", x_sync)
        ldx.add(16 * TT)
        TRQ_XT = trq.add(16 * DC * TT)

        def x_vec(e):
            e.wait_ge(trq.h, TRQ_BT)
            k = 0
            for s_ in range(NSLAB):
                for j in range(TPS):
                    e.wait_ge(ldx.h, xld0 + 16 * (k + 1))
                    if k >= 2:
                        e.wait_ge(trq.h, xtr0 + 16 * DC * (k - 1))
                    e.tensor_copy(xbf[k % 2][:], xld[k % 2][:]).then_inc(vv.h, 1)
                    k += 1
        em("vector", x_vec)
        vv.add(TT)

        def z1_mm(e):
            e.wait_ge(trq.h, TRQ_BT)
            k = 0
            for s_ in range(NSLAB):
                e.wait_ge(trq.h, xtr0 + 16 * DC * TPS * (s_ + 1))
                for nch in range(NCH):
                    ps = pbank(4 + (k % 2), SLAB)
                    if k >= 2:
                        e.wait_ge(ss.h, SS_TANH0 + k - 1)
                    for dc in range(DC):
                        i_ = e.matmul(
                            ps,
                            bt_sb[:, dc * NS + nch * P:dc * NS + (nch + 1) * P],
                            xT_sb[:, dc * SLAB:(dc + 1) * SLAB],
                            start=(dc == 0), stop=(dc == DC - 1))
                    i_.then_inc(mm.h, 1)
                    k += 1
        em("tensor", z1_mm)
        mm.add(NSLAB * NCH)

        SS_TANH0 = ss.val

        def z1_tanh(e):
            k = 0
            for s_ in range(NSLAB):
                for nch in range(NCH):
                    e.wait_ge(mm.h, z1mm0 + k + 1)
                    e.activation(
                        state_sb[:, nch * T + s_ * SLAB:nch * T + (s_ + 1) * SLAB],
                        pbank(4 + (k % 2), SLAB), AF.Tanh).then_inc(ss.h, 1)
                    k += 1
        em("scalar", z1_tanh)
        ss.add(NSLAB * NCH)
        SS_P1A = ss.val

        set_scope("p1b")
        cld0 = ldx.val
        ccast0 = vv.val
        ctr0 = trq.val

        def c_sync(e):
            e.wait_ge(trq.h, TRQ_XT)
            for i in range(min(2, DC)):
                e.dma_start(cld[i][:], c_hbm.ap()[i * P:(i + 1) * P, :]
                            ).then_inc(ldx.h, 16)
            e.wait_ge(mm.h, z1mm0 + NSLAB * NCH)
            for i in range(DC):
                e.wait_ge(vv.h, ccast0 + i + 1)
                for nch in range(NCH):
                    e.dma_start_transpose(
                        ct_sb[:, nch * D + i * P:nch * D + (i + 1) * P],
                        cbf[i % 2][:, nch * P:(nch + 1) * P]).then_inc(trq.h, 16)
                if i + 2 < DC:
                    e.dma_start(cld[i % 2][:],
                                c_hbm.ap()[(i + 2) * P:(i + 3) * P, :]
                                ).then_inc(ldx.h, 16)
        em("sync", c_sync)
        ldx.add(16 * DC)
        TRQ_CT = trq.add(16 * NCH * DC)

        def c_vec(e):
            for i in range(DC):
                e.wait_ge(ldx.h, cld0 + 16 * (i + 1))
                if i >= 2:
                    e.wait_ge(trq.h, ctr0 + 16 * NCH * (i - 1))
                e.tensor_copy(cbf[i % 2][:], cld[i % 2][:]).then_inc(vv.h, 1)
        em("vector", c_vec)
        vv.add(DC)

        x2ld0 = ldx.val
        P1B_V0 = None
        P1B_EV0 = None

        def x2_sync(e):
            e.wait_ge(vv.h, xcast0 + TT)
            for tt in range(min(2, TT)):
                e.dma_start(xld[tt][:], x_hbm.ap()[tt * P:(tt + 1) * P, :]
                            ).then_inc(ldx.h, 16)
        em("sync", x2_sync)
        ldx.add(16 * min(2, TT))

        mm2_0 = mm.val

        def mm2(e):
            e.wait_ge(trq.h, TRQ_CT)
            e.wait_ge(ss.h, SS_P1A)
            for tt in range(TT):
                if tt >= 1:
                    e.wait_ge(ev.h, P1B_EV0 + tt)
                for ds in range(NDS):
                    for nch in range(NCH):
                        i_ = e.matmul(
                            pbank(ds, DS),
                            state_sb[:, nch * T + tt * P:nch * T + (tt + 1) * P],
                            ct_sb[:, nch * D + ds * DS:nch * D + (ds + 1) * DS],
                            start=(nch == 0), stop=(nch == NCH - 1))
                i_.then_inc(mm.h, 1)
        em("tensor", mm2)
        mm.add(TT)

        P1B_V0 = vv.val
        P1B_EV0 = ev.val
        SS_SQ0 = None
        QYTR0 = None

        def p1b_vec(e):
            for tt in range(TT):
                e.wait_ge(ldx.h, x2ld0 + 16 * (tt + 1))
                e.wait_ge(mm.h, mm2_0 + tt + 1)
                w = wbuf[tt % 2]
                if tt >= 2:
                    e.wait_ge(ss.h, SS_SQ0 + tt - 1)
                if tt >= 1:
                    e.wait_ge(sty.h, 16 * tt)
                e.tensor_tensor(w[:], xld[tt % 2][:], rep8k[:], OP.mult)
                e.tensor_tensor(w[:], w[:], PS[:, 0:D], OP.add).then_inc(ev.h, 1)
                e.reduce_sum(mcol[:], w[:], axis=AX)
                e.tensor_scalar_mul(mcol[:], mcol[:], 1.0 / D)
                e.tensor_copy(m_all[:, tt:tt + 1], mcol[:])
                e.tensor_scalar(w[:], w[:], mcol[:], None, OP.subtract)
                e.reduce_max(M_all[:, tt:tt + 1], w[:], axis=AX,
                             apply_absolute_value=True)
                e.tensor_copy(ycbf[:], w[:]).then_inc(vv.h, 1)
                e.tensor_scalar(tcol[:], M_all[:, tt:tt + 1], THR_M, None, OP.max)
                e.reciprocal(tcol[:], tcol[:])
                e.tensor_scalar_mul(scol[:], tcol[:], 127.0)
                if tt >= 1:
                    e.wait_ge(trq.h, QYTR0 + 16 * DC * tt)
                if SIM_MODE:
                    e.tensor_scalar(t256[:], w[:], scol[:], 256.5,
                                    OP.mult, OP.add)
                    e.tensor_scalar(w[:], t256[:], 1.0, None, OP.mod)
                    e.scalar_tensor_tensor(qybf[:], t256[:], -256.0, w[:],
                                           OP.add, OP.subtract).then_inc(vv.h, 1)
                else:
                    e.tensor_scalar(t256[:], w[:], scol[:], None, OP.mult)
                    e.tensor_copy(ib_q[:], t256[:])
                    e.tensor_copy(qybf[:], ib_q[:]).then_inc(vv.h, 1)
        em("vector", p1b_vec)
        vv.add(2 * TT)
        ev.add(TT)

        SS_SQ0 = ss.val

        def p1b_scal(e):
            for tt in range(TT):
                e.wait_ge(vv.h, P1B_V0 + 2 * tt + 1)
                e.activation(sq_scr[:], wbuf[tt % 2][:], AF.Square,
                             accum_out=ssq_all[:, tt:tt + 1]).then_inc(ss.h, 1)
        em("scalar", p1b_scal)
        ss.add(TT)

        def p1b_gp(e):
            for tt in range(TT):
                e.wait_ge(vv.h, P1B_V0 + 2 * tt + 1)
                e.dma_start(yc_sp.ap()[tt * P:(tt + 1) * P, :], ycbf[:]
                            ).then_inc(sty.h, 16)
        em("gpsimd", p1b_gp)
        STY_DONE = sty.add(16 * TT)

        if dbg:
            def dbg1_gp(e):
                e.wait_ge(ss.h, SS_P1A)
                e.dma_start(state_d.ap(), state_sb[:]).then_inc(gg.h, 16)
            em("gpsimd", dbg1_gp)
            gg.add(16)

        QYTR0 = trq.val

        def qy_tr(e):
            for tt in range(TT):
                e.wait_ge(vv.h, P1B_V0 + 2 * tt + 2)
                for dc in range(DC):
                    e.dma_start_transpose(
                        qyT_sb[:, dc * T + tt * P:dc * T + (tt + 1) * P],
                        qybf[:, dc * P:(dc + 1) * P]).then_inc(trq.h, 16)
                if tt + 2 < TT:
                    e.dma_start(xld[tt % 2][:],
                                x_hbm.ap()[(tt + 2) * P:(tt + 3) * P, :]
                                ).then_inc(ldx.h, 16)
        em("sync", qy_tr)
        TRQ_QY = trq.add(16 * DC * TT)
        ldx.add(16 * max(0, TT - 2))

        set_scope("p1e")
        TRPC = 4 * BANK

        def p1e_scal(e):
            e.wait_ge(vv.h, P1B_V0 + 2 * TT)
            e.activation(rstd_all[:], ssq_all[:], AF.Sqrt,
                         scale=1.0 / D, bias=eps_col[:]).then_inc(ss.h, 1)
        em("scalar", p1e_scal)
        SS_RSTD = ss.add(1)

        def p1e_vec(e):
            e.wait_ge(ss.h, SS_RSTD)
            e.reciprocal(rstd_all[:], rstd_all[:])
            e.tensor_scalar(rho_all[:], M_all[:], THR_M, None, OP.max)
            e.tensor_tensor(rho_all[:], rho_all[:], rstd_all[:], OP.mult)
            e.tensor_scalar_mul(rho_all[:], rho_all[:], 1.0 / 127.0
                                ).then_inc(vv.h, 1)
        em("vector", p1e_vec)
        VV_RHO = vv.add(1)

        BCC = 5 * BANK
        RB_M0 = mm.val
        RB_V0 = vv.val

        def p1e_mmT(e):
            e.wait_ge(vv.h, VV_RHO)
            for tt in range(TT):
                if tt > 0:
                    e.wait_ge(vv.h, RB_V0 + 2 * tt - 1)
                e.transpose(PS[0:1, TRPC:TRPC + P], rho_all[:, tt:tt + 1],
                            ident[:]).then_inc(mm.h, 1)
                e.wait_ge(vv.h, RB_V0 + 2 * tt + 1)
                e.matmul(PS[:, BCC:BCC + P], ones_row[:], rowb[:],
                         start=True, stop=True).then_inc(mm.h, 1)
        em("tensor", p1e_mmT)
        mm.add(2 * TT)

        def p1e_vec2(e):
            for tt in range(TT):
                e.wait_ge(mm.h, RB_M0 + 2 * tt + 1)
                e.tensor_copy(rowb[:], PS[0:1, TRPC:TRPC + P]).then_inc(vv.h, 1)
                e.wait_ge(mm.h, RB_M0 + 2 * tt + 2)
                e.tensor_copy(rho_rep[:, tt * P:(tt + 1) * P],
                              PS[:, BCC:BCC + P]).then_inc(vv.h, 1)
        em("vector", p1e_vec2)
        RHOREP_V = vv.add(2 * TT)

        def mrun_vec(e):
            e.memset(mrun[:], 0.0).then_inc(vv.h, 1)
        em("vector", mrun_vec)
        VV_MRUN = vv.add(1)

        set_scope("p2")
        th, sq = sems["th"], sems["sq"]
        P2LD0 = ldw.val
        P2TH0 = th.val
        P2SQ0 = sq.val
        P2TERN0 = vv.val + 0
        P2MM0 = mm.val
        P2EV0 = ev.val
        P2SS0 = ss.val
        P2HV0 = None
        RPS = (1, 1, 1, 1)

        def p2_sync(e):
            e.wait_ge(vv.h, RHOREP_V)
            if dbg:
                e.wait_ge(gg.h, 16)
            for hc in range(HC):
                if hc >= 1:
                    e.wait_ge(vv.h, P2TERN0 + 2 * hc)
                e.dma_start(graw[hc % 2][:], gw_hbm.ap()[hc * P:(hc + 1) * P, :]
                            ).then_inc(ldw.h, 16)
                e.dma_start(vraw[hc % 2][:], vw_hbm.ap()[hc * P:(hc + 1) * P, :]
                            ).then_inc(ldw.h, 16)
        em("sync", p2_sync)
        ldw.add(32 * HC)

        P2TERN0 = vv.val

        def _tround(e, hc, r):
            R = 4 * hc + r
            src = gtb if r < 2 else vtb
            j = r % 2
            e.wait_ge(vv.h, P2TERN0 + 2 * hc + (1 if r < 2 else 2))
            if R >= 2:
                e.wait_ge(sq.h, P2SQ0 + R - 1)
            b = (R % 2) * 8 * P
            for i in range(8):
                i_ = e.transpose(PSW[:, b + i * P:b + (i + 1) * P],
                                 src[:, (8 * j + i) * P:(8 * j + i + 1) * P],
                                 ident_bf[:])
            i_.then_inc(th.h, 1)

        def p2_mm(e):
            e.wait_ge(trq.h, TRQ_QY)
            for r in range(4):
                _tround(e, 0, r)
            for hc in range(HC):
                e.wait_ge(sq.h, P2SQ0 + 4 * (hc + 1))
                rnext = 0
                for s_ in range(NSLAB):
                    k = hc * NSLAB + s_
                    if k >= 3:
                        e.wait_ge(ev.h, P2EV0 + k - 2)
                    for dc in range(DC):
                        e.matmul(pbank(k % 3, SLAB),
                                 gwT[hc % 2][:, dc * P:(dc + 1) * P],
                                 qyT_sb[:, dc * T + s_ * SLAB:dc * T + (s_ + 1) * SLAB],
                                 start=(dc == 0), stop=(dc == DC - 1))
                    for dc in range(DC):
                        i_ = e.matmul(pbank(3 + k % 3, SLAB),
                                      vwT[hc % 2][:, dc * P:(dc + 1) * P],
                                      qyT_sb[:, dc * T + s_ * SLAB:dc * T + (s_ + 1) * SLAB],
                                      start=(dc == 0), stop=(dc == DC - 1))
                    i_.then_inc(mm.h, 1)
                    if hc + 1 < HC:
                        for _ in range(RPS[s_]):
                            _tround(e, hc + 1, rnext)
                            rnext += 1
        em("tensor", p2_mm)
        mm.add(HC * NSLAB)
        th.add(4 * HC)

        P2HV0 = hv.val

        def p2_vec(e):
            e.wait_ge(vv.h, VV_MRUN)

            def tern(hc):
                e.wait_ge(ldw.h, P2LD0 + 32 * hc + 16)
                if hc >= 1:
                    e.wait_ge(th.h, P2TH0 + 4 * (hc - 1) + 2)
                e.tensor_scalar(dsel[:], graw[hc % 2][:],
                                wcols["g"]["thp"][:], None, OP.is_gt)
                e.scalar_tensor_tensor(gtb[:], graw[hc % 2][:],
                                       wcols["g"]["thn"][:], dsel[:],
                                       OP.is_lt, OP.subtract).then_inc(vv.h, 1)
                e.wait_ge(ldw.h, P2LD0 + 32 * (hc + 1))
                if hc >= 1:
                    e.wait_ge(th.h, P2TH0 + 4 * hc)
                e.tensor_scalar(dsel[:], vraw[hc % 2][:],
                                wcols["v"]["thp"][:], None, OP.is_gt)
                e.scalar_tensor_tensor(vtb[:], vraw[hc % 2][:],
                                       wcols["v"]["thn"][:], dsel[:],
                                       OP.is_lt, OP.subtract).then_inc(vv.h, 1)

            def epi(hc):
                for s_ in range(NSLAB):
                    k = hc * NSLAB + s_
                    e.wait_ge(mm.h, P2MM0 + k + 1)
                    if k >= 2:
                        e.wait_ge(sth.h, 16 * k)
                    e.tensor_tensor(gsl[s_ % 2][:], pbank(k % 3, SLAB),
                                    rho_rep[:, s_ * SLAB:(s_ + 1) * SLAB],
                                    OP.mult)
                    e.tensor_tensor(vsl[s_ % 2][:], pbank(3 + k % 3, SLAB),
                                    rho_rep[:, s_ * SLAB:(s_ + 1) * SLAB],
                                    OP.mult).then_inc(ev.h, 1)
                    e.wait_ge(ss.h, P2SS0 + 2 * k + 2)
                    e.tensor_tensor(hsl[s_ % 2][:], gsl[s_ % 2][:],
                                    vsl[s_ % 2][:], OP.mult)
                    e.tensor_tensor(mrun[:, s_ * SLAB:(s_ + 1) * SLAB],
                                    mrun[:, s_ * SLAB:(s_ + 1) * SLAB],
                                    hsl[s_ % 2][:], OP.max)
                    e.scalar_tensor_tensor(mrun[:, s_ * SLAB:(s_ + 1) * SLAB],
                                           hsl[s_ % 2][:], -1.0,
                                           mrun[:, s_ * SLAB:(s_ + 1) * SLAB],
                                           OP.mult, OP.max).then_inc(hv.h, 1)

            tern(0)
            for hc in range(HC):
                if hc + 1 < HC:
                    tern(hc + 1)
                epi(hc)
        em("vector", p2_vec)
        vv.add(2 * HC)
        hv.add(HC * NSLAB)
        ev.add(HC * NSLAB)

        def p2_scal(e):
            def evac(hc, r):
                R = 4 * hc + r
                e.wait_ge(th.h, P2TH0 + R + 1)
                if hc >= 2:
                    e.wait_ge(mm.h, P2MM0 + NSLAB * (hc - 1))
                dst = gwT if r < 2 else vwT
                j = r % 2
                b = (R % 2) * 8 * P
                e.activation(dst[hc % 2][:, j * 8 * P:(j + 1) * 8 * P],
                             PSW[:, b:b + 8 * P], AF.Identity
                             ).then_inc(sq.h, 1)

            def silu_slab(hc, s_):
                k = hc * NSLAB + s_
                e.wait_ge(ev.h, P2EV0 + k + 1)
                e.activation(gsl[s_ % 2][:], gsl[s_ % 2][:], AF.Silu,
                             bias=gb_sb[:, hc:hc + 1],
                             scale=wcols["g"]["nsc"][:]).then_inc(ss.h, 1)
                e.activation(vsl[s_ % 2][:], vsl[s_ % 2][:], AF.Identity,
                             bias=vb_sb[:, hc:hc + 1],
                             scale=wcols["v"]["nsc"][:]).then_inc(ss.h, 1)

            for r in range(4):
                evac(0, r)
            for hc in range(1, HC):
                rn = 0
                for s_ in range(NSLAB):
                    for _ in range(RPS[s_]):
                        evac(hc, rn)
                        rn += 1
                    silu_slab(hc - 1, s_)
            for s_ in range(NSLAB):
                silu_slab(HC - 1, s_)
        em("scalar", p2_scal)
        ss.add(2 * HC * NSLAB)
        sq.add(4 * HC)

        def p2_gp(e):
            for hc in range(HC):
                for s_ in range(NSLAB):
                    k = hc * NSLAB + s_
                    e.wait_ge(hv.h, P2HV0 + k + 1)
                    e.dma_start(h_sp.ap()[s_ * TPS:(s_ + 1) * TPS, hc, :, :]
                                .rearrange("a p b -> p a b"),
                                hsl[s_ % 2][:].rearrange("p (a b) -> p a b",
                                                         a=TPS)
                                ).then_inc(sth.h, 16)
        em("gpsimd", p2_gp)
        STH_DONE = sth.add(16 * HC * NSLAB)

        VV_SREP_HOLDER = [None]
        if dbg:
            GG_D2 = gg.val + 13 * 16

            def dbg2_gp(e):
                e.wait_ge(trq.h, TRQ_QY)
                e.wait_ge(vv.h, VV_SREP_HOLDER[0])
                e.dma_start(qyT_d.ap(), qyT_sb[:]).then_inc(gg.h, 16)
                e.dma_start(rho_d.ap(), rho_rep[:]).then_inc(gg.h, 16)
                e.dma_start(srep_d.ap(), s_h_rep[:]).then_inc(gg.h, 16)
                e.dma_start(sc3_d.ap(), sc3_all[:]).then_inc(gg.h, 16)
                for j, (wn, kk) in enumerate(
                        (w_, k_) for w_ in ("g", "v", "o")
                        for k_ in ("thp", "thn", "nsc")):
                    e.dma_start(gcol_d.ap()[:, j:j + 1], wcols[wn][kk][:],
                                allow_slow_non_contiguous=True
                                ).then_inc(gg.h, 16)
                e.wait_ge(gg.h, GG_D2)
            em("gpsimd", dbg2_gp)
            gg.add(13 * 16)

        set_scope("p2e")
        BCC = 5 * BANK
        SR_M0 = mm.val
        SR_V0 = vv.val + 1

        def p2e_mm(e):
            e.wait_ge(hv.h, P2HV0 + HC * NSLAB)
            for tt in range(TT):
                if tt > 0:
                    e.wait_ge(vv.h, SR_V0 + 3 * (tt - 1) + 1)
                e.transpose(PS[:, TRPC:TRPC + P], mrun[:, tt * P:(tt + 1) * P],
                            ident[:]).then_inc(mm.h, 1)
                e.wait_ge(vv.h, SR_V0 + 3 * tt + 1)
                e.transpose(PS[0:1, TRPC + P:TRPC + 2 * P], scol[:],
                            ident[:]).then_inc(mm.h, 1)
                e.wait_ge(vv.h, SR_V0 + 3 * tt + 2)
                e.matmul(PS[:, BCC:BCC + P], ones_row[:], rowb[:],
                         start=True, stop=True).then_inc(mm.h, 1)
        em("tensor", p2e_mm)
        mm.add(3 * TT)

        def p2e_vec(e):
            e.tensor_scalar_mul(tcol2[:], wcols["o"]["nsc"][:], 1.0 / 127.0
                                ).then_inc(vv.h, 1)
            for tt in range(TT):
                e.wait_ge(mm.h, SR_M0 + 3 * tt + 1)
                e.reduce_max(tcol[:], PS[:, TRPC:TRPC + P], axis=AX)
                e.tensor_scalar(tcol[:], tcol[:], EPS, None, OP.max)
                e.tensor_scalar(sc3_all[:, tt:tt + 1], tcol[:], tcol2[:],
                                None, OP.mult)
                e.reciprocal(tcol[:], tcol[:])
                e.tensor_scalar_mul(scol[:], tcol[:], 127.0).then_inc(vv.h, 1)
                e.wait_ge(mm.h, SR_M0 + 3 * tt + 2)
                e.tensor_copy(rowb[:], PS[0:1, TRPC + P:TRPC + 2 * P]
                              ).then_inc(vv.h, 1)
                e.wait_ge(mm.h, SR_M0 + 3 * tt + 3)
                e.tensor_copy(s_h_rep[:, tt * P:(tt + 1) * P],
                              PS[:, BCC:BCC + P]).then_inc(vv.h, 1)
        em("vector", p2e_vec)
        VV_SREP = vv.add(1 + 3 * TT)
        VV_SREP_HOLDER[0] = VV_SREP

        def ob_sync(e):
            e.wait_ge(vv.h, VV_SREP)
            e.dma_start(rep8k[0:1, :], ob_hbm.ap()).then_inc(ldx.h, 16)
        em("sync", ob_sync)
        LDX_OB = ldx.add(16)

        def ob_mm(e):
            e.wait_ge(ldx.h, LDX_OB)
            for ds in range(NDS):
                e.matmul(pbank(ds, DS), ones_row[:],
                         rep8k[0:1, ds * DS:(ds + 1) * DS],
                         start=True, stop=True).then_inc(mm.h, 1)
        em("tensor", ob_mm)
        MM_OBREP = mm.add(NDS)

        def ob_vec(e):
            e.wait_ge(mm.h, MM_OBREP)
            for ds in range(NDS):
                e.tensor_copy(rep8k[:, ds * DS:(ds + 1) * DS], pbank(ds, DS))
            e.engine_nop().then_inc(vv.h, 1)
        em("vector", ob_vec)
        VV_OBREP = vv.add(1)

        set_scope("p3")
        p3 = [dict() for _ in range(NSEG)]
        for seg in range(NSEG):
            c = p3[seg]
            c["wld0"] = ldw.val
            c["tr0"] = trq.val

            def p3w_sync(e, seg=seg, c=c):
                if seg == 0:
                    e.wait_ge(vv.h, VV_OBREP)
                    if dbg:
                        e.wait_ge(gg.h, 14 * 16)
                else:
                    e.wait_ge(vv.h, p3[seg - 1]["tern_end"])

                def load(i):
                    e.dma_start(
                        oraw[i % 2][:],
                        ow_hbm.ap()[i * P:(i + 1) * P,
                                    seg * HSEG:(seg + 1) * HSEG]
                    ).then_inc(ldw.h, 16)
                for i in range(min(2, DC)):
                    load(i)
                if seg > 0:
                    e.wait_ge(mm.h, p3[seg - 1]["mm_end"])
                for i in range(DC):
                    e.wait_ge(vv.h, c["tern0"] + i + 1)
                    for hcl in range(HCS):
                        e.dma_start_transpose(
                            owh[:, hcl * D + i * P:hcl * D + (i + 1) * P],
                            otb[:, hcl * P:(hcl + 1) * P]).then_inc(trq.h, 16)
                    if i + 2 < DC:
                        load(i + 2)
            em("sync", p3w_sync)
            ldw.add(16 * DC)
            c["trdone"] = trq.add(16 * HCS * DC)

            c["tern0"] = vv.val

            def p3w_vec(e, seg=seg, c=c):
                o = wcols["o"]
                for i in range(DC):
                    e.wait_ge(ldw.h, c["wld0"] + 16 * (i + 1))
                    if i >= 1:
                        e.wait_ge(trq.h, c["tr0"] + 16 * HCS * i)
                    e.tensor_scalar(osel[:], oraw[i % 2][:], o["thp"][:], None,
                                    OP.is_gt)
                    e.scalar_tensor_tensor(otb[:], oraw[i % 2][:], o["thn"][:],
                                           osel[:], OP.is_lt, OP.subtract
                                           ).then_inc(vv.h, 1)
            em("vector", p3w_vec)
            c["tern_end"] = vv.add(DC)

            c["hld0"] = ldh.val
            nper = 1 + (1 if seg > 0 else 0) + (1 if seg == NSEG - 1 else 0)
            c["per_tt"] = 16 * nper
            c["evs"] = 4 if seg == NSEG - 1 else 1

            def p3h_sync(e, seg=seg, c=c):
                e.wait_ge(sth.h, STH_DONE)
                if seg > 0:
                    e.wait_ge(stp.h, p3[seg - 1]["stp_end"])
                if seg == NSEG - 1:
                    e.wait_ge(sty.h, STY_DONE)
                for tt in range(TT):
                    if tt >= 2:
                        e.wait_ge(vv.h, c["qv0"] + tt - 1)
                    e.dma_start(
                        hstg[tt % 2][:].rearrange("p (a b) -> p a b", a=HCS),
                        h_sp.ap()[tt, seg * HCS:(seg + 1) * HCS, :, :]
                        .rearrange("a p b -> p a b")
                    ).then_inc(ldh.h, 16)
                    if seg > 0:
                        if tt >= 1:
                            e.wait_ge(hv.h, c["ev0"] + c["evs"] * (tt - 1) + 1)
                        e.dma_start(pbuf[:],
                                    part_sp.ap()[tt * P:(tt + 1) * P, :]
                                    ).then_inc(ldh.h, 16)
                    if seg == NSEG - 1:
                        if tt >= 1:
                            e.wait_ge(hv.h, c["ev0"] + c["evs"] * (tt - 1) + 2)
                        e.dma_start(ycld[:],
                                    yc_sp.ap()[tt * P:(tt + 1) * P, :]
                                    ).then_inc(ldh.h, 16)
            em("sync", p3h_sync)
            ldh.add(c["per_tt"] * TT)

            c["qv0"] = vv.val
            c["ev0"] = hv.val
            c["mm0"] = mm.val
            c["stp0"] = stp.val
            c["ss0"] = ss.val
            last = seg == NSEG - 1

            def p3_vec(e, seg=seg, c=c, last=last):
                def q(tt):
                    e.wait_ge(ldh.h, c["hld0"] + c["per_tt"] * tt + 16)
                    if tt >= 2:
                        e.wait_ge(mm.h, c["mm0"] + tt - 1)
                    hb = hstg[tt % 2]
                    qb = qhT[tt % 2]
                    for hcl in range(HCS):
                        e.tensor_tensor(hb[:, hcl * P:(hcl + 1) * P],
                                        hb[:, hcl * P:(hcl + 1) * P],
                                        s_h_rep[:, tt * P:(tt + 1) * P],
                                        OP.mult)
                    if SIM_MODE:
                        e.tensor_scalar(hb[:], hb[:], 256.5, None, OP.add)
                        e.tensor_scalar(qb[:], hb[:], 1.0, None, OP.mod)
                        e.scalar_tensor_tensor(qb[:], hb[:], -256.0, qb[:],
                                               OP.add, OP.subtract
                                               ).then_inc(vv.h, 1)
                    else:
                        e.tensor_copy(ib_h[tt % 2][:], hb[:])
                        e.tensor_copy(qb[:], ib_h[tt % 2][:]).then_inc(vv.h, 1)

                def evac(tt):
                    e.wait_ge(mm.h, c["mm0"] + tt + 1)
                    if seg > 0:
                        e.wait_ge(ldh.h, c["hld0"] + c["per_tt"] * tt + 32)
                    if last:
                        e.wait_ge(ldh.h, c["hld0"] + c["per_tt"] * (tt + 1))
                    o = obuf[tt % 2]
                    if not last:
                        if tt >= 2:
                            e.wait_ge(stp.h, c["stp0"] + 16 * (tt - 1))
                        for ds in range(NDS):
                            if seg == 0:
                                e.tensor_copy(o[:, ds * DS:(ds + 1) * DS],
                                              pbank(ds, DS))
                            else:
                                e.tensor_tensor(o[:, ds * DS:(ds + 1) * DS],
                                                pbank(ds, DS),
                                                pbuf[:, ds * DS:(ds + 1) * DS],
                                                OP.add)
                        e.engine_nop().then_inc(hv.h, 1)
                        return
                    if tt >= 2:
                        e.wait_ge(sto.h, 16 * (tt - 1))
                    for ds in range(NDS):
                        e.tensor_tensor(o[:, ds * DS:(ds + 1) * DS],
                                        pbank(ds, DS),
                                        pbuf[:, ds * DS:(ds + 1) * DS],
                                        OP.add)
                    e.engine_nop().then_inc(hv.h, 1)
                    e.tensor_scalar(o[:], o[:], sc3_all[:, tt:tt + 1],
                                    None, OP.mult)
                    e.tensor_tensor(o[:], o[:], rep8k[:], OP.add)
                    e.scalar_tensor_tensor(o[:], ycld[:],
                                           rstd_all[:, tt:tt + 1], o[:],
                                           OP.mult, OP.add)
                    e.reduce_sum(mcol[:], o[:], axis=AX)
                    e.tensor_scalar_mul(mcol[:], mcol[:], 1.0 / D
                                        ).then_inc(hv.h, 1)
                    e.wait_ge(ss.h, c["ss0"] + 3 * tt + 1)
                    e.tensor_scalar_mul(tcol[:], tcol[:], 1.0 / D)
                    e.tensor_tensor(tcol2[:], mcol[:], mcol[:], OP.mult)
                    e.tensor_tensor(tcol[:], tcol[:], tcol2[:],
                                    OP.subtract).then_inc(hv.h, 1)
                    e.wait_ge(ss.h, c["ss0"] + 3 * tt + 2)
                    e.reciprocal(tcol[:], tcol[:])
                    e.tensor_tensor(tcol2[:], mcol[:], tcol[:], OP.mult)
                    e.tensor_scalar_mul(tcol2[:], tcol2[:], -1.0
                                        ).then_inc(hv.h, 1)

                for tt in range(TT):
                    q(tt)
                    if tt >= 1:
                        evac(tt - 1)
                evac(TT - 1)
            em("vector", p3_vec)
            vv.add(TT)
            hv.add(c["evs"] * TT)

            def p3_mm(e, seg=seg, c=c):
                e.wait_ge(trq.h, c["trdone"])
                for tt in range(TT):
                    e.wait_ge(vv.h, c["qv0"] + tt + 1)
                    if tt >= 1:
                        e.wait_ge(hv.h, c["ev0"] + c["evs"] * (tt - 1) + 1)
                    for hcl in range(HCS):
                        for ds in range(NDS):
                            i_ = e.matmul(
                                pbank(ds, DS),
                                qhT[tt % 2][:, hcl * P:(hcl + 1) * P],
                                owh[:, hcl * D + ds * DS:hcl * D + (ds + 1) * DS],
                                start=(hcl == 0), stop=(hcl == HCS - 1))
                    i_.then_inc(mm.h, 1)
            em("tensor", p3_mm)
            c["mm_end"] = mm.add(TT)

            if not last:
                def p3e_gp(e, seg=seg, c=c):
                    for tt in range(TT):
                        e.wait_ge(hv.h, c["ev0"] + tt + 1)
                        e.dma_start(part_sp.ap()[tt * P:(tt + 1) * P, :],
                                    obuf[tt % 2][:]).then_inc(stp.h, 16)
                em("gpsimd", p3e_gp)
                c["stp_end"] = stp.add(16 * TT)
            else:
                def p3f_scal(e, c=c):
                    for tt in range(TT):
                        e.wait_ge(hv.h, c["ev0"] + 4 * tt + 2)
                        e.activation(sq_scr[:], obuf[tt % 2][:], AF.Square,
                                     accum_out=tcol[:]).then_inc(ss.h, 1)
                        e.wait_ge(hv.h, c["ev0"] + 4 * tt + 3)
                        e.activation(tcol[:], tcol[:], AF.Sqrt,
                                     bias=eps_col[:]).then_inc(ss.h, 1)
                        e.wait_ge(hv.h, c["ev0"] + 4 * tt + 4)
                        e.activation(obuf[tt % 2][:], obuf[tt % 2][:],
                                     AF.Identity, bias=tcol2[:],
                                     scale=tcol[:]).then_inc(ss.h, 1)
                em("scalar", p3f_scal)
                ss.add(3 * TT)

                def p3f_gp(e, c=c):
                    for tt in range(TT):
                        e.wait_ge(ss.h, c["ss0"] + 3 * tt + 3)
                        e.dma_start(out_hbm.ap()[tt * P:(tt + 1) * P, :],
                                    obuf[tt % 2][:]).then_inc(sto.h, 16)
                    e.wait_ge(sto.h, 16 * TT)
                em("gpsimd", p3f_gp)
                sto.add(16 * TT)

        @block.sync
        def _(e):
            for fn in prog["sync"]:
                fn(e)

        @block.scalar
        def _(e):
            g = ScalGuard(e, sems["sb"], sflush)
            for fn in prog["scalar"]:
                fn(g)

        @block.vector
        def _(e):
            g = VecGuard(e, sems["vb"])
            for fn in prog["vector"]:
                fn(g)

        @block.tensor
        def _(e):
            for fn in prog["tensor"]:
                fn(e)

        @block.gpsimd
        def _(e):
            for fn in prog["gpsimd"]:
                fn(e)

    return nc


def _prep_inputs(L0_attn_x, L0_attn_B, L0_attn_C, L0_attn_D,
                 L0_ffn_gate_w, L0_ffn_gate_b, L0_ffn_val_w, L0_ffn_val_b,
                 L0_ffn_out_w, L0_ffn_out_b, n_cores=8):
    x = np.ascontiguousarray(np.asarray(L0_attn_x, dtype=np.float32)
                             ).reshape(-1, np.asarray(L0_attn_x).shape[-1])
    T_total, D = x.shape
    T = T_total // n_cores
    H = np.asarray(L0_ffn_gate_w).shape[0]
    NS = np.asarray(L0_attn_B).shape[0]
    HC = H // P

    common = {
        "bw": np.ascontiguousarray(L0_attn_B, dtype=np.float32),
        "cw": np.ascontiguousarray(L0_attn_C, dtype=np.float32),
        "dv": np.ascontiguousarray(L0_attn_D, dtype=np.float32).reshape(1, D),
        "gw": np.ascontiguousarray(L0_ffn_gate_w, dtype=np.float32),
        "vw": np.ascontiguousarray(L0_ffn_val_w, dtype=np.float32),
        "ow": np.ascontiguousarray(L0_ffn_out_w, dtype=np.float32),
        "gb": np.ascontiguousarray(
            np.asarray(L0_ffn_gate_b, dtype=np.float32).reshape(HC, P).T),
        "vb": np.ascontiguousarray(
            np.asarray(L0_ffn_val_b, dtype=np.float32).reshape(HC, P).T),
        "ob": np.ascontiguousarray(L0_ffn_out_b, dtype=np.float32).reshape(1, D),
        "idm": np.eye(P, dtype=np.float32),
    }
    in_maps = []
    for c in range(n_cores):
        m = dict(common)
        m["x"] = np.ascontiguousarray(x[c * T:(c + 1) * T])
        in_maps.append(m)
    return in_maps, (T, D, H, NS)


def kernel(L0_attn_x, L0_attn_B, L0_attn_C, L0_attn_D,
           L0_ffn_gate_w, L0_ffn_gate_b, L0_ffn_val_w, L0_ffn_val_b,
           L0_ffn_out_w, L0_ffn_out_b):
    n_cores = 8
    in_maps, (T, D, H, NS) = _prep_inputs(
        L0_attn_x, L0_attn_B, L0_attn_C, L0_attn_D,
        L0_ffn_gate_w, L0_ffn_gate_b, L0_ffn_val_w, L0_ffn_val_b,
        L0_ffn_out_w, L0_ffn_out_b, n_cores)
    nc = build_nc(T, D, H, NS)
    res = run_bass_kernel_spmd(nc, in_maps, core_ids=list(range(n_cores)))
    out = np.concatenate([r["out"] for r in res.results], axis=0)
    return out.reshape(np.asarray(L0_attn_x).shape).astype(np.float32)


# version 14
# speedup vs baseline: 1.1141x; 1.0002x over previous
import os
import numpy as np

import concourse.bass as bass
import concourse.mybir as mybir
from concourse.bass_utils import run_bass_kernel_spmd

F32 = mybir.dt.float32
BF16 = mybir.dt.bfloat16
I32 = mybir.dt.int32
AF = mybir.ActivationFunctionType
OP = mybir.AluOpType
AX = mybir.AxisListType.X

SIM_MODE = (os.environ.get("BASS_SIM_FP64") == "1"
            or os.environ.get("KERNEL_SIMROUND") == "1")

EPS = 1e-5
THR_M = 1e-5 * (EPS ** 0.5)
P = 128


class Sem:
    def __init__(self, handle):
        self.h = handle
        self.val = 0

    def add(self, n):
        self.val += n
        return self.val


def _ap_range(ap):
    t = ap.tensor
    isz = 2 if t.dtype == BF16 else 4
    span = 1
    dims = list(ap.ap)
    for step, count in dims[1:]:
        span += int(step) * (int(count) - 1)
    base = 0
    space = type(t).__name__
    r = getattr(t, "manual_sbuf_range", None)
    if r is not None:
        base = int(r[0])
    lo = base + int(ap.offset) * isz
    return (space, lo, lo + span * isz)


def _overlap(a, b):
    return a[0] == b[0] and a[1] < b[2] and b[1] < a[2]


class VecGuard:

    _OPS = {
        "tensor_copy": (1, [0]), "tensor_tensor": (2, [0]),
        "tensor_add": (2, [0]), "tensor_sub": (2, [0]),
        "tensor_mul": (2, [0]), "tensor_max": (2, [0]),
        "tensor_scalar": (3, [0]), "tensor_scalar_mul": (2, [0]),
        "tensor_scalar_add": (2, [0]), "tensor_scalar_max": (2, [0]),
        "scalar_tensor_tensor": (4, [0]), "reduce_sum": (2, [0]),
        "reduce_max": (2, [0]), "reciprocal": (2, [0]),
    }

    def __init__(self, eng, sem):
        self._e = eng
        self._sem = sem
        self._writes = []

    def _flush(self):
        if self._writes:
            self._writes = []
            self._e.engine_nop().then_inc(self._sem.h, 1)
            self._e.wait_ge(self._sem.h, self._sem.add(1))

    def wait_ge(self, *a, **k):
        return self._e.wait_ge(*a, **k)

    def memset(self, ap, v):
        i = self._e.memset(ap, v)
        self._writes.append(_ap_range(ap))
        return i

    def engine_nop(self):
        return self._e.engine_nop()

    def __getattr__(self, name):
        fn = getattr(self._e, name)
        spec = self._OPS.get(name)
        if spec is None:
            return fn
        npos, outpos = spec

        def wrapped(*args, **kwargs):
            ins = []
            outs = []
            for i, a in enumerate(args[:npos + 2]):
                if hasattr(a, "tensor"):
                    (outs if i in outpos else ins).append(_ap_range(a))
            for k_, a in kwargs.items():
                if hasattr(a, "tensor"):
                    (outs if k_ == "accum_out" else ins).append(_ap_range(a))
            if any(_overlap(r, w) for r in ins for w in self._writes):
                self._flush()
            res = fn(*args, **kwargs)
            self._writes.extend(outs)
            return res
        return wrapped


class ScalGuard(VecGuard):

    _OPS = {
        "activation": (2, [0]), "copy": (2, [0]),
        "add": (2, [0]), "mul": (2, [0]),
    }

    def __init__(self, eng, sem, scratch):
        super().__init__(eng, sem)
        self._scr = scratch

    def _flush(self):
        if self._writes:
            self._writes = []
            self._e.copy(self._scr[:], self._scr[:]).then_inc(self._sem.h, 1)
            self._e.wait_ge(self._sem.h, self._sem.add(1))


def build_nc(T, D, H, NS, dbg=False):
    TT = T // P
    DC = D // P
    HC = H // P
    NCH = NS // P
    SLAB = min(512, T)
    NSLAB = T // SLAB
    TPS = SLAB // P
    DS = min(512, D)
    NDS = D // DS
    NSEG = 4
    HCS = HC // NSEG
    HSEG = H // NSEG
    GH = 4
    NG = HC // GH
    assert TT <= 64 and DC <= 64

    nc = bass.Bass("TRN2", target_bir_lowering=False,
                   detect_race_conditions=False)

    x_hbm = nc.dram_tensor("x", [T, D], F32, kind="ExternalInput")
    b_hbm = nc.dram_tensor("bw", [NS, D], F32, kind="ExternalInput")
    c_hbm = nc.dram_tensor("cw", [D, NS], F32, kind="ExternalInput")
    d_hbm = nc.dram_tensor("dv", [1, D], F32, kind="ExternalInput")
    gw_hbm = nc.dram_tensor("gw", [H, D], F32, kind="ExternalInput")
    vw_hbm = nc.dram_tensor("vw", [H, D], F32, kind="ExternalInput")
    ow_hbm = nc.dram_tensor("ow", [D, H], F32, kind="ExternalInput")
    gb_hbm = nc.dram_tensor("gb", [P, HC], F32, kind="ExternalInput")
    vb_hbm = nc.dram_tensor("vb", [P, HC], F32, kind="ExternalInput")
    ob_hbm = nc.dram_tensor("ob", [1, D], F32, kind="ExternalInput")
    id_hbm = nc.dram_tensor("idm", [P, P], F32, kind="ExternalInput")
    out_hbm = nc.dram_tensor("out", [T, D], F32, kind="ExternalOutput")

    _spk = "ExternalOutput" if dbg else "Internal"
    yc_sp = nc.dram_tensor("yc_sp", [T, D], BF16, kind=_spk)
    h_sp = nc.dram_tensor("h_sp", [TT, HC, P, P], F32, kind=_spk)
    part_sp = nc.dram_tensor("part_sp", [T, D], F32, kind=_spk)
    if dbg:
        state_d = nc.dram_tensor("state_d", [P, (NS // P) * T], BF16,
                                 kind="ExternalOutput")
        qyT_d = nc.dram_tensor("qyT_d", [P, (D // P) * T], BF16,
                               kind="ExternalOutput")
        rho_d = nc.dram_tensor("rho_d", [P, T], F32, kind="ExternalOutput")
        srep_d = nc.dram_tensor("srep_d", [P, T], F32, kind="ExternalOutput")
        gcol_d = nc.dram_tensor("gcol_d", [P, 16], F32, kind="ExternalOutput")
        sc3_d = nc.dram_tensor("sc3_d", [P, T // P], F32, kind="ExternalOutput")

    base0 = (int(nc.sbuf_base) + 63) & ~63
    top = int(nc.sbuf_top)
    _cur = [base0]

    def place(name, shape, dtype, at=None):
        nbytes = int(np.prod(shape[1:])) * (2 if dtype == BF16 else 4)
        if at is None:
            off = _cur[0]
            _cur[0] = (off + nbytes + 63) & ~63
        else:
            off = at
        assert off + nbytes <= top, f"SBUF overflow placing {name} at {off}"
        return nc.alloc_sbuf_tensor_at(name, list(shape), dtype, offset=off), off

    def bump():
        return _cur[0]

    ident, _ = place("ident", [P, P], F32)
    ident_bf, _ = place("ident_bf", [P, P], BF16)
    gb_sb, _ = place("gb_sb", [P, HC], F32)
    vb_sb, _ = place("vb_sb", [P, HC], F32)
    rep8k, REP8K_OFF = place("rep8k", [P, D], F32)
    gacc, GACC_OFF = place("gacc", [P, 64], F32)
    gred, _ = place("gred", [P, 1], F32)
    gfull, _ = place("gfull", [P, 1], F32)
    wcols = {}
    for wname in ("g", "v", "o"):
        wcols[wname] = {}
        for k in ("thp", "thn", "nsc"):
            wcols[wname][k], _ = place(f"{k}_{wname}", [P, 1], F32)
    m_all, _ = place("m_all", [P, TT], F32)
    M_all, _ = place("M_all", [P, TT], F32)
    ssq_all, _ = place("ssq_all", [P, TT], F32)
    rstd_all, _ = place("rstd_all", [P, TT], F32)
    rho_all, _ = place("rho_all", [P, TT], F32)
    sc3_all, _ = place("sc3_all", [P, TT], F32)
    scol, _ = place("scol", [P, 1], F32)
    mcol, _ = place("mcol", [P, 1], F32)
    tcol, _ = place("tcol", [P, 1], F32)
    tcol2, _ = place("tcol2", [P, 1], F32)
    eps_col, _ = place("eps_col", [P, 1], F32)
    ones_row, _ = place("ones_row", [1, P], F32)
    ones_col, _ = place("ones_col", [P, 1], F32)
    rowb, _ = place("rowb", [1, P], F32)
    sclr, _ = place("sclr", [1, 1], F32)
    sflush, _ = place("sflush", [1, 1], F32)

    PERS_END = bump()

    GW = min(max(D, H), 4096)
    A_SZ = max(DC * T * 2, HCS * D * 2, 2 * GW * 4 + GW * 2)
    SLOT_A = bump()
    _cur[0] = SLOT_A + ((A_SZ + 63) & ~63)
    B_SZ = max(NCH * T * 2, 4 * D * 4, 2 * HCS * P * 4 + 2 * HCS * P * 2 + D * 4)
    SLOT_B = bump()
    _cur[0] = SLOT_B + ((B_SZ + 63) & ~63)
    C_SZ = max(DC * NS * 2, NCH * D * 2,
               8 * D * 2,
               2 * HSEG * 4 + 2 * HSEG * 2)
    SLOT_C = bump()
    _cur[0] = SLOT_C + ((C_SZ + 63) & ~63)
    D_SZ = max(DC * SLAB * 2,
               3 * D * 4,
               6 * SLAB * 4 + 2 * T * 4,
               D * 2 + 2 * D * 4)
    SLOT_D = bump()
    _cur[0] = SLOT_D + ((D_SZ + 63) & ~63)
    E_SZ = max(2 * D * 4, D * 4)
    SLOT_E = bump()
    _cur[0] = SLOT_E + ((E_SZ + 63) & ~63)
    F_SZ = max(2 * D * 2, 2 * NS * 4, T * 4)
    SLOT_F = bump()
    _cur[0] = SLOT_F + ((F_SZ + 63) & ~63)
    G_SZ = 2 * NS * 2
    SLOT_G = bump()
    _cur[0] = SLOT_G + ((G_SZ + 63) & ~63)
    H_SZ = D * 2
    SLOT_H = bump()
    _cur[0] = SLOT_H + ((H_SZ + 63) & ~63)
    I_SZ = D * 2
    SLOT_I = bump()
    _cur[0] = SLOT_I + ((I_SZ + 63) & ~63)
    assert bump() <= top, f"arena overflow: {bump()} > {top}"

    def at(name, shape, dtype, off):
        t_, _ = place(name, shape, dtype, at=off)
        return t_

    s_h_rep = at("s_h_rep", [P, T], F32, SLOT_F)
    sq_scr = at("sq_scr", [P, D], BF16, SLOT_D + 3 * D * 4)

    gld = [at(f"gld{i}", [P, GW], F32, SLOT_A + i * GW * 4) for i in range(2)]
    gscr = at("gscr", [P, GW], BF16, SLOT_A + 2 * GW * 4)
    qyT_sb = at("qyT_sb", [P, DC * T], BF16, SLOT_A)
    state_sb = at("state_sb", [P, NCH * T], BF16, SLOT_B)
    bt_sb = at("bt_sb", [P, DC * NS], BF16, SLOT_C)
    ct_sb = at("ct_sb", [P, NCH * D], BF16, SLOT_C)
    xT_sb = at("xT_sb", [P, DC * SLAB], BF16, SLOT_D)
    wbuf = [at(f"wbuf{i}", [P, D], F32, SLOT_D + i * D * 4) for i in range(2)]
    t256 = at("t256", [P, D], F32, SLOT_D + 2 * D * 4)
    xld = [at(f"xld{i}", [P, D], F32, SLOT_E + i * D * 4) for i in range(2)]
    xbf = [at(f"xbf{i}", [P, D], BF16, SLOT_F + i * D * 2) for i in range(2)]
    cld = [at(f"cld{i}", [P, NS], F32, SLOT_F + i * NS * 4) for i in range(2)]
    cbf = [at(f"cbf{i}", [P, NS], BF16, SLOT_G + i * NS * 2) for i in range(2)]
    ycbf = at("ycbf", [P, D], BF16, SLOT_H)
    qybf = at("qybf", [P, D], BF16, SLOT_I)
    ib_q = at("ib_q", [P, D], I32, SLOT_D + 2 * D * 4)
    graw = [at(f"graw{i}", [P, D], F32, SLOT_B + i * D * 4) for i in range(2)]
    vraw = [at(f"vraw{i}", [P, D], F32, SLOT_B + (2 + i) * D * 4)
            for i in range(2)]
    gtb = [at(f"gtb{i}", [P, D], BF16, SLOT_C + i * D * 2)
           for i in range(2)]
    vtb = [at(f"vtb{i}", [P, D], BF16, SLOT_C + (2 + i) * D * 2)
           for i in range(2)]
    gwT = [at(f"gwT{i}", [P, D], BF16, SLOT_C + (4 + i) * D * 2)
           for i in range(2)]
    vwT = [at(f"vwT{i}", [P, D], BF16, SLOT_C + (6 + i) * D * 2)
           for i in range(2)]
    dsel = at("dsel", [P, D], BF16, REP8K_OFF)
    gsl = [at(f"gsl{i}", [P, SLAB], F32, SLOT_D + i * SLAB * 4)
           for i in range(2)]
    vsl = [at(f"vsl{i}", [P, SLAB], F32, SLOT_D + (2 + i) * SLAB * 4)
           for i in range(2)]
    hsl = [at(f"hsl{i}", [P, SLAB], F32, SLOT_D + (4 + i) * SLAB * 4)
           for i in range(2)]
    mrun = at("mrun", [P, T], F32, SLOT_D + 6 * SLAB * 4)
    rho_rep = at("rho_rep", [P, T], F32, SLOT_D + 6 * SLAB * 4 + T * 4)
    owh = at("owh", [P, HCS * D], BF16, SLOT_A)
    oraw = [at(f"oraw{i}", [P, HSEG], F32, SLOT_C + i * HSEG * 4)
            for i in range(2)]
    otb = at("otb", [P, HSEG], BF16, SLOT_C + 2 * HSEG * 4)
    osel = at("osel", [P, HSEG], BF16, SLOT_C + 2 * HSEG * 4 + HSEG * 2)
    hstg = [at(f"hstg{i}", [P, HCS * P], F32, SLOT_B + i * HCS * P * 4)
            for i in range(2)]
    ib_h = [at(f"ib_h{i}", [P, HCS * P], I32, SLOT_B + i * HCS * P * 4)
            for i in range(2)]
    qhT = [at(f"qhT{i}", [P, HCS * P], BF16,
              SLOT_B + 2 * HCS * P * 4 + i * HCS * P * 2) for i in range(2)]
    pbuf = at("pbuf", [P, D], F32, SLOT_E)
    ycld = at("ycld", [P, D], BF16, SLOT_D)
    obuf = [at(f"obuf{i}", [P, D], F32, SLOT_D + D * 2 + i * D * 4)
            for i in range(2)]

    PS = nc.alloc_psum_tensor("PS", [P, 3072], F32)
    PSW = nc.alloc_psum_tensor("PSW", [P, 2048], BF16)
    BANK = 512

    def pbank(b, width=BANK):
        return PS[:, b * BANK:b * BANK + width]

    import contextlib
    st = contextlib.ExitStack()
    with st:
        S = st.enter_context
        sems = {}
        for name in ("ldc", "ldx", "ldw", "ldh", "trq", "mm", "ev", "vv",
                     "hv", "ss", "gg", "sty", "sth", "stp", "sto",
                     "vb", "sb", "ts2", "th", "sq", "ga"):
            sems[name] = Sem(S(nc.semaphore(name)))
        ldc, ldx, ldw, ldh = (sems[k] for k in ("ldc", "ldx", "ldw", "ldh"))
        trq, mm, ev, vv = (sems[k] for k in ("trq", "mm", "ev", "vv"))
        ss, gg, hv = sems["ss"], sems["gg"], sems["hv"]
        sty, sth, stp, sto = (sems[k] for k in ("sty", "sth", "stp", "sto"))

        block = S(nc.Block())
        prog = {k: [] for k in ("sync", "scalar", "vector", "tensor", "gpsimd")}
        _scope = ["s0"]

        def em(engine, fn, scope=None):
            sc = scope or _scope[0]
            def tagged(e, fn=fn, sc=sc):
                with nc.named_scope(sc):
                    fn(e)
            prog[engine].append(tagged)

        def set_scope(name):
            _scope[0] = name

        def s0_sync(e):
            e.dma_start(ident[:], id_hbm.ap()).then_inc(ldc.h, 16)
            e.dma_start(gb_sb[:], gb_hbm.ap()).then_inc(ldc.h, 16)
            e.dma_start(vb_sb[:], vb_hbm.ap()).then_inc(ldc.h, 16)
            e.dma_start(rep8k[0:1, :], d_hbm.ap()).then_inc(ldc.h, 16)
        em("sync", s0_sync)
        ldc.add(64)

        def s0_vec(e):
            e.memset(sflush[:], 0.0)
            e.memset(eps_col[:], EPS)
            e.memset(ones_row[:], 1.0)
            e.memset(ones_col[:], 1.0).then_inc(vv.h, 1)
        em("vector", s0_vec)
        VV_ONES = vv.add(1)

        def s0_mm(e):
            e.wait_ge(vv.h, VV_ONES)
            e.wait_ge(ldc.h, 64)
            for ds in range(NDS):
                e.matmul(pbank(ds, DS), ones_row[:],
                         rep8k[0:1, ds * DS:(ds + 1) * DS],
                         start=True, stop=True).then_inc(mm.h, 1)
        em("tensor", s0_mm)
        MM_DREP = mm.add(NDS)

        def s0_vec2(e):
            e.wait_ge(mm.h, MM_DREP)
            for ds in range(NDS):
                e.tensor_scalar(rep8k[:, ds * DS:(ds + 1) * DS],
                                pbank(ds, DS), 1.0, None, OP.add)
            e.tensor_copy(ident_bf[:], ident[:])
            e.engine_nop().then_inc(vv.h, 1)
        em("vector", s0_vec2)
        VV_DREP = vv.add(1)

        set_scope("gamma")
        GBIAS = 1.0 if SIM_MODE else (1.0 + 2.3e-7)
        for wname, whbm, rows, cols in (("g", gw_hbm, H, D),
                                        ("v", vw_hbm, H, D),
                                        ("o", ow_hbm, D, H)):
            cpr = max(1, cols // GW)
            w_ = min(cols, GW)
            ntile = (rows // P) * cpr
            inv_n = GBIAS / float(rows * cols)
            assert ntile <= 64

            ld0 = ldw.val
            ab0 = ss.val
            prev_gred = vv.val

            def g_sync(e, whbm=whbm, rows=rows, cpr=cpr, w_=w_, ab0=ab0):
                for i in range(rows // P):
                    for j in range(cpr):
                        k = i * cpr + j
                        if k < 2:
                            e.wait_ge(ss.h, ab0)
                        else:
                            e.wait_ge(ss.h, ab0 + k - 1)
                        e.dma_start(
                            gld[k % 2][:, 0:w_],
                            whbm.ap()[i * P:(i + 1) * P, j * w_:(j + 1) * w_]
                        ).then_inc(ldw.h, 16)
            em("sync", g_sync)
            ldw.add(16 * ntile)

            def g_act(e, ntile=ntile, w_=w_, ld0=ld0, first=(wname == "g"),
                      prev_gred=prev_gred):
                if first:
                    e.wait_ge(ldc.h, 64)
                    e.wait_ge(vv.h, VV_ONES)
                e.wait_ge(vv.h, prev_gred)
                for k in range(ntile):
                    e.wait_ge(ldw.h, ld0 + 16 * (k + 1))
                    e.activation(gscr[:, 0:w_], gld[k % 2][:, 0:w_], AF.Abs,
                                 accum_out=gacc[:, k:k + 1]).then_inc(ss.h, 1)
            em("scalar", g_act)
            SS_ABS = ss.add(ntile)

            def g_vec(e, ntile=ntile, tgt=SS_ABS):
                e.wait_ge(ss.h, tgt)
                e.reduce_sum(gred[:], gacc[:, 0:ntile], axis=AX).then_inc(vv.h, 1)
            em("vector", g_vec)
            VV_GRED = vv.add(1)

            def g_mm1(e, tgt=VV_GRED):
                e.wait_ge(vv.h, tgt)
                e.matmul(PS[0:1, 2560:2561], gred[:], ones_col[:],
                         start=True, stop=True).then_inc(mm.h, 1)
            em("tensor", g_mm1)
            MM_GSUM = mm.add(1)

            def g_vsc(e, tgt=MM_GSUM):
                e.wait_ge(mm.h, tgt)
                e.tensor_copy(sclr[:], PS[0:1, 2560:2561]).then_inc(vv.h, 1)
            em("vector", g_vsc)
            VV_GSC = vv.add(1)

            def g_mm2(e, tgt=VV_GSC):
                e.wait_ge(vv.h, tgt)
                e.matmul(PS[:, 2561:2562], ones_row[:], sclr[:],
                         start=True, stop=True).then_inc(mm.h, 1)
            em("tensor", g_mm2)
            MM_GBC = mm.add(1)

            def g_vec2(e, wname=wname, inv_n=inv_n, tgt=MM_GBC):
                c = wcols[wname]
                e.wait_ge(mm.h, tgt)
                e.tensor_scalar(gfull[:], PS[:, 2561:2562], inv_n, EPS,
                                OP.mult, OP.max)
                e.tensor_scalar_mul(c["thp"][:], gfull[:], 0.5)
                e.tensor_scalar_mul(c["thn"][:], gfull[:], -0.5)
                e.tensor_scalar_mul(c["nsc"][:], gfull[:], -1.0
                                    ).then_inc(vv.h, 1)
            em("vector", g_vec2)
            vv.add(1)
        VV_GAMMA = vv.val

        set_scope("p1a")
        bld0 = ldx.val
        bcast0 = vv.val
        btr0 = trq.val

        def b_sync(e):
            for i in range(min(2, NCH)):
                e.dma_start(xld[i][:], b_hbm.ap()[i * P:(i + 1) * P, :]
                            ).then_inc(ldx.h, 16)
            for i in range(NCH):
                e.wait_ge(vv.h, bcast0 + i + 1)
                for dc in range(DC):
                    e.dma_start_transpose(
                        bt_sb[:, dc * NS + i * P:dc * NS + (i + 1) * P],
                        xbf[i % 2][:, dc * P:(dc + 1) * P]).then_inc(trq.h, 16)
                if i + 2 < NCH:
                    e.dma_start(xld[i % 2][:],
                                b_hbm.ap()[(i + 2) * P:(i + 3) * P, :]
                                ).then_inc(ldx.h, 16)
        em("sync", b_sync)
        ldx.add(16 * NCH)
        TRQ_BT = trq.add(16 * DC * NCH)

        def b_vec(e):
            e.wait_ge(vv.h, VV_GAMMA)
            for i in range(NCH):
                e.wait_ge(ldx.h, bld0 + 16 * (i + 1))
                if i >= 2:
                    e.wait_ge(trq.h, btr0 + 16 * DC * (i - 1))
                e.tensor_copy(xbf[i % 2][:], xld[i % 2][:]).then_inc(vv.h, 1)
        em("vector", b_vec)
        vv.add(NCH)
        VV_BCAST = vv.val

        xld0 = ldx.val
        xcast0 = vv.val
        xtr0 = trq.val
        z1mm0 = mm.val
        SS_TANH0 = None

        def x_sync(e):
            def load(k):
                if k < 2:
                    e.wait_ge(vv.h, VV_BCAST)
                e.dma_start(xld[k % 2][:],
                            x_hbm.ap()[k * P:(k + 1) * P, :]
                            ).then_inc(ldx.h, 16)
            for k in range(min(2, TT)):
                load(k)
            k = 0
            for s_ in range(NSLAB):
                if s_ >= 1:
                    e.wait_ge(mm.h, z1mm0 + s_ * NCH)
                for j in range(TPS):
                    e.wait_ge(vv.h, xcast0 + k + 1)
                    for dc in range(DC):
                        e.dma_start_transpose(
                            xT_sb[:, dc * SLAB + j * P:dc * SLAB + (j + 1) * P],
                            xbf[k % 2][:, dc * P:(dc + 1) * P]).then_inc(trq.h, 16)
                    if k + 2 < TT:
                        load(k + 2)
                    k += 1
        em("sync", x_sync)
        ldx.add(16 * TT)
        TRQ_XT = trq.add(16 * DC * TT)

        def x_vec(e):
            e.wait_ge(trq.h, TRQ_BT)
            k = 0
            for s_ in range(NSLAB):
                for j in range(TPS):
                    e.wait_ge(ldx.h, xld0 + 16 * (k + 1))
                    if k >= 2:
                        e.wait_ge(trq.h, xtr0 + 16 * DC * (k - 1))
                    e.tensor_copy(xbf[k % 2][:], xld[k % 2][:]).then_inc(vv.h, 1)
                    k += 1
        em("vector", x_vec)
        vv.add(TT)

        def z1_mm(e):
            e.wait_ge(trq.h, TRQ_BT)
            k = 0
            for s_ in range(NSLAB):
                e.wait_ge(trq.h, xtr0 + 16 * DC * TPS * (s_ + 1))
                for nch in range(NCH):
                    ps = pbank(4 + (k % 2), SLAB)
                    if k >= 2:
                        e.wait_ge(ss.h, SS_TANH0 + k - 1)
                    for dc in range(DC):
                        i_ = e.matmul(
                            ps,
                            bt_sb[:, dc * NS + nch * P:dc * NS + (nch + 1) * P],
                            xT_sb[:, dc * SLAB:(dc + 1) * SLAB],
                            start=(dc == 0), stop=(dc == DC - 1))
                    i_.then_inc(mm.h, 1)
                    k += 1
        em("tensor", z1_mm)
        mm.add(NSLAB * NCH)

        SS_TANH0 = ss.val

        def z1_tanh(e):
            k = 0
            for s_ in range(NSLAB):
                for nch in range(NCH):
                    e.wait_ge(mm.h, z1mm0 + k + 1)
                    e.activation(
                        state_sb[:, nch * T + s_ * SLAB:nch * T + (s_ + 1) * SLAB],
                        pbank(4 + (k % 2), SLAB), AF.Tanh).then_inc(ss.h, 1)
                    k += 1
        em("scalar", z1_tanh)
        ss.add(NSLAB * NCH)
        SS_P1A = ss.val

        set_scope("p1b")
        cld0 = ldx.val
        ccast0 = vv.val
        ctr0 = trq.val

        def c_sync(e):
            e.wait_ge(trq.h, TRQ_XT)
            for i in range(min(2, DC)):
                e.dma_start(cld[i][:], c_hbm.ap()[i * P:(i + 1) * P, :]
                            ).then_inc(ldx.h, 16)
            e.wait_ge(mm.h, z1mm0 + NSLAB * NCH)
            for i in range(DC):
                e.wait_ge(vv.h, ccast0 + i + 1)
                for nch in range(NCH):
                    e.dma_start_transpose(
                        ct_sb[:, nch * D + i * P:nch * D + (i + 1) * P],
                        cbf[i % 2][:, nch * P:(nch + 1) * P]).then_inc(trq.h, 16)
                if i + 2 < DC:
                    e.dma_start(cld[i % 2][:],
                                c_hbm.ap()[(i + 2) * P:(i + 3) * P, :]
                                ).then_inc(ldx.h, 16)
        em("sync", c_sync)
        ldx.add(16 * DC)
        TRQ_CT = trq.add(16 * NCH * DC)

        def c_vec(e):
            for i in range(DC):
                e.wait_ge(ldx.h, cld0 + 16 * (i + 1))
                if i >= 2:
                    e.wait_ge(trq.h, ctr0 + 16 * NCH * (i - 1))
                e.tensor_copy(cbf[i % 2][:], cld[i % 2][:]).then_inc(vv.h, 1)
        em("vector", c_vec)
        vv.add(DC)

        x2ld0 = ldx.val
        P1B_V0 = None
        P1B_EV0 = None

        def x2_sync(e):
            e.wait_ge(vv.h, xcast0 + TT)
            for tt in range(min(2, TT)):
                e.dma_start(xld[tt][:], x_hbm.ap()[tt * P:(tt + 1) * P, :]
                            ).then_inc(ldx.h, 16)
        em("sync", x2_sync)
        ldx.add(16 * min(2, TT))

        mm2_0 = mm.val

        def mm2(e):
            e.wait_ge(trq.h, TRQ_CT)
            e.wait_ge(ss.h, SS_P1A)
            for tt in range(TT):
                if tt >= 1:
                    e.wait_ge(ev.h, P1B_EV0 + tt)
                for ds in range(NDS):
                    for nch in range(NCH):
                        i_ = e.matmul(
                            pbank(ds, DS),
                            state_sb[:, nch * T + tt * P:nch * T + (tt + 1) * P],
                            ct_sb[:, nch * D + ds * DS:nch * D + (ds + 1) * DS],
                            start=(nch == 0), stop=(nch == NCH - 1))
                i_.then_inc(mm.h, 1)
        em("tensor", mm2)
        mm.add(TT)

        P1B_V0 = vv.val
        P1B_EV0 = ev.val
        SS_SQ0 = None
        QYTR0 = None

        def p1b_vec(e):
            for tt in range(TT):
                e.wait_ge(ldx.h, x2ld0 + 16 * (tt + 1))
                e.wait_ge(mm.h, mm2_0 + tt + 1)
                w = wbuf[tt % 2]
                if tt >= 2:
                    e.wait_ge(ss.h, SS_SQ0 + tt - 1)
                if tt >= 1:
                    e.wait_ge(sty.h, 16 * tt)
                e.tensor_tensor(w[:], xld[tt % 2][:], rep8k[:], OP.mult)
                e.tensor_tensor(w[:], w[:], PS[:, 0:D], OP.add).then_inc(ev.h, 1)
                e.reduce_sum(mcol[:], w[:], axis=AX)
                e.tensor_scalar_mul(mcol[:], mcol[:], 1.0 / D)
                e.tensor_copy(m_all[:, tt:tt + 1], mcol[:])
                e.tensor_scalar(w[:], w[:], mcol[:], None, OP.subtract)
                e.reduce_max(M_all[:, tt:tt + 1], w[:], axis=AX,
                             apply_absolute_value=True)
                e.tensor_copy(ycbf[:], w[:]).then_inc(vv.h, 1)
                e.tensor_scalar(tcol[:], M_all[:, tt:tt + 1], THR_M, None, OP.max)
                e.reciprocal(tcol[:], tcol[:])
                e.tensor_scalar_mul(scol[:], tcol[:], 127.0)
                if tt >= 1:
                    e.wait_ge(trq.h, QYTR0 + 16 * DC * tt)
                if SIM_MODE:
                    e.tensor_scalar(t256[:], w[:], scol[:], 256.5,
                                    OP.mult, OP.add)
                    e.tensor_scalar(w[:], t256[:], 1.0, None, OP.mod)
                    e.scalar_tensor_tensor(qybf[:], t256[:], -256.0, w[:],
                                           OP.add, OP.subtract).then_inc(vv.h, 1)
                else:
                    e.tensor_scalar(t256[:], w[:], scol[:], None, OP.mult)
                    e.tensor_copy(ib_q[:], t256[:])
                    e.tensor_copy(qybf[:], ib_q[:]).then_inc(vv.h, 1)
        em("vector", p1b_vec)
        vv.add(2 * TT)
        ev.add(TT)

        SS_SQ0 = ss.val

        def p1b_scal(e):
            for tt in range(TT):
                e.wait_ge(vv.h, P1B_V0 + 2 * tt + 1)
                e.activation(sq_scr[:], wbuf[tt % 2][:], AF.Square,
                             accum_out=ssq_all[:, tt:tt + 1]).then_inc(ss.h, 1)
        em("scalar", p1b_scal)
        ss.add(TT)

        def p1b_gp(e):
            for tt in range(TT):
                e.wait_ge(vv.h, P1B_V0 + 2 * tt + 1)
                e.dma_start(yc_sp.ap()[tt * P:(tt + 1) * P, :], ycbf[:]
                            ).then_inc(sty.h, 16)
        em("gpsimd", p1b_gp)
        STY_DONE = sty.add(16 * TT)

        if dbg:
            def dbg1_gp(e):
                e.wait_ge(ss.h, SS_P1A)
                e.dma_start(state_d.ap(), state_sb[:]).then_inc(gg.h, 16)
            em("gpsimd", dbg1_gp)
            gg.add(16)

        QYTR0 = trq.val

        def qy_tr(e):
            for tt in range(TT):
                e.wait_ge(vv.h, P1B_V0 + 2 * tt + 2)
                for dc in range(DC):
                    e.dma_start_transpose(
                        qyT_sb[:, dc * T + tt * P:dc * T + (tt + 1) * P],
                        qybf[:, dc * P:(dc + 1) * P]).then_inc(trq.h, 16)
                if tt + 2 < TT:
                    e.dma_start(xld[tt % 2][:],
                                x_hbm.ap()[(tt + 2) * P:(tt + 3) * P, :]
                                ).then_inc(ldx.h, 16)
        em("sync", qy_tr)
        TRQ_QY = trq.add(16 * DC * TT)
        ldx.add(16 * max(0, TT - 2))

        set_scope("p1e")
        TRPC = 4 * BANK

        def p1e_scal(e):
            e.wait_ge(vv.h, P1B_V0 + 2 * TT)
            e.activation(rstd_all[:], ssq_all[:], AF.Sqrt,
                         scale=1.0 / D, bias=eps_col[:]).then_inc(ss.h, 1)
        em("scalar", p1e_scal)
        SS_RSTD = ss.add(1)

        def p1e_vec(e):
            e.wait_ge(ss.h, SS_RSTD)
            e.reciprocal(rstd_all[:], rstd_all[:])
            e.tensor_scalar(rho_all[:], M_all[:], THR_M, None, OP.max)
            e.tensor_tensor(rho_all[:], rho_all[:], rstd_all[:], OP.mult)
            e.tensor_scalar_mul(rho_all[:], rho_all[:], 1.0 / 127.0
                                ).then_inc(vv.h, 1)
        em("vector", p1e_vec)
        VV_RHO = vv.add(1)

        BCC = 5 * BANK
        RB_M0 = mm.val
        RB_V0 = vv.val

        def p1e_mmT(e):
            e.wait_ge(vv.h, VV_RHO)
            for tt in range(TT):
                if tt > 0:
                    e.wait_ge(vv.h, RB_V0 + 2 * tt - 1)
                e.transpose(PS[0:1, TRPC:TRPC + P], rho_all[:, tt:tt + 1],
                            ident[:]).then_inc(mm.h, 1)
                e.wait_ge(vv.h, RB_V0 + 2 * tt + 1)
                e.matmul(PS[:, BCC:BCC + P], ones_row[:], rowb[:],
                         start=True, stop=True).then_inc(mm.h, 1)
        em("tensor", p1e_mmT)
        mm.add(2 * TT)

        def p1e_vec2(e):
            for tt in range(TT):
                e.wait_ge(mm.h, RB_M0 + 2 * tt + 1)
                e.tensor_copy(rowb[:], PS[0:1, TRPC:TRPC + P]).then_inc(vv.h, 1)
                e.wait_ge(mm.h, RB_M0 + 2 * tt + 2)
                e.tensor_copy(rho_rep[:, tt * P:(tt + 1) * P],
                              PS[:, BCC:BCC + P]).then_inc(vv.h, 1)
        em("vector", p1e_vec2)
        RHOREP_V = vv.add(2 * TT)

        def mrun_vec(e):
            e.memset(mrun[:], 0.0).then_inc(vv.h, 1)
        em("vector", mrun_vec)
        VV_MRUN = vv.add(1)

        set_scope("p2")
        th, sq = sems["th"], sems["sq"]
        ldg = [ldw, sems["ga"]]
        ldv = [sems["ts2"], ldx]
        P2LDG = [ldg[0].val, ldg[1].val]
        P2LDV = [ldv[0].val, ldv[1].val]
        P2TH0 = th.val
        P2SQ0 = sq.val
        P2TERN0 = vv.val + 0
        P2MM0 = mm.val
        P2EV0 = ev.val
        P2SS0 = ss.val
        P2HV0 = None
        RPS = (1, 1, 1, 1)

        def p2_sync(e):
            e.wait_ge(vv.h, RHOREP_V)
            if dbg:
                e.wait_ge(gg.h, 16)
            for hc in range(HC):
                if hc >= 2:
                    e.wait_ge(vv.h, P2TERN0 + 2 * (hc - 1))
                e.dma_start(graw[hc % 2][:], gw_hbm.ap()[hc * P:(hc + 1) * P, :]
                            ).then_inc(ldg[hc % 2].h, 16)
                e.dma_start(vraw[hc % 2][:], vw_hbm.ap()[hc * P:(hc + 1) * P, :]
                            ).then_inc(ldv[hc % 2].h, 16)
        em("sync", p2_sync)
        for _s in (*ldg, *ldv):
            _s.add(16 * (HC // 2))

        P2TERN0 = vv.val

        def _tround(e, hc, r):
            R = 4 * hc + r
            src = gtb[hc % 2] if r < 2 else vtb[hc % 2]
            j = r % 2
            e.wait_ge(vv.h, P2TERN0 + 2 * hc + (1 if r < 2 else 2))
            if R >= 2:
                e.wait_ge(sq.h, P2SQ0 + R - 1)
            b = (R % 2) * 8 * P
            for i in range(8):
                i_ = e.transpose(PSW[:, b + i * P:b + (i + 1) * P],
                                 src[:, (8 * j + i) * P:(8 * j + i + 1) * P],
                                 ident_bf[:])
            i_.then_inc(th.h, 1)

        def p2_mm(e):
            e.wait_ge(trq.h, TRQ_QY)
            for r in range(4):
                _tround(e, 0, r)
            for hc in range(HC):
                e.wait_ge(sq.h, P2SQ0 + 4 * (hc + 1))
                rnext = 0
                for s_ in range(NSLAB):
                    k = hc * NSLAB + s_
                    if k >= 3:
                        e.wait_ge(ev.h, P2EV0 + k - 2)
                    for dc in range(DC):
                        e.matmul(pbank(k % 3, SLAB),
                                 gwT[hc % 2][:, dc * P:(dc + 1) * P],
                                 qyT_sb[:, dc * T + s_ * SLAB:dc * T + (s_ + 1) * SLAB],
                                 start=(dc == 0), stop=(dc == DC - 1))
                    for dc in range(DC):
                        i_ = e.matmul(pbank(3 + k % 3, SLAB),
                                      vwT[hc % 2][:, dc * P:(dc + 1) * P],
                                      qyT_sb[:, dc * T + s_ * SLAB:dc * T + (s_ + 1) * SLAB],
                                      start=(dc == 0), stop=(dc == DC - 1))
                    i_.then_inc(mm.h, 1)
                    if hc + 1 < HC:
                        for _ in range(RPS[s_]):
                            _tround(e, hc + 1, rnext)
                            rnext += 1
        em("tensor", p2_mm)
        mm.add(HC * NSLAB)
        th.add(4 * HC)

        P2HV0 = hv.val

        def p2_vec(e):
            e.wait_ge(vv.h, VV_MRUN)

            def tern(hc):
                e.wait_ge(ldg[hc % 2].h, P2LDG[hc % 2] + 16 * (hc // 2 + 1))
                if hc >= 2:
                    e.wait_ge(th.h, P2TH0 + 4 * hc - 6)
                e.tensor_scalar(dsel[:], graw[hc % 2][:],
                                wcols["g"]["thp"][:], None, OP.is_gt)
                e.scalar_tensor_tensor(gtb[hc % 2][:], graw[hc % 2][:],
                                       wcols["g"]["thn"][:], dsel[:],
                                       OP.is_lt, OP.subtract).then_inc(vv.h, 1)
                e.wait_ge(ldv[hc % 2].h, P2LDV[hc % 2] + 16 * (hc // 2 + 1))
                if hc >= 2:
                    e.wait_ge(th.h, P2TH0 + 4 * hc - 4)
                e.tensor_scalar(dsel[:], vraw[hc % 2][:],
                                wcols["v"]["thp"][:], None, OP.is_gt)
                e.scalar_tensor_tensor(vtb[hc % 2][:], vraw[hc % 2][:],
                                       wcols["v"]["thn"][:], dsel[:],
                                       OP.is_lt, OP.subtract).then_inc(vv.h, 1)

            def epi(hc):
                for s_ in range(NSLAB):
                    k = hc * NSLAB + s_
                    e.wait_ge(mm.h, P2MM0 + k + 1)
                    if k >= 2:
                        e.wait_ge(sth.h, 16 * k)
                    e.tensor_tensor(gsl[s_ % 2][:], pbank(k % 3, SLAB),
                                    rho_rep[:, s_ * SLAB:(s_ + 1) * SLAB],
                                    OP.mult)
                    e.tensor_tensor(vsl[s_ % 2][:], pbank(3 + k % 3, SLAB),
                                    rho_rep[:, s_ * SLAB:(s_ + 1) * SLAB],
                                    OP.mult).then_inc(ev.h, 1)
                    e.wait_ge(ss.h, P2SS0 + 2 * k + 2)
                    e.tensor_tensor(hsl[s_ % 2][:], gsl[s_ % 2][:],
                                    vsl[s_ % 2][:], OP.mult)
                    e.tensor_tensor(mrun[:, s_ * SLAB:(s_ + 1) * SLAB],
                                    mrun[:, s_ * SLAB:(s_ + 1) * SLAB],
                                    hsl[s_ % 2][:], OP.max)
                    e.scalar_tensor_tensor(mrun[:, s_ * SLAB:(s_ + 1) * SLAB],
                                           hsl[s_ % 2][:], -1.0,
                                           mrun[:, s_ * SLAB:(s_ + 1) * SLAB],
                                           OP.mult, OP.max).then_inc(hv.h, 1)

            tern(0)
            tern(1)
            for hc in range(HC):
                if hc + 2 < HC:
                    tern(hc + 2)
                epi(hc)
        em("vector", p2_vec)
        vv.add(2 * HC)
        hv.add(HC * NSLAB)
        ev.add(HC * NSLAB)

        def p2_scal(e):
            def evac(hc, r):
                R = 4 * hc + r
                e.wait_ge(th.h, P2TH0 + R + 1)
                if hc >= 2:
                    e.wait_ge(mm.h, P2MM0 + NSLAB * (hc - 1))
                dst = gwT if r < 2 else vwT
                j = r % 2
                b = (R % 2) * 8 * P
                e.activation(dst[hc % 2][:, j * 8 * P:(j + 1) * 8 * P],
                             PSW[:, b:b + 8 * P], AF.Identity
                             ).then_inc(sq.h, 1)

            def silu_slab(hc, s_):
                k = hc * NSLAB + s_
                e.wait_ge(ev.h, P2EV0 + k + 1)
                e.activation(gsl[s_ % 2][:], gsl[s_ % 2][:], AF.Silu,
                             bias=gb_sb[:, hc:hc + 1],
                             scale=wcols["g"]["nsc"][:]).then_inc(ss.h, 1)
                e.activation(vsl[s_ % 2][:], vsl[s_ % 2][:], AF.Identity,
                             bias=vb_sb[:, hc:hc + 1],
                             scale=wcols["v"]["nsc"][:]).then_inc(ss.h, 1)

            for r in range(4):
                evac(0, r)
            for hc in range(1, HC):
                rn = 0
                for s_ in range(NSLAB):
                    for _ in range(RPS[s_]):
                        evac(hc, rn)
                        rn += 1
                    silu_slab(hc - 1, s_)
            for s_ in range(NSLAB):
                silu_slab(HC - 1, s_)
        em("scalar", p2_scal)
        ss.add(2 * HC * NSLAB)
        sq.add(4 * HC)

        def p2_gp(e):
            for hc in range(HC):
                for s_ in range(NSLAB):
                    k = hc * NSLAB + s_
                    e.wait_ge(hv.h, P2HV0 + k + 1)
                    e.dma_start(h_sp.ap()[s_ * TPS:(s_ + 1) * TPS, hc, :, :]
                                .rearrange("a p b -> p a b"),
                                hsl[s_ % 2][:].rearrange("p (a b) -> p a b",
                                                         a=TPS)
                                ).then_inc(sth.h, 16)
        em("gpsimd", p2_gp)
        STH_DONE = sth.add(16 * HC * NSLAB)

        VV_SREP_HOLDER = [None]
        if dbg:
            GG_D2 = gg.val + 13 * 16

            def dbg2_gp(e):
                e.wait_ge(trq.h, TRQ_QY)
                e.wait_ge(vv.h, VV_SREP_HOLDER[0])
                e.dma_start(qyT_d.ap(), qyT_sb[:]).then_inc(gg.h, 16)
                e.dma_start(rho_d.ap(), rho_rep[:]).then_inc(gg.h, 16)
                e.dma_start(srep_d.ap(), s_h_rep[:]).then_inc(gg.h, 16)
                e.dma_start(sc3_d.ap(), sc3_all[:]).then_inc(gg.h, 16)
                for j, (wn, kk) in enumerate(
                        (w_, k_) for w_ in ("g", "v", "o")
                        for k_ in ("thp", "thn", "nsc")):
                    e.dma_start(gcol_d.ap()[:, j:j + 1], wcols[wn][kk][:],
                                allow_slow_non_contiguous=True
                                ).then_inc(gg.h, 16)
                e.wait_ge(gg.h, GG_D2)
            em("gpsimd", dbg2_gp)
            gg.add(13 * 16)

        set_scope("p2e")
        BCC = 5 * BANK
        SR_M0 = mm.val
        SR_V0 = vv.val + 1

        def p2e_mm(e):
            e.wait_ge(hv.h, P2HV0 + HC * NSLAB)
            for tt in range(TT):
                if tt > 0:
                    e.wait_ge(vv.h, SR_V0 + 3 * (tt - 1) + 1)
                e.transpose(PS[:, TRPC:TRPC + P], mrun[:, tt * P:(tt + 1) * P],
                            ident[:]).then_inc(mm.h, 1)
                e.wait_ge(vv.h, SR_V0 + 3 * tt + 1)
                e.transpose(PS[0:1, TRPC + P:TRPC + 2 * P], scol[:],
                            ident[:]).then_inc(mm.h, 1)
                e.wait_ge(vv.h, SR_V0 + 3 * tt + 2)
                e.matmul(PS[:, BCC:BCC + P], ones_row[:], rowb[:],
                         start=True, stop=True).then_inc(mm.h, 1)
        em("tensor", p2e_mm)
        mm.add(3 * TT)

        def p2e_vec(e):
            e.tensor_scalar_mul(tcol2[:], wcols["o"]["nsc"][:], 1.0 / 127.0
                                ).then_inc(vv.h, 1)
            for tt in range(TT):
                e.wait_ge(mm.h, SR_M0 + 3 * tt + 1)
                e.reduce_max(tcol[:], PS[:, TRPC:TRPC + P], axis=AX)
                e.tensor_scalar(tcol[:], tcol[:], EPS, None, OP.max)
                e.tensor_scalar(sc3_all[:, tt:tt + 1], tcol[:], tcol2[:],
                                None, OP.mult)
                e.reciprocal(tcol[:], tcol[:])
                e.tensor_scalar_mul(scol[:], tcol[:], 127.0).then_inc(vv.h, 1)
                e.wait_ge(mm.h, SR_M0 + 3 * tt + 2)
                e.tensor_copy(rowb[:], PS[0:1, TRPC + P:TRPC + 2 * P]
                              ).then_inc(vv.h, 1)
                e.wait_ge(mm.h, SR_M0 + 3 * tt + 3)
                e.tensor_copy(s_h_rep[:, tt * P:(tt + 1) * P],
                              PS[:, BCC:BCC + P]).then_inc(vv.h, 1)
        em("vector", p2e_vec)
        VV_SREP = vv.add(1 + 3 * TT)
        VV_SREP_HOLDER[0] = VV_SREP

        def ob_sync(e):
            e.wait_ge(vv.h, VV_SREP)
            e.dma_start(rep8k[0:1, :], ob_hbm.ap()).then_inc(ldx.h, 16)
        em("sync", ob_sync)
        LDX_OB = ldx.add(16)

        def ob_mm(e):
            e.wait_ge(ldx.h, LDX_OB)
            for ds in range(NDS):
                e.matmul(pbank(ds, DS), ones_row[:],
                         rep8k[0:1, ds * DS:(ds + 1) * DS],
                         start=True, stop=True).then_inc(mm.h, 1)
        em("tensor", ob_mm)
        MM_OBREP = mm.add(NDS)

        def ob_vec(e):
            e.wait_ge(mm.h, MM_OBREP)
            for ds in range(NDS):
                e.tensor_copy(rep8k[:, ds * DS:(ds + 1) * DS], pbank(ds, DS))
            e.engine_nop().then_inc(vv.h, 1)
        em("vector", ob_vec)
        VV_OBREP = vv.add(1)

        set_scope("p3")
        p3 = [dict() for _ in range(NSEG)]
        for seg in range(NSEG):
            c = p3[seg]
            c["wld0"] = ldw.val
            c["tr0"] = trq.val

            def p3w_sync(e, seg=seg, c=c):
                if seg == 0:
                    e.wait_ge(vv.h, VV_OBREP)
                    if dbg:
                        e.wait_ge(gg.h, 14 * 16)
                else:
                    e.wait_ge(vv.h, p3[seg - 1]["tern_end"])

                def load(i):
                    e.dma_start(
                        oraw[i % 2][:],
                        ow_hbm.ap()[i * P:(i + 1) * P,
                                    seg * HSEG:(seg + 1) * HSEG]
                    ).then_inc(ldw.h, 16)
                for i in range(min(2, DC)):
                    load(i)
                if seg > 0:
                    e.wait_ge(mm.h, p3[seg - 1]["mm_end"])
                for i in range(DC):
                    e.wait_ge(vv.h, c["tern0"] + i + 1)
                    for hcl in range(HCS):
                        e.dma_start_transpose(
                            owh[:, hcl * D + i * P:hcl * D + (i + 1) * P],
                            otb[:, hcl * P:(hcl + 1) * P]).then_inc(trq.h, 16)
                    if i + 2 < DC:
                        load(i + 2)
            em("sync", p3w_sync)
            ldw.add(16 * DC)
            c["trdone"] = trq.add(16 * HCS * DC)

            c["tern0"] = vv.val

            def p3w_vec(e, seg=seg, c=c):
                o = wcols["o"]
                for i in range(DC):
                    e.wait_ge(ldw.h, c["wld0"] + 16 * (i + 1))
                    if i >= 1:
                        e.wait_ge(trq.h, c["tr0"] + 16 * HCS * i)
                    e.tensor_scalar(osel[:], oraw[i % 2][:], o["thp"][:], None,
                                    OP.is_gt)
                    e.scalar_tensor_tensor(otb[:], oraw[i % 2][:], o["thn"][:],
                                           osel[:], OP.is_lt, OP.subtract
                                           ).then_inc(vv.h, 1)
            em("vector", p3w_vec)
            c["tern_end"] = vv.add(DC)

            c["hld0"] = ldh.val
            nper = 1 + (1 if seg > 0 else 0) + (1 if seg == NSEG - 1 else 0)
            c["per_tt"] = 16 * nper
            c["evs"] = 4 if seg == NSEG - 1 else 1

            def p3h_sync(e, seg=seg, c=c):
                e.wait_ge(sth.h, STH_DONE)
                if seg > 0:
                    e.wait_ge(stp.h, p3[seg - 1]["stp_end"])
                if seg == NSEG - 1:
                    e.wait_ge(sty.h, STY_DONE)
                for tt in range(TT):
                    if tt >= 2:
                        e.wait_ge(vv.h, c["qv0"] + tt - 1)
                    e.dma_start(
                        hstg[tt % 2][:].rearrange("p (a b) -> p a b", a=HCS),
                        h_sp.ap()[tt, seg * HCS:(seg + 1) * HCS, :, :]
                        .rearrange("a p b -> p a b")
                    ).then_inc(ldh.h, 16)
                    if seg > 0:
                        if tt >= 1:
                            e.wait_ge(hv.h, c["ev0"] + c["evs"] * (tt - 1) + 1)
                        e.dma_start(pbuf[:],
                                    part_sp.ap()[tt * P:(tt + 1) * P, :]
                                    ).then_inc(ldh.h, 16)
                    if seg == NSEG - 1:
                        if tt >= 1:
                            e.wait_ge(hv.h, c["ev0"] + c["evs"] * (tt - 1) + 2)
                        e.dma_start(ycld[:],
                                    yc_sp.ap()[tt * P:(tt + 1) * P, :]
                                    ).then_inc(ldh.h, 16)
            em("sync", p3h_sync)
            ldh.add(c["per_tt"] * TT)

            c["qv0"] = vv.val
            c["ev0"] = hv.val
            c["mm0"] = mm.val
            c["stp0"] = stp.val
            c["ss0"] = ss.val
            last = seg == NSEG - 1

            def p3_vec(e, seg=seg, c=c, last=last):
                def q(tt):
                    e.wait_ge(ldh.h, c["hld0"] + c["per_tt"] * tt + 16)
                    if tt >= 2:
                        e.wait_ge(mm.h, c["mm0"] + tt - 1)
                    hb = hstg[tt % 2]
                    qb = qhT[tt % 2]
                    for hcl in range(HCS):
                        e.tensor_tensor(hb[:, hcl * P:(hcl + 1) * P],
                                        hb[:, hcl * P:(hcl + 1) * P],
                                        s_h_rep[:, tt * P:(tt + 1) * P],
                                        OP.mult)
                    if SIM_MODE:
                        e.tensor_scalar(hb[:], hb[:], 256.5, None, OP.add)
                        e.tensor_scalar(qb[:], hb[:], 1.0, None, OP.mod)
                        e.scalar_tensor_tensor(qb[:], hb[:], -256.0, qb[:],
                                               OP.add, OP.subtract
                                               ).then_inc(vv.h, 1)
                    else:
                        e.tensor_copy(ib_h[tt % 2][:], hb[:])
                        e.tensor_copy(qb[:], ib_h[tt % 2][:]).then_inc(vv.h, 1)

                def evac(tt):
                    e.wait_ge(mm.h, c["mm0"] + tt + 1)
                    if seg > 0:
                        e.wait_ge(ldh.h, c["hld0"] + c["per_tt"] * tt + 32)
                    if last:
                        e.wait_ge(ldh.h, c["hld0"] + c["per_tt"] * (tt + 1))
                    o = obuf[tt % 2]
                    if not last:
                        if tt >= 2:
                            e.wait_ge(stp.h, c["stp0"] + 16 * (tt - 1))
                        for ds in range(NDS):
                            if seg == 0:
                                e.tensor_copy(o[:, ds * DS:(ds + 1) * DS],
                                              pbank(ds, DS))
                            else:
                                e.tensor_tensor(o[:, ds * DS:(ds + 1) * DS],
                                                pbank(ds, DS),
                                                pbuf[:, ds * DS:(ds + 1) * DS],
                                                OP.add)
                        e.engine_nop().then_inc(hv.h, 1)
                        return
                    if tt >= 2:
                        e.wait_ge(sto.h, 16 * (tt - 1))
                    for ds in range(NDS):
                        e.tensor_tensor(o[:, ds * DS:(ds + 1) * DS],
                                        pbank(ds, DS),
                                        pbuf[:, ds * DS:(ds + 1) * DS],
                                        OP.add)
                    e.engine_nop().then_inc(hv.h, 1)
                    e.tensor_scalar(o[:], o[:], sc3_all[:, tt:tt + 1],
                                    None, OP.mult)
                    e.tensor_tensor(o[:], o[:], rep8k[:], OP.add)
                    e.scalar_tensor_tensor(o[:], ycld[:],
                                           rstd_all[:, tt:tt + 1], o[:],
                                           OP.mult, OP.add)
                    e.reduce_sum(mcol[:], o[:], axis=AX)
                    e.tensor_scalar_mul(mcol[:], mcol[:], 1.0 / D
                                        ).then_inc(hv.h, 1)
                    e.wait_ge(ss.h, c["ss0"] + 3 * tt + 1)
                    e.tensor_scalar_mul(tcol[:], tcol[:], 1.0 / D)
                    e.tensor_tensor(tcol2[:], mcol[:], mcol[:], OP.mult)
                    e.tensor_tensor(tcol[:], tcol[:], tcol2[:],
                                    OP.subtract).then_inc(hv.h, 1)
                    e.wait_ge(ss.h, c["ss0"] + 3 * tt + 2)
                    e.reciprocal(tcol[:], tcol[:])
                    e.tensor_tensor(tcol2[:], mcol[:], tcol[:], OP.mult)
                    e.tensor_scalar_mul(tcol2[:], tcol2[:], -1.0
                                        ).then_inc(hv.h, 1)

                for tt in range(TT):
                    q(tt)
                    if tt >= 1:
                        evac(tt - 1)
                evac(TT - 1)
            em("vector", p3_vec)
            vv.add(TT)
            hv.add(c["evs"] * TT)

            def p3_mm(e, seg=seg, c=c):
                e.wait_ge(trq.h, c["trdone"])
                for tt in range(TT):
                    e.wait_ge(vv.h, c["qv0"] + tt + 1)
                    if tt >= 1:
                        e.wait_ge(hv.h, c["ev0"] + c["evs"] * (tt - 1) + 1)
                    for hcl in range(HCS):
                        for ds in range(NDS):
                            i_ = e.matmul(
                                pbank(ds, DS),
                                qhT[tt % 2][:, hcl * P:(hcl + 1) * P],
                                owh[:, hcl * D + ds * DS:hcl * D + (ds + 1) * DS],
                                start=(hcl == 0), stop=(hcl == HCS - 1))
                    i_.then_inc(mm.h, 1)
            em("tensor", p3_mm)
            c["mm_end"] = mm.add(TT)

            if not last:
                def p3e_gp(e, seg=seg, c=c):
                    for tt in range(TT):
                        e.wait_ge(hv.h, c["ev0"] + tt + 1)
                        e.dma_start(part_sp.ap()[tt * P:(tt + 1) * P, :],
                                    obuf[tt % 2][:]).then_inc(stp.h, 16)
                em("gpsimd", p3e_gp)
                c["stp_end"] = stp.add(16 * TT)
            else:
                def p3f_scal(e, c=c):
                    for tt in range(TT):
                        e.wait_ge(hv.h, c["ev0"] + 4 * tt + 2)
                        e.activation(sq_scr[:], obuf[tt % 2][:], AF.Square,
                                     accum_out=tcol[:]).then_inc(ss.h, 1)
                        e.wait_ge(hv.h, c["ev0"] + 4 * tt + 3)
                        e.activation(tcol[:], tcol[:], AF.Sqrt,
                                     bias=eps_col[:]).then_inc(ss.h, 1)
                        e.wait_ge(hv.h, c["ev0"] + 4 * tt + 4)
                        e.activation(obuf[tt % 2][:], obuf[tt % 2][:],
                                     AF.Identity, bias=tcol2[:],
                                     scale=tcol[:]).then_inc(ss.h, 1)
                em("scalar", p3f_scal)
                ss.add(3 * TT)

                def p3f_gp(e, c=c):
                    for tt in range(TT):
                        e.wait_ge(ss.h, c["ss0"] + 3 * tt + 3)
                        e.dma_start(out_hbm.ap()[tt * P:(tt + 1) * P, :],
                                    obuf[tt % 2][:]).then_inc(sto.h, 16)
                    e.wait_ge(sto.h, 16 * TT)
                em("gpsimd", p3f_gp)
                sto.add(16 * TT)

        @block.sync
        def _(e):
            for fn in prog["sync"]:
                fn(e)

        @block.scalar
        def _(e):
            g = ScalGuard(e, sems["sb"], sflush)
            for fn in prog["scalar"]:
                fn(g)

        @block.vector
        def _(e):
            g = VecGuard(e, sems["vb"])
            for fn in prog["vector"]:
                fn(g)

        @block.tensor
        def _(e):
            for fn in prog["tensor"]:
                fn(e)

        @block.gpsimd
        def _(e):
            for fn in prog["gpsimd"]:
                fn(e)

    return nc


def _prep_inputs(L0_attn_x, L0_attn_B, L0_attn_C, L0_attn_D,
                 L0_ffn_gate_w, L0_ffn_gate_b, L0_ffn_val_w, L0_ffn_val_b,
                 L0_ffn_out_w, L0_ffn_out_b, n_cores=8):
    x = np.ascontiguousarray(np.asarray(L0_attn_x, dtype=np.float32)
                             ).reshape(-1, np.asarray(L0_attn_x).shape[-1])
    T_total, D = x.shape
    T = T_total // n_cores
    H = np.asarray(L0_ffn_gate_w).shape[0]
    NS = np.asarray(L0_attn_B).shape[0]
    HC = H // P

    common = {
        "bw": np.ascontiguousarray(L0_attn_B, dtype=np.float32),
        "cw": np.ascontiguousarray(L0_attn_C, dtype=np.float32),
        "dv": np.ascontiguousarray(L0_attn_D, dtype=np.float32).reshape(1, D),
        "gw": np.ascontiguousarray(L0_ffn_gate_w, dtype=np.float32),
        "vw": np.ascontiguousarray(L0_ffn_val_w, dtype=np.float32),
        "ow": np.ascontiguousarray(L0_ffn_out_w, dtype=np.float32),
        "gb": np.ascontiguousarray(
            np.asarray(L0_ffn_gate_b, dtype=np.float32).reshape(HC, P).T),
        "vb": np.ascontiguousarray(
            np.asarray(L0_ffn_val_b, dtype=np.float32).reshape(HC, P).T),
        "ob": np.ascontiguousarray(L0_ffn_out_b, dtype=np.float32).reshape(1, D),
        "idm": np.eye(P, dtype=np.float32),
    }
    in_maps = []
    for c in range(n_cores):
        m = dict(common)
        m["x"] = np.ascontiguousarray(x[c * T:(c + 1) * T])
        in_maps.append(m)
    return in_maps, (T, D, H, NS)


def kernel(L0_attn_x, L0_attn_B, L0_attn_C, L0_attn_D,
           L0_ffn_gate_w, L0_ffn_gate_b, L0_ffn_val_w, L0_ffn_val_b,
           L0_ffn_out_w, L0_ffn_out_b):
    n_cores = 8
    in_maps, (T, D, H, NS) = _prep_inputs(
        L0_attn_x, L0_attn_B, L0_attn_C, L0_attn_D,
        L0_ffn_gate_w, L0_ffn_gate_b, L0_ffn_val_w, L0_ffn_val_b,
        L0_ffn_out_w, L0_ffn_out_b, n_cores)
    nc = build_nc(T, D, H, NS)
    res = run_bass_kernel_spmd(nc, in_maps, core_ids=list(range(n_cores)))
    out = np.concatenate([r["out"] for r in res.results], axis=0)
    return out.reshape(np.asarray(L0_attn_x).shape).astype(np.float32)


# version 15
# speedup vs baseline: 1.2066x; 1.0831x over previous
import os
import numpy as np

import concourse.bass as bass
import concourse.mybir as mybir
from concourse.bass_utils import run_bass_kernel_spmd

F32 = mybir.dt.float32
BF16 = mybir.dt.bfloat16
I32 = mybir.dt.int32
AF = mybir.ActivationFunctionType
OP = mybir.AluOpType
AX = mybir.AxisListType.X

SIM_MODE = (os.environ.get("BASS_SIM_FP64") == "1"
            or os.environ.get("KERNEL_SIMROUND") == "1")

EPS = 1e-5
THR_M = 1e-5 * (EPS ** 0.5)
P = 128


class Sem:
    def __init__(self, handle):
        self.h = handle
        self.val = 0

    def add(self, n):
        self.val += n
        return self.val


def _ap_range(ap):
    t = ap.tensor
    isz = 2 if t.dtype == BF16 else 4
    span = 1
    dims = list(ap.ap)
    for step, count in dims[1:]:
        span += int(step) * (int(count) - 1)
    base = 0
    space = type(t).__name__
    r = getattr(t, "manual_sbuf_range", None)
    if r is not None:
        base = int(r[0])
    lo = base + int(ap.offset) * isz
    return (space, lo, lo + span * isz)


def _overlap(a, b):
    return a[0] == b[0] and a[1] < b[2] and b[1] < a[2]


class VecGuard:

    _OPS = {
        "tensor_copy": (1, [0]), "tensor_tensor": (2, [0]),
        "tensor_add": (2, [0]), "tensor_sub": (2, [0]),
        "tensor_mul": (2, [0]), "tensor_max": (2, [0]),
        "tensor_scalar": (3, [0]), "tensor_scalar_mul": (2, [0]),
        "tensor_scalar_add": (2, [0]), "tensor_scalar_max": (2, [0]),
        "scalar_tensor_tensor": (4, [0]), "reduce_sum": (2, [0]),
        "reduce_max": (2, [0]), "reciprocal": (2, [0]),
    }

    def __init__(self, eng, sem):
        self._e = eng
        self._sem = sem
        self._writes = []

    def _flush(self):
        if self._writes:
            self._writes = []
            self._e.engine_nop().then_inc(self._sem.h, 1)
            self._e.wait_ge(self._sem.h, self._sem.add(1))

    def wait_ge(self, *a, **k):
        return self._e.wait_ge(*a, **k)

    def memset(self, ap, v):
        i = self._e.memset(ap, v)
        self._writes.append(_ap_range(ap))
        return i

    def engine_nop(self):
        return self._e.engine_nop()

    def __getattr__(self, name):
        fn = getattr(self._e, name)
        spec = self._OPS.get(name)
        if spec is None:
            return fn
        npos, outpos = spec

        def wrapped(*args, **kwargs):
            ins = []
            outs = []
            for i, a in enumerate(args[:npos + 2]):
                if hasattr(a, "tensor"):
                    (outs if i in outpos else ins).append(_ap_range(a))
            for k_, a in kwargs.items():
                if hasattr(a, "tensor"):
                    (outs if k_ == "accum_out" else ins).append(_ap_range(a))
            if any(_overlap(r, w) for r in ins for w in self._writes):
                self._flush()
            res = fn(*args, **kwargs)
            self._writes.extend(outs)
            return res
        return wrapped


class ScalGuard(VecGuard):

    _OPS = {
        "activation": (2, [0]), "copy": (2, [0]),
        "add": (2, [0]), "mul": (2, [0]),
    }

    def __init__(self, eng, sem, scratch):
        super().__init__(eng, sem)
        self._scr = scratch

    def _flush(self):
        if self._writes:
            self._writes = []
            self._e.copy(self._scr[:], self._scr[:]).then_inc(self._sem.h, 1)
            self._e.wait_ge(self._sem.h, self._sem.add(1))


def build_nc(T, D, H, NS, dbg=False):
    TT = T // P
    DC = D // P
    HC = H // P
    NCH = NS // P
    SLAB = min(512, T)
    NSLAB = T // SLAB
    TPS = SLAB // P
    DS = min(512, D)
    NDS = D // DS
    NSEG = 4
    HCS = HC // NSEG
    HSEG = H // NSEG
    GH = 4
    NG = HC // GH
    assert TT <= 64 and DC <= 64

    nc = bass.Bass("TRN2", target_bir_lowering=False,
                   detect_race_conditions=False)

    x_hbm = nc.dram_tensor("x", [T, D], F32, kind="ExternalInput")
    b_hbm = nc.dram_tensor("bw", [NS, D], F32, kind="ExternalInput")
    c_hbm = nc.dram_tensor("cw", [D, NS], F32, kind="ExternalInput")
    d_hbm = nc.dram_tensor("dv", [1, D], F32, kind="ExternalInput")
    gw_hbm = nc.dram_tensor("gw", [H, D], F32, kind="ExternalInput")
    vw_hbm = nc.dram_tensor("vw", [H, D], F32, kind="ExternalInput")
    ow_hbm = nc.dram_tensor("ow", [D, H], F32, kind="ExternalInput")
    gb_hbm = nc.dram_tensor("gb", [P, HC], F32, kind="ExternalInput")
    vb_hbm = nc.dram_tensor("vb", [P, HC], F32, kind="ExternalInput")
    ob_hbm = nc.dram_tensor("ob", [1, D], F32, kind="ExternalInput")
    id_hbm = nc.dram_tensor("idm", [P, P], F32, kind="ExternalInput")
    out_hbm = nc.dram_tensor("out", [T, D], F32, kind="ExternalOutput")

    _spk = "ExternalOutput" if dbg else "Internal"
    yc_sp = nc.dram_tensor("yc_sp", [T, D], BF16, kind=_spk)
    h_sp = nc.dram_tensor("h_sp", [TT, HC, P, P], F32, kind=_spk)
    part_sp = nc.dram_tensor("part_sp", [T, D], F32, kind=_spk)
    if dbg:
        state_d = nc.dram_tensor("state_d", [P, (NS // P) * T], BF16,
                                 kind="ExternalOutput")
        qyT_d = nc.dram_tensor("qyT_d", [P, (D // P) * T], BF16,
                               kind="ExternalOutput")
        rho_d = nc.dram_tensor("rho_d", [P, T], F32, kind="ExternalOutput")
        srep_d = nc.dram_tensor("srep_d", [P, T], F32, kind="ExternalOutput")
        gcol_d = nc.dram_tensor("gcol_d", [P, 16], F32, kind="ExternalOutput")
        sc3_d = nc.dram_tensor("sc3_d", [P, T // P], F32, kind="ExternalOutput")

    base0 = (int(nc.sbuf_base) + 63) & ~63
    top = int(nc.sbuf_top)
    _cur = [base0]

    def place(name, shape, dtype, at=None):
        nbytes = int(np.prod(shape[1:])) * (2 if dtype == BF16 else 4)
        if at is None:
            off = _cur[0]
            _cur[0] = (off + nbytes + 63) & ~63
        else:
            off = at
        assert off + nbytes <= top, f"SBUF overflow placing {name} at {off}"
        return nc.alloc_sbuf_tensor_at(name, list(shape), dtype, offset=off), off

    def bump():
        return _cur[0]

    ident, _ = place("ident", [P, P], F32)
    ident_bf, _ = place("ident_bf", [P, P], BF16)
    gb_sb, _ = place("gb_sb", [P, HC], F32)
    vb_sb, _ = place("vb_sb", [P, HC], F32)
    rep8k, REP8K_OFF = place("rep8k", [P, D], F32)
    gacc, GACC_OFF = place("gacc", [P, 64], F32)
    gred, _ = place("gred", [P, 1], F32)
    gfull, _ = place("gfull", [P, 1], F32)
    wcols = {}
    for wname in ("g", "v", "o"):
        wcols[wname] = {}
        for k in ("thp", "thn", "nsc"):
            wcols[wname][k], _ = place(f"{k}_{wname}", [P, 1], F32)
    m_all, _ = place("m_all", [P, TT], F32)
    M_all, _ = place("M_all", [P, TT], F32)
    ssq_all, _ = place("ssq_all", [P, TT], F32)
    rstd_all, _ = place("rstd_all", [P, TT], F32)
    rho_all, _ = place("rho_all", [P, TT], F32)
    sc3_all, _ = place("sc3_all", [P, TT], F32)
    scol, _ = place("scol", [P, 1], F32)
    mcol, _ = place("mcol", [P, 1], F32)
    tcol, _ = place("tcol", [P, 1], F32)
    tcol2, _ = place("tcol2", [P, 1], F32)
    eps_col, _ = place("eps_col", [P, 1], F32)
    ones_row, _ = place("ones_row", [1, P], F32)
    ones_col, _ = place("ones_col", [P, 1], F32)
    rowb, _ = place("rowb", [1, P], F32)
    sclr, _ = place("sclr", [1, 1], F32)
    sflush, _ = place("sflush", [1, 1], F32)

    PERS_END = bump()

    GW = min(max(D, H), 4096)
    A_SZ = max(DC * T * 2, HCS * D * 2, 2 * GW * 4 + GW * 2)
    SLOT_A = bump()
    _cur[0] = SLOT_A + ((A_SZ + 63) & ~63)
    B_SZ = max(NCH * T * 2, 4 * D * 4, 2 * HCS * P * 4 + 2 * HCS * P * 2 + D * 4)
    SLOT_B = bump()
    _cur[0] = SLOT_B + ((B_SZ + 63) & ~63)
    C_SZ = max(DC * NS * 2, NCH * D * 2,
               8 * D * 2,
               2 * HSEG * 4 + 2 * HSEG * 2)
    SLOT_C = bump()
    _cur[0] = SLOT_C + ((C_SZ + 63) & ~63)
    D_SZ = max(DC * SLAB * 2,
               3 * D * 4,
               6 * SLAB * 4 + 2 * T * 4,
               D * 2 + 2 * D * 4)
    SLOT_D = bump()
    _cur[0] = SLOT_D + ((D_SZ + 63) & ~63)
    E_SZ = max(2 * D * 4, D * 4)
    SLOT_E = bump()
    _cur[0] = SLOT_E + ((E_SZ + 63) & ~63)
    F_SZ = max(2 * D * 2, 2 * NS * 4, T * 4)
    SLOT_F = bump()
    _cur[0] = SLOT_F + ((F_SZ + 63) & ~63)
    G_SZ = 2 * NS * 2
    SLOT_G = bump()
    _cur[0] = SLOT_G + ((G_SZ + 63) & ~63)
    H_SZ = D * 2
    SLOT_H = bump()
    _cur[0] = SLOT_H + ((H_SZ + 63) & ~63)
    I_SZ = D * 2
    SLOT_I = bump()
    _cur[0] = SLOT_I + ((I_SZ + 63) & ~63)
    assert bump() <= top, f"arena overflow: {bump()} > {top}"

    def at(name, shape, dtype, off):
        t_, _ = place(name, shape, dtype, at=off)
        return t_

    s_h_rep = at("s_h_rep", [P, T], F32, SLOT_F)
    sq_scr = at("sq_scr", [P, D], BF16, SLOT_D + 3 * D * 4)

    gld = [at(f"gld{i}", [P, GW], F32, SLOT_A + i * GW * 4) for i in range(2)]
    gscr = at("gscr", [P, GW], BF16, SLOT_A + 2 * GW * 4)
    qyT_sb = at("qyT_sb", [P, DC * T], BF16, SLOT_A)
    state_sb = at("state_sb", [P, NCH * T], BF16, SLOT_B)
    bt_sb = at("bt_sb", [P, DC * NS], BF16, SLOT_C)
    ct_sb = at("ct_sb", [P, NCH * D], BF16, SLOT_C)
    xT_sb = at("xT_sb", [P, DC * SLAB], BF16, SLOT_D)
    wbuf = [at(f"wbuf{i}", [P, D], F32, SLOT_D + i * D * 4) for i in range(2)]
    t256 = at("t256", [P, D], F32, SLOT_D + 2 * D * 4)
    xld = [at(f"xld{i}", [P, D], F32, SLOT_E + i * D * 4) for i in range(2)]
    xbf = [at(f"xbf{i}", [P, D], BF16, SLOT_F + i * D * 2) for i in range(2)]
    cld = [at(f"cld{i}", [P, NS], F32, SLOT_F + i * NS * 4) for i in range(2)]
    cbf = [at(f"cbf{i}", [P, NS], BF16, SLOT_G + i * NS * 2) for i in range(2)]
    ycbf = at("ycbf", [P, D], BF16, SLOT_H)
    qybf = at("qybf", [P, D], BF16, SLOT_I)
    ib_q = at("ib_q", [P, D], I32, SLOT_D + 2 * D * 4)
    graw = [at(f"graw{i}", [P, D], F32, SLOT_B + i * D * 4) for i in range(2)]
    vraw = [at(f"vraw{i}", [P, D], F32, SLOT_B + (2 + i) * D * 4)
            for i in range(2)]
    gtb = [at(f"gtb{i}", [P, D], BF16, SLOT_C + i * D * 2)
           for i in range(2)]
    vtb = [at(f"vtb{i}", [P, D], BF16, SLOT_C + (2 + i) * D * 2)
           for i in range(2)]
    gwT = [at(f"gwT{i}", [P, D], BF16, SLOT_C + (4 + i) * D * 2)
           for i in range(2)]
    vwT = [at(f"vwT{i}", [P, D], BF16, SLOT_C + (6 + i) * D * 2)
           for i in range(2)]
    dsel = at("dsel", [P, D], BF16, REP8K_OFF)
    gsl = [at(f"gsl{i}", [P, SLAB], F32, SLOT_D + i * SLAB * 4)
           for i in range(2)]
    vsl = [at(f"vsl{i}", [P, SLAB], F32, SLOT_D + (2 + i) * SLAB * 4)
           for i in range(2)]
    hsl = [at(f"hsl{i}", [P, SLAB], F32,
              (SLOT_D + (4 + i) * SLAB * 4) if i < 2 else
              (SLOT_E + (i - 2) * SLAB * 4))
           for i in range(4)]
    mrun = at("mrun", [P, T], F32, SLOT_D + 6 * SLAB * 4)
    rho_rep = at("rho_rep", [P, T], F32, SLOT_D + 6 * SLAB * 4 + T * 4)
    owh = at("owh", [P, HCS * D], BF16, SLOT_A)
    oraw = [at(f"oraw{i}", [P, HSEG], F32, SLOT_C + i * HSEG * 4)
            for i in range(2)]
    otb = at("otb", [P, HSEG], BF16, SLOT_C + 2 * HSEG * 4)
    osel = at("osel", [P, HSEG], BF16, SLOT_C + 2 * HSEG * 4 + HSEG * 2)
    hstg = [at(f"hstg{i}", [P, HCS * P], F32, SLOT_B + i * HCS * P * 4)
            for i in range(2)]
    ib_h = [at(f"ib_h{i}", [P, HCS * P], I32, SLOT_B + i * HCS * P * 4)
            for i in range(2)]
    qhT = [at(f"qhT{i}", [P, HCS * P], BF16,
              SLOT_B + 2 * HCS * P * 4 + i * HCS * P * 2) for i in range(2)]
    pbuf = at("pbuf", [P, D], F32, SLOT_E)
    ycld = at("ycld", [P, D], BF16, SLOT_D)
    obuf = [at(f"obuf{i}", [P, D], F32, SLOT_D + D * 2 + i * D * 4)
            for i in range(2)]

    PS = nc.alloc_psum_tensor("PS", [P, 3072], F32)
    PSW = nc.alloc_psum_tensor("PSW", [P, 2048], BF16)
    BANK = 512

    def pbank(b, width=BANK):
        return PS[:, b * BANK:b * BANK + width]

    import contextlib
    st = contextlib.ExitStack()
    with st:
        S = st.enter_context
        sems = {}
        for name in ("ldc", "ldx", "ldw", "ldh", "trq", "mm", "ev", "vv",
                     "hv", "ss", "gg", "sty", "sth", "stp", "sto",
                     "vb", "sb", "ts2", "th", "sq", "ga"):
            sems[name] = Sem(S(nc.semaphore(name)))
        ldc, ldx, ldw, ldh = (sems[k] for k in ("ldc", "ldx", "ldw", "ldh"))
        trq, mm, ev, vv = (sems[k] for k in ("trq", "mm", "ev", "vv"))
        ss, gg, hv = sems["ss"], sems["gg"], sems["hv"]
        sty, sth, stp, sto = (sems[k] for k in ("sty", "sth", "stp", "sto"))

        block = S(nc.Block())
        prog = {k: [] for k in ("sync", "scalar", "vector", "tensor", "gpsimd")}
        _scope = ["s0"]

        def em(engine, fn, scope=None):
            sc = scope or _scope[0]
            def tagged(e, fn=fn, sc=sc):
                with nc.named_scope(sc):
                    fn(e)
            prog[engine].append(tagged)

        def set_scope(name):
            _scope[0] = name

        def s0_sync(e):
            e.dma_start(ident[:], id_hbm.ap()).then_inc(ldc.h, 16)
            e.dma_start(gb_sb[:], gb_hbm.ap()).then_inc(ldc.h, 16)
            e.dma_start(vb_sb[:], vb_hbm.ap()).then_inc(ldc.h, 16)
            e.dma_start(rep8k[0:1, :], d_hbm.ap()).then_inc(ldc.h, 16)
        em("sync", s0_sync)
        ldc.add(64)

        def s0_vec(e):
            e.memset(sflush[:], 0.0)
            e.memset(eps_col[:], EPS)
            e.memset(ones_row[:], 1.0)
            e.memset(ones_col[:], 1.0).then_inc(vv.h, 1)
        em("vector", s0_vec)
        VV_ONES = vv.add(1)

        def s0_mm(e):
            e.wait_ge(vv.h, VV_ONES)
            e.wait_ge(ldc.h, 64)
            for ds in range(NDS):
                e.matmul(pbank(ds, DS), ones_row[:],
                         rep8k[0:1, ds * DS:(ds + 1) * DS],
                         start=True, stop=True).then_inc(mm.h, 1)
        em("tensor", s0_mm)
        MM_DREP = mm.add(NDS)

        def s0_vec2(e):
            e.wait_ge(mm.h, MM_DREP)
            for ds in range(NDS):
                e.tensor_scalar(rep8k[:, ds * DS:(ds + 1) * DS],
                                pbank(ds, DS), 1.0, None, OP.add)
            e.tensor_copy(ident_bf[:], ident[:])
            e.engine_nop().then_inc(vv.h, 1)
        em("vector", s0_vec2)
        VV_DREP = vv.add(1)

        set_scope("gamma")
        GBIAS = 1.0 if SIM_MODE else (1.0 + 2.3e-7)
        for wname, whbm, rows, cols in (("g", gw_hbm, H, D),
                                        ("v", vw_hbm, H, D),
                                        ("o", ow_hbm, D, H)):
            cpr = max(1, cols // GW)
            w_ = min(cols, GW)
            ntile = (rows // P) * cpr
            inv_n = GBIAS / float(rows * cols)
            assert ntile <= 64

            ld0 = ldw.val
            ab0 = ss.val
            prev_gred = vv.val

            def g_sync(e, whbm=whbm, rows=rows, cpr=cpr, w_=w_, ab0=ab0):
                for i in range(rows // P):
                    for j in range(cpr):
                        k = i * cpr + j
                        if k < 2:
                            e.wait_ge(ss.h, ab0)
                        else:
                            e.wait_ge(ss.h, ab0 + k - 1)
                        e.dma_start(
                            gld[k % 2][:, 0:w_],
                            whbm.ap()[i * P:(i + 1) * P, j * w_:(j + 1) * w_]
                        ).then_inc(ldw.h, 16)
            em("sync", g_sync)
            ldw.add(16 * ntile)

            def g_act(e, ntile=ntile, w_=w_, ld0=ld0, first=(wname == "g"),
                      prev_gred=prev_gred):
                if first:
                    e.wait_ge(ldc.h, 64)
                    e.wait_ge(vv.h, VV_ONES)
                e.wait_ge(vv.h, prev_gred)
                for k in range(ntile):
                    e.wait_ge(ldw.h, ld0 + 16 * (k + 1))
                    e.activation(gscr[:, 0:w_], gld[k % 2][:, 0:w_], AF.Abs,
                                 accum_out=gacc[:, k:k + 1]).then_inc(ss.h, 1)
            em("scalar", g_act)
            SS_ABS = ss.add(ntile)

            def g_vec(e, ntile=ntile, tgt=SS_ABS):
                e.wait_ge(ss.h, tgt)
                e.reduce_sum(gred[:], gacc[:, 0:ntile], axis=AX).then_inc(vv.h, 1)
            em("vector", g_vec)
            VV_GRED = vv.add(1)

            def g_mm1(e, tgt=VV_GRED):
                e.wait_ge(vv.h, tgt)
                e.matmul(PS[0:1, 2560:2561], gred[:], ones_col[:],
                         start=True, stop=True).then_inc(mm.h, 1)
            em("tensor", g_mm1)
            MM_GSUM = mm.add(1)

            def g_vsc(e, tgt=MM_GSUM):
                e.wait_ge(mm.h, tgt)
                e.tensor_copy(sclr[:], PS[0:1, 2560:2561]).then_inc(vv.h, 1)
            em("vector", g_vsc)
            VV_GSC = vv.add(1)

            def g_mm2(e, tgt=VV_GSC):
                e.wait_ge(vv.h, tgt)
                e.matmul(PS[:, 2561:2562], ones_row[:], sclr[:],
                         start=True, stop=True).then_inc(mm.h, 1)
            em("tensor", g_mm2)
            MM_GBC = mm.add(1)

            def g_vec2(e, wname=wname, inv_n=inv_n, tgt=MM_GBC):
                c = wcols[wname]
                e.wait_ge(mm.h, tgt)
                e.tensor_scalar(gfull[:], PS[:, 2561:2562], inv_n, EPS,
                                OP.mult, OP.max)
                e.tensor_scalar_mul(c["thp"][:], gfull[:], 0.5)
                e.tensor_scalar_mul(c["thn"][:], gfull[:], -0.5)
                e.tensor_scalar_mul(c["nsc"][:], gfull[:], -1.0
                                    ).then_inc(vv.h, 1)
            em("vector", g_vec2)
            vv.add(1)
        VV_GAMMA = vv.val

        set_scope("p1a")
        bld0 = ldx.val
        bcast0 = vv.val
        btr0 = trq.val

        def b_sync(e):
            for i in range(min(2, NCH)):
                e.dma_start(xld[i][:], b_hbm.ap()[i * P:(i + 1) * P, :]
                            ).then_inc(ldx.h, 16)
            for i in range(NCH):
                e.wait_ge(vv.h, bcast0 + i + 1)
                for dc in range(DC):
                    e.dma_start_transpose(
                        bt_sb[:, dc * NS + i * P:dc * NS + (i + 1) * P],
                        xbf[i % 2][:, dc * P:(dc + 1) * P]).then_inc(trq.h, 16)
                if i + 2 < NCH:
                    e.dma_start(xld[i % 2][:],
                                b_hbm.ap()[(i + 2) * P:(i + 3) * P, :]
                                ).then_inc(ldx.h, 16)
        em("sync", b_sync)
        ldx.add(16 * NCH)
        TRQ_BT = trq.add(16 * DC * NCH)

        def b_vec(e):
            e.wait_ge(vv.h, VV_GAMMA)
            for i in range(NCH):
                e.wait_ge(ldx.h, bld0 + 16 * (i + 1))
                if i >= 2:
                    e.wait_ge(trq.h, btr0 + 16 * DC * (i - 1))
                e.tensor_copy(xbf[i % 2][:], xld[i % 2][:]).then_inc(vv.h, 1)
        em("vector", b_vec)
        vv.add(NCH)
        VV_BCAST = vv.val

        xld0 = ldx.val
        xcast0 = vv.val
        xtr0 = trq.val
        z1mm0 = mm.val
        SS_TANH0 = None

        def x_sync(e):
            def load(k):
                if k < 2:
                    e.wait_ge(vv.h, VV_BCAST)
                e.dma_start(xld[k % 2][:],
                            x_hbm.ap()[k * P:(k + 1) * P, :]
                            ).then_inc(ldx.h, 16)
            for k in range(min(2, TT)):
                load(k)
            k = 0
            for s_ in range(NSLAB):
                if s_ >= 1:
                    e.wait_ge(mm.h, z1mm0 + s_ * NCH)
                for j in range(TPS):
                    e.wait_ge(vv.h, xcast0 + k + 1)
                    for dc in range(DC):
                        e.dma_start_transpose(
                            xT_sb[:, dc * SLAB + j * P:dc * SLAB + (j + 1) * P],
                            xbf[k % 2][:, dc * P:(dc + 1) * P]).then_inc(trq.h, 16)
                    if k + 2 < TT:
                        load(k + 2)
                    k += 1
        em("sync", x_sync)
        ldx.add(16 * TT)
        TRQ_XT = trq.add(16 * DC * TT)

        def x_vec(e):
            e.wait_ge(trq.h, TRQ_BT)
            k = 0
            for s_ in range(NSLAB):
                for j in range(TPS):
                    e.wait_ge(ldx.h, xld0 + 16 * (k + 1))
                    if k >= 2:
                        e.wait_ge(trq.h, xtr0 + 16 * DC * (k - 1))
                    e.tensor_copy(xbf[k % 2][:], xld[k % 2][:]).then_inc(vv.h, 1)
                    k += 1
        em("vector", x_vec)
        vv.add(TT)

        def z1_mm(e):
            e.wait_ge(trq.h, TRQ_BT)
            k = 0
            for s_ in range(NSLAB):
                e.wait_ge(trq.h, xtr0 + 16 * DC * TPS * (s_ + 1))
                for nch in range(NCH):
                    ps = pbank(4 + (k % 2), SLAB)
                    if k >= 2:
                        e.wait_ge(ss.h, SS_TANH0 + k - 1)
                    for dc in range(DC):
                        i_ = e.matmul(
                            ps,
                            bt_sb[:, dc * NS + nch * P:dc * NS + (nch + 1) * P],
                            xT_sb[:, dc * SLAB:(dc + 1) * SLAB],
                            start=(dc == 0), stop=(dc == DC - 1))
                    i_.then_inc(mm.h, 1)
                    k += 1
        em("tensor", z1_mm)
        mm.add(NSLAB * NCH)

        SS_TANH0 = ss.val

        def z1_tanh(e):
            k = 0
            for s_ in range(NSLAB):
                for nch in range(NCH):
                    e.wait_ge(mm.h, z1mm0 + k + 1)
                    e.activation(
                        state_sb[:, nch * T + s_ * SLAB:nch * T + (s_ + 1) * SLAB],
                        pbank(4 + (k % 2), SLAB), AF.Tanh).then_inc(ss.h, 1)
                    k += 1
        em("scalar", z1_tanh)
        ss.add(NSLAB * NCH)
        SS_P1A = ss.val

        set_scope("p1b")
        cld0 = ldx.val
        ccast0 = vv.val
        ctr0 = trq.val

        def c_sync(e):
            e.wait_ge(trq.h, TRQ_XT)
            for i in range(min(2, DC)):
                e.dma_start(cld[i][:], c_hbm.ap()[i * P:(i + 1) * P, :]
                            ).then_inc(ldx.h, 16)
            e.wait_ge(mm.h, z1mm0 + NSLAB * NCH)
            for i in range(DC):
                e.wait_ge(vv.h, ccast0 + i + 1)
                for nch in range(NCH):
                    e.dma_start_transpose(
                        ct_sb[:, nch * D + i * P:nch * D + (i + 1) * P],
                        cbf[i % 2][:, nch * P:(nch + 1) * P]).then_inc(trq.h, 16)
                if i + 2 < DC:
                    e.dma_start(cld[i % 2][:],
                                c_hbm.ap()[(i + 2) * P:(i + 3) * P, :]
                                ).then_inc(ldx.h, 16)
        em("sync", c_sync)
        ldx.add(16 * DC)
        TRQ_CT = trq.add(16 * NCH * DC)

        def c_vec(e):
            for i in range(DC):
                e.wait_ge(ldx.h, cld0 + 16 * (i + 1))
                if i >= 2:
                    e.wait_ge(trq.h, ctr0 + 16 * NCH * (i - 1))
                e.tensor_copy(cbf[i % 2][:], cld[i % 2][:]).then_inc(vv.h, 1)
        em("vector", c_vec)
        vv.add(DC)

        x2ld0 = ldx.val
        P1B_V0 = None
        P1B_EV0 = None

        def x2_sync(e):
            e.wait_ge(vv.h, xcast0 + TT)
            for tt in range(min(2, TT)):
                e.dma_start(xld[tt][:], x_hbm.ap()[tt * P:(tt + 1) * P, :]
                            ).then_inc(ldx.h, 16)
        em("sync", x2_sync)
        ldx.add(16 * min(2, TT))

        mm2_0 = mm.val

        def mm2(e):
            e.wait_ge(trq.h, TRQ_CT)
            e.wait_ge(ss.h, SS_P1A)
            for tt in range(TT):
                if tt >= 1:
                    e.wait_ge(ev.h, P1B_EV0 + tt)
                for ds in range(NDS):
                    for nch in range(NCH):
                        i_ = e.matmul(
                            pbank(ds, DS),
                            state_sb[:, nch * T + tt * P:nch * T + (tt + 1) * P],
                            ct_sb[:, nch * D + ds * DS:nch * D + (ds + 1) * DS],
                            start=(nch == 0), stop=(nch == NCH - 1))
                i_.then_inc(mm.h, 1)
        em("tensor", mm2)
        mm.add(TT)

        P1B_V0 = vv.val
        P1B_EV0 = ev.val
        SS_SQ0 = None
        QYTR0 = None

        def p1b_vec(e):
            for tt in range(TT):
                e.wait_ge(ldx.h, x2ld0 + 16 * (tt + 1))
                e.wait_ge(mm.h, mm2_0 + tt + 1)
                w = wbuf[tt % 2]
                if tt >= 2:
                    e.wait_ge(ss.h, SS_SQ0 + tt - 1)
                if tt >= 1:
                    e.wait_ge(sty.h, 16 * tt)
                e.tensor_tensor(w[:], xld[tt % 2][:], rep8k[:], OP.mult)
                e.tensor_tensor(w[:], w[:], PS[:, 0:D], OP.add).then_inc(ev.h, 1)
                e.reduce_sum(mcol[:], w[:], axis=AX)
                e.tensor_scalar_mul(mcol[:], mcol[:], 1.0 / D)
                e.tensor_copy(m_all[:, tt:tt + 1], mcol[:])
                e.tensor_scalar(w[:], w[:], mcol[:], None, OP.subtract)
                e.reduce_max(M_all[:, tt:tt + 1], w[:], axis=AX,
                             apply_absolute_value=True)
                e.tensor_copy(ycbf[:], w[:]).then_inc(vv.h, 1)
                e.tensor_scalar(tcol[:], M_all[:, tt:tt + 1], THR_M, None, OP.max)
                e.reciprocal(tcol[:], tcol[:])
                e.tensor_scalar_mul(scol[:], tcol[:], 127.0)
                if tt >= 1:
                    e.wait_ge(trq.h, QYTR0 + 16 * DC * tt)
                if SIM_MODE:
                    e.tensor_scalar(t256[:], w[:], scol[:], 256.5,
                                    OP.mult, OP.add)
                    e.tensor_scalar(w[:], t256[:], 1.0, None, OP.mod)
                    e.scalar_tensor_tensor(qybf[:], t256[:], -256.0, w[:],
                                           OP.add, OP.subtract).then_inc(vv.h, 1)
                else:
                    e.tensor_scalar(t256[:], w[:], scol[:], None, OP.mult)
                    e.tensor_copy(ib_q[:], t256[:])
                    e.tensor_copy(qybf[:], ib_q[:]).then_inc(vv.h, 1)
        em("vector", p1b_vec)
        vv.add(2 * TT)
        ev.add(TT)

        SS_SQ0 = ss.val

        def p1b_scal(e):
            for tt in range(TT):
                e.wait_ge(vv.h, P1B_V0 + 2 * tt + 1)
                e.activation(sq_scr[:], wbuf[tt % 2][:], AF.Square,
                             accum_out=ssq_all[:, tt:tt + 1]).then_inc(ss.h, 1)
        em("scalar", p1b_scal)
        ss.add(TT)

        def p1b_gp(e):
            for tt in range(TT):
                e.wait_ge(vv.h, P1B_V0 + 2 * tt + 1)
                e.dma_start(yc_sp.ap()[tt * P:(tt + 1) * P, :], ycbf[:]
                            ).then_inc(sty.h, 16)
        em("gpsimd", p1b_gp)
        STY_DONE = sty.add(16 * TT)

        if dbg:
            def dbg1_gp(e):
                e.wait_ge(ss.h, SS_P1A)
                e.dma_start(state_d.ap(), state_sb[:]).then_inc(gg.h, 16)
            em("gpsimd", dbg1_gp)
            gg.add(16)

        QYTR0 = trq.val

        def qy_tr(e):
            for tt in range(TT):
                e.wait_ge(vv.h, P1B_V0 + 2 * tt + 2)
                for dc in range(DC):
                    e.dma_start_transpose(
                        qyT_sb[:, dc * T + tt * P:dc * T + (tt + 1) * P],
                        qybf[:, dc * P:(dc + 1) * P]).then_inc(trq.h, 16)
                if tt + 2 < TT:
                    e.dma_start(xld[tt % 2][:],
                                x_hbm.ap()[(tt + 2) * P:(tt + 3) * P, :]
                                ).then_inc(ldx.h, 16)
        em("sync", qy_tr)
        TRQ_QY = trq.add(16 * DC * TT)
        ldx.add(16 * max(0, TT - 2))

        set_scope("p1e")
        TRPC = 4 * BANK

        def p1e_scal(e):
            e.wait_ge(vv.h, P1B_V0 + 2 * TT)
            e.activation(rstd_all[:], ssq_all[:], AF.Sqrt,
                         scale=1.0 / D, bias=eps_col[:]).then_inc(ss.h, 1)
        em("scalar", p1e_scal)
        SS_RSTD = ss.add(1)

        def p1e_vec(e):
            e.wait_ge(ss.h, SS_RSTD)
            e.reciprocal(rstd_all[:], rstd_all[:])
            e.tensor_scalar(rho_all[:], M_all[:], THR_M, None, OP.max)
            e.tensor_tensor(rho_all[:], rho_all[:], rstd_all[:], OP.mult)
            e.tensor_scalar_mul(rho_all[:], rho_all[:], 1.0 / 127.0
                                ).then_inc(vv.h, 1)
        em("vector", p1e_vec)
        VV_RHO = vv.add(1)

        BCC = 5 * BANK
        RB_M0 = mm.val
        RB_V0 = vv.val

        def p1e_mmT(e):
            e.wait_ge(vv.h, VV_RHO)
            for tt in range(TT):
                if tt > 0:
                    e.wait_ge(vv.h, RB_V0 + 2 * tt - 1)
                e.transpose(PS[0:1, TRPC:TRPC + P], rho_all[:, tt:tt + 1],
                            ident[:]).then_inc(mm.h, 1)
                e.wait_ge(vv.h, RB_V0 + 2 * tt + 1)
                e.matmul(PS[:, BCC:BCC + P], ones_row[:], rowb[:],
                         start=True, stop=True).then_inc(mm.h, 1)
        em("tensor", p1e_mmT)
        mm.add(2 * TT)

        def p1e_vec2(e):
            for tt in range(TT):
                e.wait_ge(mm.h, RB_M0 + 2 * tt + 1)
                e.tensor_copy(rowb[:], PS[0:1, TRPC:TRPC + P]).then_inc(vv.h, 1)
                e.wait_ge(mm.h, RB_M0 + 2 * tt + 2)
                e.tensor_copy(rho_rep[:, tt * P:(tt + 1) * P],
                              PS[:, BCC:BCC + P]).then_inc(vv.h, 1)
        em("vector", p1e_vec2)
        RHOREP_V = vv.add(2 * TT)

        def mrun_vec(e):
            e.memset(mrun[:], 0.0).then_inc(vv.h, 1)
        em("vector", mrun_vec)
        VV_MRUN = vv.add(1)

        set_scope("p2")
        th, sq = sems["th"], sems["sq"]
        S4 = [sth, sems["sty"], sems["stp"], sems["sto"]]
        B4 = [x.val for x in S4]
        ldg = [ldw, sems["ga"]]
        ldv = [sems["ts2"], ldx]
        P2LDG = [ldg[0].val, ldg[1].val]
        P2LDV = [ldv[0].val, ldv[1].val]
        P2TH0 = th.val
        P2SQ0 = sq.val
        P2TERN0 = vv.val + 0
        P2MM0 = mm.val
        P2EV0 = ev.val
        P2SS0 = ss.val
        P2HV0 = None
        RPS = (1, 1, 1, 1)

        def p2_sync(e):
            e.wait_ge(vv.h, RHOREP_V)
            if dbg:
                e.wait_ge(gg.h, 16)
            for hc in range(HC):
                if hc >= 2:
                    e.wait_ge(vv.h, P2TERN0 + 2 * (hc - 1))
                e.dma_start(graw[hc % 2][:], gw_hbm.ap()[hc * P:(hc + 1) * P, :]
                            ).then_inc(ldg[hc % 2].h, 16)
                e.dma_start(vraw[hc % 2][:], vw_hbm.ap()[hc * P:(hc + 1) * P, :]
                            ).then_inc(ldv[hc % 2].h, 16)
        em("sync", p2_sync)
        for _s in (*ldg, *ldv):
            _s.add(16 * (HC // 2))

        P2TERN0 = vv.val

        def _tround(e, hc, r):
            R = 4 * hc + r
            src = gtb[hc % 2] if r < 2 else vtb[hc % 2]
            j = r % 2
            e.wait_ge(vv.h, P2TERN0 + 2 * hc + (1 if r < 2 else 2))
            if R >= 2:
                e.wait_ge(sq.h, P2SQ0 + R - 1)
            b = (R % 2) * 8 * P
            for i in range(8):
                i_ = e.transpose(PSW[:, b + i * P:b + (i + 1) * P],
                                 src[:, (8 * j + i) * P:(8 * j + i + 1) * P],
                                 ident_bf[:])
            i_.then_inc(th.h, 1)

        def p2_mm(e):
            e.wait_ge(trq.h, TRQ_QY)
            for r in range(4):
                _tround(e, 0, r)
            for hc in range(HC):
                e.wait_ge(sq.h, P2SQ0 + 4 * (hc + 1))
                rnext = 0
                for s_ in range(NSLAB):
                    k = hc * NSLAB + s_
                    if k >= 3:
                        e.wait_ge(ev.h, P2EV0 + k - 2)
                    for dc in range(DC):
                        e.matmul(pbank(k % 3, SLAB),
                                 gwT[hc % 2][:, dc * P:(dc + 1) * P],
                                 qyT_sb[:, dc * T + s_ * SLAB:dc * T + (s_ + 1) * SLAB],
                                 start=(dc == 0), stop=(dc == DC - 1))
                    for dc in range(DC):
                        i_ = e.matmul(pbank(3 + k % 3, SLAB),
                                      vwT[hc % 2][:, dc * P:(dc + 1) * P],
                                      qyT_sb[:, dc * T + s_ * SLAB:dc * T + (s_ + 1) * SLAB],
                                      start=(dc == 0), stop=(dc == DC - 1))
                    i_.then_inc(mm.h, 1)
                    if hc + 1 < HC:
                        for _ in range(RPS[s_]):
                            _tround(e, hc + 1, rnext)
                            rnext += 1
        em("tensor", p2_mm)
        mm.add(HC * NSLAB)
        th.add(4 * HC)

        P2HV0 = hv.val

        def p2_vec(e):
            e.wait_ge(vv.h, VV_MRUN)

            def tern(hc):
                e.wait_ge(ldg[hc % 2].h, P2LDG[hc % 2] + 16 * (hc // 2 + 1))
                if hc >= 2:
                    e.wait_ge(th.h, P2TH0 + 4 * hc - 6)
                e.tensor_scalar(dsel[:], graw[hc % 2][:],
                                wcols["g"]["thp"][:], None, OP.is_gt)
                e.scalar_tensor_tensor(gtb[hc % 2][:], graw[hc % 2][:],
                                       wcols["g"]["thn"][:], dsel[:],
                                       OP.is_lt, OP.subtract).then_inc(vv.h, 1)
                e.wait_ge(ldv[hc % 2].h, P2LDV[hc % 2] + 16 * (hc // 2 + 1))
                if hc >= 2:
                    e.wait_ge(th.h, P2TH0 + 4 * hc - 4)
                e.tensor_scalar(dsel[:], vraw[hc % 2][:],
                                wcols["v"]["thp"][:], None, OP.is_gt)
                e.scalar_tensor_tensor(vtb[hc % 2][:], vraw[hc % 2][:],
                                       wcols["v"]["thn"][:], dsel[:],
                                       OP.is_lt, OP.subtract).then_inc(vv.h, 1)

            def epi(hc):
                for s_ in range(NSLAB):
                    k = hc * NSLAB + s_
                    e.wait_ge(mm.h, P2MM0 + k + 1)
                    if hc >= 1:
                        e.wait_ge(S4[s_].h, B4[s_] + 16 * hc)
                    e.tensor_tensor(gsl[s_ % 2][:], pbank(k % 3, SLAB),
                                    rho_rep[:, s_ * SLAB:(s_ + 1) * SLAB],
                                    OP.mult)
                    e.tensor_tensor(vsl[s_ % 2][:], pbank(3 + k % 3, SLAB),
                                    rho_rep[:, s_ * SLAB:(s_ + 1) * SLAB],
                                    OP.mult).then_inc(ev.h, 1)
                    e.wait_ge(ss.h, P2SS0 + 2 * k + 2)
                    e.tensor_tensor(hsl[s_][:], gsl[s_ % 2][:],
                                    vsl[s_ % 2][:], OP.mult)
                    e.tensor_tensor(mrun[:, s_ * SLAB:(s_ + 1) * SLAB],
                                    mrun[:, s_ * SLAB:(s_ + 1) * SLAB],
                                    hsl[s_][:], OP.max)
                    e.scalar_tensor_tensor(mrun[:, s_ * SLAB:(s_ + 1) * SLAB],
                                           hsl[s_][:], -1.0,
                                           mrun[:, s_ * SLAB:(s_ + 1) * SLAB],
                                           OP.mult, OP.max).then_inc(hv.h, 1)

            tern(0)
            tern(1)
            for hc in range(HC):
                if hc + 2 < HC:
                    tern(hc + 2)
                epi(hc)
        em("vector", p2_vec)
        vv.add(2 * HC)
        hv.add(HC * NSLAB)
        ev.add(HC * NSLAB)

        def p2_scal(e):
            def evac(hc, r):
                R = 4 * hc + r
                e.wait_ge(th.h, P2TH0 + R + 1)
                if hc >= 2:
                    e.wait_ge(mm.h, P2MM0 + NSLAB * (hc - 1))
                dst = gwT if r < 2 else vwT
                j = r % 2
                b = (R % 2) * 8 * P
                e.activation(dst[hc % 2][:, j * 8 * P:(j + 1) * 8 * P],
                             PSW[:, b:b + 8 * P], AF.Identity
                             ).then_inc(sq.h, 1)

            def silu_slab(hc, s_):
                k = hc * NSLAB + s_
                e.wait_ge(ev.h, P2EV0 + k + 1)
                e.activation(gsl[s_ % 2][:], gsl[s_ % 2][:], AF.Silu,
                             bias=gb_sb[:, hc:hc + 1],
                             scale=wcols["g"]["nsc"][:]).then_inc(ss.h, 1)
                e.activation(vsl[s_ % 2][:], vsl[s_ % 2][:], AF.Identity,
                             bias=vb_sb[:, hc:hc + 1],
                             scale=wcols["v"]["nsc"][:]).then_inc(ss.h, 1)

            for r in range(4):
                evac(0, r)
            for hc in range(1, HC):
                evac(hc, 0)
                evac(hc, 1)
                silu_slab(hc - 1, 0)
                evac(hc, 2)
                silu_slab(hc - 1, 1)
                evac(hc, 3)
                silu_slab(hc - 1, 2)
                silu_slab(hc - 1, 3)
            for s_ in range(NSLAB):
                silu_slab(HC - 1, s_)
        em("scalar", p2_scal)
        ss.add(2 * HC * NSLAB)
        sq.add(4 * HC)

        def p2_gp(e):
            for hc in range(HC):
                for s_ in range(NSLAB):
                    k = hc * NSLAB + s_
                    e.wait_ge(hv.h, P2HV0 + k + 1)
                    if hc >= 1:
                        e.wait_ge(S4[s_].h, B4[s_] + 16 * hc)
                    e.dma_start(h_sp.ap()[s_ * TPS:(s_ + 1) * TPS, hc, :, :]
                                .rearrange("a p b -> p a b"),
                                hsl[s_][:].rearrange("p (a b) -> p a b",
                                                     a=TPS)
                                ).then_inc(S4[s_].h, 16)
        em("gpsimd", p2_gp)
        SPILL_DONE = [(x.h, x.add(16 * HC)) for x in S4]

        VV_SREP_HOLDER = [None]
        if dbg:
            GG_D2 = gg.val + 13 * 16

            def dbg2_gp(e):
                e.wait_ge(trq.h, TRQ_QY)
                e.wait_ge(vv.h, VV_SREP_HOLDER[0])
                e.dma_start(qyT_d.ap(), qyT_sb[:]).then_inc(gg.h, 16)
                e.dma_start(rho_d.ap(), rho_rep[:]).then_inc(gg.h, 16)
                e.dma_start(srep_d.ap(), s_h_rep[:]).then_inc(gg.h, 16)
                e.dma_start(sc3_d.ap(), sc3_all[:]).then_inc(gg.h, 16)
                for j, (wn, kk) in enumerate(
                        (w_, k_) for w_ in ("g", "v", "o")
                        for k_ in ("thp", "thn", "nsc")):
                    e.dma_start(gcol_d.ap()[:, j:j + 1], wcols[wn][kk][:],
                                allow_slow_non_contiguous=True
                                ).then_inc(gg.h, 16)
                e.wait_ge(gg.h, GG_D2)
            em("gpsimd", dbg2_gp)
            gg.add(13 * 16)

        set_scope("p2e")
        BCC = 5 * BANK
        SR_M0 = mm.val
        SR_V0 = vv.val + 1

        def p2e_mm(e):
            e.wait_ge(hv.h, P2HV0 + HC * NSLAB)
            for tt in range(TT):
                if tt > 0:
                    e.wait_ge(vv.h, SR_V0 + 3 * (tt - 1) + 1)
                e.transpose(PS[:, TRPC:TRPC + P], mrun[:, tt * P:(tt + 1) * P],
                            ident[:]).then_inc(mm.h, 1)
                e.wait_ge(vv.h, SR_V0 + 3 * tt + 1)
                e.transpose(PS[0:1, TRPC + P:TRPC + 2 * P], scol[:],
                            ident[:]).then_inc(mm.h, 1)
                e.wait_ge(vv.h, SR_V0 + 3 * tt + 2)
                e.matmul(PS[:, BCC:BCC + P], ones_row[:], rowb[:],
                         start=True, stop=True).then_inc(mm.h, 1)
        em("tensor", p2e_mm)
        mm.add(3 * TT)

        def p2e_vec(e):
            e.tensor_scalar_mul(tcol2[:], wcols["o"]["nsc"][:], 1.0 / 127.0
                                ).then_inc(vv.h, 1)
            for tt in range(TT):
                e.wait_ge(mm.h, SR_M0 + 3 * tt + 1)
                e.reduce_max(tcol[:], PS[:, TRPC:TRPC + P], axis=AX)
                e.tensor_scalar(tcol[:], tcol[:], EPS, None, OP.max)
                e.tensor_scalar(sc3_all[:, tt:tt + 1], tcol[:], tcol2[:],
                                None, OP.mult)
                e.reciprocal(tcol[:], tcol[:])
                e.tensor_scalar_mul(scol[:], tcol[:], 127.0).then_inc(vv.h, 1)
                e.wait_ge(mm.h, SR_M0 + 3 * tt + 2)
                e.tensor_copy(rowb[:], PS[0:1, TRPC + P:TRPC + 2 * P]
                              ).then_inc(vv.h, 1)
                e.wait_ge(mm.h, SR_M0 + 3 * tt + 3)
                e.tensor_copy(s_h_rep[:, tt * P:(tt + 1) * P],
                              PS[:, BCC:BCC + P]).then_inc(vv.h, 1)
        em("vector", p2e_vec)
        VV_SREP = vv.add(1 + 3 * TT)
        VV_SREP_HOLDER[0] = VV_SREP

        def ob_sync(e):
            e.wait_ge(vv.h, VV_SREP)
            e.dma_start(rep8k[0:1, :], ob_hbm.ap()).then_inc(ldx.h, 16)
        em("sync", ob_sync)
        LDX_OB = ldx.add(16)

        def ob_mm(e):
            e.wait_ge(ldx.h, LDX_OB)
            for ds in range(NDS):
                e.matmul(pbank(ds, DS), ones_row[:],
                         rep8k[0:1, ds * DS:(ds + 1) * DS],
                         start=True, stop=True).then_inc(mm.h, 1)
        em("tensor", ob_mm)
        MM_OBREP = mm.add(NDS)

        def ob_vec(e):
            e.wait_ge(mm.h, MM_OBREP)
            for ds in range(NDS):
                e.tensor_copy(rep8k[:, ds * DS:(ds + 1) * DS], pbank(ds, DS))
            e.engine_nop().then_inc(vv.h, 1)
        em("vector", ob_vec)
        VV_OBREP = vv.add(1)

        set_scope("p3")
        p3 = [dict() for _ in range(NSEG)]
        for seg in range(NSEG):
            c = p3[seg]
            c["wld0"] = ldw.val
            c["tr0"] = trq.val

            def p3w_sync(e, seg=seg, c=c):
                if seg == 0:
                    e.wait_ge(vv.h, VV_OBREP)
                    if dbg:
                        e.wait_ge(gg.h, 14 * 16)
                else:
                    e.wait_ge(vv.h, p3[seg - 1]["tern_end"])

                def load(i):
                    e.dma_start(
                        oraw[i % 2][:],
                        ow_hbm.ap()[i * P:(i + 1) * P,
                                    seg * HSEG:(seg + 1) * HSEG]
                    ).then_inc(ldw.h, 16)
                for i in range(min(2, DC)):
                    load(i)
                if seg > 0:
                    e.wait_ge(mm.h, p3[seg - 1]["mm_end"])
                for i in range(DC):
                    e.wait_ge(vv.h, c["tern0"] + i + 1)
                    for hcl in range(HCS):
                        e.dma_start_transpose(
                            owh[:, hcl * D + i * P:hcl * D + (i + 1) * P],
                            otb[:, hcl * P:(hcl + 1) * P]).then_inc(trq.h, 16)
                    if i + 2 < DC:
                        load(i + 2)
            em("sync", p3w_sync)
            ldw.add(16 * DC)
            c["trdone"] = trq.add(16 * HCS * DC)

            c["tern0"] = vv.val

            def p3w_vec(e, seg=seg, c=c):
                o = wcols["o"]
                for i in range(DC):
                    e.wait_ge(ldw.h, c["wld0"] + 16 * (i + 1))
                    if i >= 1:
                        e.wait_ge(trq.h, c["tr0"] + 16 * HCS * i)
                    e.tensor_scalar(osel[:], oraw[i % 2][:], o["thp"][:], None,
                                    OP.is_gt)
                    e.scalar_tensor_tensor(otb[:], oraw[i % 2][:], o["thn"][:],
                                           osel[:], OP.is_lt, OP.subtract
                                           ).then_inc(vv.h, 1)
            em("vector", p3w_vec)
            c["tern_end"] = vv.add(DC)

            c["hld0"] = ldh.val
            nper = 1 + (1 if seg > 0 else 0) + (1 if seg == NSEG - 1 else 0)
            c["per_tt"] = 16 * nper
            c["evs"] = 4 if seg == NSEG - 1 else 1

            def p3h_sync(e, seg=seg, c=c):
                for _h, _v in SPILL_DONE:
                    e.wait_ge(_h, _v)
                if seg > 0:
                    e.wait_ge(stp.h, p3[seg - 1]["stp_end"])
                if seg == NSEG - 1:
                    e.wait_ge(sty.h, STY_DONE)
                for tt in range(TT):
                    if tt >= 2:
                        e.wait_ge(vv.h, c["qv0"] + tt - 1)
                    e.dma_start(
                        hstg[tt % 2][:].rearrange("p (a b) -> p a b", a=HCS),
                        h_sp.ap()[tt, seg * HCS:(seg + 1) * HCS, :, :]
                        .rearrange("a p b -> p a b")
                    ).then_inc(ldh.h, 16)
                    if seg > 0:
                        if tt >= 1:
                            e.wait_ge(hv.h, c["ev0"] + c["evs"] * (tt - 1) + 1)
                        e.dma_start(pbuf[:],
                                    part_sp.ap()[tt * P:(tt + 1) * P, :]
                                    ).then_inc(ldh.h, 16)
                    if seg == NSEG - 1:
                        if tt >= 1:
                            e.wait_ge(hv.h, c["ev0"] + c["evs"] * (tt - 1) + 2)
                        e.dma_start(ycld[:],
                                    yc_sp.ap()[tt * P:(tt + 1) * P, :]
                                    ).then_inc(ldh.h, 16)
            em("sync", p3h_sync)
            ldh.add(c["per_tt"] * TT)

            c["qv0"] = vv.val
            c["ev0"] = hv.val
            c["mm0"] = mm.val
            c["stp0"] = stp.val
            c["ss0"] = ss.val
            last = seg == NSEG - 1

            def p3_vec(e, seg=seg, c=c, last=last):
                def q(tt):
                    e.wait_ge(ldh.h, c["hld0"] + c["per_tt"] * tt + 16)
                    if tt >= 2:
                        e.wait_ge(mm.h, c["mm0"] + tt - 1)
                    hb = hstg[tt % 2]
                    qb = qhT[tt % 2]
                    for hcl in range(HCS):
                        e.tensor_tensor(hb[:, hcl * P:(hcl + 1) * P],
                                        hb[:, hcl * P:(hcl + 1) * P],
                                        s_h_rep[:, tt * P:(tt + 1) * P],
                                        OP.mult)
                    if SIM_MODE:
                        e.tensor_scalar(hb[:], hb[:], 256.5, None, OP.add)
                        e.tensor_scalar(qb[:], hb[:], 1.0, None, OP.mod)
                        e.scalar_tensor_tensor(qb[:], hb[:], -256.0, qb[:],
                                               OP.add, OP.subtract
                                               ).then_inc(vv.h, 1)
                    else:
                        e.tensor_copy(ib_h[tt % 2][:], hb[:])
                        e.tensor_copy(qb[:], ib_h[tt % 2][:]).then_inc(vv.h, 1)

                def evac(tt):
                    e.wait_ge(mm.h, c["mm0"] + tt + 1)
                    if seg > 0:
                        e.wait_ge(ldh.h, c["hld0"] + c["per_tt"] * tt + 32)
                    if last:
                        e.wait_ge(ldh.h, c["hld0"] + c["per_tt"] * (tt + 1))
                    o = obuf[tt % 2]
                    if not last:
                        if tt >= 2:
                            e.wait_ge(stp.h, c["stp0"] + 16 * (tt - 1))
                        for ds in range(NDS):
                            if seg == 0:
                                e.tensor_copy(o[:, ds * DS:(ds + 1) * DS],
                                              pbank(ds, DS))
                            else:
                                e.tensor_tensor(o[:, ds * DS:(ds + 1) * DS],
                                                pbank(ds, DS),
                                                pbuf[:, ds * DS:(ds + 1) * DS],
                                                OP.add)
                        e.engine_nop().then_inc(hv.h, 1)
                        return
                    if tt >= 2:
                        e.wait_ge(sto.h, SO3 + 16 * (tt - 1))
                    for ds in range(NDS):
                        e.tensor_tensor(o[:, ds * DS:(ds + 1) * DS],
                                        pbank(ds, DS),
                                        pbuf[:, ds * DS:(ds + 1) * DS],
                                        OP.add)
                    e.engine_nop().then_inc(hv.h, 1)
                    e.tensor_scalar(o[:], o[:], sc3_all[:, tt:tt + 1],
                                    None, OP.mult)
                    e.tensor_tensor(o[:], o[:], rep8k[:], OP.add)
                    e.scalar_tensor_tensor(o[:], ycld[:],
                                           rstd_all[:, tt:tt + 1], o[:],
                                           OP.mult, OP.add)
                    e.reduce_sum(mcol[:], o[:], axis=AX)
                    e.tensor_scalar_mul(mcol[:], mcol[:], 1.0 / D
                                        ).then_inc(hv.h, 1)
                    e.wait_ge(ss.h, c["ss0"] + 3 * tt + 1)
                    e.tensor_scalar_mul(tcol[:], tcol[:], 1.0 / D)
                    e.tensor_tensor(tcol2[:], mcol[:], mcol[:], OP.mult)
                    e.tensor_tensor(tcol[:], tcol[:], tcol2[:],
                                    OP.subtract).then_inc(hv.h, 1)
                    e.wait_ge(ss.h, c["ss0"] + 3 * tt + 2)
                    e.reciprocal(tcol[:], tcol[:])
                    e.tensor_tensor(tcol2[:], mcol[:], tcol[:], OP.mult)
                    e.tensor_scalar_mul(tcol2[:], tcol2[:], -1.0
                                        ).then_inc(hv.h, 1)

                for tt in range(TT):
                    q(tt)
                    if tt >= 1:
                        evac(tt - 1)
                evac(TT - 1)
            em("vector", p3_vec)
            vv.add(TT)
            hv.add(c["evs"] * TT)

            def p3_mm(e, seg=seg, c=c):
                e.wait_ge(trq.h, c["trdone"])
                for tt in range(TT):
                    e.wait_ge(vv.h, c["qv0"] + tt + 1)
                    if tt >= 1:
                        e.wait_ge(hv.h, c["ev0"] + c["evs"] * (tt - 1) + 1)
                    for hcl in range(HCS):
                        for ds in range(NDS):
                            i_ = e.matmul(
                                pbank(ds, DS),
                                qhT[tt % 2][:, hcl * P:(hcl + 1) * P],
                                owh[:, hcl * D + ds * DS:hcl * D + (ds + 1) * DS],
                                start=(hcl == 0), stop=(hcl == HCS - 1))
                    i_.then_inc(mm.h, 1)
            em("tensor", p3_mm)
            c["mm_end"] = mm.add(TT)

            if not last:
                def p3e_gp(e, seg=seg, c=c):
                    for tt in range(TT):
                        e.wait_ge(hv.h, c["ev0"] + tt + 1)
                        e.dma_start(part_sp.ap()[tt * P:(tt + 1) * P, :],
                                    obuf[tt % 2][:]).then_inc(stp.h, 16)
                em("gpsimd", p3e_gp)
                c["stp_end"] = stp.add(16 * TT)
            else:
                SO3 = sto.val

                def p3f_scal(e, c=c):
                    for tt in range(TT):
                        e.wait_ge(hv.h, c["ev0"] + 4 * tt + 2)
                        e.activation(sq_scr[:], obuf[tt % 2][:], AF.Square,
                                     accum_out=tcol[:]).then_inc(ss.h, 1)
                        e.wait_ge(hv.h, c["ev0"] + 4 * tt + 3)
                        e.activation(tcol[:], tcol[:], AF.Sqrt,
                                     bias=eps_col[:]).then_inc(ss.h, 1)
                        e.wait_ge(hv.h, c["ev0"] + 4 * tt + 4)
                        e.activation(obuf[tt % 2][:], obuf[tt % 2][:],
                                     AF.Identity, bias=tcol2[:],
                                     scale=tcol[:]).then_inc(ss.h, 1)
                em("scalar", p3f_scal)
                ss.add(3 * TT)

                def p3f_gp(e, c=c):
                    for tt in range(TT):
                        e.wait_ge(ss.h, c["ss0"] + 3 * tt + 3)
                        e.dma_start(out_hbm.ap()[tt * P:(tt + 1) * P, :],
                                    obuf[tt % 2][:]).then_inc(sto.h, 16)
                    e.wait_ge(sto.h, SO3 + 16 * TT)
                em("gpsimd", p3f_gp)
                sto.add(16 * TT)

        @block.sync
        def _(e):
            for fn in prog["sync"]:
                fn(e)

        @block.scalar
        def _(e):
            g = ScalGuard(e, sems["sb"], sflush)
            for fn in prog["scalar"]:
                fn(g)

        @block.vector
        def _(e):
            g = VecGuard(e, sems["vb"])
            for fn in prog["vector"]:
                fn(g)

        @block.tensor
        def _(e):
            for fn in prog["tensor"]:
                fn(e)

        @block.gpsimd
        def _(e):
            for fn in prog["gpsimd"]:
                fn(e)

    return nc


def _prep_inputs(L0_attn_x, L0_attn_B, L0_attn_C, L0_attn_D,
                 L0_ffn_gate_w, L0_ffn_gate_b, L0_ffn_val_w, L0_ffn_val_b,
                 L0_ffn_out_w, L0_ffn_out_b, n_cores=8):
    x = np.ascontiguousarray(np.asarray(L0_attn_x, dtype=np.float32)
                             ).reshape(-1, np.asarray(L0_attn_x).shape[-1])
    T_total, D = x.shape
    T = T_total // n_cores
    H = np.asarray(L0_ffn_gate_w).shape[0]
    NS = np.asarray(L0_attn_B).shape[0]
    HC = H // P

    common = {
        "bw": np.ascontiguousarray(L0_attn_B, dtype=np.float32),
        "cw": np.ascontiguousarray(L0_attn_C, dtype=np.float32),
        "dv": np.ascontiguousarray(L0_attn_D, dtype=np.float32).reshape(1, D),
        "gw": np.ascontiguousarray(L0_ffn_gate_w, dtype=np.float32),
        "vw": np.ascontiguousarray(L0_ffn_val_w, dtype=np.float32),
        "ow": np.ascontiguousarray(L0_ffn_out_w, dtype=np.float32),
        "gb": np.ascontiguousarray(
            np.asarray(L0_ffn_gate_b, dtype=np.float32).reshape(HC, P).T),
        "vb": np.ascontiguousarray(
            np.asarray(L0_ffn_val_b, dtype=np.float32).reshape(HC, P).T),
        "ob": np.ascontiguousarray(L0_ffn_out_b, dtype=np.float32).reshape(1, D),
        "idm": np.eye(P, dtype=np.float32),
    }
    in_maps = []
    for c in range(n_cores):
        m = dict(common)
        m["x"] = np.ascontiguousarray(x[c * T:(c + 1) * T])
        in_maps.append(m)
    return in_maps, (T, D, H, NS)


def kernel(L0_attn_x, L0_attn_B, L0_attn_C, L0_attn_D,
           L0_ffn_gate_w, L0_ffn_gate_b, L0_ffn_val_w, L0_ffn_val_b,
           L0_ffn_out_w, L0_ffn_out_b):
    n_cores = 8
    in_maps, (T, D, H, NS) = _prep_inputs(
        L0_attn_x, L0_attn_B, L0_attn_C, L0_attn_D,
        L0_ffn_gate_w, L0_ffn_gate_b, L0_ffn_val_w, L0_ffn_val_b,
        L0_ffn_out_w, L0_ffn_out_b, n_cores)
    nc = build_nc(T, D, H, NS)
    res = run_bass_kernel_spmd(nc, in_maps, core_ids=list(range(n_cores)))
    out = np.concatenate([r["out"] for r in res.results], axis=0)
    return out.reshape(np.asarray(L0_attn_x).shape).astype(np.float32)


# version 16
# speedup vs baseline: 1.4987x; 1.2420x over previous
import os
import numpy as np

import concourse.bass as bass
import concourse.mybir as mybir
from concourse.bass_utils import run_bass_kernel_spmd

F32 = mybir.dt.float32
BF16 = mybir.dt.bfloat16
I32 = mybir.dt.int32
AF = mybir.ActivationFunctionType
OP = mybir.AluOpType
AX = mybir.AxisListType.X

SIM_MODE = (os.environ.get("BASS_SIM_FP64") == "1"
            or os.environ.get("KERNEL_SIMROUND") == "1")

EPS = 1e-5
THR_M = 1e-5 * (EPS ** 0.5)
P = 128


class Sem:
    def __init__(self, handle):
        self.h = handle
        self.val = 0

    def add(self, n):
        self.val += n
        return self.val


def _ap_range(ap):
    t = ap.tensor
    isz = 2 if t.dtype == BF16 else 4
    span = 1
    dims = list(ap.ap)
    for step, count in dims[1:]:
        span += int(step) * (int(count) - 1)
    base = 0
    space = type(t).__name__
    r = getattr(t, "manual_sbuf_range", None)
    if r is not None:
        base = int(r[0])
    lo = base + int(ap.offset) * isz
    return (space, lo, lo + span * isz)


def _overlap(a, b):
    return a[0] == b[0] and a[1] < b[2] and b[1] < a[2]


class VecGuard:

    _OPS = {
        "tensor_copy": (1, [0]), "tensor_tensor": (2, [0]),
        "tensor_add": (2, [0]), "tensor_sub": (2, [0]),
        "tensor_mul": (2, [0]), "tensor_max": (2, [0]),
        "tensor_scalar": (3, [0]), "tensor_scalar_mul": (2, [0]),
        "tensor_scalar_add": (2, [0]), "tensor_scalar_max": (2, [0]),
        "scalar_tensor_tensor": (4, [0]), "reduce_sum": (2, [0]),
        "reduce_max": (2, [0]), "reciprocal": (2, [0]),
    }

    def __init__(self, eng, sem):
        self._e = eng
        self._sem = sem
        self._writes = []

    def _flush(self):
        if self._writes:
            self._writes = []
            self._e.engine_nop().then_inc(self._sem.h, 1)
            self._e.wait_ge(self._sem.h, self._sem.add(1))

    def wait_ge(self, *a, **k):
        return self._e.wait_ge(*a, **k)

    def memset(self, ap, v):
        i = self._e.memset(ap, v)
        self._writes.append(_ap_range(ap))
        return i

    def engine_nop(self):
        return self._e.engine_nop()

    def __getattr__(self, name):
        fn = getattr(self._e, name)
        spec = self._OPS.get(name)
        if spec is None:
            return fn
        npos, outpos = spec

        def wrapped(*args, **kwargs):
            ins = []
            outs = []
            for i, a in enumerate(args[:npos + 2]):
                if hasattr(a, "tensor"):
                    (outs if i in outpos else ins).append(_ap_range(a))
            for k_, a in kwargs.items():
                if hasattr(a, "tensor"):
                    (outs if k_ == "accum_out" else ins).append(_ap_range(a))
            if any(_overlap(r, w) for r in ins for w in self._writes):
                self._flush()
            res = fn(*args, **kwargs)
            self._writes.extend(outs)
            return res
        return wrapped


class ScalGuard(VecGuard):

    _OPS = {
        "activation": (2, [0]), "copy": (2, [0]),
        "add": (2, [0]), "mul": (2, [0]),
    }

    def __init__(self, eng, sem, scratch):
        super().__init__(eng, sem)
        self._scr = scratch

    def _flush(self):
        if self._writes:
            self._writes = []
            self._e.copy(self._scr[:], self._scr[:]).then_inc(self._sem.h, 1)
            self._e.wait_ge(self._sem.h, self._sem.add(1))


def build_nc(T, D, H, NS, dbg=False):
    TT = T // P
    DC = D // P
    HC = H // P
    NCH = NS // P
    SLAB = min(512, T)
    NSLAB = T // SLAB
    TPS = SLAB // P
    DS = min(512, D)
    NDS = D // DS
    NSEG = 4
    HCS = HC // NSEG
    HSEG = H // NSEG
    GH = 4
    NG = HC // GH
    assert TT <= 64 and DC <= 64

    nc = bass.Bass("TRN2", target_bir_lowering=False,
                   detect_race_conditions=False)

    x_hbm = nc.dram_tensor("x", [T, D], F32, kind="ExternalInput")
    b_hbm = nc.dram_tensor("bw", [NS, D], F32, kind="ExternalInput")
    c_hbm = nc.dram_tensor("cw", [D, NS], F32, kind="ExternalInput")
    d_hbm = nc.dram_tensor("dv", [1, D], F32, kind="ExternalInput")
    gw_hbm = nc.dram_tensor("gw", [H, D], F32, kind="ExternalInput")
    vw_hbm = nc.dram_tensor("vw", [H, D], F32, kind="ExternalInput")
    ow_hbm = nc.dram_tensor("ow", [D, H], F32, kind="ExternalInput")
    gb_hbm = nc.dram_tensor("gb", [P, HC], F32, kind="ExternalInput")
    vb_hbm = nc.dram_tensor("vb", [P, HC], F32, kind="ExternalInput")
    ob_hbm = nc.dram_tensor("ob", [1, D], F32, kind="ExternalInput")
    id_hbm = nc.dram_tensor("idm", [P, P], F32, kind="ExternalInput")
    out_hbm = nc.dram_tensor("out", [T, D], F32, kind="ExternalOutput")

    _spk = "ExternalOutput" if dbg else "Internal"
    yc_sp = nc.dram_tensor("yc_sp", [T, D], BF16, kind=_spk)
    owT_sp = nc.dram_tensor("owT_sp", [H, D], BF16, kind="Internal")
    h_sp = nc.dram_tensor("h_sp", [TT, HC, P, P], F32, kind=_spk)
    part_sp = nc.dram_tensor("part_sp", [T, D], F32, kind=_spk)
    if dbg:
        state_d = nc.dram_tensor("state_d", [P, (NS // P) * T], BF16,
                                 kind="ExternalOutput")
        qyT_d = nc.dram_tensor("qyT_d", [P, (D // P) * T], BF16,
                               kind="ExternalOutput")
        rho_d = nc.dram_tensor("rho_d", [P, T], F32, kind="ExternalOutput")
        srep_d = nc.dram_tensor("srep_d", [P, T], F32, kind="ExternalOutput")
        gcol_d = nc.dram_tensor("gcol_d", [P, 16], F32, kind="ExternalOutput")
        sc3_d = nc.dram_tensor("sc3_d", [P, T // P], F32, kind="ExternalOutput")

    base0 = (int(nc.sbuf_base) + 63) & ~63
    top = int(nc.sbuf_top)
    _cur = [base0]

    def place(name, shape, dtype, at=None):
        nbytes = int(np.prod(shape[1:])) * (2 if dtype == BF16 else 4)
        if at is None:
            off = _cur[0]
            _cur[0] = (off + nbytes + 63) & ~63
        else:
            off = at
        assert off + nbytes <= top, f"SBUF overflow placing {name} at {off}"
        return nc.alloc_sbuf_tensor_at(name, list(shape), dtype, offset=off), off

    def bump():
        return _cur[0]

    ident, _ = place("ident", [P, P], F32)
    ident_bf, _ = place("ident_bf", [P, P], BF16)
    gb_sb, _ = place("gb_sb", [P, HC], F32)
    vb_sb, _ = place("vb_sb", [P, HC], F32)
    rep8k, REP8K_OFF = place("rep8k", [P, D], F32)
    gacc, GACC_OFF = place("gacc", [P, 64], F32)
    gred, _ = place("gred", [P, 1], F32)
    gfull, _ = place("gfull", [P, 1], F32)
    wcols = {}
    for wname in ("g", "v", "o"):
        wcols[wname] = {}
        for k in ("thp", "thn", "nsc"):
            wcols[wname][k], _ = place(f"{k}_{wname}", [P, 1], F32)
    m_all, _ = place("m_all", [P, TT], F32)
    M_all, _ = place("M_all", [P, TT], F32)
    ssq_all, _ = place("ssq_all", [P, TT], F32)
    rstd_all, _ = place("rstd_all", [P, TT], F32)
    rho_all, _ = place("rho_all", [P, TT], F32)
    sc3_all, _ = place("sc3_all", [P, TT], F32)
    scol, _ = place("scol", [P, 1], F32)
    mcol, _ = place("mcol", [P, 1], F32)
    tcol, _ = place("tcol", [P, 1], F32)
    tcol2, _ = place("tcol2", [P, 1], F32)
    eps_col, _ = place("eps_col", [P, 1], F32)
    ones_row, _ = place("ones_row", [1, P], F32)
    ones_col, _ = place("ones_col", [P, 1], F32)
    rowb, _ = place("rowb", [1, P], F32)
    sclr, _ = place("sclr", [1, 1], F32)
    sflush, _ = place("sflush", [1, 1], F32)

    PERS_END = bump()

    GW = min(max(D, H), 4096)
    A_SZ = max(DC * T * 2, HCS * D * 2, 2 * GW * 4 + GW * 2)
    SLOT_A = bump()
    _cur[0] = SLOT_A + ((A_SZ + 63) & ~63)
    B_SZ = max(NCH * T * 2, 4 * D * 4, 2 * HCS * P * 4 + 2 * HCS * P * 2 + D * 4)
    SLOT_B = bump()
    _cur[0] = SLOT_B + ((B_SZ + 63) & ~63)
    C_SZ = max(DC * NS * 2, NCH * D * 2,
               8 * D * 2,
               0)
    SLOT_C = bump()
    _cur[0] = SLOT_C + ((C_SZ + 63) & ~63)
    D_SZ = max(DC * SLAB * 2,
               3 * D * 4,
               6 * SLAB * 4 + 2 * T * 4,
               D * 2 + 2 * D * 4)
    SLOT_D = bump()
    _cur[0] = SLOT_D + ((D_SZ + 63) & ~63)
    E_SZ = max(2 * D * 4, D * 4)
    SLOT_E = bump()
    _cur[0] = SLOT_E + ((E_SZ + 63) & ~63)
    F_SZ = max(2 * D * 2, 2 * NS * 4, T * 4)
    SLOT_F = bump()
    _cur[0] = SLOT_F + ((F_SZ + 63) & ~63)
    G_SZ = 2 * NS * 2
    SLOT_G = bump()
    _cur[0] = SLOT_G + ((G_SZ + 63) & ~63)
    H_SZ = D * 2
    SLOT_H = bump()
    _cur[0] = SLOT_H + ((H_SZ + 63) & ~63)
    I_SZ = D * 2
    SLOT_I = bump()
    _cur[0] = SLOT_I + ((I_SZ + 63) & ~63)
    assert bump() <= top, f"arena overflow: {bump()} > {top}"

    def at(name, shape, dtype, off):
        t_, _ = place(name, shape, dtype, at=off)
        return t_

    s_h_rep = at("s_h_rep", [P, T], F32, SLOT_F)
    sq_scr = at("sq_scr", [P, D], BF16, SLOT_D + 3 * D * 4)

    gld = [at(f"gld{i}", [P, GW], F32, SLOT_A + i * GW * 4) for i in range(2)]
    gscr = at("gscr", [P, GW], BF16, SLOT_A + 2 * GW * 4)
    qyT_sb = at("qyT_sb", [P, DC * T], BF16, SLOT_A)
    state_sb = at("state_sb", [P, NCH * T], BF16, SLOT_B)
    bt_sb = at("bt_sb", [P, DC * NS], BF16, SLOT_C)
    ct_sb = at("ct_sb", [P, NCH * D], BF16, SLOT_C)
    xT_sb = at("xT_sb", [P, DC * SLAB], BF16, SLOT_D)
    wbuf = [at(f"wbuf{i}", [P, D], F32, SLOT_D + i * D * 4) for i in range(2)]
    t256 = at("t256", [P, D], F32, SLOT_D + 2 * D * 4)
    xld = [at(f"xld{i}", [P, D], F32, SLOT_E + i * D * 4) for i in range(2)]
    xbf = [at(f"xbf{i}", [P, D], BF16, SLOT_F + i * D * 2) for i in range(2)]
    cld = [at(f"cld{i}", [P, NS], F32, SLOT_F + i * NS * 4) for i in range(2)]
    cbf = [at(f"cbf{i}", [P, NS], BF16, SLOT_G + i * NS * 2) for i in range(2)]
    ycbf = at("ycbf", [P, D], BF16, SLOT_H)
    qybf = at("qybf", [P, D], BF16, SLOT_I)
    ib_q = at("ib_q", [P, D], I32, SLOT_D + 2 * D * 4)
    graw = [at(f"graw{i}", [P, D], F32, SLOT_B + i * D * 4) for i in range(2)]
    vraw = [at(f"vraw{i}", [P, D], F32, SLOT_B + (2 + i) * D * 4)
            for i in range(2)]
    gtb = [at(f"gtb{i}", [P, D], BF16, SLOT_C + i * D * 2)
           for i in range(2)]
    vtb = [at(f"vtb{i}", [P, D], BF16, SLOT_C + (2 + i) * D * 2)
           for i in range(2)]
    gwT = [at(f"gwT{i}", [P, D], BF16, SLOT_C + (4 + i) * D * 2)
           for i in range(2)]
    vwT = [at(f"vwT{i}", [P, D], BF16, SLOT_C + (6 + i) * D * 2)
           for i in range(2)]
    dsel = at("dsel", [P, D], BF16, REP8K_OFF)
    po_raw = [at(f"po_raw{i}", [P, 1024], F32, SLOT_D + 16384 + i * 4096)
              for i in range(2)]
    po_sel = at("po_sel", [P, 1024], BF16, SLOT_D + 24576)
    po_t = [at("po_t0", [P, 1024], BF16, SLOT_D + 26624),
            at("po_t1", [P, 1024], BF16, SLOT_I)]
    po_stg = [at(f"po_stg{i}", [P, 1024], BF16, SLOT_H + i * 2048)
              for i in range(2)]
    gsl = [at(f"gsl{i}", [P, SLAB], F32, SLOT_D + i * SLAB * 4)
           for i in range(2)]
    vsl = [at(f"vsl{i}", [P, SLAB], F32, SLOT_D + (2 + i) * SLAB * 4)
           for i in range(2)]
    hsl = [at(f"hsl{i}", [P, SLAB], F32,
              (SLOT_D + (4 + i) * SLAB * 4) if i < 2 else
              (SLOT_E + (i - 2) * SLAB * 4))
           for i in range(4)]
    mrun = at("mrun", [P, T], F32, SLOT_D + 6 * SLAB * 4)
    rho_rep = at("rho_rep", [P, T], F32, SLOT_D + 6 * SLAB * 4 + T * 4)
    owh = at("owh", [P, HCS * D], BF16, SLOT_A)
    hstg = [at(f"hstg{i}", [P, HCS * P], F32, SLOT_B + i * HCS * P * 4)
            for i in range(2)]
    ib_h = [at(f"ib_h{i}", [P, HCS * P], I32, SLOT_B + i * HCS * P * 4)
            for i in range(2)]
    qhT = [at(f"qhT{i}", [P, HCS * P], BF16,
              SLOT_B + 2 * HCS * P * 4 + i * HCS * P * 2) for i in range(2)]
    pbuf = at("pbuf", [P, D], F32, SLOT_E)
    ycld = at("ycld", [P, D], BF16, SLOT_D)
    obuf = [at(f"obuf{i}", [P, D], F32, SLOT_D + D * 2 + i * D * 4)
            for i in range(2)]

    PS = nc.alloc_psum_tensor("PS", [P, 3072], F32)
    PSW = nc.alloc_psum_tensor("PSW", [P, 2048], BF16)
    BANK = 512

    def pbank(b, width=BANK):
        return PS[:, b * BANK:b * BANK + width]

    import contextlib
    st = contextlib.ExitStack()
    with st:
        S = st.enter_context
        sems = {}
        for name in ("ldc", "ldx", "ldw", "ldh", "trq", "mm", "ev", "vv",
                     "hv", "ss", "gg", "sty", "sth", "stp", "sto",
                     "vb", "sb", "ts2", "th", "sq", "ga",
                     "pol", "pom", "pos", "pot"):
            sems[name] = Sem(S(nc.semaphore(name)))
        ldc, ldx, ldw, ldh = (sems[k] for k in ("ldc", "ldx", "ldw", "ldh"))
        trq, mm, ev, vv = (sems[k] for k in ("trq", "mm", "ev", "vv"))
        ss, gg, hv = sems["ss"], sems["gg"], sems["hv"]
        sty, sth, stp, sto = (sems[k] for k in ("sty", "sth", "stp", "sto"))

        block = S(nc.Block())
        prog = {k: [] for k in ("sync", "scalar", "vector", "tensor", "gpsimd")}
        _scope = ["s0"]

        def em(engine, fn, scope=None):
            sc = scope or _scope[0]
            def tagged(e, fn=fn, sc=sc):
                with nc.named_scope(sc):
                    fn(e)
            prog[engine].append(tagged)

        def set_scope(name):
            _scope[0] = name

        def s0_sync(e):
            e.dma_start(ident[:], id_hbm.ap()).then_inc(ldc.h, 16)
            e.dma_start(gb_sb[:], gb_hbm.ap()).then_inc(ldc.h, 16)
            e.dma_start(vb_sb[:], vb_hbm.ap()).then_inc(ldc.h, 16)
            e.dma_start(rep8k[0:1, :], d_hbm.ap()).then_inc(ldc.h, 16)
        em("sync", s0_sync)
        ldc.add(64)

        def s0_vec(e):
            e.memset(sflush[:], 0.0)
            e.memset(eps_col[:], EPS)
            e.memset(ones_row[:], 1.0)
            e.memset(ones_col[:], 1.0).then_inc(vv.h, 1)
        em("vector", s0_vec)
        VV_ONES = vv.add(1)

        def s0_mm(e):
            e.wait_ge(vv.h, VV_ONES)
            e.wait_ge(ldc.h, 64)
            for ds in range(NDS):
                e.matmul(pbank(ds, DS), ones_row[:],
                         rep8k[0:1, ds * DS:(ds + 1) * DS],
                         start=True, stop=True).then_inc(mm.h, 1)
        em("tensor", s0_mm)
        MM_DREP = mm.add(NDS)

        def s0_vec2(e):
            e.wait_ge(mm.h, MM_DREP)
            for ds in range(NDS):
                e.tensor_scalar(rep8k[:, ds * DS:(ds + 1) * DS],
                                pbank(ds, DS), 1.0, None, OP.add)
            e.tensor_copy(ident_bf[:], ident[:])
            e.engine_nop().then_inc(vv.h, 1)
        em("vector", s0_vec2)
        VV_DREP = vv.add(1)

        set_scope("gamma")
        GBIAS = 1.0 if SIM_MODE else (1.0 + 2.3e-7)
        NPU = DC * (H // 1024)
        PO_DRAIN = [None]
        th, sq = sems["th"], sems["sq"]
        polds = [sems["pol"], sems["pom"]]
        posts = [sems["pos"], sems["pot"]]
        _prev_gred = [0]
        for wi, (wname, whbm, rows, cols) in enumerate(
                (("o", ow_hbm, D, H),
                 ("g", gw_hbm, H, D),
                 ("v", vw_hbm, H, D))):
            cpr = max(1, cols // GW)
            w_ = min(cols, GW)
            ntile = (rows // P) * cpr
            inv_n = GBIAS / float(rows * cols)
            assert ntile <= 64

            ld0 = ldw.val
            ab0 = ss.val
            prev_gred = _prev_gred[0]

            def g_sync(e, whbm=whbm, rows=rows, cpr=cpr, w_=w_, ab0=ab0):
                for i in range(rows // P):
                    for j in range(cpr):
                        k = i * cpr + j
                        if k < 2:
                            e.wait_ge(ss.h, ab0)
                        else:
                            e.wait_ge(ss.h, ab0 + k - 1)
                        e.dma_start(
                            gld[k % 2][:, 0:w_],
                            whbm.ap()[i * P:(i + 1) * P, j * w_:(j + 1) * w_]
                        ).then_inc(ldw.h, 16)
            em("sync", g_sync)
            ldw.add(16 * ntile)

            def g_act(e, ntile=ntile, w_=w_, ld0=ld0, first=(wi == 0),
                      drain=(wi >= 1), prev_gred=prev_gred):
                if first:
                    e.wait_ge(ldc.h, 64)
                    e.wait_ge(vv.h, VV_ONES)
                e.wait_ge(vv.h, prev_gred)
                for k in range(ntile):
                    e.wait_ge(ldw.h, ld0 + 16 * (k + 1))
                    e.activation(gscr[:, 0:w_], gld[k % 2][:, 0:w_], AF.Abs,
                                 accum_out=gacc[:, k:k + 1]).then_inc(ss.h, 1)
                    if drain and PO_DRAIN[0] is not None:
                        PO_DRAIN[0](e)
            em("scalar", g_act)
            SS_ABS = ss.add(ntile)

            def g_vec(e, ntile=ntile, tgt=SS_ABS):
                e.wait_ge(ss.h, tgt)
                e.reduce_sum(gred[:], gacc[:, 0:ntile], axis=AX).then_inc(vv.h, 1)
            em("vector", g_vec)
            VV_GRED = vv.add(1)
            _prev_gred[0] = VV_GRED

            def g_mm1(e, tgt=VV_GRED):
                e.wait_ge(vv.h, tgt)
                e.matmul(PS[0:1, 2560:2561], gred[:], ones_col[:],
                         start=True, stop=True).then_inc(mm.h, 1)
            em("tensor", g_mm1)
            MM_GSUM = mm.add(1)

            def g_vsc(e, tgt=MM_GSUM):
                e.wait_ge(mm.h, tgt)
                e.tensor_copy(sclr[:], PS[0:1, 2560:2561]).then_inc(vv.h, 1)
            em("vector", g_vsc)
            VV_GSC = vv.add(1)

            def g_mm2(e, tgt=VV_GSC):
                e.wait_ge(vv.h, tgt)
                e.matmul(PS[:, 2561:2562], ones_row[:], sclr[:],
                         start=True, stop=True).then_inc(mm.h, 1)
            em("tensor", g_mm2)
            MM_GBC = mm.add(1)

            def g_vec2(e, wname=wname, inv_n=inv_n, tgt=MM_GBC):
                c = wcols[wname]
                e.wait_ge(mm.h, tgt)
                e.tensor_scalar(gfull[:], PS[:, 2561:2562], inv_n, EPS,
                                OP.mult, OP.max)
                e.tensor_scalar_mul(c["thp"][:], gfull[:], 0.5)
                e.tensor_scalar_mul(c["thn"][:], gfull[:], -0.5)
                e.tensor_scalar_mul(c["nsc"][:], gfull[:], -1.0
                                    ).then_inc(vv.h, 1)
            em("vector", g_vec2)
            vv.add(1)
            if wi == 1:
                def po_vec2(e):
                    po_vec_part(e, POSPLIT, NPU)
                em("vector", po_vec2)
                vv.add(NPU - POSPLIT)

                def po_mm2(e):
                    po_mm_part(e, POSPLIT, NPU)
                em("tensor", po_mm2)
                th.add(NPU - POSPLIT)

            if wi == 0:
                POV0 = vv.val
                POTH0 = th.val
                POSQ0 = sq.val
                POL0 = [x.val for x in polds]
                POS0 = [x.val for x in posts]
                NC2 = H // 1024

                def po_gp(e):
                    def store(m):
                        i_m, c2m = divmod(m, NC2)
                        e.wait_ge(sq.h, POSQ0 + m + 1)
                        if m >= 2:
                            e.wait_ge(posts[m % 2].h,
                                      POS0[m % 2] + 16 * (m // 2))
                        e.dma_start(
                            owT_sp.ap()[c2m * 1024:(c2m + 1) * 1024,
                                        i_m * P:(i_m + 1) * P]
                            .rearrange("(a p) d -> p a d", p=P),
                            po_stg[m % 2][:].rearrange("p (a d) -> p a d",
                                                       a=8)
                        ).then_inc(posts[m % 2].h, 16)

                    for n in range(NPU):
                        i_t, c2 = divmod(n, NC2)
                        if n >= 2:
                            e.wait_ge(vv.h, tvv(n - 2))
                        e.dma_start(po_raw[n % 2][:],
                                    ow_hbm.ap()[i_t * P:(i_t + 1) * P,
                                                c2 * 1024:(c2 + 1) * 1024]
                                    ).then_inc(polds[n % 2].h, 16)
                        if n >= 2:
                            store(n - 2)
                    store(NPU - 2)
                    store(NPU - 1)
                em("gpsimd", po_gp)
                for x in polds:
                    x.add(16 * (NPU // 2))
                PO_ST_END = [(x.h, x.add(16 * (NPU // 2))) for x in posts]

                POSPLIT = 64

                def tvv(n):
                    return POV0 + n + 1 + (3 if n >= POSPLIT else 0)

                def po_vec_part(e, lo, hi):
                    o = wcols["o"]
                    for n in range(lo, hi):
                        e.wait_ge(polds[n % 2].h,
                                  POL0[n % 2] + 16 * (n // 2 + 1))
                        if n >= 2:
                            e.wait_ge(th.h, POTH0 + n - 1)
                        e.tensor_scalar(po_sel[:], po_raw[n % 2][:],
                                        o["thp"][:], None, OP.is_gt)
                        e.scalar_tensor_tensor(po_t[n % 2][:],
                                               po_raw[n % 2][:],
                                               o["thn"][:], po_sel[:],
                                               OP.is_lt, OP.subtract
                                               ).then_inc(vv.h, 1)

                def po_vec1(e):
                    po_vec_part(e, 0, POSPLIT)
                em("vector", po_vec1)
                vv.add(POSPLIT)

                def po_mm_part(e, lo, hi):
                    for n in range(lo, hi):
                        e.wait_ge(vv.h, tvv(n))
                        if n >= 2:
                            e.wait_ge(sq.h, POSQ0 + n - 1)
                        b = (n % 2) * 8 * P
                        for j in range(8):
                            i_ = e.transpose(
                                PSW[:, b + j * P:b + (j + 1) * P],
                                po_t[n % 2][:, j * P:(j + 1) * P],
                                ident_bf[:])
                        i_.then_inc(th.h, 1)

                def po_mm1(e):
                    po_mm_part(e, 0, POSPLIT)
                em("tensor", po_mm1)
                th.add(POSPLIT)

                _po_n = [0]

                def po_drain(e):
                    n = _po_n[0]
                    if n >= NPU:
                        return
                    _po_n[0] = n + 1
                    e.wait_ge(th.h, POTH0 + n + 1)
                    if n >= 2:
                        e.wait_ge(posts[n % 2].h,
                                  POS0[n % 2] + 16 * (n // 2))
                    e.activation(po_stg[n % 2][:],
                                 PSW[:, (n % 2) * 8 * P:(n % 2 + 1) * 8 * P],
                                 AF.Identity).then_inc(sq.h, 1)
                PO_DRAIN[0] = po_drain
                sq.add(NPU)
        def po_tail(e):
            while _po_n[0] < NPU:
                PO_DRAIN[0](e)
        em("scalar", po_tail)
        VV_GAMMA = vv.val

        set_scope("p1a")
        bld0 = ldx.val
        bcast0 = vv.val
        btr0 = trq.val

        def b_sync(e):
            for i in range(min(2, NCH)):
                e.dma_start(xld[i][:], b_hbm.ap()[i * P:(i + 1) * P, :]
                            ).then_inc(ldx.h, 16)
            for i in range(NCH):
                e.wait_ge(vv.h, bcast0 + i + 1)
                for dc in range(DC):
                    e.dma_start_transpose(
                        bt_sb[:, dc * NS + i * P:dc * NS + (i + 1) * P],
                        xbf[i % 2][:, dc * P:(dc + 1) * P]).then_inc(trq.h, 16)
                if i + 2 < NCH:
                    e.dma_start(xld[i % 2][:],
                                b_hbm.ap()[(i + 2) * P:(i + 3) * P, :]
                                ).then_inc(ldx.h, 16)
        em("sync", b_sync)
        ldx.add(16 * NCH)
        TRQ_BT = trq.add(16 * DC * NCH)

        def b_vec(e):
            e.wait_ge(vv.h, VV_GAMMA)
            for i in range(NCH):
                e.wait_ge(ldx.h, bld0 + 16 * (i + 1))
                if i >= 2:
                    e.wait_ge(trq.h, btr0 + 16 * DC * (i - 1))
                e.tensor_copy(xbf[i % 2][:], xld[i % 2][:]).then_inc(vv.h, 1)
        em("vector", b_vec)
        vv.add(NCH)
        VV_BCAST = vv.val

        xld0 = ldx.val
        xcast0 = vv.val
        xtr0 = trq.val
        z1mm0 = mm.val
        SS_TANH0 = None

        def x_sync(e):
            def load(k):
                if k < 2:
                    e.wait_ge(vv.h, VV_BCAST)
                e.dma_start(xld[k % 2][:],
                            x_hbm.ap()[k * P:(k + 1) * P, :]
                            ).then_inc(ldx.h, 16)
            for k in range(min(2, TT)):
                load(k)
            k = 0
            for s_ in range(NSLAB):
                if s_ >= 1:
                    e.wait_ge(mm.h, z1mm0 + s_ * NCH)
                for j in range(TPS):
                    e.wait_ge(vv.h, xcast0 + k + 1)
                    for dc in range(DC):
                        e.dma_start_transpose(
                            xT_sb[:, dc * SLAB + j * P:dc * SLAB + (j + 1) * P],
                            xbf[k % 2][:, dc * P:(dc + 1) * P]).then_inc(trq.h, 16)
                    if k + 2 < TT:
                        load(k + 2)
                    k += 1
        em("sync", x_sync)
        ldx.add(16 * TT)
        TRQ_XT = trq.add(16 * DC * TT)

        def x_vec(e):
            e.wait_ge(trq.h, TRQ_BT)
            k = 0
            for s_ in range(NSLAB):
                for j in range(TPS):
                    e.wait_ge(ldx.h, xld0 + 16 * (k + 1))
                    if k >= 2:
                        e.wait_ge(trq.h, xtr0 + 16 * DC * (k - 1))
                    e.tensor_copy(xbf[k % 2][:], xld[k % 2][:]).then_inc(vv.h, 1)
                    k += 1
        em("vector", x_vec)
        vv.add(TT)

        def z1_mm(e):
            e.wait_ge(trq.h, TRQ_BT)
            k = 0
            for s_ in range(NSLAB):
                e.wait_ge(trq.h, xtr0 + 16 * DC * TPS * (s_ + 1))
                for nch in range(NCH):
                    ps = pbank(4 + (k % 2), SLAB)
                    if k >= 2:
                        e.wait_ge(ss.h, SS_TANH0 + k - 1)
                    for dc in range(DC):
                        i_ = e.matmul(
                            ps,
                            bt_sb[:, dc * NS + nch * P:dc * NS + (nch + 1) * P],
                            xT_sb[:, dc * SLAB:(dc + 1) * SLAB],
                            start=(dc == 0), stop=(dc == DC - 1))
                    i_.then_inc(mm.h, 1)
                    k += 1
        em("tensor", z1_mm)
        mm.add(NSLAB * NCH)

        SS_TANH0 = ss.val

        def z1_tanh(e):
            k = 0
            for s_ in range(NSLAB):
                for nch in range(NCH):
                    e.wait_ge(mm.h, z1mm0 + k + 1)
                    e.activation(
                        state_sb[:, nch * T + s_ * SLAB:nch * T + (s_ + 1) * SLAB],
                        pbank(4 + (k % 2), SLAB), AF.Tanh).then_inc(ss.h, 1)
                    k += 1
        em("scalar", z1_tanh)
        ss.add(NSLAB * NCH)
        SS_P1A = ss.val

        set_scope("p1b")
        cld0 = ldx.val
        ccast0 = vv.val
        ctr0 = trq.val

        def c_sync(e):
            e.wait_ge(trq.h, TRQ_XT)
            for i in range(min(2, DC)):
                e.dma_start(cld[i][:], c_hbm.ap()[i * P:(i + 1) * P, :]
                            ).then_inc(ldx.h, 16)
            e.wait_ge(mm.h, z1mm0 + NSLAB * NCH)
            for i in range(DC):
                e.wait_ge(vv.h, ccast0 + i + 1)
                for nch in range(NCH):
                    e.dma_start_transpose(
                        ct_sb[:, nch * D + i * P:nch * D + (i + 1) * P],
                        cbf[i % 2][:, nch * P:(nch + 1) * P]).then_inc(trq.h, 16)
                if i + 2 < DC:
                    e.dma_start(cld[i % 2][:],
                                c_hbm.ap()[(i + 2) * P:(i + 3) * P, :]
                                ).then_inc(ldx.h, 16)
        em("sync", c_sync)
        ldx.add(16 * DC)
        TRQ_CT = trq.add(16 * NCH * DC)

        def c_vec(e):
            for i in range(DC):
                e.wait_ge(ldx.h, cld0 + 16 * (i + 1))
                if i >= 2:
                    e.wait_ge(trq.h, ctr0 + 16 * NCH * (i - 1))
                e.tensor_copy(cbf[i % 2][:], cld[i % 2][:]).then_inc(vv.h, 1)
        em("vector", c_vec)
        vv.add(DC)

        x2ld0 = ldx.val
        P1B_V0 = None
        P1B_EV0 = None

        def x2_sync(e):
            e.wait_ge(vv.h, xcast0 + TT)
            for tt in range(min(2, TT)):
                e.dma_start(xld[tt][:], x_hbm.ap()[tt * P:(tt + 1) * P, :]
                            ).then_inc(ldx.h, 16)
        em("sync", x2_sync)
        ldx.add(16 * min(2, TT))

        mm2_0 = mm.val

        def mm2(e):
            e.wait_ge(trq.h, TRQ_CT)
            e.wait_ge(ss.h, SS_P1A)
            for tt in range(TT):
                if tt >= 1:
                    e.wait_ge(ev.h, P1B_EV0 + tt)
                for ds in range(NDS):
                    for nch in range(NCH):
                        i_ = e.matmul(
                            pbank(ds, DS),
                            state_sb[:, nch * T + tt * P:nch * T + (tt + 1) * P],
                            ct_sb[:, nch * D + ds * DS:nch * D + (ds + 1) * DS],
                            start=(nch == 0), stop=(nch == NCH - 1))
                i_.then_inc(mm.h, 1)
        em("tensor", mm2)
        mm.add(TT)

        P1B_V0 = vv.val
        P1B_EV0 = ev.val
        SS_SQ0 = None
        QYTR0 = None

        def p1b_vec(e):
            for _h, _v in PO_ST_END:
                e.wait_ge(_h, _v)
            for tt in range(TT):
                e.wait_ge(ldx.h, x2ld0 + 16 * (tt + 1))
                e.wait_ge(mm.h, mm2_0 + tt + 1)
                w = wbuf[tt % 2]
                if tt >= 2:
                    e.wait_ge(ss.h, SS_SQ0 + tt - 1)
                if tt >= 1:
                    e.wait_ge(sty.h, 16 * tt)
                e.tensor_tensor(w[:], xld[tt % 2][:], rep8k[:], OP.mult)
                e.tensor_tensor(w[:], w[:], PS[:, 0:D], OP.add).then_inc(ev.h, 1)
                e.reduce_sum(mcol[:], w[:], axis=AX)
                e.tensor_scalar_mul(mcol[:], mcol[:], 1.0 / D)
                e.tensor_copy(m_all[:, tt:tt + 1], mcol[:])
                e.tensor_scalar(w[:], w[:], mcol[:], None, OP.subtract)
                e.reduce_max(M_all[:, tt:tt + 1], w[:], axis=AX,
                             apply_absolute_value=True)
                e.tensor_copy(ycbf[:], w[:]).then_inc(vv.h, 1)
                e.tensor_scalar(tcol[:], M_all[:, tt:tt + 1], THR_M, None, OP.max)
                e.reciprocal(tcol[:], tcol[:])
                e.tensor_scalar_mul(scol[:], tcol[:], 127.0)
                if tt >= 1:
                    e.wait_ge(trq.h, QYTR0 + 16 * DC * tt)
                if SIM_MODE:
                    e.tensor_scalar(t256[:], w[:], scol[:], 256.5,
                                    OP.mult, OP.add)
                    e.tensor_scalar(w[:], t256[:], 1.0, None, OP.mod)
                    e.scalar_tensor_tensor(qybf[:], t256[:], -256.0, w[:],
                                           OP.add, OP.subtract).then_inc(vv.h, 1)
                else:
                    e.tensor_scalar(t256[:], w[:], scol[:], None, OP.mult)
                    e.tensor_copy(ib_q[:], t256[:])
                    e.tensor_copy(qybf[:], ib_q[:]).then_inc(vv.h, 1)
        em("vector", p1b_vec)
        vv.add(2 * TT)
        ev.add(TT)

        SS_SQ0 = ss.val

        def p1b_scal(e):
            for tt in range(TT):
                e.wait_ge(vv.h, P1B_V0 + 2 * tt + 1)
                e.activation(sq_scr[:], wbuf[tt % 2][:], AF.Square,
                             accum_out=ssq_all[:, tt:tt + 1]).then_inc(ss.h, 1)
        em("scalar", p1b_scal)
        ss.add(TT)

        def p1b_gp(e):
            for tt in range(TT):
                e.wait_ge(vv.h, P1B_V0 + 2 * tt + 1)
                e.dma_start(yc_sp.ap()[tt * P:(tt + 1) * P, :], ycbf[:]
                            ).then_inc(sty.h, 16)
        em("gpsimd", p1b_gp)
        STY_DONE = sty.add(16 * TT)

        if dbg:
            def dbg1_gp(e):
                e.wait_ge(ss.h, SS_P1A)
                e.dma_start(state_d.ap(), state_sb[:]).then_inc(gg.h, 16)
            em("gpsimd", dbg1_gp)
            gg.add(16)

        QYTR0 = trq.val

        def qy_tr(e):
            for tt in range(TT):
                e.wait_ge(vv.h, P1B_V0 + 2 * tt + 2)
                for dc in range(DC):
                    e.dma_start_transpose(
                        qyT_sb[:, dc * T + tt * P:dc * T + (tt + 1) * P],
                        qybf[:, dc * P:(dc + 1) * P]).then_inc(trq.h, 16)
                if tt + 2 < TT:
                    e.dma_start(xld[tt % 2][:],
                                x_hbm.ap()[(tt + 2) * P:(tt + 3) * P, :]
                                ).then_inc(ldx.h, 16)
        em("sync", qy_tr)
        TRQ_QY = trq.add(16 * DC * TT)
        ldx.add(16 * max(0, TT - 2))

        set_scope("p1e")
        TRPC = 4 * BANK

        def p1e_scal(e):
            e.wait_ge(vv.h, P1B_V0 + 2 * TT)
            e.activation(rstd_all[:], ssq_all[:], AF.Sqrt,
                         scale=1.0 / D, bias=eps_col[:]).then_inc(ss.h, 1)
        em("scalar", p1e_scal)
        SS_RSTD = ss.add(1)

        def p1e_vec(e):
            e.wait_ge(ss.h, SS_RSTD)
            e.reciprocal(rstd_all[:], rstd_all[:])
            e.tensor_scalar(rho_all[:], M_all[:], THR_M, None, OP.max)
            e.tensor_tensor(rho_all[:], rho_all[:], rstd_all[:], OP.mult)
            e.tensor_scalar_mul(rho_all[:], rho_all[:], 1.0 / 127.0
                                ).then_inc(vv.h, 1)
        em("vector", p1e_vec)
        VV_RHO = vv.add(1)

        BCC = 5 * BANK
        RB_M0 = mm.val
        RB_V0 = vv.val

        def p1e_mmT(e):
            e.wait_ge(vv.h, VV_RHO)
            for tt in range(TT):
                if tt > 0:
                    e.wait_ge(vv.h, RB_V0 + 2 * tt - 1)
                e.transpose(PS[0:1, TRPC:TRPC + P], rho_all[:, tt:tt + 1],
                            ident[:]).then_inc(mm.h, 1)
                e.wait_ge(vv.h, RB_V0 + 2 * tt + 1)
                e.matmul(PS[:, BCC:BCC + P], ones_row[:], rowb[:],
                         start=True, stop=True).then_inc(mm.h, 1)
        em("tensor", p1e_mmT)
        mm.add(2 * TT)

        def p1e_vec2(e):
            for tt in range(TT):
                e.wait_ge(mm.h, RB_M0 + 2 * tt + 1)
                e.tensor_copy(rowb[:], PS[0:1, TRPC:TRPC + P]).then_inc(vv.h, 1)
                e.wait_ge(mm.h, RB_M0 + 2 * tt + 2)
                e.tensor_copy(rho_rep[:, tt * P:(tt + 1) * P],
                              PS[:, BCC:BCC + P]).then_inc(vv.h, 1)
        em("vector", p1e_vec2)
        RHOREP_V = vv.add(2 * TT)

        def mrun_vec(e):
            e.memset(mrun[:], 0.0).then_inc(vv.h, 1)
        em("vector", mrun_vec)
        VV_MRUN = vv.add(1)

        set_scope("p2")
        th, sq = sems["th"], sems["sq"]
        S4 = [sth, sems["sty"], sems["stp"], sems["sto"]]
        B4 = [x.val for x in S4]
        ldg = [ldw, sems["ga"]]
        ldv = [sems["ts2"], ldx]
        P2LDG = [ldg[0].val, ldg[1].val]
        P2LDV = [ldv[0].val, ldv[1].val]
        P2TH0 = th.val
        P2SQ0 = sq.val
        P2TERN0 = vv.val + 0
        P2MM0 = mm.val
        P2EV0 = ev.val
        P2SS0 = ss.val
        P2HV0 = None
        RPS = (1, 1, 1, 1)

        def p2_sync(e):
            e.wait_ge(vv.h, RHOREP_V)
            if dbg:
                e.wait_ge(gg.h, 16)
            for hc in range(HC):
                if hc >= 2:
                    e.wait_ge(vv.h, P2TERN0 + 2 * (hc - 1))
                e.dma_start(graw[hc % 2][:], gw_hbm.ap()[hc * P:(hc + 1) * P, :]
                            ).then_inc(ldg[hc % 2].h, 16)
                e.dma_start(vraw[hc % 2][:], vw_hbm.ap()[hc * P:(hc + 1) * P, :]
                            ).then_inc(ldv[hc % 2].h, 16)
        em("sync", p2_sync)
        for _s in (*ldg, *ldv):
            _s.add(16 * (HC // 2))

        P2TERN0 = vv.val

        def _tround(e, hc, r):
            R = 4 * hc + r
            src = gtb[hc % 2] if r < 2 else vtb[hc % 2]
            j = r % 2
            e.wait_ge(vv.h, P2TERN0 + 2 * hc + (1 if r < 2 else 2))
            if R >= 2:
                e.wait_ge(sq.h, P2SQ0 + R - 1)
            b = (R % 2) * 8 * P
            for i in range(8):
                i_ = e.transpose(PSW[:, b + i * P:b + (i + 1) * P],
                                 src[:, (8 * j + i) * P:(8 * j + i + 1) * P],
                                 ident_bf[:])
            i_.then_inc(th.h, 1)

        def p2_mm(e):
            e.wait_ge(trq.h, TRQ_QY)
            for r in range(4):
                _tround(e, 0, r)
            for hc in range(HC):
                e.wait_ge(sq.h, P2SQ0 + 4 * (hc + 1))
                rnext = 0
                for s_ in range(NSLAB):
                    k = hc * NSLAB + s_
                    if k >= 3:
                        e.wait_ge(ev.h, P2EV0 + k - 2)
                    for dc in range(DC):
                        e.matmul(pbank(k % 3, SLAB),
                                 gwT[hc % 2][:, dc * P:(dc + 1) * P],
                                 qyT_sb[:, dc * T + s_ * SLAB:dc * T + (s_ + 1) * SLAB],
                                 start=(dc == 0), stop=(dc == DC - 1))
                    for dc in range(DC):
                        i_ = e.matmul(pbank(3 + k % 3, SLAB),
                                      vwT[hc % 2][:, dc * P:(dc + 1) * P],
                                      qyT_sb[:, dc * T + s_ * SLAB:dc * T + (s_ + 1) * SLAB],
                                      start=(dc == 0), stop=(dc == DC - 1))
                    i_.then_inc(mm.h, 1)
                    if hc + 1 < HC:
                        for _ in range(RPS[s_]):
                            _tround(e, hc + 1, rnext)
                            rnext += 1
        em("tensor", p2_mm)
        mm.add(HC * NSLAB)
        th.add(4 * HC)

        P2HV0 = hv.val

        def p2_vec(e):
            e.wait_ge(vv.h, VV_MRUN)

            def tern(hc):
                e.wait_ge(ldg[hc % 2].h, P2LDG[hc % 2] + 16 * (hc // 2 + 1))
                if hc >= 2:
                    e.wait_ge(th.h, P2TH0 + 4 * hc - 6)
                e.tensor_scalar(dsel[:], graw[hc % 2][:],
                                wcols["g"]["thp"][:], None, OP.is_gt)
                e.scalar_tensor_tensor(gtb[hc % 2][:], graw[hc % 2][:],
                                       wcols["g"]["thn"][:], dsel[:],
                                       OP.is_lt, OP.subtract).then_inc(vv.h, 1)
                e.wait_ge(ldv[hc % 2].h, P2LDV[hc % 2] + 16 * (hc // 2 + 1))
                if hc >= 2:
                    e.wait_ge(th.h, P2TH0 + 4 * hc - 4)
                e.tensor_scalar(dsel[:], vraw[hc % 2][:],
                                wcols["v"]["thp"][:], None, OP.is_gt)
                e.scalar_tensor_tensor(vtb[hc % 2][:], vraw[hc % 2][:],
                                       wcols["v"]["thn"][:], dsel[:],
                                       OP.is_lt, OP.subtract).then_inc(vv.h, 1)

            def epi(hc):
                for s_ in range(NSLAB):
                    k = hc * NSLAB + s_
                    e.wait_ge(mm.h, P2MM0 + k + 1)
                    if hc >= 1:
                        e.wait_ge(S4[s_].h, B4[s_] + 16 * hc)
                    e.tensor_tensor(gsl[s_ % 2][:], pbank(k % 3, SLAB),
                                    rho_rep[:, s_ * SLAB:(s_ + 1) * SLAB],
                                    OP.mult)
                    e.tensor_tensor(vsl[s_ % 2][:], pbank(3 + k % 3, SLAB),
                                    rho_rep[:, s_ * SLAB:(s_ + 1) * SLAB],
                                    OP.mult).then_inc(ev.h, 1)
                    e.wait_ge(ss.h, P2SS0 + 2 * k + 2)
                    e.tensor_tensor(hsl[s_][:], gsl[s_ % 2][:],
                                    vsl[s_ % 2][:], OP.mult)
                    e.tensor_tensor(mrun[:, s_ * SLAB:(s_ + 1) * SLAB],
                                    mrun[:, s_ * SLAB:(s_ + 1) * SLAB],
                                    hsl[s_][:], OP.max)
                    e.scalar_tensor_tensor(mrun[:, s_ * SLAB:(s_ + 1) * SLAB],
                                           hsl[s_][:], -1.0,
                                           mrun[:, s_ * SLAB:(s_ + 1) * SLAB],
                                           OP.mult, OP.max).then_inc(hv.h, 1)

            tern(0)
            tern(1)
            for hc in range(HC):
                if hc + 2 < HC:
                    tern(hc + 2)
                epi(hc)
        em("vector", p2_vec)
        vv.add(2 * HC)
        hv.add(HC * NSLAB)
        ev.add(HC * NSLAB)

        def p2_scal(e):
            def evac(hc, r):
                R = 4 * hc + r
                e.wait_ge(th.h, P2TH0 + R + 1)
                if hc >= 2:
                    e.wait_ge(mm.h, P2MM0 + NSLAB * (hc - 1))
                dst = gwT if r < 2 else vwT
                j = r % 2
                b = (R % 2) * 8 * P
                e.activation(dst[hc % 2][:, j * 8 * P:(j + 1) * 8 * P],
                             PSW[:, b:b + 8 * P], AF.Identity
                             ).then_inc(sq.h, 1)

            def silu_slab(hc, s_):
                k = hc * NSLAB + s_
                e.wait_ge(ev.h, P2EV0 + k + 1)
                e.activation(gsl[s_ % 2][:], gsl[s_ % 2][:], AF.Silu,
                             bias=gb_sb[:, hc:hc + 1],
                             scale=wcols["g"]["nsc"][:]).then_inc(ss.h, 1)
                e.activation(vsl[s_ % 2][:], vsl[s_ % 2][:], AF.Identity,
                             bias=vb_sb[:, hc:hc + 1],
                             scale=wcols["v"]["nsc"][:]).then_inc(ss.h, 1)

            for r in range(4):
                evac(0, r)
            for hc in range(1, HC):
                evac(hc, 0)
                evac(hc, 1)
                silu_slab(hc - 1, 0)
                evac(hc, 2)
                silu_slab(hc - 1, 1)
                evac(hc, 3)
                silu_slab(hc - 1, 2)
                silu_slab(hc - 1, 3)
            for s_ in range(NSLAB):
                silu_slab(HC - 1, s_)
        em("scalar", p2_scal)
        ss.add(2 * HC * NSLAB)
        sq.add(4 * HC)

        def p2_gp(e):
            for hc in range(HC):
                for s_ in range(NSLAB):
                    k = hc * NSLAB + s_
                    e.wait_ge(hv.h, P2HV0 + k + 1)
                    if hc >= 1:
                        e.wait_ge(S4[s_].h, B4[s_] + 16 * hc)
                    e.dma_start(h_sp.ap()[s_ * TPS:(s_ + 1) * TPS, hc, :, :]
                                .rearrange("a p b -> p a b"),
                                hsl[s_][:].rearrange("p (a b) -> p a b",
                                                     a=TPS)
                                ).then_inc(S4[s_].h, 16)
        em("gpsimd", p2_gp)
        SPILL_DONE = [(x.h, x.add(16 * HC)) for x in S4]

        VV_SREP_HOLDER = [None]
        if dbg:
            GG_D2 = gg.val + 13 * 16

            def dbg2_gp(e):
                e.wait_ge(trq.h, TRQ_QY)
                e.wait_ge(vv.h, VV_SREP_HOLDER[0])
                e.dma_start(qyT_d.ap(), qyT_sb[:]).then_inc(gg.h, 16)
                e.dma_start(rho_d.ap(), rho_rep[:]).then_inc(gg.h, 16)
                e.dma_start(srep_d.ap(), s_h_rep[:]).then_inc(gg.h, 16)
                e.dma_start(sc3_d.ap(), sc3_all[:]).then_inc(gg.h, 16)
                for j, (wn, kk) in enumerate(
                        (w_, k_) for w_ in ("g", "v", "o")
                        for k_ in ("thp", "thn", "nsc")):
                    e.dma_start(gcol_d.ap()[:, j:j + 1], wcols[wn][kk][:],
                                allow_slow_non_contiguous=True
                                ).then_inc(gg.h, 16)
                e.wait_ge(gg.h, GG_D2)
            em("gpsimd", dbg2_gp)
            gg.add(13 * 16)

        set_scope("p2e")
        BCC = 5 * BANK
        SR_M0 = mm.val
        SR_V0 = vv.val + 1

        def p2e_mm(e):
            e.wait_ge(hv.h, P2HV0 + HC * NSLAB)
            for tt in range(TT):
                if tt > 0:
                    e.wait_ge(vv.h, SR_V0 + 3 * (tt - 1) + 1)
                e.transpose(PS[:, TRPC:TRPC + P], mrun[:, tt * P:(tt + 1) * P],
                            ident[:]).then_inc(mm.h, 1)
                e.wait_ge(vv.h, SR_V0 + 3 * tt + 1)
                e.transpose(PS[0:1, TRPC + P:TRPC + 2 * P], scol[:],
                            ident[:]).then_inc(mm.h, 1)
                e.wait_ge(vv.h, SR_V0 + 3 * tt + 2)
                e.matmul(PS[:, BCC:BCC + P], ones_row[:], rowb[:],
                         start=True, stop=True).then_inc(mm.h, 1)
        em("tensor", p2e_mm)
        mm.add(3 * TT)

        def p2e_vec(e):
            e.tensor_scalar_mul(tcol2[:], wcols["o"]["nsc"][:], 1.0 / 127.0
                                ).then_inc(vv.h, 1)
            for tt in range(TT):
                e.wait_ge(mm.h, SR_M0 + 3 * tt + 1)
                e.reduce_max(tcol[:], PS[:, TRPC:TRPC + P], axis=AX)
                e.tensor_scalar(tcol[:], tcol[:], EPS, None, OP.max)
                e.tensor_scalar(sc3_all[:, tt:tt + 1], tcol[:], tcol2[:],
                                None, OP.mult)
                e.reciprocal(tcol[:], tcol[:])
                e.tensor_scalar_mul(scol[:], tcol[:], 127.0).then_inc(vv.h, 1)
                e.wait_ge(mm.h, SR_M0 + 3 * tt + 2)
                e.tensor_copy(rowb[:], PS[0:1, TRPC + P:TRPC + 2 * P]
                              ).then_inc(vv.h, 1)
                e.wait_ge(mm.h, SR_M0 + 3 * tt + 3)
                e.tensor_copy(s_h_rep[:, tt * P:(tt + 1) * P],
                              PS[:, BCC:BCC + P]).then_inc(vv.h, 1)
        em("vector", p2e_vec)
        VV_SREP = vv.add(1 + 3 * TT)
        VV_SREP_HOLDER[0] = VV_SREP

        def ob_sync(e):
            e.wait_ge(vv.h, VV_SREP)
            e.dma_start(rep8k[0:1, :], ob_hbm.ap()).then_inc(ldx.h, 16)
        em("sync", ob_sync)
        LDX_OB = ldx.add(16)

        def ob_mm(e):
            e.wait_ge(ldx.h, LDX_OB)
            for ds in range(NDS):
                e.matmul(pbank(ds, DS), ones_row[:],
                         rep8k[0:1, ds * DS:(ds + 1) * DS],
                         start=True, stop=True).then_inc(mm.h, 1)
        em("tensor", ob_mm)
        MM_OBREP = mm.add(NDS)

        def ob_vec(e):
            e.wait_ge(mm.h, MM_OBREP)
            for ds in range(NDS):
                e.tensor_copy(rep8k[:, ds * DS:(ds + 1) * DS], pbank(ds, DS))
            e.engine_nop().then_inc(vv.h, 1)
        em("vector", ob_vec)
        VV_OBREP = vv.add(1)

        set_scope("p3")
        TR30 = trq.val
        p3 = [dict() for _ in range(NSEG)]
        for seg in range(NSEG):
            c = p3[seg]

            c["hld0"] = ldh.val
            nper = 1 + (1 if seg > 0 else 0) + (1 if seg == NSEG - 1 else 0)
            c["per_tt"] = 16 * nper
            c["evs"] = 4 if seg == NSEG - 1 else 1

            def p3h_sync(e, seg=seg, c=c):
                for _h, _v in SPILL_DONE:
                    e.wait_ge(_h, _v)
                if seg == 0:
                    e.wait_ge(vv.h, VV_OBREP)
                    if dbg:
                        e.wait_ge(gg.h, 14 * 16)
                else:
                    e.wait_ge(stp.h, p3[seg - 1]["stp_end"])
                    e.wait_ge(mm.h, p3[seg - 1]["mm_end"])
                if seg == NSEG - 1:
                    e.wait_ge(sty.h, STY_DONE)
                e.dma_start(owh[:].rearrange("p (a d) -> p a d", a=HCS),
                            owT_sp.ap()[seg * HSEG:(seg + 1) * HSEG, :]
                            .rearrange("(a p) d -> p a d", p=P)
                            ).then_inc(trq.h, 16)
                for tt in range(TT):
                    if tt >= 2:
                        e.wait_ge(vv.h, c["qv0"] + tt - 1)
                    e.dma_start(
                        hstg[tt % 2][:].rearrange("p (a b) -> p a b", a=HCS),
                        h_sp.ap()[tt, seg * HCS:(seg + 1) * HCS, :, :]
                        .rearrange("a p b -> p a b")
                    ).then_inc(ldh.h, 16)
                    if seg > 0:
                        if tt >= 1:
                            e.wait_ge(hv.h, c["ev0"] + c["evs"] * (tt - 1) + 1)
                        e.dma_start(pbuf[:],
                                    part_sp.ap()[tt * P:(tt + 1) * P, :]
                                    ).then_inc(ldh.h, 16)
                    if seg == NSEG - 1:
                        if tt >= 1:
                            e.wait_ge(hv.h, c["ev0"] + c["evs"] * (tt - 1) + 2)
                        e.dma_start(ycld[:],
                                    yc_sp.ap()[tt * P:(tt + 1) * P, :]
                                    ).then_inc(ldh.h, 16)
            em("sync", p3h_sync)
            ldh.add(c["per_tt"] * TT)
            trq.add(16)

            c["qv0"] = vv.val
            c["ev0"] = hv.val
            c["mm0"] = mm.val
            c["stp0"] = stp.val
            c["ss0"] = ss.val
            last = seg == NSEG - 1

            def p3_vec(e, seg=seg, c=c, last=last):
                def q(tt):
                    e.wait_ge(ldh.h, c["hld0"] + c["per_tt"] * tt + 16)
                    if tt >= 2:
                        e.wait_ge(mm.h, c["mm0"] + tt - 1)
                    hb = hstg[tt % 2]
                    qb = qhT[tt % 2]
                    for hcl in range(HCS):
                        e.tensor_tensor(hb[:, hcl * P:(hcl + 1) * P],
                                        hb[:, hcl * P:(hcl + 1) * P],
                                        s_h_rep[:, tt * P:(tt + 1) * P],
                                        OP.mult)
                    if SIM_MODE:
                        e.tensor_scalar(hb[:], hb[:], 256.5, None, OP.add)
                        e.tensor_scalar(qb[:], hb[:], 1.0, None, OP.mod)
                        e.scalar_tensor_tensor(qb[:], hb[:], -256.0, qb[:],
                                               OP.add, OP.subtract
                                               ).then_inc(vv.h, 1)
                    else:
                        e.tensor_copy(ib_h[tt % 2][:], hb[:])
                        e.tensor_copy(qb[:], ib_h[tt % 2][:]).then_inc(vv.h, 1)

                def evac(tt):
                    e.wait_ge(mm.h, c["mm0"] + tt + 1)
                    if seg > 0:
                        e.wait_ge(ldh.h, c["hld0"] + c["per_tt"] * tt + 32)
                    if last:
                        e.wait_ge(ldh.h, c["hld0"] + c["per_tt"] * (tt + 1))
                    o = obuf[tt % 2]
                    if not last:
                        if tt >= 2:
                            e.wait_ge(stp.h, c["stp0"] + 16 * (tt - 1))
                        for ds in range(NDS):
                            if seg == 0:
                                e.tensor_copy(o[:, ds * DS:(ds + 1) * DS],
                                              pbank(ds, DS))
                            else:
                                e.tensor_tensor(o[:, ds * DS:(ds + 1) * DS],
                                                pbank(ds, DS),
                                                pbuf[:, ds * DS:(ds + 1) * DS],
                                                OP.add)
                        e.engine_nop().then_inc(hv.h, 1)
                        return
                    if tt >= 2:
                        e.wait_ge(sto.h, SO3 + 16 * (tt - 1))
                    for ds in range(NDS):
                        e.tensor_tensor(o[:, ds * DS:(ds + 1) * DS],
                                        pbank(ds, DS),
                                        pbuf[:, ds * DS:(ds + 1) * DS],
                                        OP.add)
                    e.engine_nop().then_inc(hv.h, 1)
                    e.tensor_scalar(o[:], o[:], sc3_all[:, tt:tt + 1],
                                    None, OP.mult)
                    e.tensor_tensor(o[:], o[:], rep8k[:], OP.add)
                    e.scalar_tensor_tensor(o[:], ycld[:],
                                           rstd_all[:, tt:tt + 1], o[:],
                                           OP.mult, OP.add)
                    e.reduce_sum(mcol[:], o[:], axis=AX)
                    e.tensor_scalar_mul(mcol[:], mcol[:], 1.0 / D
                                        ).then_inc(hv.h, 1)
                    e.wait_ge(ss.h, c["ss0"] + 3 * tt + 1)
                    e.tensor_scalar_mul(tcol[:], tcol[:], 1.0 / D)
                    e.tensor_tensor(tcol2[:], mcol[:], mcol[:], OP.mult)
                    e.tensor_tensor(tcol[:], tcol[:], tcol2[:],
                                    OP.subtract).then_inc(hv.h, 1)
                    e.wait_ge(ss.h, c["ss0"] + 3 * tt + 2)
                    e.reciprocal(tcol[:], tcol[:])
                    e.tensor_tensor(tcol2[:], mcol[:], tcol[:], OP.mult)
                    e.tensor_scalar_mul(tcol2[:], tcol2[:], -1.0
                                        ).then_inc(hv.h, 1)

                for tt in range(TT):
                    q(tt)
                    if tt >= 1:
                        evac(tt - 1)
                evac(TT - 1)
            em("vector", p3_vec)
            vv.add(TT)
            hv.add(c["evs"] * TT)

            def p3_mm(e, seg=seg, c=c):
                e.wait_ge(trq.h, TR30 + 16 * (seg + 1))
                for tt in range(TT):
                    e.wait_ge(vv.h, c["qv0"] + tt + 1)
                    if tt >= 1:
                        e.wait_ge(hv.h, c["ev0"] + c["evs"] * (tt - 1) + 1)
                    for hcl in range(HCS):
                        for ds in range(NDS):
                            i_ = e.matmul(
                                pbank(ds, DS),
                                qhT[tt % 2][:, hcl * P:(hcl + 1) * P],
                                owh[:, hcl * D + ds * DS:hcl * D + (ds + 1) * DS],
                                start=(hcl == 0), stop=(hcl == HCS - 1))
                    i_.then_inc(mm.h, 1)
            em("tensor", p3_mm)
            c["mm_end"] = mm.add(TT)

            if not last:
                def p3e_gp(e, seg=seg, c=c):
                    for tt in range(TT):
                        e.wait_ge(hv.h, c["ev0"] + tt + 1)
                        e.dma_start(part_sp.ap()[tt * P:(tt + 1) * P, :],
                                    obuf[tt % 2][:]).then_inc(stp.h, 16)
                em("gpsimd", p3e_gp)
                c["stp_end"] = stp.add(16 * TT)
            else:
                SO3 = sto.val

                def p3f_scal(e, c=c):
                    for tt in range(TT):
                        e.wait_ge(hv.h, c["ev0"] + 4 * tt + 2)
                        e.activation(sq_scr[:], obuf[tt % 2][:], AF.Square,
                                     accum_out=tcol[:]).then_inc(ss.h, 1)
                        e.wait_ge(hv.h, c["ev0"] + 4 * tt + 3)
                        e.activation(tcol[:], tcol[:], AF.Sqrt,
                                     bias=eps_col[:]).then_inc(ss.h, 1)
                        e.wait_ge(hv.h, c["ev0"] + 4 * tt + 4)
                        e.activation(obuf[tt % 2][:], obuf[tt % 2][:],
                                     AF.Identity, bias=tcol2[:],
                                     scale=tcol[:]).then_inc(ss.h, 1)
                em("scalar", p3f_scal)
                ss.add(3 * TT)

                def p3f_gp(e, c=c):
                    for tt in range(TT):
                        e.wait_ge(ss.h, c["ss0"] + 3 * tt + 3)
                        e.dma_start(out_hbm.ap()[tt * P:(tt + 1) * P, :],
                                    obuf[tt % 2][:]).then_inc(sto.h, 16)
                    e.wait_ge(sto.h, SO3 + 16 * TT)
                em("gpsimd", p3f_gp)
                sto.add(16 * TT)

        @block.sync
        def _(e):
            for fn in prog["sync"]:
                fn(e)

        @block.scalar
        def _(e):
            g = ScalGuard(e, sems["sb"], sflush)
            for fn in prog["scalar"]:
                fn(g)

        @block.vector
        def _(e):
            g = VecGuard(e, sems["vb"])
            for fn in prog["vector"]:
                fn(g)

        @block.tensor
        def _(e):
            for fn in prog["tensor"]:
                fn(e)

        @block.gpsimd
        def _(e):
            for fn in prog["gpsimd"]:
                fn(e)

    return nc


def _prep_inputs(L0_attn_x, L0_attn_B, L0_attn_C, L0_attn_D,
                 L0_ffn_gate_w, L0_ffn_gate_b, L0_ffn_val_w, L0_ffn_val_b,
                 L0_ffn_out_w, L0_ffn_out_b, n_cores=8):
    x = np.ascontiguousarray(np.asarray(L0_attn_x, dtype=np.float32)
                             ).reshape(-1, np.asarray(L0_attn_x).shape[-1])
    T_total, D = x.shape
    T = T_total // n_cores
    H = np.asarray(L0_ffn_gate_w).shape[0]
    NS = np.asarray(L0_attn_B).shape[0]
    HC = H // P

    common = {
        "bw": np.ascontiguousarray(L0_attn_B, dtype=np.float32),
        "cw": np.ascontiguousarray(L0_attn_C, dtype=np.float32),
        "dv": np.ascontiguousarray(L0_attn_D, dtype=np.float32).reshape(1, D),
        "gw": np.ascontiguousarray(L0_ffn_gate_w, dtype=np.float32),
        "vw": np.ascontiguousarray(L0_ffn_val_w, dtype=np.float32),
        "ow": np.ascontiguousarray(L0_ffn_out_w, dtype=np.float32),
        "gb": np.ascontiguousarray(
            np.asarray(L0_ffn_gate_b, dtype=np.float32).reshape(HC, P).T),
        "vb": np.ascontiguousarray(
            np.asarray(L0_ffn_val_b, dtype=np.float32).reshape(HC, P).T),
        "ob": np.ascontiguousarray(L0_ffn_out_b, dtype=np.float32).reshape(1, D),
        "idm": np.eye(P, dtype=np.float32),
    }
    in_maps = []
    for c in range(n_cores):
        m = dict(common)
        m["x"] = np.ascontiguousarray(x[c * T:(c + 1) * T])
        in_maps.append(m)
    return in_maps, (T, D, H, NS)


def kernel(L0_attn_x, L0_attn_B, L0_attn_C, L0_attn_D,
           L0_ffn_gate_w, L0_ffn_gate_b, L0_ffn_val_w, L0_ffn_val_b,
           L0_ffn_out_w, L0_ffn_out_b):
    n_cores = 8
    in_maps, (T, D, H, NS) = _prep_inputs(
        L0_attn_x, L0_attn_B, L0_attn_C, L0_attn_D,
        L0_ffn_gate_w, L0_ffn_gate_b, L0_ffn_val_w, L0_ffn_val_b,
        L0_ffn_out_w, L0_ffn_out_b, n_cores)
    nc = build_nc(T, D, H, NS)
    res = run_bass_kernel_spmd(nc, in_maps, core_ids=list(range(n_cores)))
    out = np.concatenate([r["out"] for r in res.results], axis=0)
    return out.reshape(np.asarray(L0_attn_x).shape).astype(np.float32)


# version 18
# speedup vs baseline: 1.7312x; 1.1552x over previous
import os
import numpy as np

import concourse.bass as bass
import concourse.mybir as mybir
from concourse.bass_utils import run_bass_kernel_spmd

F32 = mybir.dt.float32
BF16 = mybir.dt.bfloat16
I32 = mybir.dt.int32
AF = mybir.ActivationFunctionType
OP = mybir.AluOpType
AX = mybir.AxisListType.X

SIM_MODE = (os.environ.get("BASS_SIM_FP64") == "1"
            or os.environ.get("KERNEL_SIMROUND") == "1")

EPS = 1e-5
THR_M = 1e-5 * (EPS ** 0.5)
P = 128


class Sem:
    def __init__(self, handle):
        self.h = handle
        self.val = 0

    def add(self, n):
        self.val += n
        return self.val


def _ap_range(ap):
    t = ap.tensor
    isz = 2 if t.dtype == BF16 else 4
    span = 1
    dims = list(ap.ap)
    for step, count in dims[1:]:
        span += int(step) * (int(count) - 1)
    base = 0
    space = type(t).__name__
    r = getattr(t, "manual_sbuf_range", None)
    if r is not None:
        base = int(r[0])
    lo = base + int(ap.offset) * isz
    return (space, lo, lo + span * isz)


def _overlap(a, b):
    return a[0] == b[0] and a[1] < b[2] and b[1] < a[2]


class VecGuard:

    _OPS = {
        "tensor_copy": (1, [0]), "tensor_tensor": (2, [0]),
        "tensor_add": (2, [0]), "tensor_sub": (2, [0]),
        "tensor_mul": (2, [0]), "tensor_max": (2, [0]),
        "tensor_scalar": (3, [0]), "tensor_scalar_mul": (2, [0]),
        "tensor_scalar_add": (2, [0]), "tensor_scalar_max": (2, [0]),
        "scalar_tensor_tensor": (4, [0]), "reduce_sum": (2, [0]),
        "reduce_max": (2, [0]), "reciprocal": (2, [0]),
    }

    def __init__(self, eng, sem):
        self._e = eng
        self._sem = sem
        self._writes = []

    def _flush(self):
        if self._writes:
            self._writes = []
            self._e.engine_nop().then_inc(self._sem.h, 1)
            self._e.wait_ge(self._sem.h, self._sem.add(1))

    def wait_ge(self, *a, **k):
        return self._e.wait_ge(*a, **k)

    def memset(self, ap, v):
        i = self._e.memset(ap, v)
        self._writes.append(_ap_range(ap))
        return i

    def engine_nop(self):
        return self._e.engine_nop()

    def __getattr__(self, name):
        fn = getattr(self._e, name)
        spec = self._OPS.get(name)
        if spec is None:
            return fn
        npos, outpos = spec

        def wrapped(*args, **kwargs):
            ins = []
            outs = []
            for i, a in enumerate(args[:npos + 2]):
                if hasattr(a, "tensor"):
                    (outs if i in outpos else ins).append(_ap_range(a))
            for k_, a in kwargs.items():
                if hasattr(a, "tensor"):
                    (outs if k_ == "accum_out" else ins).append(_ap_range(a))
            if any(_overlap(r, w) for r in ins for w in self._writes):
                self._flush()
            res = fn(*args, **kwargs)
            self._writes.extend(outs)
            return res
        return wrapped


class ScalGuard(VecGuard):

    _OPS = {
        "activation": (2, [0]), "copy": (2, [0]),
        "add": (2, [0]), "mul": (2, [0]),
    }

    def __init__(self, eng, sem, scratch):
        super().__init__(eng, sem)
        self._scr = scratch

    def _flush(self):
        if self._writes:
            self._writes = []
            self._e.copy(self._scr[:], self._scr[:]).then_inc(self._sem.h, 1)
            self._e.wait_ge(self._sem.h, self._sem.add(1))


def build_nc(T, D, H, NS, dbg=False):
    TT = T // P
    DC = D // P
    HC = H // P
    NCH = NS // P
    SLAB = min(512, T)
    NSLAB = T // SLAB
    TPS = SLAB // P
    DS = min(512, D)
    NDS = D // DS
    NSEG = 4
    HCS = HC // NSEG
    HSEG = H // NSEG
    GH = 4
    NG = HC // GH
    assert TT <= 64 and DC <= 64

    nc = bass.Bass("TRN2", target_bir_lowering=False,
                   detect_race_conditions=False)

    x_hbm = nc.dram_tensor("x", [T, D], F32, kind="ExternalInput")
    b_hbm = nc.dram_tensor("bw", [NS, D], F32, kind="ExternalInput")
    c_hbm = nc.dram_tensor("cw", [D, NS], F32, kind="ExternalInput")
    d_hbm = nc.dram_tensor("dv", [1, D], F32, kind="ExternalInput")
    gw_hbm = nc.dram_tensor("gw", [H, D], F32, kind="ExternalInput")
    vw_hbm = nc.dram_tensor("vw", [H, D], F32, kind="ExternalInput")
    ow_hbm = nc.dram_tensor("ow", [D, H], F32, kind="ExternalInput")
    gb_hbm = nc.dram_tensor("gb", [P, HC], F32, kind="ExternalInput")
    vb_hbm = nc.dram_tensor("vb", [P, HC], F32, kind="ExternalInput")
    ob_hbm = nc.dram_tensor("ob", [1, D], F32, kind="ExternalInput")
    id_hbm = nc.dram_tensor("idm", [P, P], F32, kind="ExternalInput")
    out_hbm = nc.dram_tensor("out", [T, D], F32, kind="ExternalOutput")

    _spk = "ExternalOutput" if dbg else "Internal"
    yc_sp = nc.dram_tensor("yc_sp", [T, D], BF16, kind=_spk)
    owT_sp = nc.dram_tensor("owT_sp", [H, D], BF16, kind="Internal")
    h_sp = nc.dram_tensor("h_sp", [TT, HC, P, P], F32, kind=_spk)
    part_sp = nc.dram_tensor("part_sp", [T, D], F32, kind=_spk)
    if dbg:
        state_d = nc.dram_tensor("state_d", [P, (NS // P) * T], BF16,
                                 kind="ExternalOutput")
        qyT_d = nc.dram_tensor("qyT_d", [P, (D // P) * T], BF16,
                               kind="ExternalOutput")
        rho_d = nc.dram_tensor("rho_d", [P, T], F32, kind="ExternalOutput")
        srep_d = nc.dram_tensor("srep_d", [P, T], F32, kind="ExternalOutput")
        gcol_d = nc.dram_tensor("gcol_d", [P, 16], F32, kind="ExternalOutput")
        sc3_d = nc.dram_tensor("sc3_d", [P, T // P], F32, kind="ExternalOutput")

    base0 = (int(nc.sbuf_base) + 63) & ~63
    top = int(nc.sbuf_top)
    _cur = [base0]

    def place(name, shape, dtype, at=None):
        nbytes = int(np.prod(shape[1:])) * (2 if dtype == BF16 else 4)
        if at is None:
            off = _cur[0]
            _cur[0] = (off + nbytes + 63) & ~63
        else:
            off = at
        assert off + nbytes <= top, f"SBUF overflow placing {name} at {off}"
        return nc.alloc_sbuf_tensor_at(name, list(shape), dtype, offset=off), off

    def bump():
        return _cur[0]

    ident, _ = place("ident", [P, P], F32)
    ident_bf, _ = place("ident_bf", [P, P], BF16)
    gb_sb, _ = place("gb_sb", [P, HC], F32)
    vb_sb, _ = place("vb_sb", [P, HC], F32)
    rep8k, REP8K_OFF = place("rep8k", [P, D], F32)
    gacc, GACC_OFF = place("gacc", [P, 64], F32)
    gred, _ = place("gred", [P, 1], F32)
    gfull, _ = place("gfull", [P, 1], F32)
    wcols = {}
    for wname in ("g", "v", "o"):
        wcols[wname] = {}
        for k in ("thp", "thn", "nsc"):
            wcols[wname][k], _ = place(f"{k}_{wname}", [P, 1], F32)
    m_all, _ = place("m_all", [P, TT], F32)
    M_all, _ = place("M_all", [P, TT], F32)
    ssq_all, _ = place("ssq_all", [P, TT], F32)
    rstd_all, _ = place("rstd_all", [P, TT], F32)
    rho_all, _ = place("rho_all", [P, TT], F32)
    sc3_all, _ = place("sc3_all", [P, TT], F32)
    scol, _ = place("scol", [P, 1], F32)
    mcol, _ = place("mcol", [P, 1], F32)
    tcol, _ = place("tcol", [P, 1], F32)
    tcol2, _ = place("tcol2", [P, 1], F32)
    eps_col, _ = place("eps_col", [P, 1], F32)
    ones_row, _ = place("ones_row", [1, P], F32)
    ones_col, _ = place("ones_col", [P, 1], F32)
    rowb, _ = place("rowb", [1, P], F32)
    sclr, _ = place("sclr", [1, 1], F32)
    sflush, _ = place("sflush", [1, 1], F32)

    PERS_END = bump()

    GW = min(max(D, H), 4096)
    A_SZ = max(DC * T * 2, HCS * D * 2, 2 * GW * 4 + GW * 2)
    SLOT_A = bump()
    _cur[0] = SLOT_A + ((A_SZ + 63) & ~63)
    B_SZ = max(NCH * T * 2, 4 * D * 4, 2 * HCS * P * 4 + 2 * HCS * P * 2 + D * 4)
    SLOT_B = bump()
    _cur[0] = SLOT_B + ((B_SZ + 63) & ~63)
    C_SZ = max(DC * NS * 2, NCH * D * 2,
               8 * D * 2,
               0)
    SLOT_C = bump()
    _cur[0] = SLOT_C + ((C_SZ + 63) & ~63)
    D_SZ = max(DC * SLAB * 2,
               3 * D * 4,
               6 * SLAB * 4 + 2 * T * 4,
               D * 2 + 2 * D * 4)
    SLOT_D = bump()
    _cur[0] = SLOT_D + ((D_SZ + 63) & ~63)
    E_SZ = max(2 * D * 4, D * 4)
    SLOT_E = bump()
    _cur[0] = SLOT_E + ((E_SZ + 63) & ~63)
    F_SZ = max(2 * D * 2, 2 * NS * 4, T * 4)
    SLOT_F = bump()
    _cur[0] = SLOT_F + ((F_SZ + 63) & ~63)
    G_SZ = 2 * NS * 2
    SLOT_G = bump()
    _cur[0] = SLOT_G + ((G_SZ + 63) & ~63)
    H_SZ = D * 2
    SLOT_H = bump()
    _cur[0] = SLOT_H + ((H_SZ + 63) & ~63)
    I_SZ = D * 2
    SLOT_I = bump()
    _cur[0] = SLOT_I + ((I_SZ + 63) & ~63)
    assert bump() <= top, f"arena overflow: {bump()} > {top}"

    def at(name, shape, dtype, off):
        t_, _ = place(name, shape, dtype, at=off)
        return t_

    s_h_rep = at("s_h_rep", [P, T], F32, SLOT_F)
    sq_scr = at("sq_scr", [P, D], BF16, SLOT_D + 3 * D * 4)

    gld = [at(f"gld{i}", [P, GW], F32, SLOT_A + i * GW * 4) for i in range(2)]
    gscr = at("gscr", [P, GW], BF16, SLOT_A + 2 * GW * 4)
    qyT_sb = at("qyT_sb", [P, DC * T], BF16, SLOT_A)
    state_sb = at("state_sb", [P, NCH * T], BF16, SLOT_B)
    bt_sb = at("bt_sb", [P, DC * NS], BF16, SLOT_C)
    ct_sb = at("ct_sb", [P, NCH * D], BF16, SLOT_C)
    xT_sb = at("xT_sb", [P, DC * SLAB], BF16, SLOT_D)
    wbuf = [at(f"wbuf{i}", [P, D], F32, SLOT_D + i * D * 4) for i in range(2)]
    t256 = at("t256", [P, D], F32, SLOT_D + 2 * D * 4)
    xld = [at(f"xld{i}", [P, D], F32, SLOT_E + i * D * 4) for i in range(2)]
    xbf = [at(f"xbf{i}", [P, D], BF16, SLOT_F + i * D * 2) for i in range(2)]
    cld = [at(f"cld{i}", [P, NS], F32, SLOT_F + i * NS * 4) for i in range(2)]
    cbf = [at(f"cbf{i}", [P, NS], BF16, SLOT_G + i * NS * 2) for i in range(2)]
    ycbf = at("ycbf", [P, D], BF16, SLOT_H)
    qybf = at("qybf", [P, D], BF16, SLOT_I)
    ib_q = at("ib_q", [P, D], I32, SLOT_D + 2 * D * 4)
    graw = [at(f"graw{i}", [P, D], F32, SLOT_B + i * D * 4) for i in range(2)]
    vraw = [at(f"vraw{i}", [P, D], F32, SLOT_B + (2 + i) * D * 4)
            for i in range(2)]
    gtb = [at(f"gtb{i}", [P, D], BF16, SLOT_C + i * D * 2)
           for i in range(2)]
    vtb = [at(f"vtb{i}", [P, D], BF16, SLOT_C + (2 + i) * D * 2)
           for i in range(2)]
    gwT = [at(f"gwT{i}", [P, D], BF16, SLOT_C + (4 + i) * D * 2)
           for i in range(2)]
    vwT = [at(f"vwT{i}", [P, D], BF16, SLOT_C + (6 + i) * D * 2)
           for i in range(2)]
    dsel = at("dsel", [P, D], BF16, REP8K_OFF)
    po_raw = [at(f"po_raw{i}", [P, 1024], F32, SLOT_D + 16384 + i * 4096)
              for i in range(2)]
    po_sel = at("po_sel", [P, 1024], BF16, SLOT_D + 24576)
    po_t = [at("po_t0", [P, 1024], BF16, SLOT_D + 26624),
            at("po_t1", [P, 1024], BF16, SLOT_I)]
    po_stg = [at(f"po_stg{i}", [P, 1024], BF16, SLOT_H + i * 2048)
              for i in range(2)]
    gsl = [at(f"gsl{i}", [P, SLAB], F32, SLOT_D + i * SLAB * 4)
           for i in range(2)]
    vsl = [at(f"vsl{i}", [P, SLAB], F32, SLOT_D + (2 + i) * SLAB * 4)
           for i in range(2)]
    hsl = [at(f"hsl{i}", [P, SLAB], F32,
              (SLOT_D + (4 + i) * SLAB * 4) if i < 2 else
              (SLOT_E + (i - 2) * SLAB * 4))
           for i in range(4)]
    mrun = at("mrun", [P, T], F32, SLOT_D + 6 * SLAB * 4)
    rho_rep = at("rho_rep", [P, T], F32, SLOT_D + 6 * SLAB * 4 + T * 4)
    owhh = [at("owhA", [P, 8 * D], BF16, SLOT_A),
            at("owhB", [P, 8 * D], BF16, SLOT_A + 8 * D * 2),
            at("owhC", [P, 8 * D], BF16, SLOT_C)]
    hstg = [at(f"hstg{i}", [P, HCS * P], F32, SLOT_B + i * HCS * P * 4)
            for i in range(2)]
    ib_h = [at(f"ib_h{i}", [P, HCS * P], I32, SLOT_B + i * HCS * P * 4)
            for i in range(2)]
    qhT = [at(f"qhT{i}", [P, HCS * P], BF16,
              SLOT_B + 2 * HCS * P * 4 + i * HCS * P * 2) for i in range(2)]
    pbuf = [at(f"pbuf{i}", [P, D], F32, SLOT_E + i * D * 4) for i in range(2)]
    ycld = [at(f"ycld{i}", [P, D], BF16,
               SLOT_D if i == 0 else SLOT_D + 20480)
            for i in range(2)]
    obuf = [at(f"obuf{i}", [P, D], F32, SLOT_D + D * 2 + i * D * 4)
            for i in range(2)]

    PS = nc.alloc_psum_tensor("PS", [P, 3072], F32)
    PSW = nc.alloc_psum_tensor("PSW", [P, 2048], BF16)
    BANK = 512

    def pbank(b, width=BANK):
        return PS[:, b * BANK:b * BANK + width]

    import contextlib
    st = contextlib.ExitStack()
    with st:
        S = st.enter_context
        sems = {}
        for name in ("ldc", "ldx", "ldw", "ldh", "trq", "mm", "ev", "vv",
                     "hv", "ss", "gg", "sty", "sth", "stp", "sto",
                     "vb", "sb", "ts2", "th", "sq", "ga",
                     "pol", "pom", "pos", "pot"):
            sems[name] = Sem(S(nc.semaphore(name)))
        ldc, ldx, ldw, ldh = (sems[k] for k in ("ldc", "ldx", "ldw", "ldh"))
        trq, mm, ev, vv = (sems[k] for k in ("trq", "mm", "ev", "vv"))
        ss, gg, hv = sems["ss"], sems["gg"], sems["hv"]
        sty, sth, stp, sto = (sems[k] for k in ("sty", "sth", "stp", "sto"))

        block = S(nc.Block())
        prog = {k: [] for k in ("sync", "scalar", "vector", "tensor", "gpsimd")}
        _scope = ["s0"]

        def em(engine, fn, scope=None):
            sc = scope or _scope[0]
            def tagged(e, fn=fn, sc=sc):
                with nc.named_scope(sc):
                    fn(e)
            prog[engine].append(tagged)

        def set_scope(name):
            _scope[0] = name

        def s0_sync(e):
            e.dma_start(ident[:], id_hbm.ap()).then_inc(ldc.h, 16)
            e.dma_start(gb_sb[:], gb_hbm.ap()).then_inc(ldc.h, 16)
            e.dma_start(vb_sb[:], vb_hbm.ap()).then_inc(ldc.h, 16)
            e.dma_start(rep8k[0:1, :], d_hbm.ap()).then_inc(ldc.h, 16)
        em("sync", s0_sync)
        ldc.add(64)

        def s0_vec(e):
            e.memset(sflush[:], 0.0)
            e.memset(eps_col[:], EPS)
            e.memset(ones_row[:], 1.0)
            e.memset(ones_col[:], 1.0).then_inc(vv.h, 1)
        em("vector", s0_vec)
        VV_ONES = vv.add(1)

        def s0_mm(e):
            e.wait_ge(vv.h, VV_ONES)
            e.wait_ge(ldc.h, 64)
            for ds in range(NDS):
                e.matmul(pbank(ds, DS), ones_row[:],
                         rep8k[0:1, ds * DS:(ds + 1) * DS],
                         start=True, stop=True).then_inc(mm.h, 1)
        em("tensor", s0_mm)
        MM_DREP = mm.add(NDS)

        def s0_vec2(e):
            e.wait_ge(mm.h, MM_DREP)
            for ds in range(NDS):
                e.tensor_scalar(rep8k[:, ds * DS:(ds + 1) * DS],
                                pbank(ds, DS), 1.0, None, OP.add)
            e.tensor_copy(ident_bf[:], ident[:])
            e.engine_nop().then_inc(vv.h, 1)
        em("vector", s0_vec2)
        VV_DREP = vv.add(1)

        set_scope("gamma")
        GBIAS = 1.0 if SIM_MODE else (1.0 + 2.3e-7)
        NPU = DC * (H // 1024)
        PO_DRAIN = [None]
        th, sq = sems["th"], sems["sq"]
        polds = [sems["pol"], sems["pom"]]
        posts = [sems["pos"], sems["pot"]]
        _prev_gred = [0]
        for wi, (wname, whbm, rows, cols) in enumerate(
                (("o", ow_hbm, D, H),
                 ("g", gw_hbm, H, D),
                 ("v", vw_hbm, H, D))):
            cpr = max(1, cols // GW)
            w_ = min(cols, GW)
            ntile = (rows // P) * cpr
            inv_n = GBIAS / float(rows * cols)
            assert ntile <= 64

            ld0 = ldw.val
            ab0 = ss.val
            prev_gred = _prev_gred[0]

            def g_sync(e, whbm=whbm, rows=rows, cpr=cpr, w_=w_, ab0=ab0):
                for i in range(rows // P):
                    for j in range(cpr):
                        k = i * cpr + j
                        if k < 2:
                            e.wait_ge(ss.h, ab0)
                        else:
                            e.wait_ge(ss.h, ab0 + k - 1)
                        e.dma_start(
                            gld[k % 2][:, 0:w_],
                            whbm.ap()[i * P:(i + 1) * P, j * w_:(j + 1) * w_]
                        ).then_inc(ldw.h, 16)
            em("sync", g_sync)
            ldw.add(16 * ntile)

            def g_act(e, ntile=ntile, w_=w_, ld0=ld0, first=(wi == 0),
                      drain=(wi >= 1), prev_gred=prev_gred):
                if first:
                    e.wait_ge(ldc.h, 64)
                    e.wait_ge(vv.h, VV_ONES)
                e.wait_ge(vv.h, prev_gred)
                for k in range(ntile):
                    e.wait_ge(ldw.h, ld0 + 16 * (k + 1))
                    e.activation(gscr[:, 0:w_], gld[k % 2][:, 0:w_], AF.Abs,
                                 accum_out=gacc[:, k:k + 1]).then_inc(ss.h, 1)
                    if drain and PO_DRAIN[0] is not None:
                        PO_DRAIN[0](e)
            em("scalar", g_act)
            SS_ABS = ss.add(ntile)

            def g_vec(e, ntile=ntile, tgt=SS_ABS):
                e.wait_ge(ss.h, tgt)
                e.reduce_sum(gred[:], gacc[:, 0:ntile], axis=AX).then_inc(vv.h, 1)
            em("vector", g_vec)
            VV_GRED = vv.add(1)
            _prev_gred[0] = VV_GRED

            def g_mm1(e, tgt=VV_GRED):
                e.wait_ge(vv.h, tgt)
                e.matmul(PS[0:1, 2560:2561], gred[:], ones_col[:],
                         start=True, stop=True).then_inc(mm.h, 1)
            em("tensor", g_mm1)
            MM_GSUM = mm.add(1)

            def g_vsc(e, tgt=MM_GSUM):
                e.wait_ge(mm.h, tgt)
                e.tensor_copy(sclr[:], PS[0:1, 2560:2561]).then_inc(vv.h, 1)
            em("vector", g_vsc)
            VV_GSC = vv.add(1)

            def g_mm2(e, tgt=VV_GSC):
                e.wait_ge(vv.h, tgt)
                e.matmul(PS[:, 2561:2562], ones_row[:], sclr[:],
                         start=True, stop=True).then_inc(mm.h, 1)
            em("tensor", g_mm2)
            MM_GBC = mm.add(1)

            def g_vec2(e, wname=wname, inv_n=inv_n, tgt=MM_GBC):
                c = wcols[wname]
                e.wait_ge(mm.h, tgt)
                e.tensor_scalar(gfull[:], PS[:, 2561:2562], inv_n, EPS,
                                OP.mult, OP.max)
                e.tensor_scalar_mul(c["thp"][:], gfull[:], 0.5)
                e.tensor_scalar_mul(c["thn"][:], gfull[:], -0.5)
                e.tensor_scalar_mul(c["nsc"][:], gfull[:], -1.0
                                    ).then_inc(vv.h, 1)
            em("vector", g_vec2)
            vv.add(1)
            if wi == 1:
                def po_vec2(e):
                    po_vec_part(e, POSPLIT, NPU)
                em("vector", po_vec2)
                vv.add(NPU - POSPLIT)

                def po_mm2(e):
                    po_mm_part(e, POSPLIT, NPU)
                em("tensor", po_mm2)
                th.add(NPU - POSPLIT)

            if wi == 0:
                POV0 = vv.val
                POTH0 = th.val
                POSQ0 = sq.val
                POL0 = [x.val for x in polds]
                POS0 = [x.val for x in posts]
                NC2 = H // 1024

                def po_gp(e):
                    def store(m):
                        i_m, c2m = divmod(m, NC2)
                        e.wait_ge(sq.h, POSQ0 + m + 1)
                        if m >= 2:
                            e.wait_ge(posts[m % 2].h,
                                      POS0[m % 2] + 16 * (m // 2))
                        e.dma_start(
                            owT_sp.ap()[c2m * 1024:(c2m + 1) * 1024,
                                        i_m * P:(i_m + 1) * P]
                            .rearrange("(a p) d -> p a d", p=P),
                            po_stg[m % 2][:].rearrange("p (a d) -> p a d",
                                                       a=8)
                        ).then_inc(posts[m % 2].h, 16)

                    for n in range(NPU):
                        i_t, c2 = divmod(n, NC2)
                        if n >= 2:
                            e.wait_ge(vv.h, tvv(n - 2))
                        e.dma_start(po_raw[n % 2][:],
                                    ow_hbm.ap()[i_t * P:(i_t + 1) * P,
                                                c2 * 1024:(c2 + 1) * 1024]
                                    ).then_inc(polds[n % 2].h, 16)
                        if n >= 2:
                            store(n - 2)
                    store(NPU - 2)
                    store(NPU - 1)
                em("gpsimd", po_gp)
                for x in polds:
                    x.add(16 * (NPU // 2))
                PO_ST_END = [(x.h, x.add(16 * (NPU // 2))) for x in posts]

                POSPLIT = 64

                def tvv(n):
                    return POV0 + n + 1 + (3 if n >= POSPLIT else 0)

                def po_vec_part(e, lo, hi):
                    o = wcols["o"]
                    for n in range(lo, hi):
                        e.wait_ge(polds[n % 2].h,
                                  POL0[n % 2] + 16 * (n // 2 + 1))
                        if n >= 2:
                            e.wait_ge(th.h, POTH0 + n - 1)
                        e.tensor_scalar(po_sel[:], po_raw[n % 2][:],
                                        o["thp"][:], None, OP.is_gt)
                        e.scalar_tensor_tensor(po_t[n % 2][:],
                                               po_raw[n % 2][:],
                                               o["thn"][:], po_sel[:],
                                               OP.is_lt, OP.subtract
                                               ).then_inc(vv.h, 1)

                def po_vec1(e):
                    po_vec_part(e, 0, POSPLIT)
                em("vector", po_vec1)
                vv.add(POSPLIT)

                def po_mm_part(e, lo, hi):
                    for n in range(lo, hi):
                        e.wait_ge(vv.h, tvv(n))
                        if n >= 2:
                            e.wait_ge(sq.h, POSQ0 + n - 1)
                        b = (n % 2) * 8 * P
                        for j in range(8):
                            i_ = e.transpose(
                                PSW[:, b + j * P:b + (j + 1) * P],
                                po_t[n % 2][:, j * P:(j + 1) * P],
                                ident_bf[:])
                        i_.then_inc(th.h, 1)

                def po_mm1(e):
                    po_mm_part(e, 0, POSPLIT)
                em("tensor", po_mm1)
                th.add(POSPLIT)

                _po_n = [0]

                def po_drain(e):
                    n = _po_n[0]
                    if n >= NPU:
                        return
                    _po_n[0] = n + 1
                    e.wait_ge(th.h, POTH0 + n + 1)
                    if n >= 2:
                        e.wait_ge(posts[n % 2].h,
                                  POS0[n % 2] + 16 * (n // 2))
                    e.activation(po_stg[n % 2][:],
                                 PSW[:, (n % 2) * 8 * P:(n % 2 + 1) * 8 * P],
                                 AF.Identity).then_inc(sq.h, 1)
                PO_DRAIN[0] = po_drain
                sq.add(NPU)
        def po_tail(e):
            while _po_n[0] < NPU:
                PO_DRAIN[0](e)
        em("scalar", po_tail)
        VV_GAMMA = vv.val

        set_scope("p1a")
        P1TH0 = th.val
        P1SQ0 = sq.val
        RX = 2 * NCH
        RC = RX + 2 * TT
        RQ = RC + DC
        NR1 = RQ + 2 * TT

        bld0 = ldx.val
        bcast0 = vv.val

        def b_sync(e):
            for i in range(NCH):
                if i >= 2:
                    e.wait_ge(vv.h, bcast0 + i - 1)
                e.dma_start(xld[i % 2][:], b_hbm.ap()[i * P:(i + 1) * P, :]
                            ).then_inc(ldx.h, 16)
        em("sync", b_sync)
        ldx.add(16 * NCH)

        def b_vec(e):
            e.wait_ge(vv.h, VV_GAMMA)
            for i in range(NCH):
                e.wait_ge(ldx.h, bld0 + 16 * (i + 1))
                if i >= 2:
                    e.wait_ge(th.h, P1TH0 + 2 * (i - 1))
                e.tensor_copy(xbf[i % 2][:], xld[i % 2][:]).then_inc(vv.h, 1)
        em("vector", b_vec)
        vv.add(NCH)
        VV_BCAST = vv.val

        xld0 = ldx.val
        xcast0 = vv.val
        z1mm0 = mm.val
        SS_TANH0 = None

        def x_sync(e):
            for k in range(TT):
                if k < 2:
                    e.wait_ge(vv.h, VV_BCAST)
                else:
                    e.wait_ge(vv.h, xcast0 + k - 1)
                e.dma_start(xld[k % 2][:], x_hbm.ap()[k * P:(k + 1) * P, :]
                            ).then_inc(ldx.h, 16)
        em("sync", x_sync)
        ldx.add(16 * TT)

        def x_vec(e):
            for k in range(TT):
                e.wait_ge(ldx.h, xld0 + 16 * (k + 1))
                e.wait_ge(th.h, P1TH0 + RX + max(0, 2 * (k - 1)))
                e.tensor_copy(xbf[k % 2][:], xld[k % 2][:]).then_inc(vv.h, 1)
        em("vector", x_vec)
        vv.add(TT)

        def _t8(e, n, src, half):
            e.wait_ge(sq.h, P1SQ0 + max(0, n - 1))
            b = (n % 2) * 8 * P
            for j in range(8):
                i_ = e.transpose(PSW[:, b + j * P:b + (j + 1) * P],
                                 src[:, (8 * half + j) * P:
                                      (8 * half + j + 1) * P],
                                 ident_bf[:])
            i_.then_inc(th.h, 1)

        def z1_mm(e):
            for i in range(NCH):
                e.wait_ge(vv.h, bcast0 + i + 1)
                for r in range(2):
                    _t8(e, 2 * i + r, xbf[i % 2], r)
            kmm = 0
            for s_ in range(NSLAB):
                for kk in range(TPS):
                    k = s_ * TPS + kk
                    e.wait_ge(vv.h, xcast0 + k + 1)
                    for r in range(2):
                        _t8(e, RX + 2 * k + r, xbf[k % 2], r)
                e.wait_ge(sq.h, P1SQ0 + RX + 2 * TPS * (s_ + 1))
                for nch in range(NCH):
                    ps = pbank(4 + (kmm % 2), SLAB)
                    if kmm >= 2:
                        e.wait_ge(ss.h, SS_TANH0 + kmm - 1)
                    for dc in range(DC):
                        i_ = e.matmul(
                            ps,
                            bt_sb[:, dc * NS + nch * P:dc * NS + (nch + 1) * P],
                            xT_sb[:, dc * SLAB:(dc + 1) * SLAB],
                            start=(dc == 0), stop=(dc == DC - 1))
                    i_.then_inc(mm.h, 1)
                    kmm += 1
        em("tensor", z1_mm)
        mm.add(NSLAB * NCH)
        th.add(RC)

        SS_TANH0 = ss.val

        def z1_tanh(e):
            def evac(n, dst):
                e.wait_ge(th.h, P1TH0 + n + 1)
                b = (n % 2) * 8 * P
                e.activation(dst, PSW[:, b:b + 8 * P], AF.Identity
                             ).then_inc(sq.h, 1)

            bt3 = bt_sb[:].rearrange("p (a n) -> p a n", a=DC)
            for n in range(RX):
                evac(n, bt3[:, (n % 2) * 8:(n % 2) * 8 + 8,
                            (n // 2) * P:(n // 2 + 1) * P])
            xT3 = xT_sb[:].rearrange("p (a s) -> p a s", a=DC)
            k = 0
            for s_ in range(NSLAB):
                for kk in range(TPS):
                    for r in range(2):
                        n = RX + 2 * (s_ * TPS + kk) + r
                        evac(n, xT3[:, 8 * r:8 * r + 8,
                                    kk * P:(kk + 1) * P])
                for nch in range(NCH):
                    e.wait_ge(mm.h, z1mm0 + k + 1)
                    e.activation(
                        state_sb[:, nch * T + s_ * SLAB:nch * T + (s_ + 1) * SLAB],
                        pbank(4 + (k % 2), SLAB), AF.Tanh).then_inc(ss.h, 1)
                    k += 1
        em("scalar", z1_tanh)
        ss.add(NSLAB * NCH)
        sq.add(RC)
        SS_P1A = ss.val

        set_scope("p1b")
        cld0 = ldx.val
        ccast0 = vv.val

        def c_sync(e):
            e.wait_ge(th.h, P1TH0 + RC)
            for i in range(DC):
                if i >= 2:
                    e.wait_ge(vv.h, ccast0 + i - 1)
                e.dma_start(cld[i % 2][:], c_hbm.ap()[i * P:(i + 1) * P, :]
                            ).then_inc(ldx.h, 16)
        em("sync", c_sync)
        ldx.add(16 * DC)

        def c_vec(e):
            for i in range(DC):
                e.wait_ge(ldx.h, cld0 + 16 * (i + 1))
                if i >= 2:
                    e.wait_ge(th.h, P1TH0 + RC + i - 1)
                e.tensor_copy(cbf[i % 2][:], cld[i % 2][:]).then_inc(vv.h, 1)
        em("vector", c_vec)
        vv.add(DC)

        x2ld0 = ldx.val
        P1B_V0 = None
        P1B_EV0 = None

        def x2_sync(e):
            e.wait_ge(vv.h, xcast0 + TT)
            for tt in range(min(2, TT)):
                e.dma_start(xld[tt][:], x_hbm.ap()[tt * P:(tt + 1) * P, :]
                            ).then_inc(ldx.h, 16)
        em("sync", x2_sync)
        ldx.add(16 * min(2, TT))

        mm2_0 = mm.val

        def mm2(e):
            for i in range(DC):
                e.wait_ge(vv.h, ccast0 + i + 1)
                _t8(e, RC + i, cbf[i % 2], 0)
            e.wait_ge(sq.h, P1SQ0 + RQ)
            e.wait_ge(ss.h, SS_P1A)
            for tt in range(TT):
                if tt >= 1:
                    e.wait_ge(ev.h, P1B_EV0 + tt)
                for ds in range(NDS):
                    for nch in range(NCH):
                        i_ = e.matmul(
                            pbank(ds, DS),
                            state_sb[:, nch * T + tt * P:nch * T + (tt + 1) * P],
                            ct_sb[:, nch * D + ds * DS:nch * D + (ds + 1) * DS],
                            start=(nch == 0), stop=(nch == NCH - 1))
                i_.then_inc(mm.h, 1)
                if tt >= 1:
                    qt = tt - 1
                    e.wait_ge(vv.h, P1B_V0 + 2 * qt + 2)
                    for r in range(2):
                        _t8(e, RQ + 2 * qt + r, qybf, r)
            for r in range(2):
                e.wait_ge(vv.h, P1B_V0 + 2 * TT)
                _t8(e, RQ + 2 * (TT - 1) + r, qybf, r)
        em("tensor", mm2)
        mm.add(TT)
        th.add(DC + 2 * TT)

        P1B_V0 = vv.val
        P1B_EV0 = ev.val
        SS_SQ0 = None

        def p1b_vec(e):
            for _h, _v in PO_ST_END:
                e.wait_ge(_h, _v)
            for tt in range(TT):
                e.wait_ge(ldx.h, x2ld0 + 16 * (tt + 1))
                e.wait_ge(mm.h, mm2_0 + tt + 1)
                w = wbuf[tt % 2]
                if tt >= 2:
                    e.wait_ge(ss.h, SS_SQ0 + tt - 1)
                if tt >= 1:
                    e.wait_ge(sty.h, STY0 + 16 * tt)
                e.tensor_tensor(w[:], xld[tt % 2][:], rep8k[:], OP.mult)
                e.tensor_tensor(w[:], w[:], PS[:, 0:D], OP.add).then_inc(ev.h, 1)
                e.reduce_sum(mcol[:], w[:], axis=AX)
                e.tensor_scalar_mul(mcol[:], mcol[:], 1.0 / D)
                e.tensor_copy(m_all[:, tt:tt + 1], mcol[:])
                e.tensor_scalar(w[:], w[:], mcol[:], None, OP.subtract)
                e.reduce_max(M_all[:, tt:tt + 1], w[:], axis=AX,
                             apply_absolute_value=True)
                e.tensor_copy(ycbf[:], w[:]).then_inc(vv.h, 1)
                e.tensor_scalar(tcol[:], M_all[:, tt:tt + 1], THR_M, None, OP.max)
                e.reciprocal(tcol[:], tcol[:])
                e.tensor_scalar_mul(scol[:], tcol[:], 127.0)
                if tt >= 1:
                    e.wait_ge(th.h, P1TH0 + RQ + 2 * tt)
                if SIM_MODE:
                    e.tensor_scalar(t256[:], w[:], scol[:], 256.5,
                                    OP.mult, OP.add)
                    e.tensor_scalar(w[:], t256[:], 1.0, None, OP.mod)
                    e.scalar_tensor_tensor(qybf[:], t256[:], -256.0, w[:],
                                           OP.add, OP.subtract).then_inc(vv.h, 1)
                else:
                    e.tensor_scalar(t256[:], w[:], scol[:], None, OP.mult)
                    e.tensor_copy(ib_q[:], t256[:])
                    e.tensor_copy(qybf[:], ib_q[:]).then_inc(vv.h, 1)
        em("vector", p1b_vec)
        vv.add(2 * TT)
        ev.add(TT)

        SS_SQ0 = ss.val

        def p1b_scal(e):
            qyT3 = qyT_sb[:].rearrange("p (a t) -> p a t", a=DC)
            ct3 = ct_sb[:].rearrange("p (a d) -> p a d", a=NCH)

            def evacq(qt):
                for r in range(2):
                    n = RQ + 2 * qt + r
                    e.wait_ge(th.h, P1TH0 + n + 1)
                    b = (n % 2) * 8 * P
                    e.activation(qyT3[:, 8 * r:8 * r + 8,
                                      qt * P:(qt + 1) * P],
                                 PSW[:, b:b + 8 * P], AF.Identity
                                 ).then_inc(sq.h, 1)

            for i in range(DC):
                e.wait_ge(th.h, P1TH0 + RC + i + 1)
                b = ((RC + i) % 2) * 8 * P
                e.activation(ct3[:, 0:NCH, i * P:(i + 1) * P],
                             PSW[:, b:b + 8 * P], AF.Identity
                             ).then_inc(sq.h, 1)
            for tt in range(TT):
                e.wait_ge(vv.h, P1B_V0 + 2 * tt + 1)
                e.activation(sq_scr[:], wbuf[tt % 2][:], AF.Square,
                             accum_out=ssq_all[:, tt:tt + 1]).then_inc(ss.h, 1)
                if tt >= 1:
                    evacq(tt - 1)
            evacq(TT - 1)
        em("scalar", p1b_scal)
        ss.add(TT)
        sq.add(DC + 2 * TT)
        QY_SQ_END = sq.val

        STY0 = sty.val

        def p1b_gp(e):
            for tt in range(TT):
                e.wait_ge(vv.h, P1B_V0 + 2 * tt + 1)
                e.dma_start(yc_sp.ap()[tt * P:(tt + 1) * P, :], ycbf[:]
                            ).then_inc(sty.h, 16)
        em("gpsimd", p1b_gp)
        STY_DONE = sty.add(16 * TT)

        if dbg:
            def dbg1_gp(e):
                e.wait_ge(ss.h, SS_P1A)
                e.dma_start(state_d.ap(), state_sb[:]).then_inc(gg.h, 16)
            em("gpsimd", dbg1_gp)
            gg.add(16)

        def x2b_sync(e):
            for tt in range(2, TT):
                e.wait_ge(vv.h, P1B_V0 + 2 * (tt - 2) + 1)
                e.dma_start(xld[tt % 2][:], x_hbm.ap()[tt * P:(tt + 1) * P, :]
                            ).then_inc(ldx.h, 16)
        em("sync", x2b_sync)
        ldx.add(16 * max(0, TT - 2))

        set_scope("p1e")
        TRPC = 4 * BANK

        def p1e_scal(e):
            e.wait_ge(vv.h, P1B_V0 + 2 * TT)
            e.activation(rstd_all[:], ssq_all[:], AF.Sqrt,
                         scale=1.0 / D, bias=eps_col[:]).then_inc(ss.h, 1)
        em("scalar", p1e_scal)
        SS_RSTD = ss.add(1)

        def p1e_vec(e):
            e.wait_ge(ss.h, SS_RSTD)
            e.reciprocal(rstd_all[:], rstd_all[:])
            e.tensor_scalar(rho_all[:], M_all[:], THR_M, None, OP.max)
            e.tensor_tensor(rho_all[:], rho_all[:], rstd_all[:], OP.mult)
            e.tensor_scalar_mul(rho_all[:], rho_all[:], 1.0 / 127.0
                                ).then_inc(vv.h, 1)
        em("vector", p1e_vec)
        VV_RHO = vv.add(1)

        BCC = 5 * BANK
        RB_M0 = mm.val
        RB_V0 = vv.val

        def p1e_mmT(e):
            e.wait_ge(vv.h, VV_RHO)
            for tt in range(TT):
                if tt > 0:
                    e.wait_ge(vv.h, RB_V0 + 2 * tt - 1)
                e.transpose(PS[0:1, TRPC:TRPC + P], rho_all[:, tt:tt + 1],
                            ident[:]).then_inc(mm.h, 1)
                e.wait_ge(vv.h, RB_V0 + 2 * tt + 1)
                e.matmul(PS[:, BCC:BCC + P], ones_row[:], rowb[:],
                         start=True, stop=True).then_inc(mm.h, 1)
        em("tensor", p1e_mmT)
        mm.add(2 * TT)

        def p1e_vec2(e):
            for tt in range(TT):
                e.wait_ge(mm.h, RB_M0 + 2 * tt + 1)
                e.tensor_copy(rowb[:], PS[0:1, TRPC:TRPC + P]).then_inc(vv.h, 1)
                e.wait_ge(mm.h, RB_M0 + 2 * tt + 2)
                e.tensor_copy(rho_rep[:, tt * P:(tt + 1) * P],
                              PS[:, BCC:BCC + P]).then_inc(vv.h, 1)
        em("vector", p1e_vec2)
        RHOREP_V = vv.add(2 * TT)

        def mrun_vec(e):
            e.memset(mrun[:], 0.0).then_inc(vv.h, 1)
        em("vector", mrun_vec)
        VV_MRUN = vv.add(1)

        set_scope("p2")
        th, sq = sems["th"], sems["sq"]
        S4 = [sth, sems["sty"], sems["stp"], sems["sto"]]
        B4 = [x.val for x in S4]
        ldg = [ldw, sems["ga"]]
        ldv = [sems["ts2"], ldx]
        P2LDG = [ldg[0].val, ldg[1].val]
        P2LDV = [ldv[0].val, ldv[1].val]
        P2TH0 = th.val
        P2SQ0 = sq.val
        P2TERN0 = vv.val + 0
        P2MM0 = mm.val
        P2EV0 = ev.val
        P2SS0 = ss.val
        P2HV0 = None
        RPS = (1, 1, 1, 1)

        def p2_sync(e):
            e.wait_ge(vv.h, RHOREP_V)
            if dbg:
                e.wait_ge(gg.h, 16)
            for hc in range(HC):
                if hc >= 2:
                    e.wait_ge(vv.h, P2TERN0 + 2 * (hc - 1))
                e.dma_start(graw[hc % 2][:], gw_hbm.ap()[hc * P:(hc + 1) * P, :]
                            ).then_inc(ldg[hc % 2].h, 16)
                e.dma_start(vraw[hc % 2][:], vw_hbm.ap()[hc * P:(hc + 1) * P, :]
                            ).then_inc(ldv[hc % 2].h, 16)
        em("sync", p2_sync)
        for _s in (*ldg, *ldv):
            _s.add(16 * (HC // 2))

        P2TERN0 = vv.val

        def _tround(e, hc, r):
            R = 4 * hc + r
            src = gtb[hc % 2] if r < 2 else vtb[hc % 2]
            j = r % 2
            e.wait_ge(vv.h, P2TERN0 + 2 * hc + (1 if r < 2 else 2))
            if R >= 2:
                e.wait_ge(sq.h, P2SQ0 + R - 1)
            b = (R % 2) * 8 * P
            for i in range(8):
                i_ = e.transpose(PSW[:, b + i * P:b + (i + 1) * P],
                                 src[:, (8 * j + i) * P:(8 * j + i + 1) * P],
                                 ident_bf[:])
            i_.then_inc(th.h, 1)

        def p2_mm(e):
            e.wait_ge(sq.h, QY_SQ_END)
            for r in range(4):
                _tround(e, 0, r)
            for hc in range(HC):
                e.wait_ge(sq.h, P2SQ0 + 4 * (hc + 1))
                rnext = 0
                for s_ in range(NSLAB):
                    k = hc * NSLAB + s_
                    if k >= 3:
                        e.wait_ge(ev.h, P2EV0 + k - 2)
                    for dc in range(DC):
                        e.matmul(pbank(k % 3, SLAB),
                                 gwT[hc % 2][:, dc * P:(dc + 1) * P],
                                 qyT_sb[:, dc * T + s_ * SLAB:dc * T + (s_ + 1) * SLAB],
                                 start=(dc == 0), stop=(dc == DC - 1))
                    for dc in range(DC):
                        i_ = e.matmul(pbank(3 + k % 3, SLAB),
                                      vwT[hc % 2][:, dc * P:(dc + 1) * P],
                                      qyT_sb[:, dc * T + s_ * SLAB:dc * T + (s_ + 1) * SLAB],
                                      start=(dc == 0), stop=(dc == DC - 1))
                    i_.then_inc(mm.h, 1)
                    if hc + 1 < HC:
                        for _ in range(RPS[s_]):
                            _tround(e, hc + 1, rnext)
                            rnext += 1
        em("tensor", p2_mm)
        mm.add(HC * NSLAB)
        th.add(4 * HC)

        P2HV0 = hv.val

        def p2_vec(e):
            e.wait_ge(vv.h, VV_MRUN)

            def tern(hc):
                e.wait_ge(ldg[hc % 2].h, P2LDG[hc % 2] + 16 * (hc // 2 + 1))
                if hc >= 2:
                    e.wait_ge(th.h, P2TH0 + 4 * hc - 6)
                e.tensor_scalar(dsel[:], graw[hc % 2][:],
                                wcols["g"]["thp"][:], None, OP.is_gt)
                e.scalar_tensor_tensor(gtb[hc % 2][:], graw[hc % 2][:],
                                       wcols["g"]["thn"][:], dsel[:],
                                       OP.is_lt, OP.subtract).then_inc(vv.h, 1)
                e.wait_ge(ldv[hc % 2].h, P2LDV[hc % 2] + 16 * (hc // 2 + 1))
                if hc >= 2:
                    e.wait_ge(th.h, P2TH0 + 4 * hc - 4)
                e.tensor_scalar(dsel[:], vraw[hc % 2][:],
                                wcols["v"]["thp"][:], None, OP.is_gt)
                e.scalar_tensor_tensor(vtb[hc % 2][:], vraw[hc % 2][:],
                                       wcols["v"]["thn"][:], dsel[:],
                                       OP.is_lt, OP.subtract).then_inc(vv.h, 1)

            def epi(hc):
                for s_ in range(NSLAB):
                    k = hc * NSLAB + s_
                    e.wait_ge(mm.h, P2MM0 + k + 1)
                    if hc >= 1:
                        e.wait_ge(S4[s_].h, B4[s_] + 16 * hc)
                    e.tensor_tensor(gsl[s_ % 2][:], pbank(k % 3, SLAB),
                                    rho_rep[:, s_ * SLAB:(s_ + 1) * SLAB],
                                    OP.mult)
                    e.tensor_tensor(vsl[s_ % 2][:], pbank(3 + k % 3, SLAB),
                                    rho_rep[:, s_ * SLAB:(s_ + 1) * SLAB],
                                    OP.mult).then_inc(ev.h, 1)
                    e.wait_ge(ss.h, P2SS0 + 2 * k + 2)
                    e.tensor_tensor(hsl[s_][:], gsl[s_ % 2][:],
                                    vsl[s_ % 2][:], OP.mult)
                    e.tensor_tensor(mrun[:, s_ * SLAB:(s_ + 1) * SLAB],
                                    mrun[:, s_ * SLAB:(s_ + 1) * SLAB],
                                    hsl[s_][:], OP.max)
                    e.scalar_tensor_tensor(mrun[:, s_ * SLAB:(s_ + 1) * SLAB],
                                           hsl[s_][:], -1.0,
                                           mrun[:, s_ * SLAB:(s_ + 1) * SLAB],
                                           OP.mult, OP.max).then_inc(hv.h, 1)

            tern(0)
            tern(1)
            for hc in range(HC):
                if hc + 2 < HC:
                    tern(hc + 2)
                epi(hc)
        em("vector", p2_vec)
        vv.add(2 * HC)
        hv.add(HC * NSLAB)
        ev.add(HC * NSLAB)

        def p2_scal(e):
            def evac(hc, r):
                R = 4 * hc + r
                e.wait_ge(th.h, P2TH0 + R + 1)
                if hc >= 2:
                    e.wait_ge(mm.h, P2MM0 + NSLAB * (hc - 1))
                dst = gwT if r < 2 else vwT
                j = r % 2
                b = (R % 2) * 8 * P
                e.activation(dst[hc % 2][:, j * 8 * P:(j + 1) * 8 * P],
                             PSW[:, b:b + 8 * P], AF.Identity
                             ).then_inc(sq.h, 1)

            def silu_slab(hc, s_):
                k = hc * NSLAB + s_
                e.wait_ge(ev.h, P2EV0 + k + 1)
                e.activation(gsl[s_ % 2][:], gsl[s_ % 2][:], AF.Silu,
                             bias=gb_sb[:, hc:hc + 1],
                             scale=wcols["g"]["nsc"][:]).then_inc(ss.h, 1)
                e.activation(vsl[s_ % 2][:], vsl[s_ % 2][:], AF.Identity,
                             bias=vb_sb[:, hc:hc + 1],
                             scale=wcols["v"]["nsc"][:]).then_inc(ss.h, 1)

            for r in range(4):
                evac(0, r)
            for hc in range(1, HC):
                evac(hc, 0)
                evac(hc, 1)
                silu_slab(hc - 1, 0)
                evac(hc, 2)
                silu_slab(hc - 1, 1)
                evac(hc, 3)
                silu_slab(hc - 1, 2)
                silu_slab(hc - 1, 3)
            for s_ in range(NSLAB):
                silu_slab(HC - 1, s_)
        em("scalar", p2_scal)
        ss.add(2 * HC * NSLAB)
        sq.add(4 * HC)

        def p2_gp(e):
            for hc in range(HC):
                for s_ in range(NSLAB):
                    k = hc * NSLAB + s_
                    e.wait_ge(hv.h, P2HV0 + k + 1)
                    if hc >= 1:
                        e.wait_ge(S4[s_].h, B4[s_] + 16 * hc)
                    e.dma_start(h_sp.ap()[s_ * TPS:(s_ + 1) * TPS, hc, :, :]
                                .rearrange("a p b -> p a b"),
                                hsl[s_][:].rearrange("p (a b) -> p a b",
                                                     a=TPS)
                                ).then_inc(S4[s_].h, 16)
        em("gpsimd", p2_gp)
        SPILL_DONE = [(x.h, x.add(16 * HC)) for x in S4]

        VV_SREP_HOLDER = [None]
        if dbg:
            GG_D2 = gg.val + 13 * 16

            def dbg2_gp(e):
                e.wait_ge(sq.h, QY_SQ_END)
                e.wait_ge(vv.h, VV_SREP_HOLDER[0])
                e.dma_start(qyT_d.ap(), qyT_sb[:]).then_inc(gg.h, 16)
                e.dma_start(rho_d.ap(), rho_rep[:]).then_inc(gg.h, 16)
                e.dma_start(srep_d.ap(), s_h_rep[:]).then_inc(gg.h, 16)
                e.dma_start(sc3_d.ap(), sc3_all[:]).then_inc(gg.h, 16)
                for j, (wn, kk) in enumerate(
                        (w_, k_) for w_ in ("g", "v", "o")
                        for k_ in ("thp", "thn", "nsc")):
                    e.dma_start(gcol_d.ap()[:, j:j + 1], wcols[wn][kk][:],
                                allow_slow_non_contiguous=True
                                ).then_inc(gg.h, 16)
                e.wait_ge(gg.h, GG_D2)
            em("gpsimd", dbg2_gp)
            gg.add(13 * 16)

        set_scope("p2e")
        BCC = 5 * BANK
        SR_M0 = mm.val
        SR_V0 = vv.val + 1

        def p2e_mm(e):
            e.wait_ge(hv.h, P2HV0 + HC * NSLAB)
            for tt in range(TT):
                if tt > 0:
                    e.wait_ge(vv.h, SR_V0 + 3 * (tt - 1) + 1)
                e.transpose(PS[:, TRPC:TRPC + P], mrun[:, tt * P:(tt + 1) * P],
                            ident[:]).then_inc(mm.h, 1)
                e.wait_ge(vv.h, SR_V0 + 3 * tt + 1)
                e.transpose(PS[0:1, TRPC + P:TRPC + 2 * P], scol[:],
                            ident[:]).then_inc(mm.h, 1)
                e.wait_ge(vv.h, SR_V0 + 3 * tt + 2)
                e.matmul(PS[:, BCC:BCC + P], ones_row[:], rowb[:],
                         start=True, stop=True).then_inc(mm.h, 1)
        em("tensor", p2e_mm)
        mm.add(3 * TT)

        def p2e_vec(e):
            e.tensor_scalar_mul(tcol2[:], wcols["o"]["nsc"][:], 1.0 / 127.0
                                ).then_inc(vv.h, 1)
            for tt in range(TT):
                e.wait_ge(mm.h, SR_M0 + 3 * tt + 1)
                e.reduce_max(tcol[:], PS[:, TRPC:TRPC + P], axis=AX)
                e.tensor_scalar(tcol[:], tcol[:], EPS, None, OP.max)
                e.tensor_scalar(sc3_all[:, tt:tt + 1], tcol[:], tcol2[:],
                                None, OP.mult)
                e.reciprocal(tcol[:], tcol[:])
                e.tensor_scalar_mul(scol[:], tcol[:], 127.0).then_inc(vv.h, 1)
                e.wait_ge(mm.h, SR_M0 + 3 * tt + 2)
                e.tensor_copy(rowb[:], PS[0:1, TRPC + P:TRPC + 2 * P]
                              ).then_inc(vv.h, 1)
                e.wait_ge(mm.h, SR_M0 + 3 * tt + 3)
                e.tensor_copy(s_h_rep[:, tt * P:(tt + 1) * P],
                              PS[:, BCC:BCC + P]).then_inc(vv.h, 1)
        em("vector", p2e_vec)
        VV_SREP = vv.add(1 + 3 * TT)
        VV_SREP_HOLDER[0] = VV_SREP

        def ob_sync(e):
            e.wait_ge(vv.h, VV_SREP)
            e.dma_start(rep8k[0:1, :], ob_hbm.ap()).then_inc(ldx.h, 16)
        em("sync", ob_sync)
        LDX_OB = ldx.add(16)

        def ob_mm(e):
            e.wait_ge(ldx.h, LDX_OB)
            for ds in range(NDS):
                e.matmul(pbank(ds, DS), ones_row[:],
                         rep8k[0:1, ds * DS:(ds + 1) * DS],
                         start=True, stop=True).then_inc(mm.h, 1)
        em("tensor", ob_mm)
        MM_OBREP = mm.add(NDS)

        def ob_vec(e):
            e.wait_ge(mm.h, MM_OBREP)
            for ds in range(NDS):
                e.tensor_copy(rep8k[:, ds * DS:(ds + 1) * DS], pbank(ds, DS))
            e.engine_nop().then_inc(vv.h, 1)
        em("vector", ob_vec)
        VV_OBREP = vv.add(1)

        set_scope("p3")
        TR30 = trq.val
        LDC30 = ldc.val
        OWL = [0, 2, 1, 0]
        OWU = [1, 0, 2, 1]
        p3 = [dict() for _ in range(NSEG)]
        for seg in range(NSEG):
            c = p3[seg]

            c["hld0"] = ldh.val
            nper = 1 + (1 if seg > 0 else 0) + (1 if seg == NSEG - 1 else 0)
            c["per_tt"] = 16 * nper
            c["evs"] = 4 if seg == NSEG - 1 else 1

            def _ow_load(e, sg, half, bi):
                e.dma_start(
                    owhh[bi][:].rearrange("p (a d) -> p a d", a=8),
                    owT_sp.ap()[sg * HSEG + half * 8 * P:
                                sg * HSEG + (half + 1) * 8 * P, :]
                    .rearrange("(a p) d -> p a d", p=P)
                ).then_inc((ldc if half == 0 else trq).h, 16)

            def p3h_sync(e, seg=seg, c=c):
                for _h, _v in SPILL_DONE:
                    e.wait_ge(_h, _v)
                if seg == 0:
                    e.wait_ge(vv.h, VV_OBREP)
                    if dbg:
                        e.wait_ge(gg.h, 14 * 16)
                    _ow_load(e, 0, 0, OWL[0])
                else:
                    e.wait_ge(stp.h, p3[seg - 1]["stp_end"])
                if seg == NSEG - 1:
                    e.wait_ge(sty.h, STY_DONE)
                _ow_load(e, seg, 1, OWU[seg])
                if seg + 1 < NSEG:
                    e.wait_ge(ldc.h, LDC30 + 16 * (seg + 1))
                    _ow_load(e, seg + 1, 0, OWL[seg + 1])
                for tt in range(TT):
                    if tt >= 2:
                        e.wait_ge(vv.h, c["qv0"] + tt - 1)
                    e.dma_start(
                        hstg[tt % 2][:].rearrange("p (a b) -> p a b", a=HCS),
                        h_sp.ap()[tt, seg * HCS:(seg + 1) * HCS, :, :]
                        .rearrange("a p b -> p a b")
                    ).then_inc(ldh.h, 16)
                    if seg > 0:
                        if tt >= 2:
                            e.wait_ge(hv.h, c["ev0"] + c["evs"] * (tt - 2) + 1)
                        e.dma_start(pbuf[tt % 2][:],
                                    part_sp.ap()[tt * P:(tt + 1) * P, :]
                                    ).then_inc(ldh.h, 16)
                    if seg == NSEG - 1:
                        if tt >= 2:
                            e.wait_ge(hv.h, c["ev0"] + c["evs"] * (tt - 2) + 2)
                        e.dma_start(ycld[tt % 2][:],
                                    yc_sp.ap()[tt * P:(tt + 1) * P, :]
                                    ).then_inc(ldh.h, 16)
            em("sync", p3h_sync)
            ldh.add(c["per_tt"] * TT)
            trq.add(16)
            if seg == 0:
                ldc.add(16)
            if seg + 1 < NSEG:
                ldc.add(16)

            c["qv0"] = vv.val
            c["ev0"] = hv.val
            c["mm0"] = mm.val
            c["stp0"] = stp.val
            c["ss0"] = ss.val
            last = seg == NSEG - 1

            def p3_vec(e, seg=seg, c=c, last=last):
                def q(tt):
                    e.wait_ge(ldh.h, c["hld0"] + c["per_tt"] * tt + 16)
                    if tt >= 2:
                        e.wait_ge(mm.h, c["mm0"] + tt - 1)
                    hb = hstg[tt % 2]
                    qb = qhT[tt % 2]
                    sb_ = s_h_rep[:, tt * P:(tt + 1) * P].unsqueeze(1) \
                        .broadcast_to([P, HCS, P])
                    e.tensor_tensor(hb[:].rearrange("p (a j) -> p a j", a=HCS),
                                    hb[:].rearrange("p (a j) -> p a j", a=HCS),
                                    sb_, OP.mult)
                    if SIM_MODE:
                        e.tensor_scalar(hb[:], hb[:], 256.5, None, OP.add)
                        e.tensor_scalar(qb[:], hb[:], 1.0, None, OP.mod)
                        e.scalar_tensor_tensor(qb[:], hb[:], -256.0, qb[:],
                                               OP.add, OP.subtract
                                               ).then_inc(vv.h, 1)
                    else:
                        e.tensor_copy(ib_h[tt % 2][:], hb[:])
                        e.tensor_copy(qb[:], ib_h[tt % 2][:]).then_inc(vv.h, 1)

                def evac(tt):
                    e.wait_ge(mm.h, c["mm0"] + tt + 1)
                    if seg > 0:
                        e.wait_ge(ldh.h, c["hld0"] + c["per_tt"] * tt + 32)
                    if last:
                        e.wait_ge(ldh.h, c["hld0"] + c["per_tt"] * (tt + 1))
                    o = obuf[tt % 2]
                    if not last:
                        if tt >= 2:
                            e.wait_ge(stp.h, c["stp0"] + 16 * (tt - 1))
                        for ds in range(NDS):
                            if seg == 0:
                                e.tensor_copy(o[:, ds * DS:(ds + 1) * DS],
                                              pbank(ds, DS))
                            else:
                                e.tensor_tensor(o[:, ds * DS:(ds + 1) * DS],
                                                pbank(ds, DS),
                                                pbuf[tt % 2][:, ds * DS:(ds + 1) * DS],
                                                OP.add)
                        e.engine_nop().then_inc(hv.h, 1)
                        return
                    if tt >= 2:
                        e.wait_ge(sto.h, SO3 + 16 * (tt - 1))
                    for ds in range(NDS):
                        e.tensor_tensor(o[:, ds * DS:(ds + 1) * DS],
                                        pbank(ds, DS),
                                        pbuf[tt % 2][:, ds * DS:(ds + 1) * DS],
                                        OP.add)
                    e.engine_nop().then_inc(hv.h, 1)
                    e.scalar_tensor_tensor(o[:], o[:], sc3_all[:, tt:tt + 1],
                                           rep8k[:], OP.mult, OP.add)
                    e.scalar_tensor_tensor(o[:], ycld[tt % 2][:],
                                           rstd_all[:, tt:tt + 1], o[:],
                                           OP.mult, OP.add)
                    e.reduce_sum(mcol[:], o[:], axis=AX)
                    e.tensor_scalar_mul(mcol[:], mcol[:], 1.0 / D
                                        ).then_inc(hv.h, 1)
                    e.wait_ge(ss.h, c["ss0"] + 3 * tt + 1)
                    e.tensor_scalar_mul(tcol[:], tcol[:], 1.0 / D)
                    e.tensor_tensor(tcol2[:], mcol[:], mcol[:], OP.mult)
                    e.tensor_tensor(tcol[:], tcol[:], tcol2[:],
                                    OP.subtract).then_inc(hv.h, 1)
                    e.wait_ge(ss.h, c["ss0"] + 3 * tt + 2)
                    e.reciprocal(tcol[:], tcol[:])
                    e.tensor_tensor(tcol2[:], mcol[:], tcol[:], OP.mult)
                    e.tensor_scalar_mul(tcol2[:], tcol2[:], -1.0
                                        ).then_inc(hv.h, 1)

                for tt in range(TT):
                    q(tt)
                    if tt >= 1:
                        evac(tt - 1)
                evac(TT - 1)
            em("vector", p3_vec)
            vv.add(TT)
            hv.add(c["evs"] * TT)

            def p3_mm(e, seg=seg, c=c):
                e.wait_ge(trq.h, TR30 + 16 * (seg + 1))
                e.wait_ge(ldc.h, LDC30 + 16 * (seg + 1))
                lo_b, up_b = owhh[OWL[seg]], owhh[OWU[seg]]
                for tt in range(TT):
                    e.wait_ge(vv.h, c["qv0"] + tt + 1)
                    if tt >= 1:
                        e.wait_ge(hv.h, c["ev0"] + c["evs"] * (tt - 1) + 1)
                    for hcl in range(HCS):
                        hb_ = lo_b if hcl < 8 else up_b
                        hc8 = hcl % 8
                        for ds in range(NDS):
                            i_ = e.matmul(
                                pbank(ds, DS),
                                qhT[tt % 2][:, hcl * P:(hcl + 1) * P],
                                hb_[:, hc8 * D + ds * DS:hc8 * D + (ds + 1) * DS],
                                start=(hcl == 0), stop=(hcl == HCS - 1))
                    i_.then_inc(mm.h, 1)
            em("tensor", p3_mm)
            c["mm_end"] = mm.add(TT)

            if not last:
                def p3e_gp(e, seg=seg, c=c):
                    for tt in range(TT):
                        e.wait_ge(hv.h, c["ev0"] + tt + 1)
                        e.dma_start(part_sp.ap()[tt * P:(tt + 1) * P, :],
                                    obuf[tt % 2][:]).then_inc(stp.h, 16)
                em("gpsimd", p3e_gp)
                c["stp_end"] = stp.add(16 * TT)
            else:
                SO3 = sto.val

                def p3f_scal(e, c=c):
                    for tt in range(TT):
                        e.wait_ge(hv.h, c["ev0"] + 4 * tt + 2)
                        e.activation(sq_scr[:], obuf[tt % 2][:], AF.Square,
                                     accum_out=tcol[:]).then_inc(ss.h, 1)
                        e.wait_ge(hv.h, c["ev0"] + 4 * tt + 3)
                        e.activation(tcol[:], tcol[:], AF.Sqrt,
                                     bias=eps_col[:]).then_inc(ss.h, 1)
                        e.wait_ge(hv.h, c["ev0"] + 4 * tt + 4)
                        e.activation(obuf[tt % 2][:], obuf[tt % 2][:],
                                     AF.Identity, bias=tcol2[:],
                                     scale=tcol[:]).then_inc(ss.h, 1)
                em("scalar", p3f_scal)
                ss.add(3 * TT)

                def p3f_gp(e, c=c):
                    for tt in range(TT):
                        e.wait_ge(ss.h, c["ss0"] + 3 * tt + 3)
                        e.dma_start(out_hbm.ap()[tt * P:(tt + 1) * P, :],
                                    obuf[tt % 2][:]).then_inc(sto.h, 16)
                    e.wait_ge(sto.h, SO3 + 16 * TT)
                em("gpsimd", p3f_gp)
                sto.add(16 * TT)

        @block.sync
        def _(e):
            for fn in prog["sync"]:
                fn(e)

        @block.scalar
        def _(e):
            g = ScalGuard(e, sems["sb"], sflush)
            for fn in prog["scalar"]:
                fn(g)

        @block.vector
        def _(e):
            g = VecGuard(e, sems["vb"])
            for fn in prog["vector"]:
                fn(g)

        @block.tensor
        def _(e):
            for fn in prog["tensor"]:
                fn(e)

        @block.gpsimd
        def _(e):
            for fn in prog["gpsimd"]:
                fn(e)

    return nc


def _prep_inputs(L0_attn_x, L0_attn_B, L0_attn_C, L0_attn_D,
                 L0_ffn_gate_w, L0_ffn_gate_b, L0_ffn_val_w, L0_ffn_val_b,
                 L0_ffn_out_w, L0_ffn_out_b, n_cores=8):
    x = np.ascontiguousarray(np.asarray(L0_attn_x, dtype=np.float32)
                             ).reshape(-1, np.asarray(L0_attn_x).shape[-1])
    T_total, D = x.shape
    T = T_total // n_cores
    H = np.asarray(L0_ffn_gate_w).shape[0]
    NS = np.asarray(L0_attn_B).shape[0]
    HC = H // P

    common = {
        "bw": np.ascontiguousarray(L0_attn_B, dtype=np.float32),
        "cw": np.ascontiguousarray(L0_attn_C, dtype=np.float32),
        "dv": np.ascontiguousarray(L0_attn_D, dtype=np.float32).reshape(1, D),
        "gw": np.ascontiguousarray(L0_ffn_gate_w, dtype=np.float32),
        "vw": np.ascontiguousarray(L0_ffn_val_w, dtype=np.float32),
        "ow": np.ascontiguousarray(L0_ffn_out_w, dtype=np.float32),
        "gb": np.ascontiguousarray(
            np.asarray(L0_ffn_gate_b, dtype=np.float32).reshape(HC, P).T),
        "vb": np.ascontiguousarray(
            np.asarray(L0_ffn_val_b, dtype=np.float32).reshape(HC, P).T),
        "ob": np.ascontiguousarray(L0_ffn_out_b, dtype=np.float32).reshape(1, D),
        "idm": np.eye(P, dtype=np.float32),
    }
    in_maps = []
    for c in range(n_cores):
        m = dict(common)
        m["x"] = np.ascontiguousarray(x[c * T:(c + 1) * T])
        in_maps.append(m)
    return in_maps, (T, D, H, NS)


def kernel(L0_attn_x, L0_attn_B, L0_attn_C, L0_attn_D,
           L0_ffn_gate_w, L0_ffn_gate_b, L0_ffn_val_w, L0_ffn_val_b,
           L0_ffn_out_w, L0_ffn_out_b):
    n_cores = 8
    in_maps, (T, D, H, NS) = _prep_inputs(
        L0_attn_x, L0_attn_B, L0_attn_C, L0_attn_D,
        L0_ffn_gate_w, L0_ffn_gate_b, L0_ffn_val_w, L0_ffn_val_b,
        L0_ffn_out_w, L0_ffn_out_b, n_cores)
    nc = build_nc(T, D, H, NS)
    res = run_bass_kernel_spmd(nc, in_maps, core_ids=list(range(n_cores)))
    out = np.concatenate([r["out"] for r in res.results], axis=0)
    return out.reshape(np.asarray(L0_attn_x).shape).astype(np.float32)
